# version 04
# baseline (speedup 1.0000x reference)
import numpy as np
import ml_dtypes
from contextlib import ExitStack
import concourse.bass as bass
import concourse.mybir as mybir
from concourse.bass_utils import run_bass_kernel_spmd

F32 = mybir.dt.float32
BF16 = mybir.dt.bfloat16
AF = mybir.ActivationFunctionType
ALU = mybir.AluOpType
AX = mybir.AxisListType
BF = ml_dtypes.bfloat16

NC = 8
D = 1024
DI = 2048
DS = 16
TCORE = 2048
HALO = 128
TA = TCORE + HALO
LN_EPS = 1e-5


def f32c(a):
    return np.ascontiguousarray(a, dtype=np.float32)


def chunks(total, step, base=0):
    out = []
    o = base
    while o < base + total:
        n = min(step, base + total - o)
        out.append((o, n))
        o += n
    return out



class EW:

    def __init__(self, eng, sem, autosync=True):
        self.e = eng
        self.sem = sem
        self.n = 0
        self.autosync = autosync

    def wait(self, other, val=None):
        self.e.wait_ge(other.sem, other.n if val is None else val)

    def wait_sem(self, sem, val):
        self.e.wait_ge(sem, val)

    def raw(self):
        return self.e

    def __getattr__(self, name):
        fn = getattr(self.e, name)
        if not self.autosync or name in ("wait_ge", "nop", "sem_inc"):
            return fn

        def em(*a, **k):
            if self.n > 0:
                self.e.wait_ge(self.sem, self.n)
            inst = fn(*a, **k)
            inst.then_inc(self.sem, 1)
            self.n += 1
            return inst
        return em

    def inc_last(self, inst):
        inst.then_inc(self.sem, 1)
        self.n += 1
        return inst


class ConvertPool:

    def __init__(self, SY, GP, stage_f, stage_b, ld_sems, st_sems, pairs):
        self.SY, self.GP = SY, GP
        self.sf, self.sb = stage_f, stage_b
        self.ld, self.st = ld_sems, st_sems
        self.nld = [0, 0]
        self.nst = [0, 0]
        self.g_after = {}
        C = stage_f[0].shape[1]
        self.work = []
        for s, d in pairs:
            R, Cs = s.shape
            assert Cs == C and R % 128 == 0, (s.shape, C)
            for r in range(R // 128):
                self.work.append((s[128 * r:128 * (r + 1), :], d[128 * r:128 * (r + 1), :]))
        self.i = 0

    def step(self, n=1):
        for _ in range(n):
            if self.i >= len(self.work):
                return
            sap, dap = self.work[self.i]
            b = self.i % 2
            SY, GP = self.SY, self.GP
            if b in self.g_after:
                SY.wait_ge(GP.sem, self.g_after[b])
            SY.dma_start(out=self.sf[b][:], in_=sap).then_inc(self.ld[b], 16)
            self.nld[b] += 16
            GP.wait_sem(self.ld[b], self.nld[b])
            if self.nst[b]:
                GP.wait_sem(self.st[b], self.nst[b])
            GP.tensor_copy(out=self.sb[b][:], in_=self.sf[b][:])
            self.g_after[b] = GP.n
            SY.wait_ge(GP.sem, GP.n)
            SY.dma_start(out=dap, in_=self.sb[b][:]).then_inc(self.st[b], 16)
            self.nst[b] += 16
            self.i += 1

    def drain(self):
        self.step(len(self.work) - self.i)

    def finish(self):
        self.SY.wait_ge(self.st[0], self.nst[0])
        self.SY.wait_ge(self.st[1], self.nst[1])


def phase_barrier(engines, wrappers, dma_sems):
    for E in engines:
        for w in wrappers:
            E.wait_ge(w.sem, w.n)
        for sem, tot in dma_sems:
            E.wait_ge(sem, tot)


def build_A():
    nc = bass.Bass("TRN2", target_bir_lowering=False, debug=False)
    xi = nc.dram_tensor("xi", [TA, D], F32, kind="ExternalInput").ap()
    ipw = nc.dram_tensor("ipw", [D, 2 * DI], F32, kind="ExternalInput").ap()
    convw = nc.dram_tensor("convw", [DI, 4], F32, kind="ExternalInput").ap()
    convb = nc.dram_tensor("convb", [DI, 1], F32, kind="ExternalInput").ap()
    xpw = nc.dram_tensor("xpw", [DI, 96], F32, kind="ExternalInput").ap()
    dtw = nc.dram_tensor("dtw", [64, DI], F32, kind="ExternalInput").ap()
    dtb = nc.dram_tensor("dtb", [DI, 1], F32, kind="ExternalInput").ap()
    wgate = nc.dram_tensor("wgate", [D, 8], F32, kind="ExternalInput").ap()

    o_xnbf = nc.dram_tensor("o_xnbf", [D, TCORE], BF16, kind="ExternalOutput").ap()
    o_xbs = nc.dram_tensor("o_xbs", [DI, TCORE], F32, kind="ExternalOutput").ap()
    o_sz = nc.dram_tensor("o_sz", [DI, TCORE], F32, kind="ExternalOutput").ap()
    o_dt = nc.dram_tensor("o_dt", [DI, TCORE], F32, kind="ExternalOutput").ap()
    o_pbc = nc.dram_tensor("o_pbc", [32, TCORE], F32, kind="ExternalOutput").ap()
    o_wmoe = nc.dram_tensor("o_wmoe", [TCORE, 8], F32, kind="ExternalOutput").ap()
    w1f = nc.dram_tensor("w1f", [8 * D, DI], F32, kind="ExternalInput").ap()
    w3f = nc.dram_tensor("w3f", [8 * D, DI], F32, kind="ExternalInput").ap()
    w2f = nc.dram_tensor("w2f", [8 * DI, D], F32, kind="ExternalInput").ap()
    opwf = nc.dram_tensor("opwf", [DI, D], F32, kind="ExternalInput").ap()
    o_w1b = nc.dram_tensor("o_w1b", [8 * D, DI], BF16, kind="ExternalOutput").ap()
    o_w3b = nc.dram_tensor("o_w3b", [8 * D, DI], BF16, kind="ExternalOutput").ap()
    o_w2b = nc.dram_tensor("o_w2b", [8 * DI, D], BF16, kind="ExternalOutput").ap()
    o_opwb = nc.dram_tensor("o_opwb", [DI, D], BF16, kind="ExternalOutput").ap()

    NT = TA // 128
    CH5 = chunks(TA, 512)
    CT4 = chunks(TCORE, 512)

    with ExitStack() as ctx:
        e = ctx.enter_context
        ident_bf_d = nc.inline_tensor(np.eye(128, dtype=BF), name="ident_bf")
        ident_f_d = nc.inline_tensor(np.eye(128, dtype=np.float32), name="ident_f")

        xnT_bf = [e(nc.sbuf_tensor(f"xnT{j}", [128, TA], BF16)) for j in range(8)]
        ident_bf = e(nc.sbuf_tensor("identbf", [128, 128], BF16))
        ident_f = e(nc.sbuf_tensor("identf", [128, 128], F32))
        wgate_sb = [e(nc.sbuf_tensor(f"wg{j}", [128, 8], F32)) for j in range(8)]
        convw_sb = [e(nc.sbuf_tensor(f"cw{m}", [128, 4], F32)) for m in range(16)]
        convb_sb = [e(nc.sbuf_tensor(f"cb{m}", [128, 1], F32)) for m in range(16)]
        dtb_sb = [e(nc.sbuf_tensor(f"dbb{m}", [128, 1], F32)) for m in range(16)]
        xpw_bf = [e(nc.sbuf_tensor(f"xpw{m}", [128, 96], BF16)) for m in range(16)]
        dtw_bf = e(nc.sbuf_tensor("dtw_bf", [64, DI], BF16))
        ipwm = [e(nc.sbuf_tensor(f"ipwm{b}", [128, 8 * 128], BF16)) for b in range(2)]
        xb_raw = e(nc.sbuf_tensor("xb_raw", [128, TA], F32))
        conv_acc = e(nc.sbuf_tensor("conv_acc", [128, TCORE], F32))
        xbs_f = e(nc.sbuf_tensor("xbs_f", [128, TCORE], F32))
        xbs_b = e(nc.sbuf_tensor("xbs_b", [128, TCORE], BF16))
        sz_t = e(nc.sbuf_tensor("sz_t", [128, TCORE], F32))
        projT_bf = e(nc.sbuf_tensor("projT_bf", [64, TCORE], BF16))
        pbc_sb = e(nc.sbuf_tensor("pbc_sb", [32, TCORE], F32))
        dtv = e(nc.sbuf_tensor("dtv", [128, TCORE], F32))
        ab = e(nc.sbuf_tensor("ab", [128, TCORE], F32))
        enl = e(nc.sbuf_tensor("enl", [128, TCORE], F32))
        rl = e(nc.sbuf_tensor("rl", [128, TCORE], F32))
        dt_t = e(nc.sbuf_tensor("dt_t", [128, TCORE], F32))
        lg_t = e(nc.sbuf_tensor("lg_t", [128, 8], F32))
        mx8 = e(nc.sbuf_tensor("mx8", [128, 8], F32))
        sc1 = e(nc.sbuf_tensor("sc1", [128, 4], F32))
        negm1 = e(nc.sbuf_tensor("negm1", [128, 1], F32))
        el = e(nc.sbuf_tensor("el", [128, 8], F32))
        msk = e(nc.sbuf_tensor("msk", [128, 8], F32))
        wtm = e(nc.sbuf_tensor("wtm", [128, 8], F32))
        epsb = e(nc.sbuf_tensor("epsb", [128, 1], F32))
        cstage_f = [e(nc.sbuf_tensor(f"cstagef{b}", [128, DI], F32)) for b in range(2)]
        cstage = [e(nc.sbuf_tensor(f"cstage{b}", [128, DI], BF16)) for b in range(2)]
        ld_cv = [e(nc.semaphore(f"ld_cv{b}")) for b in range(2)]
        st_cv = [e(nc.semaphore(f"st_cv{b}")) for b in range(2)]
        sem_g = e(nc.semaphore("sem_g"))

        ld = e(nc.semaphore("s_ld"))
        ldg = e(nc.semaphore("s_ldg"))
        ldx = [e(nc.semaphore(f"s_ldx{b}")) for b in range(2)]
        ldw = [e(nc.semaphore(f"s_ldw{b}")) for b in range(2)]
        sem_v = e(nc.semaphore("sem_v"))
        sem_s = e(nc.semaphore("sem_s"))
        sem_t = e(nc.semaphore("sem_t"))
        st = e(nc.semaphore("st"))
        st_xbs = e(nc.semaphore("st_xbs"))
        st_sz = e(nc.semaphore("st_sz"))
        st_dt = e(nc.semaphore("st_dt"))
        st_w = e(nc.semaphore("st_w"))

        V = EW(nc.vector, sem_v)
        S = EW(nc.scalar, sem_s)
        T = EW(nc.tensor, sem_t, autosync=False)
        SY = nc.sync
        G_ = nc.gpsimd
        nld = [0]
        nlg = [0]
        nst = [0]
        nldx = [0, 0]
        nldw = [0, 0]
        nst_b = dict(xbs=0, sz=0, dt=0, w=0)

        V.memset(epsb[:], LN_EPS)
        GP = EW(nc.gpsimd, sem_g)
        cp = ConvertPool(SY, GP, cstage_f, cstage, ld_cv, st_cv,
                         [(w1f, o_w1b), (w3f, o_w3b),
                          (w2f.rearrange("(a b) c -> a (b c)", b=2),
                           o_w2b.rearrange("(a b) c -> a (b c)", b=2)),
                          (opwf.rearrange("(a b) c -> a (b c)", b=2),
                           o_opwb.rearrange("(a b) c -> a (b c)", b=2))])

        SY.dma_start(out=ident_bf[:], in_=ident_bf_d.ap()).then_inc(ld, 16)
        SY.dma_start(out=ident_f[:], in_=ident_f_d.ap()).then_inc(ld, 16)
        for j in range(8):
            SY.dma_start(out=wgate_sb[j][:], in_=wgate[128 * j:128 * (j + 1), :]).then_inc(ld, 16)
        for m in range(16):
            SY.dma_start(out=convw_sb[m][:], in_=convw[128 * m:128 * (m + 1), :]).then_inc(ld, 16)
            SY.dma_start(out=convb_sb[m][:], in_=convb[128 * m:128 * (m + 1), :]).then_inc(ld, 16)
            SY.dma_start(out=dtb_sb[m][:], in_=dtb[128 * m:128 * (m + 1), :]).then_inc(ld, 16)
        nld[0] += 16 * 58
        LD_MISC = nld[0]
        for m in range(16):
            G_.dma_start(out=xpw_bf[m][:], in_=xpw[128 * m:128 * (m + 1), :]).then_inc(ldg, 16)
        G_.dma_start(out=dtw_bf[:], in_=dtw).then_inc(ldg, 16)
        nlg[0] += 16 * 17
        LD_W2 = nlg[0]

        with ExitStack() as tctx:
            te = tctx.enter_context
            xtile = [te(nc.sbuf_tensor(f"xtile{b}", [128, D], F32)) for b in range(2)]
            xnbf_t = [te(nc.sbuf_tensor(f"xnbf{b}", [128, D], BF16)) for b in range(2)]
            sqt = te(nc.sbuf_tensor("sqt", [128, D], F32))
            muv = te(nc.sbuf_tensor("muv", [128, 1], F32))
            ssv = te(nc.sbuf_tensor("ssv", [128, 1], F32))
            rstd = te(nc.sbuf_tensor("rstd", [128, 1], F32))
            tf32 = [te(nc.sbuf_tensor(f"tf32{b}", [128, 128], F32)) for b in range(2)]
            ps_tp = [te(nc.psum_tensor(f"ps_tp{b}", [128, 128], BF16)) for b in range(2)]
            ps_tf = [te(nc.psum_tensor(f"ps_tf{b}", [128, 128], F32)) for b in range(2)]
            ps_l = te(nc.psum_tensor("ps_l", [128, 8], F32))

            t_done_after = []
            ev_after_tp = {}
            glob_tp = [0]
            for i in range(NT):
                b = i % 2
                if i >= 2:
                    SY.wait_ge(sem_t, t_done_after[i - 2])
                SY.dma_start(out=xtile[b][:], in_=xi[128 * i:128 * (i + 1), :]).then_inc(ldx[b], 16)
                nldx[b] += 16
                V.wait_sem(ldx[b], nldx[b])
                V.reduce_sum(out=muv[:], in_=xtile[b][:], axis=AX.X)
                V.tensor_scalar(out=muv[:], in0=muv[:], scalar1=1.0 / D, scalar2=None, op0=ALU.mult)
                V.tensor_scalar(out=xtile[b][:], in0=xtile[b][:], scalar1=muv[:, 0:1],
                                scalar2=None, op0=ALU.subtract)
                S.wait(V)
                S.activation(out=sqt[:], in_=xtile[b][:], func=AF.Square, accum_out=ssv[:])
                S.activation(out=rstd[:], in_=ssv[:], func=AF.Ln, scale=1.0 / D, bias=epsb[:, 0:1])
                S.activation(out=rstd[:], in_=rstd[:], func=AF.Exp, scale=-0.5)
                V.wait(S)
                V.tensor_scalar(out=xtile[b][:], in0=xtile[b][:], scalar1=rstd[:, 0:1],
                                scalar2=None, op0=ALU.mult)
                V.tensor_copy(out=xnbf_t[b][:], in_=xtile[b][:])
                T.wait(V)
                if i == 0:
                    T.wait_sem(ld, LD_MISC)
                for j in range(8):
                    k = glob_tp[0]
                    if k >= 2:
                        V_after = ev_after_tp[k - 2]
                        T.wait(V, V_after)
                    T.inc_last(T.transpose(ps_tp[j % 2][:], xnbf_t[b][:, 128 * j:128 * (j + 1)],
                                           ident_bf[:]))
                    V.wait(T)
                    V.tensor_copy(out=xnT_bf[j][:, 128 * i:128 * (i + 1)], in_=ps_tp[j % 2][:])
                    ev_after_tp[k] = V.n
                    glob_tp[0] += 1
                if i >= 1:
                    for j in range(8):
                        k = glob_tp[0]
                        T.wait(V, ev_after_tp[k - 2])
                        T.inc_last(T.transpose(ps_tf[j % 2][:], xtile[b][:, 128 * j:128 * (j + 1)],
                                               ident_f[:]))
                        V.wait(T)
                        V.tensor_copy(out=tf32[j % 2][:], in_=ps_tf[j % 2][:])
                        ev_after_tp[k] = V.n
                        glob_tp[0] += 1
                        T.wait(V)
                        mm = T.matmul(ps_l[:], tf32[j % 2][:], wgate_sb[j][:],
                                      start=(j == 0), stop=(j == 7))
                        if j == 7:
                            T.inc_last(mm)
                    V.wait(T)
                    V.tensor_copy(out=lg_t[:], in_=ps_l[:])
                    V.max(out=mx8[:], in_=lg_t[:])
                    V.tensor_tensor(out=sc1[:, 0:1], in0=mx8[:, 1:2], in1=mx8[:, 0:1],
                                    op=ALU.subtract)
                    V.tensor_scalar(out=negm1[:], in0=mx8[:, 0:1], scalar1=-1.0,
                                    scalar2=None, op0=ALU.mult)
                    S.wait(V)
                    S.activation(out=sc1[:, 1:2], in_=sc1[:, 0:1], func=AF.Exp)
                    S.activation(out=el[:], in_=lg_t[:], func=AF.Exp, bias=negm1[:, 0:1])
                    V.wait(S)
                    V.tensor_scalar(out=sc1[:, 2:3], in0=sc1[:, 1:2], scalar1=1.0,
                                    scalar2=None, op0=ALU.add)
                    V.reciprocal(out=sc1[:, 3:4], in_=sc1[:, 2:3])
                    V.tensor_scalar(out=msk[:], in0=lg_t[:], scalar1=mx8[:, 1:2],
                                    scalar2=None, op0=ALU.is_ge)
                    V.tensor_scalar(out=wtm[:], in0=el[:], scalar1=sc1[:, 3:4],
                                    scalar2=None, op0=ALU.mult)
                    V.tensor_tensor(out=wtm[:], in0=wtm[:], in1=msk[:], op=ALU.mult)
                    SY.wait_ge(sem_v, V.n)
                    SY.dma_start(out=o_wmoe[128 * (i - 1):128 * i, :], in_=wtm[:]).then_inc(st_w, 16)
                    nst_b["w"] += 16
                    V.wait_sem(st_w, nst_b["w"])
                t_done_after.append(T.n)
                cp.step(3)
            SY.wait_ge(sem_v, V.n)
            for j in range(8):
                SY.dma_start(out=o_xnbf[128 * j:128 * (j + 1), :],
                             in_=xnT_bf[j][:, HALO:TA]).then_inc(st, 16)
                nst[0] += 16

        phase_barrier([nc.vector, nc.scalar, nc.tensor, nc.sync, nc.gpsimd],
                      [V, S, T],
                      [(ld, nld[0]), (ldg, nlg[0]), (ldx[0], nldx[0]), (ldx[1], nldx[1]),
                       (st_w, nst_b["w"]), (st, nst[0])])
        with ExitStack() as tctx:
            te = tctx.enter_context
            ps_ab = [te(nc.psum_tensor(f"ps_ab{b}", [128, 512], F32)) for b in range(2)]
            ps_pj = [te(nc.psum_tensor(f"ps_pj{i}", [96, 512], F32)) for i in range(4)]

            V.wait_sem(ld, LD_MISC)
            S.wait_sem(ld, LD_MISC)
            T.wait_sem(ldg, LD_W2)

            def load_ipw_slice(mcol, buf):
                for kt in range(8):
                    G_.dma_start(out=ipwm[buf][:, 128 * kt:128 * (kt + 1)],
                                 in_=ipw[128 * kt:128 * (kt + 1),
                                         128 * mcol:128 * (mcol + 1)]).then_inc(ldw[buf], 16)
                nldw[buf] += 16 * 8

            ev_after_g = {}
            ipw_ready = {}
            load_ipw_slice(0, 0)
            ipw_ready[0] = (0, nldw[0])
            t_after_m = {}
            gidx = [0]
            for m in range(16):
                nb = (m + 1) % 2
                if m + 1 < 16:
                    if m >= 1:
                        G_.wait_ge(sem_t, t_after_m[m - 1])
                    load_ipw_slice(m + 1, nb)
                    ipw_ready[m + 1] = (nb, nldw[nb])
                for ci, (c0, cl) in enumerate(CH5):
                    g = gidx[0]
                    ps = ps_ab[g % 2]
                    if g >= 2:
                        eng, cnt = ev_after_g[g - 2]
                        T.wait(eng, cnt)
                    if ci == 0:
                        rb, rv = ipw_ready[m]
                        T.wait_sem(ldw[rb], rv)
                    for kt in range(8):
                        mm = T.matmul(ps[:, 0:cl], ipwm[m % 2][:, 128 * kt:128 * (kt + 1)],
                                      xnT_bf[kt][:, c0:c0 + cl], start=(kt == 0), stop=(kt == 7))
                    T.inc_last(mm)
                    V.wait(T)
                    V.tensor_copy(out=xb_raw[:, c0:c0 + cl], in_=ps[:, 0:cl])
                    ev_after_g[g] = (V, V.n)
                    gidx[0] += 1
                V.tensor_scalar(out=conv_acc[:], in0=xb_raw[:, 125:125 + TCORE],
                                scalar1=convw_sb[m][:, 0:1], scalar2=None, op0=ALU.mult)
                for j in range(1, 4):
                    V.scalar_tensor_tensor(out=conv_acc[:], in0=xb_raw[:, 125 + j:125 + j + TCORE],
                                           scalar=convw_sb[m][:, j:j + 1], in1=conv_acc[:],
                                           op0=ALU.mult, op1=ALU.add)
                S.wait(V)
                S.wait_sem(st_xbs, nst_b["xbs"])
                S.activation(out=xbs_f[:], in_=conv_acc[:], func=AF.Silu,
                             bias=convb_sb[m][:, 0:1])
                SY.wait_ge(sem_s, S.n)
                SY.dma_start(out=o_xbs[128 * m:128 * (m + 1), :], in_=xbs_f[:]).then_inc(st_xbs, 16)
                nst_b["xbs"] += 16
                V.wait(S)
                if m >= 1:
                    V.wait(T, t_after_m[m - 1])
                V.tensor_copy(out=xbs_b[:], in_=xbs_f[:])
                T.wait(V)
                for ci, (c0, cl) in enumerate(CT4):
                    mm = T.matmul(ps_pj[ci][:], xpw_bf[m][:], xbs_b[:, c0:c0 + cl],
                                  start=(m == 0), stop=(m == 15))
                T.inc_last(mm)
                t_after_m[m] = T.n
                cp.step(3)
            for m in range(16):
                nb = m % 2
                G_.wait_ge(sem_t, T.n)
                load_ipw_slice(16 + m, nb)
                zr = nldw[nb]
                for ci, (c0, cl) in enumerate(CT4):
                    g = gidx[0]
                    ps = ps_ab[g % 2]
                    eng, cnt = ev_after_g[g - 2]
                    T.wait(eng, cnt)
                    if ci == 0:
                        T.wait_sem(ldw[nb], zr)
                    for kt in range(8):
                        mm = T.matmul(ps[:, 0:cl], ipwm[nb][:, 128 * kt:128 * (kt + 1)],
                                      xnT_bf[kt][:, HALO + c0:HALO + c0 + cl],
                                      start=(kt == 0), stop=(kt == 7))
                    T.inc_last(mm)
                    S.wait(T)
                    if ci == 0:
                        S.wait_sem(st_sz, nst_b["sz"])
                    S.activation(out=sz_t[:, c0:c0 + cl], in_=ps[:, 0:cl], func=AF.Silu)
                    ev_after_g[g] = (S, S.n)
                    gidx[0] += 1
                SY.wait_ge(sem_s, S.n)
                SY.dma_start(out=o_sz[128 * m:128 * (m + 1), :], in_=sz_t[:]).then_inc(st_sz, 16)
                nst_b["sz"] += 16

            V.wait(T)
            for ci, (c0, cl) in enumerate(CT4):
                V.tensor_copy(out=projT_bf[:, c0:c0 + cl], in_=ps_pj[ci][0:64, :])
                V.tensor_copy(out=pbc_sb[0:32, c0:c0 + cl], in_=ps_pj[ci][64:96, :])
            SY.wait_ge(sem_v, V.n)
            SY.dma_start(out=o_pbc[:, :], in_=pbc_sb[0:32, :]).then_inc(st, 16)
            nst[0] += 16
            T.wait(V)
            for m in range(16):
                for ci, (c0, cl) in enumerate(CT4):
                    g = gidx[0]
                    ps = ps_ab[g % 2]
                    eng, cnt = ev_after_g[g - 2]
                    T.wait(eng, cnt)
                    T.inc_last(T.matmul(ps[:, 0:cl], dtw_bf[:, 128 * m:128 * (m + 1)],
                                        projT_bf[:, c0:c0 + cl], start=True, stop=True))
                    V.wait(T)
                    V.tensor_scalar(out=dtv[:, c0:c0 + cl], in0=ps[:, 0:cl],
                                    scalar1=dtb_sb[m][:, 0:1], scalar2=None, op0=ALU.add)
                    ev_after_g[g] = (V, V.n)
                    gidx[0] += 1
                V.tensor_scalar(out=rl[:], in0=dtv[:], scalar1=-1.0, scalar2=None, op0=ALU.mult)
                V.tensor_tensor(out=ab[:], in0=dtv[:], in1=rl[:], op=ALU.min)
                V.tensor_scalar(out=rl[:], in0=dtv[:], scalar1=0.0, scalar2=None, op0=ALU.max)
                S.wait(V)
                S.activation(out=enl[:], in_=ab[:], func=AF.Exp)
                S.activation(out=enl[:], in_=enl[:], func=AF.Ln, bias=1.0)
                V.wait(S)
                V.wait_sem(st_dt, nst_b["dt"])
                V.tensor_tensor(out=dt_t[:], in0=rl[:], in1=enl[:], op=ALU.add)
                SY.wait_ge(sem_v, V.n)
                SY.dma_start(out=o_dt[128 * m:128 * (m + 1), :], in_=dt_t[:]).then_inc(st_dt, 16)
                nst_b["dt"] += 16

        cp.drain()
        cp.finish()
        SY.wait_ge(st, nst[0])
        SY.wait_ge(st_xbs, nst_b["xbs"])
        SY.wait_ge(st_sz, nst_b["sz"])
        SY.wait_ge(st_dt, nst_b["dt"])
        SY.wait_ge(st_w, nst_b["w"])
        G_.wait_ge(st, nst[0])
    return nc


def build_D():
    nc = bass.Bass("TRN2", target_bir_lowering=False, debug=False)
    TB = 8192
    CH = 512
    NCH = TB // CH
    dtT = nc.dram_tensor("dtT", [512, TB], F32, kind="ExternalInput").ap()
    xbsT = nc.dram_tensor("xbsT", [512, TB], F32, kind="ExternalInput").ap()
    pbcT = nc.dram_tensor("pbcT", [32, TB], F32, kind="ExternalInput").ap()
    alog = nc.dram_tensor("alog", [512, 16], F32, kind="ExternalInput").ap()
    dpar = nc.dram_tensor("dpar", [512, 1], F32, kind="ExternalInput").ap()
    o_y = nc.dram_tensor("o_y", [512, TB], F32, kind="ExternalOutput").ap()

    with ExitStack() as ctx:
        e = ctx.enter_context
        dt_c = [[e(nc.sbuf_tensor(f"dt{p}_{d}", [128, CH], F32)) for d in range(4)] for p in range(2)]
        xb_c = [[e(nc.sbuf_tensor(f"xb{p}_{d}", [128, CH], F32)) for d in range(4)] for p in range(2)]
        pb_c = [e(nc.sbuf_tensor(f"pb{p}", [1, 32 * CH], F32)) for p in range(2)]
        a_sb = [e(nc.sbuf_tensor(f"a{d}", [128, 16], F32)) for d in range(4)]
        dp_sb = [e(nc.sbuf_tensor(f"dp{d}", [128, 1], F32)) for d in range(4)]
        state = [e(nc.sbuf_tensor(f"stt{d}", [128, 16], F32)) for d in range(4)]
        ones1 = e(nc.sbuf_tensor("ones1", [1, 128], F32))
        dtu_t = e(nc.sbuf_tensor("dtu_t", [128, CH], F32))
        da_sb = [e(nc.sbuf_tensor(f"da{b}", [128, CH], F32)) for b in range(2)]
        dbu = e(nc.sbuf_tensor("dbu", [128, CH], F32))
        sct = e(nc.sbuf_tensor("sct", [128, CH], F32))
        yterm = e(nc.sbuf_tensor("yterm", [128, CH], F32))
        y_acc = [e(nc.sbuf_tensor(f"ya{d}", [128, CH], F32)) for d in range(4)]

        ps_bc = [e(nc.psum_tensor(f"ps_bc{i}", [128, CH], F32)) for i in range(4)]

        ld = e(nc.semaphore("ld"))
        ldc = [e(nc.semaphore(f"ldc{p}")) for p in range(2)]
        sem_v = e(nc.semaphore("sem_v"))
        sem_s = e(nc.semaphore("sem_s"))
        sem_t = e(nc.semaphore("sem_t"))
        st_y = [e(nc.semaphore(f"st_y{d}")) for d in range(4)]

        V = EW(nc.vector, sem_v)
        S = EW(nc.scalar, sem_s)
        T = EW(nc.tensor, sem_t, autosync=False)
        SY = nc.sync
        nld = 0
        nldc = [0, 0]
        nst = [0]

        V.memset(ones1[:], 1.0)
        for d in range(4):
            V.memset(state[d][:], 0.0)

        for d in range(4):
            SY.dma_start(out=a_sb[d][:], in_=alog[128 * d:128 * (d + 1), :]).then_inc(ld, 16)
            SY.dma_start(out=dp_sb[d][:], in_=dpar[128 * d:128 * (d + 1), :]).then_inc(ld, 16)
        nld += 16 * 8
        S.wait_sem(ld, nld)
        for d in range(4):
            S.activation(out=a_sb[d][:], in_=a_sb[d][:], func=AF.Exp, scale=1.0)
        for d in range(4):
            S.mul(out=a_sb[d][:], in_=a_sb[d][:], mul=-1.0)

        def load_chunk(ch, p):
            c0 = ch * CH
            for d in range(4):
                SY.dma_start(out=dt_c[p][d][:], in_=dtT[128 * d:128 * (d + 1), c0:c0 + CH]).then_inc(ldc[p], 16)
                SY.dma_start(out=xb_c[p][d][:], in_=xbsT[128 * d:128 * (d + 1), c0:c0 + CH]).then_inc(ldc[p], 16)
            SY.dma_start(out=pb_c[p][0:1, :].rearrange("o (r c) -> o r c", r=32),
                         in_=pbcT[:, c0:c0 + CH].unsqueeze(0)).then_inc(ldc[p], 16)
            nldc[p] += 16 * 9

        v_after_chunk = {}
        vas = {}
        load_chunk(0, 0)
        rdy = {0: nldc[0]}
        bc_reader = {}
        for ch in range(NCH):
            p = ch % 2
            if ch + 1 < NCH:
                if ch >= 1:
                    SY.wait_ge(sem_v, v_after_chunk[ch - 1])
                load_chunk(ch + 1, (ch + 1) % 2)
                rdy[ch + 1] = nldc[(ch + 1) % 2]
            V.wait_sem(ldc[p], rdy[ch])
            S.wait_sem(ldc[p], rdy[ch])
            T.wait_sem(ldc[p], rdy[ch])
            if ch == 0:
                T.wait(V)
            for d in range(4):
                if ch >= 1:
                    V.wait_sem(st_y[d], 16 * ch)
                V.tensor_mul(out=dtu_t[:], in0=dt_c[p][d][:], in1=xb_c[p][d][:])
                for s in range(16):
                    k = (d * 16 + s) % 2
                    bb, cb = ps_bc[2 * k], ps_bc[2 * k + 1]
                    for bank, ps_t, row in ((2 * k, bb, s), (2 * k + 1, cb, 16 + s)):
                        if bank in bc_reader:
                            T.wait(V, bc_reader[bank])
                        T.inc_last(T.matmul(ps_t[:], ones1[0:1, :],
                                            pb_c[p][0:1, row * CH:(row + 1) * CH],
                                            start=True, stop=True))
                    db = s % 2
                    if db in vas:
                        S.wait(V, vas[db])
                    S.activation(out=da_sb[db][:], in_=dt_c[p][d][:], func=AF.Exp,
                                 scale=a_sb[d][:, s:s + 1])
                    V.wait(T)
                    V.wait(S)
                    V.tensor_mul(out=dbu[:], in0=dtu_t[:], in1=bb[:])
                    V.tensor_tensor_scan(out=sct[:], data0=da_sb[db][:], data1=dbu[:],
                                         initial=state[d][:, s:s + 1], op0=ALU.mult,
                                         op1=ALU.add)
                    vas[db] = V.n
                    V.tensor_copy(out=state[d][:, s:s + 1], in_=sct[:, CH - 1:CH])
                    if s == 0:
                        V.tensor_mul(out=y_acc[d][:], in0=sct[:], in1=cb[:])
                    else:
                        V.tensor_mul(out=yterm[:], in0=sct[:], in1=cb[:])
                        V.tensor_add(out=y_acc[d][:], in0=y_acc[d][:], in1=yterm[:])
                    bc_reader[2 * k] = V.n
                    bc_reader[2 * k + 1] = V.n
                V.scalar_tensor_tensor(out=y_acc[d][:], in0=xb_c[p][d][:],
                                       scalar=dp_sb[d][:, 0:1], in1=y_acc[d][:],
                                       op0=ALU.mult, op1=ALU.add)
                SY.wait_ge(sem_v, V.n)
                SY.dma_start(out=o_y[128 * d:128 * (d + 1), ch * CH:(ch + 1) * CH],
                             in_=y_acc[d][:]).then_inc(st_y[d], 16)
                nst[0] += 16
            v_after_chunk[ch] = V.n
        for d in range(4):
            SY.wait_ge(st_y[d], 16 * NCH)
    return nc


def run_D(A_outs, inputs):
    dtT = np.concatenate([np.asarray(A_outs[c]["o_dt"]) for c in range(NC)], axis=1)
    xbsT = np.concatenate([np.asarray(A_outs[c]["o_xbs"]) for c in range(NC)], axis=1)
    pbcT = np.concatenate([np.asarray(A_outs[c]["o_pbc"]) for c in range(NC)], axis=1)
    alog = f32c(inputs["L0_attn_A_log"])
    dpar = f32c(inputs["L0_attn_D_param"].reshape(DI, 1))
    in_maps = []
    for c in range(NC):
        b = c // 4
        r0 = 512 * (c % 4)
        cols = slice(8192 * b, 8192 * (b + 1))
        in_maps.append({
            "dtT": f32c(dtT[r0:r0 + 512, cols]),
            "xbsT": f32c(xbsT[r0:r0 + 512, cols]),
            "pbcT": f32c(pbcT[:, cols]),
            "alog": f32c(alog[r0:r0 + 512]),
            "dpar": f32c(dpar[r0:r0 + 512]),
        })
    if "D" not in _A_CACHE:
        _A_CACHE["D"] = build_D()
    res = run_bass_kernel_spmd(_A_CACHE["D"], in_maps, core_ids=list(range(NC)))
    return res.results


def build_M():
    nc = bass.Bass("TRN2", target_bir_lowering=False, debug=False)
    yT = nc.dram_tensor("yT", [DI, TCORE], F32, kind="ExternalInput").ap()
    szT = nc.dram_tensor("szT", [DI, TCORE], F32, kind="ExternalInput").ap()
    xnbfT = nc.dram_tensor("xnbfT", [D, TCORE], BF16, kind="ExternalInput").ap()
    wmoe = nc.dram_tensor("wmoe", [TCORE, 8], F32, kind="ExternalInput").ap()
    x_tm = nc.dram_tensor("x_tm", [TCORE, D], F32, kind="ExternalInput").ap()
    opw = nc.dram_tensor("opw", [DI, D], BF16, kind="ExternalInput").ap()
    w1 = nc.dram_tensor("w1", [8 * D, DI], BF16, kind="ExternalInput").ap()
    w3 = nc.dram_tensor("w3", [8 * D, DI], BF16, kind="ExternalInput").ap()
    w2 = nc.dram_tensor("w2", [8 * DI, D], BF16, kind="ExternalInput").ap()
    lng = nc.dram_tensor("lng", [1, D], F32, kind="ExternalInput").ap()
    lnb = nc.dram_tensor("lnb", [1, D], F32, kind="ExternalInput").ap()

    o_x1 = nc.dram_tensor("o_x1", [TCORE, D], F32, kind="ExternalOutput").ap()
    o_xn1 = nc.dram_tensor("o_xn1", [D, TCORE], BF16, kind="ExternalOutput").ap()
    o_xat = nc.dram_tensor("o_xat", [D, TCORE], BF16, kind="ExternalOutput").ap()

    with ExitStack() as ctx:
        e = ctx.enter_context
        ident_bf_d = nc.inline_tensor(np.eye(128, dtype=BF), name="ident_bf")

        acc = [e(nc.sbuf_tensor(f"acc{t}", [128, D], F32)) for t in range(16)]
        xnbf = [e(nc.sbuf_tensor(f"xnbf{j}", [128, TCORE], BF16)) for j in range(8)]
        wm_sb = [e(nc.sbuf_tensor(f"wm{t}", [128, 8], F32)) for t in range(16)]
        ident_bf = e(nc.sbuf_tensor("identbf", [128, 128], BF16))
        ones1 = e(nc.sbuf_tensor("ones1", [1, 128], F32))

        epsb = e(nc.sbuf_tensor("epsb", [128, 1], F32))

        ld = e(nc.semaphore("ld"))
        ldg = e(nc.semaphore("ldg"))
        ldy = [e(nc.semaphore(f"ldy{p}")) for p in range(2)]
        ldw_s = [e(nc.semaphore(f"ldws{p}")) for p in range(2)]
        ldw2 = [e(nc.semaphore(f"ldw2{p}")) for p in range(2)]
        sem_v = e(nc.semaphore("sem_v"))
        sem_s = e(nc.semaphore("sem_s"))
        sem_t = e(nc.semaphore("sem_t"))
        st = e(nc.semaphore("st"))

        V = EW(nc.vector, sem_v)
        S = EW(nc.scalar, sem_s)
        T = EW(nc.tensor, sem_t, autosync=False)
        SY = nc.sync
        G_ = nc.gpsimd
        nld = [0]
        nlg = [0]
        nldy = [0, 0]
        nldws = [0, 0]
        nldw2 = [0, 0]
        nst = [0]

        V.memset(epsb[:], LN_EPS)
        V.memset(ones1[:], 1.0)

        SY.dma_start(out=ident_bf[:], in_=ident_bf_d.ap()).then_inc(ld, 16)
        for t in range(16):
            SY.dma_start(out=wm_sb[t][:], in_=wmoe[128 * t:128 * (t + 1), :]).then_inc(ld, 16)
            SY.dma_start(out=acc[t][:], in_=x_tm[128 * t:128 * (t + 1), :]).then_inc(ld, 16)
        for j in range(8):
            SY.dma_start(out=xnbf[j][:], in_=xnbfT[128 * j:128 * (j + 1), :]).then_inc(ld, 16)
        nld[0] += 16 * 41
        LD_BASE = nld[0]

        with ExitStack() as c1:
            t1 = c1.enter_context
            yf_bf = [t1(nc.sbuf_tensor(f"yf{k}", [128, TCORE], BF16)) for k in range(8)]
            ybuf = t1(nc.sbuf_tensor("ybuf", [128, TCORE], F32))
            zbuf = t1(nc.sbuf_tensor("zbuf", [128, TCORE], F32))
            opw_bf = [t1(nc.sbuf_tensor(f"opw{k}", [128, D], BF16)) for k in range(16)]
            ps_m = [t1(nc.psum_tensor(f"ps_m{i}", [128, 512], F32)) for i in range(4)]
            for k in range(16):
                SY.dma_start(out=opw_bf[k][:], in_=opw[128 * k:128 * (k + 1), :]).then_inc(ldg, 16)
            nlg[0] += 16 * 16
            t_after_g = {}
            ev_ps = {}
            for g in range(2):
                for k8 in range(8):
                    k = 8 * g + k8
                    SY.wait_ge(sem_v, V.n)
                    if g >= 1:
                        SY.wait_ge(sem_t, t_after_g[g - 1])
                    SY.dma_start(out=ybuf[:], in_=yT[128 * k:128 * (k + 1), :]).then_inc(ldy[0], 16)
                    SY.dma_start(out=zbuf[:], in_=szT[128 * k:128 * (k + 1), :]).then_inc(ldy[0], 16)
                    nldy[0] += 32
                    V.wait_sem(ldy[0], nldy[0])
                    V.tensor_mul(out=yf_bf[k8][:], in0=ybuf[:], in1=zbuf[:])
                if g == 0:
                    T.wait_sem(ldg, nlg[0])
                T.wait(V)
                for ts in range(16):
                    for half in range(2):
                        gi = 2 * ts + half
                        pi = gi % 4
                        if gi >= 4 or g >= 1:
                            eng, cnt = ev_ps[pi]
                            T.wait(eng, cnt)
                        for k8 in range(8):
                            mm = T.matmul(ps_m[pi][:], yf_bf[k8][:, 128 * ts:128 * (ts + 1)],
                                          opw_bf[8 * g + k8][:, 512 * half:512 * (half + 1)],
                                          start=(k8 == 0), stop=(k8 == 7))
                        T.inc_last(mm)
                        V.wait(T)
                        V.wait_sem(ld, LD_BASE)
                        V.tensor_add(out=acc[ts][:, 512 * half:512 * (half + 1)],
                                     in0=acc[ts][:, 512 * half:512 * (half + 1)],
                                     in1=ps_m[pi][:])
                        ev_ps[pi] = (V, V.n)
                t_after_g[g] = T.n

        phase_barrier([nc.vector, nc.scalar, nc.tensor, nc.sync, nc.gpsimd],
                      [V, S, T],
                      [(ld, nld[0]), (ldg, nlg[0]), (ldy[0], nldy[0])])
        with ExitStack() as c3:
            t3 = c3.enter_context
            hc = [t3(nc.sbuf_tensor(f"hc{p}", [128, 16 * 512], BF16)) for p in range(2)]
            w13m = [t3(nc.sbuf_tensor(f"w13m{p}", [128, 2 * 8 * 128], BF16)) for p in range(2)]
            w2_bf = [t3(nc.sbuf_tensor(f"w2bf{k}", [128, D], BF16)) for k in range(16)]
            hstg = [t3(nc.sbuf_tensor(f"hstg{p}", [128, 2048], BF16)) for p in range(2)]
            t1s = t3(nc.sbuf_tensor("t1s", [128, 512], F32))
            ps_h = [t3(nc.psum_tensor(f"ps_h{i}", [128, 512], F32)) for i in range(4)]
            ps_y = [t3(nc.psum_tensor(f"ps_y{i}", [128, 512], F32)) for i in range(4)]
            d_h = [nc.dram_tensor(f"d_h{b}", [DI, TCORE], BF16) for b in range(2)]

            def load_w13(eidx, hm, buf):
                SY.dma_start(out=w13m[buf][:, 0:1024].rearrange("p (k c) -> p k c", k=8),
                             in_=w1[D * eidx:D * (eidx + 1),
                                    128 * hm:128 * (hm + 1)].rearrange(
                                 "(k p) c -> p k c", p=128)).then_inc(ldw_s[buf], 16)
                SY.dma_start(out=w13m[buf][:, 1024:2048].rearrange("p (k c) -> p k c", k=8),
                             in_=w3[D * eidx:D * (eidx + 1),
                                    128 * hm:128 * (hm + 1)].rearrange(
                                 "(k p) c -> p k c", p=128)).then_inc(ldw_s[buf], 16)
                nldws[buf] += 32

            def load_w2(eidx):
                for k in range(16):
                    SY.dma_start(out=w2_bf[k][:],
                                 in_=w2[DI * eidx + 128 * k:DI * eidx + 128 * (k + 1),
                                        :]).then_inc(ldw2[0], 16)
                nldw2[0] += 16 * 16

            ev_h = {}
            ev_y = {}
            st_hd = [t3(nc.semaphore(f"st_hd{p}")) for p in range(2)]
            ldh_y = [t3(nc.semaphore(f"ldh_y{p}")) for p in range(2)]
            nst_hd = [0, 0]
            nldh_y = [0, 0]
            t_after_w13 = {}
            t_after_y = {}
            st_hd_after = {}
            t_after_h = {}

            def emit_h(ei):
                eb = ei % 2
                for hm in range(16):
                    si = 16 * ei + hm
                    sb = si % 2
                    if si >= 2:
                        SY.wait_ge(sem_t, t_after_w13[si - 2])
                    load_w13(ei, hm, sb)
                    wrdy = nldws[sb]
                    T.wait_sem(ldw_s[sb], wrdy)
                    for ci in range(4):
                        pp = ci % 2
                        pa, pb = ps_h[2 * pp], ps_h[2 * pp + 1]
                        if ("h", pp) in ev_h:
                            eng, cnt = ev_h[("h", pp)]
                            T.wait(eng, cnt)
                        for kt in range(8):
                            T.matmul(pa[:], w13m[sb][:, 128 * kt:128 * (kt + 1)],
                                     xnbf[kt][:, 512 * ci:512 * (ci + 1)],
                                     start=(kt == 0), stop=(kt == 7))
                        for kt in range(8):
                            mm = T.matmul(pb[:], w13m[sb][:, 1024 + 128 * kt:1024 + 128 * (kt + 1)],
                                          xnbf[kt][:, 512 * ci:512 * (ci + 1)],
                                          start=(kt == 0), stop=(kt == 7))
                        T.inc_last(mm)
                        S.wait(T)
                        S.wait(V)
                        S.activation(out=t1s[:], in_=pa[:], func=AF.Silu)
                        V.wait(S)
                        if ci == 0:
                            V.wait_sem(st_hd[sb], nst_hd[sb])
                        V.tensor_mul(out=hstg[sb][:, 512 * ci:512 * (ci + 1)],
                                     in0=t1s[:], in1=pb[:])
                        ev_h[("h", pp)] = (V, V.n)
                    t_after_w13[si] = T.n
                    SY.wait_ge(sem_v, V.n)
                    SY.dma_start(out=d_h[eb].ap()[128 * hm:128 * (hm + 1), :],
                                 in_=hstg[sb][:]).then_inc(st_hd[sb], 16)
                    nst_hd[sb] += 16
                t_after_h[ei] = T.n
                st_hd_after[ei] = (nst_hd[0], nst_hd[1])

            def emit_y(ei):
                eb = ei % 2
                T.wait_sem(ldw2[0], w2rdy[ei])
                for ci in range(4):
                    cb = ci % 2
                    if ci >= 2:
                        G_.wait_ge(sem_t, t_after_ci[cb])
                    elif ei >= 1:
                        G_.wait_ge(sem_t, t_after_ci[cb])
                    if ci == 0:
                        a0, a1 = st_hd_after[ei]
                        G_.wait_ge(st_hd[0], a0)
                        G_.wait_ge(st_hd[1], a1)
                    G_.dma_start(out=hc[cb][:].rearrange("p (k c) -> p k c", k=16),
                                 in_=d_h[eb].ap()[:, 512 * ci:512 * (ci + 1)].rearrange(
                                     "(k p) c -> p k c", p=128)).then_inc(ldh_y[cb], 16)
                    nldh_y[cb] += 16
                    T.wait_sem(ldh_y[cb], nldh_y[cb])
                    for tsl in range(4):
                        ts = 4 * ci + tsl
                        for nh in range(2):
                            pj = (2 * tsl + nh) % 4
                            if ("y", pj) in ev_y:
                                eng, cnt = ev_y[("y", pj)]
                                T.wait(eng, cnt)
                            for k in range(16):
                                mm = T.matmul(ps_y[pj][:], hc[cb][:, 512 * k + 128 * tsl:
                                                                  512 * k + 128 * (tsl + 1)],
                                              w2_bf[k][:, 512 * nh:512 * (nh + 1)],
                                              start=(k == 0), stop=(k == 15))
                            T.inc_last(mm)
                            V.wait(T)
                            V.scalar_tensor_tensor(out=acc[ts][:, 512 * nh:512 * (nh + 1)],
                                                   in0=ps_y[pj][:],
                                                   scalar=wm_sb[ts][:, ei:ei + 1],
                                                   in1=acc[ts][:, 512 * nh:512 * (nh + 1)],
                                                   op0=ALU.mult, op1=ALU.add)
                            ev_y[("y", pj)] = (V, V.n)
                    t_after_ci[cb] = T.n
                t_after_y[ei] = T.n

            w2rdy = {}
            t_after_ci = {0: 0, 1: 0}
            for ei in range(9):
                if ei < 8:
                    emit_h(ei)
                if ei >= 1:
                    if ei >= 2:
                        SY.wait_ge(sem_t, t_after_y[ei - 2])
                    load_w2(ei - 1)
                    w2rdy[ei - 1] = nldw2[0]
                    emit_y(ei - 1)

        phase_barrier([nc.vector, nc.scalar, nc.tensor, nc.sync, nc.gpsimd],
                      [V, S, T],
                      [(ldw_s[0], nldws[0]), (ldw_s[1], nldws[1]), (ldw2[0], nldw2[0])])
        with ExitStack() as c4:
            t4 = c4.enter_context
            g_bc = t4(nc.sbuf_tensor("g_bc", [128, D], F32))
            b_bc = t4(nc.sbuf_tensor("b_bc", [128, D], F32))
            gb_row = t4(nc.sbuf_tensor("gb_row", [1, 2 * D], F32))
            SY.dma_start(out=gb_row[0:1, 0:D], in_=lng).then_inc(ld, 16)
            SY.dma_start(out=gb_row[0:1, D:2 * D], in_=lnb).then_inc(ld, 16)
            nld[0] += 32
            xn1T = [t4(nc.sbuf_tensor(f"xn1T{j}", [128, TCORE], BF16)) for j in range(8)]
            xatT = [t4(nc.sbuf_tensor(f"xatT{j}", [128, TCORE], BF16)) for j in range(8)]
            xn1t = t4(nc.sbuf_tensor("xn1t", [128, D], F32))
            xat = t4(nc.sbuf_tensor("xat", [128, D], F32))
            xbf = [t4(nc.sbuf_tensor(f"xbfb{b}", [128, D], BF16)) for b in range(2)]
            sqt = t4(nc.sbuf_tensor("sqt", [128, D], F32))
            muv = t4(nc.sbuf_tensor("muv", [128, 1], F32))
            ssv = t4(nc.sbuf_tensor("ssv", [128, 1], F32))
            rstd = t4(nc.sbuf_tensor("rstd", [128, 1], F32))
            ps_tp = [t4(nc.psum_tensor(f"ps_tp{b}", [128, 128], BF16)) for b in range(2)]
            ps_g = [t4(nc.psum_tensor(f"ps_g{b}", [128, 512], F32)) for b in range(2)]

            T.wait_sem(ld, nld[0])
            T.wait(V)
            for half in range(2):
                T.inc_last(T.matmul(ps_g[half][:], ones1[0:1, :],
                                    gb_row[0:1, 512 * half:512 * (half + 1)],
                                    start=True, stop=True))
            V.wait(T)
            V.tensor_copy(out=g_bc[:, 0:512], in_=ps_g[0][:])
            V.tensor_copy(out=g_bc[:, 512:1024], in_=ps_g[1][:])
            for half in range(2):
                T.wait(V)
                T.inc_last(T.matmul(ps_g[half][:], ones1[0:1, :],
                                    gb_row[0:1, D + 512 * half:D + 512 * (half + 1)],
                                    start=True, stop=True))
            V.wait(T)
            V.tensor_copy(out=b_bc[:, 0:512], in_=ps_g[0][:])
            V.tensor_copy(out=b_bc[:, 512:1024], in_=ps_g[1][:])

            def ln_tmajor(src_ap, dst, scale_gb):
                V.reduce_sum(out=muv[:], in_=src_ap, axis=AX.X)
                V.tensor_scalar(out=muv[:], in0=muv[:], scalar1=1.0 / D, scalar2=None,
                                op0=ALU.mult)
                V.tensor_scalar(out=dst[:], in0=src_ap, scalar1=muv[:, 0:1], scalar2=None,
                                op0=ALU.subtract)
                S.wait(V)
                S.activation(out=sqt[:], in_=dst[:], func=AF.Square, accum_out=ssv[:])
                S.activation(out=rstd[:], in_=ssv[:], func=AF.Ln, scale=1.0 / D,
                             bias=epsb[:, 0:1])
                S.activation(out=rstd[:], in_=rstd[:], func=AF.Exp, scale=-0.5)
                V.wait(S)
                V.tensor_scalar(out=dst[:], in0=dst[:], scalar1=rstd[:, 0:1], scalar2=None,
                                op0=ALU.mult)
                if scale_gb:
                    V.tensor_mul(out=dst[:], in0=dst[:], in1=g_bc[:])
                    V.tensor_add(out=dst[:], in0=dst[:], in1=b_bc[:])

            ev_tp = {}
            ktp = [0]

            def transposes(src_bf, dstT, ti):
                for j in range(8):
                    k = ktp[0]
                    bk = k % 2
                    if k >= 2:
                        eng, cnt = ev_tp[k - 2]
                        T.wait(eng, cnt)
                    T.inc_last(T.transpose(ps_tp[bk][:], src_bf[:, 128 * j:128 * (j + 1)],
                                           ident_bf[:]))
                    V.wait(T)
                    V.tensor_copy(out=dstT[j][:, 128 * ti:128 * (ti + 1)], in_=ps_tp[bk][:])
                    ev_tp[k] = (V, V.n)
                    ktp[0] += 1

            for ts in range(16):
                ln_tmajor(acc[ts][:], xn1t, False)
                V.tensor_copy(out=xbf[0][:], in_=xn1t[:])
                T.wait(V)
                transposes(xbf[0], xn1T, ts)
                ln_tmajor(xn1t[:], xat, True)
                V.tensor_copy(out=xbf[1][:], in_=xat[:])
                T.wait(V)
                transposes(xbf[1], xatT, ts)
                SY.wait_ge(sem_v, V.n)
                SY.dma_start(out=o_x1[128 * ts:128 * (ts + 1), :], in_=acc[ts][:]).then_inc(st, 16)
                nst[0] += 16
            SY.wait_ge(sem_v, V.n)
            for j in range(8):
                SY.dma_start(out=o_xn1[128 * j:128 * (j + 1), :], in_=xn1T[j][:]).then_inc(st, 16)
                SY.dma_start(out=o_xat[128 * j:128 * (j + 1), :], in_=xatT[j][:]).then_inc(st, 16)
                nst[0] += 32
            SY.wait_ge(st, nst[0])
            G_.wait_ge(st, nst[0])
    return nc


def run_M(A_outs, D_outs, inputs):
    yT = np.concatenate(
        [np.concatenate([np.asarray(D_outs[4 * b + j]["o_y"]) for j in range(4)], axis=0)
         for b in range(2)], axis=1)
    in_maps = []
    for c in range(NC):
        cols = slice(TCORE * c, TCORE * (c + 1))
        x_tm = inputs["L0_attn_x"].reshape(16384, D)[cols]
        in_maps.append({
            "yT": f32c(yT[:, cols]),
            "szT": f32c(np.asarray(A_outs[c]["o_sz"])),
            "xnbfT": np.ascontiguousarray(np.asarray(A_outs[c]["o_xnbf"]).astype(BF)),
            "wmoe": f32c(np.asarray(A_outs[c]["o_wmoe"])),
            "x_tm": f32c(x_tm),
            "opw": np.ascontiguousarray(np.asarray(A_outs[c]["o_opwb"]).astype(BF)),
            "w1": np.ascontiguousarray(np.asarray(A_outs[c]["o_w1b"]).astype(BF)),
            "w3": np.ascontiguousarray(np.asarray(A_outs[c]["o_w3b"]).astype(BF)),
            "w2": np.ascontiguousarray(np.asarray(A_outs[c]["o_w2b"]).astype(BF)),
            "lng": f32c(inputs["L1_attn_ln_g"].reshape(1, D)),
            "lnb": f32c(inputs["L1_attn_ln_b"].reshape(1, D)),
        })
    if "M" not in _A_CACHE:
        _A_CACHE["M"] = build_M()
    res = run_bass_kernel_spmd(_A_CACHE["M"], in_maps, core_ids=list(range(NC)))
    return res.results


def build_F():
    nc = bass.Bass("TRN2", target_bir_lowering=False, debug=False)
    TB = 8192
    NKT = TB // 128
    NQC = TB // 512
    xatT = nc.dram_tensor("xatT", [D, TB], BF16, kind="ExternalInput").ap()
    wqT = nc.dram_tensor("wqT", [D, 256], F32, kind="ExternalInput").ap()
    wkT = nc.dram_tensor("wkT", [D, 256], F32, kind="ExternalInput").ap()
    wvT = nc.dram_tensor("wvT", [D, 256], F32, kind="ExternalInput").ap()
    qg = nc.dram_tensor("qg", [64, 1], F32, kind="ExternalInput").ap()
    kg = nc.dram_tensor("kg", [64, 1], F32, kind="ExternalInput").ap()
    o_oh = nc.dram_tensor("o_oh", [256, TB], BF16, kind="ExternalOutput").ap()
    H4 = 4096
    cv_shapes = [[H4, H4], [H4, H4], [D, H4], [D, H4], [H4, D], [D, D]]
    cv_in = [nc.dram_tensor(f"cv_in{i}", shp, F32, kind="ExternalInput").ap()
             for i, shp in enumerate(cv_shapes)]
    cv_out = [nc.dram_tensor(f"cv_out{i}", shp, BF16, kind="ExternalOutput").ap()
              for i, shp in enumerate(cv_shapes)]

    RMS_EPS = 1e-8
    SCALE = 0.125

    with ExitStack() as ctx:
        e = ctx.enter_context
        xat = [e(nc.sbuf_tensor(f"xat{j}", [128, TB], BF16)) for j in range(8)]
        qn_bf = e(nc.sbuf_tensor("qn_bf", [128, TB], BF16))
        kn_bf = e(nc.sbuf_tensor("kn_bf", [128, TB], BF16))
        vo_all = e(nc.sbuf_tensor("vo_all", [128, 65 * NKT], BF16))
        vo = [vo_all[:, 65 * t:65 * (t + 1)] for t in range(NKT)]
        w_sb = [e(nc.sbuf_tensor(f"w_sb{i}", [128, 256], BF16)) for i in range(24)]
        gam = e(nc.sbuf_tensor("gam", [64, 2], F32))
        ones1 = e(nc.sbuf_tensor("ones1", [1, 128], F32))
        onesc = e(nc.sbuf_tensor("onesc", [64, 1], F32))
        sq_t = e(nc.sbuf_tensor("sq_t", [64, 512], F32))
        tmp_t = e(nc.sbuf_tensor("tmp_t", [64, 512], F32))
        row_a = e(nc.sbuf_tensor("row_a", [1, 512], F32))
        row_b = e(nc.sbuf_tensor("row_b", [1, 512], F32))
        cacc = e(nc.sbuf_tensor("cacc", [1, 4], F32))
        negC_bc = e(nc.sbuf_tensor("negC_bc", [128, 1], F32))
        exp_bf = [e(nc.sbuf_tensor(f"exp{b}", [128, 1024], BF16)) for b in range(3)]
        rec_bc = e(nc.sbuf_tensor("rec_bc", [64, 512], F32))
        o_sb = e(nc.sbuf_tensor("o_sb", [64, 512], BF16))

        cstage_f = [e(nc.sbuf_tensor(f"cstagef{b}", [128, 512], F32)) for b in range(2)]
        cstage_b = [e(nc.sbuf_tensor(f"cstageb{b}", [128, 512], BF16)) for b in range(2)]
        ld_cv = [e(nc.semaphore(f"ld_cv{b}")) for b in range(2)]
        st_cv = [e(nc.semaphore(f"st_cv{b}")) for b in range(2)]
        sem_g = e(nc.semaphore("sem_g"))
        ld = e(nc.semaphore("ld"))
        ldg = e(nc.semaphore("ldg"))
        sem_v = e(nc.semaphore("sem_v"))
        sem_s = e(nc.semaphore("sem_s"))
        sem_t = e(nc.semaphore("sem_t"))
        st_o = e(nc.semaphore("st_o"))

        V = EW(nc.vector, sem_v)
        S = EW(nc.scalar, sem_s)
        T = EW(nc.tensor, sem_t, autosync=False)
        SY = nc.sync
        G_ = nc.gpsimd
        nld = [0]
        nlg = [0]
        nst = [0]

        V.memset(ones1[:], 1.0)
        V.memset(onesc[:], 1.0)
        GP = EW(nc.gpsimd, sem_g)
        cv_pairs = []
        for i in range(6):
            si, so = cv_in[i], cv_out[i]
            f = si.shape[1] // 512
            if f > 1:
                si = si.rearrange("a (b c) -> (a b) c", b=f)
                so = so.rearrange("a (b c) -> (a b) c", b=f)
            cv_pairs.append((si, so))
        cp = ConvertPool(SY, GP, cstage_f, cstage_b, ld_cv, st_cv, cv_pairs)
        for j in range(8):
            SY.dma_start(out=xat[j][:], in_=xatT[128 * j:128 * (j + 1), :]).then_inc(ld, 16)
        SY.dma_start(out=gam[0:64, 0:1], in_=qg).then_inc(ld, 16)
        SY.dma_start(out=gam[0:64, 1:2], in_=kg).then_inc(ld, 16)
        nld[0] += 16 * 10
        for j in range(8):
            G_.dma_start(out=w_sb[j][:], in_=wqT[128 * j:128 * (j + 1), :]).then_inc(ldg, 16)
            G_.dma_start(out=w_sb[8 + j][:], in_=wkT[128 * j:128 * (j + 1), :]).then_inc(ldg, 16)
            G_.dma_start(out=w_sb[16 + j][:], in_=wvT[128 * j:128 * (j + 1), :]).then_inc(ldg, 16)
        nlg[0] += 16 * 24
        V.wait_sem(ld, nld[0])
        V.tensor_scalar(out=gam[:, 0:1], in0=gam[:, 0:1], scalar1=SCALE, scalar2=None,
                        op0=ALU.mult)

        with ExitStack() as cq:
            te = cq.enter_context

            for h in range(4):
              with ExitStack() as cqk:
                tq = cqk.enter_context
                ps_q = tq(nc.psum_tensor(f"ps_q{h}", [64, 512], F32))
                ps_n = tq(nc.psum_tensor(f"ps_n{h}", [1, 512], F32))
                ps_r = tq(nc.psum_tensor(f"ps_r{h}", [128, 512], F32))
                ps_v = [tq(nc.psum_tensor(f"ps_v{h}_{b}", [128, 64], F32)) for b in range(2)]
                for which, wbase, gcol, dst, ccol in ((0, 0, 0, qn_bf, 0), (1, 8, 1, kn_bf, 1)):
                  for cidx in range(NQC):
                      c0 = 512 * cidx
                      T.wait_sem(ldg, nlg[0])
                      T.wait(V)
                      for kt in range(8):
                          mm = T.matmul(ps_q[:], w_sb[wbase + kt][:, 64 * h:64 * (h + 1)],
                                        xat[kt][:, c0:c0 + 512],
                                        start=(kt == 0), stop=(kt == 7))
                      T.inc_last(mm)
                      S.wait(T)
                      S.wait(V)
                      S.activation(out=sq_t[:], in_=ps_q[:], func=AF.Square)
                      T.wait(S)
                      T.inc_last(T.matmul(ps_n[:], onesc[:], sq_t[:], start=True, stop=True))
                      S.wait(T)
                      S.wait(V)
                      S.activation(out=row_a[:], in_=ps_n[:], func=AF.Sqrt)
                      V.wait(S)
                      V.tensor_scalar(out=row_a[:], in0=row_a[:], scalar1=SCALE,
                                      scalar2=RMS_EPS, op0=ALU.mult, op1=ALU.max)
                      V.reciprocal(out=row_b[:], in_=row_a[:])
                      T.wait(V)
                      T.inc_last(T.matmul(ps_r[0:64, :], ones1[0:1, 0:64], row_b[:],
                                          start=True, stop=True))
                      V.tensor_scalar(out=tmp_t[:], in0=ps_q[:], scalar1=gam[:, gcol:gcol + 1],
                                      scalar2=None, op0=ALU.mult)
                      V.wait(T)
                      V.tensor_mul(out=dst[0:64, c0:c0 + 512], in0=tmp_t[:], in1=ps_r[0:64, :])
                      S.wait(V)
                      S.activation(out=sq_t[:], in_=tmp_t[:], func=AF.Square)
                      T.wait(S)
                      T.inc_last(T.matmul(ps_n[:], onesc[:], sq_t[:], start=True, stop=True))
                      V.wait(T)
                      V.tensor_mul(out=row_a[:], in0=ps_n[:], in1=row_b[:])
                      V.tensor_mul(out=row_a[:], in0=row_a[:], in1=row_b[:])
                      V.reduce_max(out=row_a[0:1, 0:1], in_=row_a[:], axis=AX.X)
                      if cidx == 0:
                          V.tensor_copy(out=cacc[0:1, ccol:ccol + 1], in_=row_a[0:1, 0:1])
                      else:
                          V.tensor_max(out=cacc[0:1, ccol:ccol + 1],
                                       in0=cacc[0:1, ccol:ccol + 1], in1=row_a[0:1, 0:1])
              SY.wait_ge(sem_v, V.n)
              SY.dma_start(out=qn_bf[64:128, :], in_=qn_bf[0:64, :]).then_inc(ld, 16)
              SY.dma_start(out=kn_bf[64:128, :], in_=kn_bf[0:64, :]).then_inc(ld, 16)
              nld[0] += 32
              S.wait(V)
              S.activation(out=cacc[0:1, 0:1], in_=cacc[0:1, 0:1], func=AF.Sqrt)
              S.activation(out=cacc[0:1, 1:2], in_=cacc[0:1, 1:2], func=AF.Sqrt)
              V.wait(S)
              V.tensor_mul(out=cacc[0:1, 2:3], in0=cacc[0:1, 0:1], in1=cacc[0:1, 1:2])
              V.tensor_scalar(out=cacc[0:1, 3:4], in0=cacc[0:1, 2:3], scalar1=-1.0,
                              scalar2=None, op0=ALU.mult)
              T.wait(V)
              T.inc_last(T.matmul(ps_r[:, 0:1], ones1[0:1, :], cacc[0:1, 3:4],
                                  start=True, stop=True))
              V.wait(T)
              V.tensor_copy(out=negC_bc[0:128, 0:1], in_=ps_r[0:128, 0:1])
              for t in range(NKT):
                  b = t % 2
                  if t >= 2:
                      T.wait(V)
                  for kt in range(8):
                      mm = T.matmul(ps_v[b][:], xat[kt][:, 128 * t:128 * (t + 1)],
                                    w_sb[16 + kt][:, 64 * h:64 * (h + 1)],
                                    start=(kt == 0), stop=(kt == 7))
                  T.inc_last(mm)
                  V.wait(T)
                  V.tensor_copy(out=vo[t][:, 0:64], in_=ps_v[b][:])
                  V.memset(vo[t][:, 64:65], 1.0)
                  cp.step(2)
              with ExitStack() as ca:
                    ta = ca.enter_context
                    ps_s = [ta(nc.psum_tensor(f"ps_s{h}_{b}", [128, 1024], F32)) for b in range(3)]
                    ps_o = ta(nc.psum_tensor(f"ps_o{h}", [65, 512], F32))
                    ps_r = ta(nc.psum_tensor(f"ps_r2{h}", [64, 512], F32))
                    T.wait(V)
                    T.wait(S)
                    T.wait_sem(ld, nld[0])
                    s_after_exp = {}
                    t_after_av = {}
                    NP = NKT // 2
                    for qc in range(NQC):
                        c0 = 512 * qc
                        T.wait(V)
                        for p in range(NP + 1):
                            if p < NP:
                                pb = p % 3
                                if pb in s_after_exp:
                                    T.wait(S, s_after_exp[pb])
                                kc = 2 * p
                                T.matmul(ps_s[pb][:, 0:512],
                                         kn_bf[0:64, 128 * kc:128 * (kc + 1)],
                                         qn_bf[0:64, c0:c0 + 512],
                                         start=True, stop=True, tile_position=(0, 0))
                                mm = T.matmul(ps_s[pb][:, 512:1024],
                                              kn_bf[64:128, 128 * (kc + 1):128 * (kc + 2)],
                                              qn_bf[64:128, c0:c0 + 512],
                                              start=True, stop=True, tile_position=(64, 0))
                                T.inc_last(mm)
                                if pb in t_after_av:
                                    S.wait(T, t_after_av[pb])
                                S.wait(T)
                                S.activation(out=exp_bf[pb][:], in_=ps_s[pb][:], func=AF.Exp,
                                             bias=negC_bc[:, 0:1])
                                s_after_exp[pb] = S.n
                            if p > 0:
                                pb1 = (p - 1) % 3
                                T.wait(S, s_after_exp[pb1])
                                for u in range(2):
                                    kc = 2 * (p - 1) + u
                                    mm = T.matmul(ps_o[:], vo[kc],
                                                  exp_bf[pb1][:, 512 * u:512 * (u + 1)],
                                                  start=(kc == 0), stop=(kc == NKT - 1))
                                T.inc_last(mm)
                                t_after_av[pb1] = T.n
                        V.wait(T)
                        V.wait_sem(st_o, nst[0])
                        V.tensor_scalar(out=row_a[:], in0=ps_o[64:65, :], scalar1=1e-30,
                                        scalar2=None, op0=ALU.add)
                        V.reciprocal(out=row_b[:], in_=row_a[:])
                        T.wait(V)
                        T.inc_last(T.matmul(ps_r[0:64, :], ones1[0:1, 0:64], row_b[:],
                                            start=True, stop=True))
                        V.wait(T)
                        V.tensor_copy(out=rec_bc[:], in_=ps_r[0:64, :])
                        V.tensor_mul(out=o_sb[:], in0=ps_o[0:64, :], in1=rec_bc[:])
                        SY.wait_ge(sem_v, V.n)
                        SY.dma_start(out=o_oh[64 * h:64 * (h + 1), c0:c0 + 512],
                                     in_=o_sb[:]).then_inc(st_o, 16)
                        nst[0] += 16
                        cp.step(4)

        cp.drain()
        cp.finish()
        SY.wait_ge(st_o, nst[0])
        G_.wait_ge(st_o, nst[0])
    return nc


def run_F(M_outs, inputs):
    in_maps = []
    for c in range(NC):
        b = c // 4
        j = c % 4
        xatT = np.concatenate([np.asarray(M_outs[4 * b + i]["o_xat"]) for i in range(4)], axis=1)
        hs = slice(256 * j, 256 * (j + 1))
        in_maps.append({
            "xatT": np.ascontiguousarray(xatT.astype(BF)),
            "wqT": f32c(inputs["L1_attn_Wq"].T[:, hs]),
            "wkT": f32c(inputs["L1_attn_Wk"].T[:, hs]),
            "wvT": f32c(inputs["L1_attn_Wv"].T[:, hs]),
            "qg": f32c(inputs["L1_attn_q_gamma"].reshape(64, 1)),
            "kg": f32c(inputs["L1_attn_k_gamma"].reshape(64, 1)),
            "cv_in0": f32c(inputs["L1_ffn_h1_w"].T),
            "cv_in1": f32c(inputs["L1_ffn_h2_w"].T),
            "cv_in2": f32c(inputs["L1_ffn_w1_w"].T),
            "cv_in3": f32c(inputs["L1_ffn_w2_w"].T),
            "cv_in4": f32c(inputs["L1_ffn_w3_w"].T),
            "cv_in5": f32c(inputs["L1_attn_Wo"].T),
        })
    if "F" not in _A_CACHE:
        _A_CACHE["F"] = build_F()
    res = run_bass_kernel_spmd(_A_CACHE["F"], in_maps, core_ids=list(range(NC)))
    return res.results


def build_G():
    nc = bass.Bass("TRN2", target_bir_lowering=False, debug=False)
    H4 = 4096
    xn1T = nc.dram_tensor("xn1T", [D, TCORE], BF16, kind="ExternalInput").ap()
    ohT = nc.dram_tensor("ohT", [D, TCORE], BF16, kind="ExternalInput").ap()
    x1 = nc.dram_tensor("x1", [TCORE, D], F32, kind="ExternalInput").ap()
    woT = nc.dram_tensor("woT", [D, D], BF16, kind="ExternalInput").ap()
    bo = nc.dram_tensor("bo", [1, D], F32, kind="ExternalInput").ap()
    w1T = nc.dram_tensor("w1T", [D, H4], BF16, kind="ExternalInput").ap()
    w2T = nc.dram_tensor("w2T", [D, H4], BF16, kind="ExternalInput").ap()
    h1T = nc.dram_tensor("h1T", [H4, H4], BF16, kind="ExternalInput").ap()
    h2T = nc.dram_tensor("h2T", [H4, H4], BF16, kind="ExternalInput").ap()
    w3T = nc.dram_tensor("w3T", [H4, D], BF16, kind="ExternalInput").ap()
    s1 = nc.dram_tensor("s1", [H4, 1], F32, kind="ExternalInput").ap()
    s2 = nc.dram_tensor("s2", [H4, 1], F32, kind="ExternalInput").ap()
    sh1 = nc.dram_tensor("sh1", [H4, 1], F32, kind="ExternalInput").ap()
    sh2 = nc.dram_tensor("sh2", [H4, 1], F32, kind="ExternalInput").ap()
    s3r = nc.dram_tensor("s3r", [1, D], F32, kind="ExternalInput").ap()
    o_x2 = nc.dram_tensor("o_x2", [TCORE, D], F32, kind="ExternalOutput").ap()

    with ExitStack() as ctx:
        e = ctx.enter_context
        acc = [e(nc.sbuf_tensor(f"acc{t}", [128, D], F32)) for t in range(16)]
        xn1 = [e(nc.sbuf_tensor(f"xn1_{j}", [128, TCORE], BF16)) for j in range(8)]
        bo_bc = e(nc.sbuf_tensor("bo_bc", [128, D], F32))
        s3_bc = e(nc.sbuf_tensor("s3_bc", [128, D], F32))
        bs_row = e(nc.sbuf_tensor("bs_row", [1, 2 * D], F32))
        ones1 = e(nc.sbuf_tensor("ones1", [1, 128], F32))
        sv_all = e(nc.sbuf_tensor("sv_all", [128, 128], F32))
        sv_off = {"s1": 0, "s2": 32, "sh1": 64, "sh2": 96}
        t1s = e(nc.sbuf_tensor("t1s", [128, 512], F32))
        t2s = e(nc.sbuf_tensor("t2s", [128, 512], F32))

        ld = e(nc.semaphore("ld"))
        ldg = e(nc.semaphore("ldg"))
        ldh = [e(nc.semaphore(f"ldh{p}")) for p in range(2)]
        ldw3 = [e(nc.semaphore(f"ldw3{p}")) for p in range(2)]
        sem_v = e(nc.semaphore("sem_v"))
        sem_s = e(nc.semaphore("sem_s"))
        sem_t = e(nc.semaphore("sem_t"))
        st = e(nc.semaphore("st"))

        V = EW(nc.vector, sem_v)
        S = EW(nc.scalar, sem_s)
        T = EW(nc.tensor, sem_t, autosync=False)
        SY = nc.sync
        G_ = nc.gpsimd
        nld = [0]
        nlg = [0]
        nldh = [0, 0]
        nldw3 = [0, 0]
        nst = [0]

        V.memset(ones1[:], 1.0)
        SY.dma_start(out=bs_row[0:1, 0:D], in_=bo).then_inc(ld, 16)
        SY.dma_start(out=bs_row[0:1, D:2 * D], in_=s3r).then_inc(ld, 16)
        for t in range(16):
            SY.dma_start(out=acc[t][:], in_=x1[128 * t:128 * (t + 1), :]).then_inc(ld, 16)
        for j in range(8):
            SY.dma_start(out=xn1[j][:], in_=xn1T[128 * j:128 * (j + 1), :]).then_inc(ld, 16)
        for nm, t in (("s1", s1), ("s2", s2), ("sh1", sh1), ("sh2", sh2)):
            o = sv_off[nm]
            for m in range(32):
                SY.dma_start(out=sv_all[:, o + m:o + m + 1],
                             in_=t[128 * m:128 * (m + 1), :]).then_inc(ld, 16)
        nld[0] += 16 * (2 + 16 + 8 + 128)
        LD_BASE = nld[0]

        with ExitStack() as c0:
            t0 = c0.enter_context
            ps_b = [t0(nc.psum_tensor(f"ps_bc{b}", [128, 512], F32)) for b in range(2)]
            T.wait_sem(ld, LD_BASE)
            T.wait(V)
            for half in range(2):
                T.inc_last(T.matmul(ps_b[half][:], ones1[0:1, :],
                                    bs_row[0:1, 512 * half:512 * (half + 1)],
                                    start=True, stop=True))
            V.wait(T)
            V.tensor_copy(out=bo_bc[:, 0:512], in_=ps_b[0][:])
            V.tensor_copy(out=bo_bc[:, 512:1024], in_=ps_b[1][:])
            for half in range(2):
                T.wait(V)
                T.inc_last(T.matmul(ps_b[half][:], ones1[0:1, :],
                                    bs_row[0:1, D + 512 * half:D + 512 * (half + 1)],
                                    start=True, stop=True))
            V.wait(T)
            V.tensor_copy(out=s3_bc[:, 0:512], in_=ps_b[0][:])
            V.tensor_copy(out=s3_bc[:, 512:1024], in_=ps_b[1][:])

        phase_barrier([nc.vector, nc.scalar, nc.tensor, nc.sync, nc.gpsimd],
                      [V, S, T], [(ld, nld[0])])
        with ExitStack() as c1:
            t1 = c1.enter_context
            oh_sb = [t1(nc.sbuf_tensor(f"oh{j}", [128, TCORE], BF16)) for j in range(8)]
            wo_bf = [t1(nc.sbuf_tensor(f"wo{j}", [128, D], BF16)) for j in range(8)]
            ps_m = [t1(nc.psum_tensor(f"ps_m{i}", [128, 512], F32)) for i in range(4)]
            for j in range(8):
                SY.dma_start(out=oh_sb[j][:], in_=ohT[128 * j:128 * (j + 1), :]).then_inc(ld, 16)
                SY.dma_start(out=wo_bf[j][:], in_=woT[128 * j:128 * (j + 1), :]).then_inc(ldg, 16)
            nld[0] += 16 * 8
            nlg[0] += 16 * 8
            T.wait_sem(ld, nld[0])
            T.wait_sem(ldg, nlg[0])
            ev_ps = {}
            for ts in range(16):
                for nh in range(2):
                    gi = 2 * ts + nh
                    pi = gi % 4
                    if gi >= 4:
                        eng, cnt = ev_ps[pi]
                        T.wait(eng, cnt)
                    for kt in range(8):
                        mm = T.matmul(ps_m[pi][:], oh_sb[kt][:, 128 * ts:128 * (ts + 1)],
                                      wo_bf[kt][:, 512 * nh:512 * (nh + 1)],
                                      start=(kt == 0), stop=(kt == 7))
                    T.inc_last(mm)
                    V.wait(T)
                    V.tensor_add(out=acc[ts][:, 512 * nh:512 * (nh + 1)],
                                 in0=acc[ts][:, 512 * nh:512 * (nh + 1)], in1=ps_m[pi][:])
                    ev_ps[pi] = (V, V.n)
                V.tensor_add(out=acc[ts][:], in0=acc[ts][:], in1=bo_bc[:])

        phase_barrier([nc.vector, nc.scalar, nc.tensor, nc.sync, nc.gpsimd],
                      [V, S, T], [(ld, nld[0]), (ldg, nlg[0])])
        with ExitStack() as c2:
            t2 = c2.enter_context
            h_bf = [t2(nc.sbuf_tensor(f"hbf{k}", [128, 512], BF16)) for k in range(32)]
            hh_bf = [t2(nc.sbuf_tensor(f"hh{k}", [128, 512], BF16)) for k in range(32)]
            wsl = [t2(nc.sbuf_tensor("wsl0", [128, 2 * 8 * 128], BF16))]
            hsl = [t2(nc.sbuf_tensor(f"hsl{p}", [128, 32 * 128], BF16)) for p in range(2)]
            w3b = [t2(nc.sbuf_tensor(f"w3b{p}", [128, D], BF16)) for p in range(2)]
            ps_p = [t2(nc.psum_tensor(f"ps_p{i}", [128, 512], F32)) for i in range(4)]
            ps_q5 = [t2(nc.psum_tensor(f"ps_q5{i}", [128, 512], F32)) for i in range(4)]

            def load_w12(hm, buf):
                SY.dma_start(out=wsl[buf][:, 0:1024].rearrange("p (k c) -> p k c", k=8),
                             in_=w1T[:, 128 * hm:128 * (hm + 1)].rearrange(
                                 "(k p) c -> p k c", p=128)).then_inc(ldh[buf], 16)
                SY.dma_start(out=wsl[buf][:, 1024:2048].rearrange("p (k c) -> p k c", k=8),
                             in_=w2T[:, 128 * hm:128 * (hm + 1)].rearrange(
                                 "(k p) c -> p k c", p=128)).then_inc(ldh[buf], 16)
                nldh[buf] += 32

            def load_hsl(mat, om, buf):
                for a in range(4):
                    SY.dma_start(
                        out=hsl[buf][:, 1024 * a:1024 * (a + 1)].rearrange(
                            "p (k c) -> p k c", k=8),
                        in_=mat[1024 * a:1024 * (a + 1),
                                128 * om:128 * (om + 1)].rearrange("(k p) c -> p k c", p=128)
                    ).then_inc(ldh[buf], 16)
                nldh[buf] += 16 * 4

            sidx = [0]
            t_after_s = {}
            ev_pp = {}
            for tq in range(4):
                tb = 512 * tq
                V.wait(T)
                for hm in range(32):
                    si = sidx[0]
                    sb = 0
                    if si >= 1:
                        SY.wait_ge(sem_t, t_after_s[si - 1])
                    load_w12(hm, sb)
                    wrdy = nldh[sb]
                    sidx[0] += 1
                    T.wait_sem(ldh[sb], wrdy)
                    pp = hm % 2
                    pa, pb = ps_p[2 * pp], ps_p[2 * pp + 1]
                    if ("g2", pp) in ev_pp:
                        eng, cnt = ev_pp[("g2", pp)]
                        T.wait(eng, cnt)
                    for kt in range(8):
                        T.matmul(pa[:], wsl[sb][:, 128 * kt:128 * (kt + 1)],
                                 xn1[kt][:, tb:tb + 512], start=(kt == 0), stop=(kt == 7))
                    for kt in range(8):
                        mm = T.matmul(pb[:], wsl[sb][:, 1024 + 128 * kt:1024 + 128 * (kt + 1)],
                                      xn1[kt][:, tb:tb + 512], start=(kt == 0), stop=(kt == 7))
                    T.inc_last(mm)
                    t_after_s[si] = T.n
                    S.wait(T)
                    S.wait(V)
                    S.activation(out=t1s[:], in_=pa[:], func=AF.Silu,
                                 scale=sv_all[:, sv_off["s1"] + hm:sv_off["s1"] + hm + 1])
                    V.wait(S)
                    V.tensor_scalar(out=t2s[:], in0=pb[:],
                                    scalar1=sv_all[:, sv_off["s2"] + hm:sv_off["s2"] + hm + 1],
                                    scalar2=None, op0=ALU.mult)
                    V.tensor_mul(out=h_bf[hm][:], in0=t1s[:], in1=t2s[:])
                    ev_pp[("g2", pp)] = (V, V.n)
                for layer, mat, svec, src_t, dst_t in (
                        (0, h1T, "sh1", h_bf, hh_bf), (1, h2T, "sh2", hh_bf, h_bf)):
                    hrdy = {}
                    t_after_om = {}
                    SY.wait_ge(sem_t, T.n)
                    load_hsl(mat, 0, 0)
                    hrdy[0] = (0, nldh[0])
                    for om in range(32):
                        sb = om % 2
                        if om + 1 < 32:
                            nb = (om + 1) % 2
                            if om >= 1:
                                SY.wait_ge(sem_t, t_after_om[om - 1])
                            load_hsl(mat, om + 1, nb)
                            hrdy[om + 1] = (nb, nldh[nb])
                        rb, rv = hrdy[om]
                        T.wait_sem(ldh[rb], rv)
                        pp = om % 2
                        pa = ps_p[2 * pp]
                        if ("g3", pp) in ev_pp:
                            eng, cnt = ev_pp[("g3", pp)]
                            T.wait(eng, cnt)
                        if om == 0:
                            T.wait(S)
                            T.wait(V)
                        for kt in range(32):
                            mm = T.matmul(pa[:], hsl[sb][:, 128 * kt:128 * (kt + 1)],
                                          src_t[kt][:], start=(kt == 0), stop=(kt == 31))
                        T.inc_last(mm)
                        t_after_om[om] = T.n
                        S.wait(T)
                        S.activation(out=dst_t[om][:], in_=pa[:], func=AF.Gelu,
                                     scale=sv_all[:, sv_off[svec] + om:sv_off[svec] + om + 1])
                        ev_pp[("g3", pp)] = (S, S.n)
                psall = ps_p + ps_q5
                T.wait(S)
                T.wait(V)
                ldw3_r = {}
                t_after_w3 = {}
                nb0 = (2 * tq) % 2
                SY.wait_ge(sem_t, T.n)
                SY.dma_start(out=w3b[0][:], in_=w3T[0:128, :]).then_inc(ldw3[0], 16)
                nldw3[0] += 16
                ldw3_r[0] = nldw3[0]
                for kt in range(32):
                    kb = kt % 2
                    if kt + 1 < 32:
                        nb = (kt + 1) % 2
                        if kt >= 1:
                            SY.wait_ge(sem_t, t_after_w3[kt - 1])
                        SY.dma_start(out=w3b[nb][:],
                                     in_=w3T[128 * (kt + 1):128 * (kt + 2), :]).then_inc(ldw3[nb], 16)
                        nldw3[nb] += 16
                        ldw3_r[kt + 1] = nldw3[nb]
                    T.wait_sem(ldw3[kb], ldw3_r[kt])
                    for gi in range(8):
                        tsl, nh = divmod(gi, 2)
                        mm = T.matmul(psall[gi][:], h_bf[kt][:, 128 * tsl:128 * (tsl + 1)],
                                      w3b[kb][:, 512 * nh:512 * (nh + 1)],
                                      start=(kt == 0), stop=(kt == 31))
                    T.inc_last(mm)
                    t_after_w3[kt] = T.n
                V.wait(T)
                for gi in range(8):
                    tsl, nh = divmod(gi, 2)
                    ts = 4 * tq + tsl
                    V.tensor_mul(out=t1s[:], in0=psall[gi][:],
                                 in1=s3_bc[:, 512 * nh:512 * (nh + 1)])
                    V.tensor_add(out=acc[ts][:, 512 * nh:512 * (nh + 1)],
                                 in0=acc[ts][:, 512 * nh:512 * (nh + 1)], in1=t1s[:])
                for key in (("g2", 0), ("g2", 1), ("g3", 0), ("g3", 1)):
                    ev_pp[key] = (V, V.n)
                for tsl in range(4):
                    ts = 4 * tq + tsl
                    SY.wait_ge(sem_v, V.n)
                    SY.dma_start(out=o_x2[128 * ts:128 * (ts + 1), :],
                                 in_=acc[ts][:]).then_inc(st, 16)
                    nst[0] += 16
            SY.wait_ge(st, nst[0])
            G_.wait_ge(st, nst[0])
    return nc


def run_G(M_outs, F_outs, inputs):
    in_maps = []
    for c in range(NC):
        b = c // 4
        ohT_b = np.concatenate([np.asarray(F_outs[4 * b + i]["o_oh"]) for i in range(4)], axis=0)
        cols = slice(TCORE * (c % 4), TCORE * (c % 4 + 1))
        in_maps.append({
            "xn1T": np.ascontiguousarray(np.asarray(M_outs[c]["o_xn1"]).astype(BF)),
            "ohT": np.ascontiguousarray(ohT_b[:, cols].astype(BF)),
            "x1": f32c(np.asarray(M_outs[c]["o_x1"])),
            "woT": np.ascontiguousarray(np.asarray(F_outs[c]["cv_out5"]).astype(BF)),
            "bo": f32c(inputs["L1_attn_bo"].reshape(1, D)),
            "w1T": np.ascontiguousarray(np.asarray(F_outs[c]["cv_out2"]).astype(BF)),
            "w2T": np.ascontiguousarray(np.asarray(F_outs[c]["cv_out3"]).astype(BF)),
            "h1T": np.ascontiguousarray(np.asarray(F_outs[c]["cv_out0"]).astype(BF)),
            "h2T": np.ascontiguousarray(np.asarray(F_outs[c]["cv_out1"]).astype(BF)),
            "w3T": np.ascontiguousarray(np.asarray(F_outs[c]["cv_out4"]).astype(BF)),
            "s1": f32c(inputs["L1_ffn_w1_s"].reshape(-1, 1)),
            "s2": f32c(inputs["L1_ffn_w2_s"].reshape(-1, 1)),
            "sh1": f32c(inputs["L1_ffn_h1_s"].reshape(-1, 1)),
            "sh2": f32c(inputs["L1_ffn_h2_s"].reshape(-1, 1)),
            "s3r": f32c(inputs["L1_ffn_w3_s"].reshape(1, D)),
        })
    if "G" not in _A_CACHE:
        _A_CACHE["G"] = build_G()
    res = run_bass_kernel_spmd(_A_CACHE["G"], in_maps, core_ids=list(range(NC)))
    return res.results


    return nc


_A_CACHE = {}


def run_A(inputs):
    x = inputs["L0_attn_x"].reshape(16384, D)
    in_maps = []
    for c in range(NC):
        t0 = TCORE * c
        if c % 4 == 0:
            xh = np.concatenate([np.zeros((HALO, D), np.float32),
                                 x[t0:t0 + TCORE]], axis=0)
        else:
            xh = x[t0 - HALO:t0 + TCORE]
        in_maps.append({
            "xi": f32c(xh),
            "ipw": f32c(inputs["L0_attn_in_proj_w"]),
            "convw": f32c(inputs["L0_attn_conv_w"]),
            "convb": f32c(inputs["L0_attn_conv_b"].reshape(DI, 1)),
            "xpw": f32c(inputs["L0_attn_x_proj_w"]),
            "dtw": f32c(inputs["L0_attn_dt_proj_w"]),
            "dtb": f32c(inputs["L0_attn_dt_proj_b"].reshape(DI, 1)),
            "wgate": f32c(inputs["L0_ffn_w_gate"]),
            "w1f": f32c(inputs["L0_ffn_w1"].reshape(8 * D, DI)),
            "w3f": f32c(inputs["L0_ffn_w3"].reshape(8 * D, DI)),
            "w2f": f32c(inputs["L0_ffn_w2"].reshape(8 * DI, D)),
            "opwf": f32c(inputs["L0_attn_out_proj_w"]),
        })
    if "A" not in _A_CACHE:
        _A_CACHE["A"] = build_A()
    res = run_bass_kernel_spmd(_A_CACHE["A"], in_maps, core_ids=list(range(NC)))
    return res.results


def kernel(**inputs):
    A = run_A(inputs)
    Dr = run_D(A, inputs)
    M = run_M(A, Dr, inputs)
    F = run_F(M, inputs)
    G = run_G(M, F, inputs)
    x2 = np.concatenate([np.asarray(G[c]["o_x2"]) for c in range(NC)], axis=0)
    return np.ascontiguousarray(x2.reshape(2, 8192, D), dtype=np.float32)


# version 05
# speedup vs baseline: 1.1215x; 1.1215x over previous
import numpy as np
import ml_dtypes
from contextlib import ExitStack
import concourse.bass as bass
import concourse.mybir as mybir
from concourse.bass_utils import run_bass_kernel_spmd

F32 = mybir.dt.float32
BF16 = mybir.dt.bfloat16
AF = mybir.ActivationFunctionType
ALU = mybir.AluOpType
AX = mybir.AxisListType
BF = ml_dtypes.bfloat16

NC = 8
D = 1024
DI = 2048
DS = 16
TCORE = 2048
HALO = 128
TA = TCORE + HALO
LN_EPS = 1e-5


def f32c(a):
    return np.ascontiguousarray(a, dtype=np.float32)


def chunks(total, step, base=0):
    out = []
    o = base
    while o < base + total:
        n = min(step, base + total - o)
        out.append((o, n))
        o += n
    return out



class EW:

    def __init__(self, eng, sem, autosync=True):
        self.e = eng
        self.sem = sem
        self.n = 0
        self.autosync = autosync

    def wait(self, other, val=None):
        self.e.wait_ge(other.sem, other.n if val is None else val)

    def wait_sem(self, sem, val):
        self.e.wait_ge(sem, val)

    def raw(self):
        return self.e

    def __getattr__(self, name):
        fn = getattr(self.e, name)
        if not self.autosync or name in ("wait_ge", "nop", "sem_inc"):
            return fn

        def em(*a, **k):
            if self.n > 0:
                self.e.wait_ge(self.sem, self.n)
            inst = fn(*a, **k)
            inst.then_inc(self.sem, 1)
            self.n += 1
            return inst
        return em

    def inc_last(self, inst):
        inst.then_inc(self.sem, 1)
        self.n += 1
        return inst


class ConvertPool:

    def __init__(self, SY, GP, stage_f, stage_b, ld_sems, st_sems, pairs):
        self.SY, self.GP = SY, GP
        self.sf, self.sb = stage_f, stage_b
        self.ld, self.st = ld_sems, st_sems
        self.nld = [0, 0]
        self.nst = [0, 0]
        self.g_after = {}
        C = stage_f[0].shape[1]
        self.work = []
        for s, d in pairs:
            R, Cs = s.shape
            assert Cs == C and R % 128 == 0, (s.shape, C)
            for r in range(R // 128):
                self.work.append((s[128 * r:128 * (r + 1), :], d[128 * r:128 * (r + 1), :]))
        self.i = 0

    def step(self, n=1):
        for _ in range(n):
            if self.i >= len(self.work):
                return
            sap, dap = self.work[self.i]
            b = self.i % 2
            Q, GP = self.SY, self.GP
            if b in self.g_after:
                Q.wait_ge(GP.sem, self.g_after[b])
            Q.dma_start(out=self.sf[b][:], in_=sap).then_inc(self.ld[b], 16)
            self.nld[b] += 16
            GP.wait_sem(self.ld[b], self.nld[b])
            if self.nst[b]:
                GP.wait_sem(self.st[b], self.nst[b])
            GP.tensor_copy(out=self.sb[b][:], in_=self.sf[b][:])
            self.g_after[b] = GP.n
            Q.wait_ge(GP.sem, GP.n)
            Q.dma_start(out=dap, in_=self.sb[b][:]).then_inc(self.st[b], 16)
            self.nst[b] += 16
            self.i += 1

    def drain(self):
        self.step(len(self.work) - self.i)

    def finish(self):
        self.SY.wait_ge(self.st[0], self.nst[0])
        self.SY.wait_ge(self.st[1], self.nst[1])


def phase_barrier(engines, wrappers, dma_sems):
    for E in engines:
        for w in wrappers:
            E.wait_ge(w.sem, w.n)
        for sem, tot in dma_sems:
            E.wait_ge(sem, tot)


def build_A():
    nc = bass.Bass("TRN2", target_bir_lowering=False, debug=False)
    xi = nc.dram_tensor("xi", [TA, D], F32, kind="ExternalInput").ap()
    ipw = nc.dram_tensor("ipw", [D, 2 * DI], F32, kind="ExternalInput").ap()
    convw = nc.dram_tensor("convw", [DI, 4], F32, kind="ExternalInput").ap()
    convb = nc.dram_tensor("convb", [DI, 1], F32, kind="ExternalInput").ap()
    xpw = nc.dram_tensor("xpw", [DI, 96], F32, kind="ExternalInput").ap()
    dtw = nc.dram_tensor("dtw", [64, DI], F32, kind="ExternalInput").ap()
    dtb = nc.dram_tensor("dtb", [DI, 1], F32, kind="ExternalInput").ap()
    wgate = nc.dram_tensor("wgate", [D, 8], F32, kind="ExternalInput").ap()

    o_xnbf = nc.dram_tensor("o_xnbf", [D, TCORE], BF16, kind="ExternalOutput").ap()
    o_xbs = nc.dram_tensor("o_xbs", [DI, TCORE], F32, kind="ExternalOutput").ap()
    o_sz = nc.dram_tensor("o_sz", [DI, TCORE], F32, kind="ExternalOutput").ap()
    o_dt = nc.dram_tensor("o_dt", [DI, TCORE], F32, kind="ExternalOutput").ap()
    o_pbc = nc.dram_tensor("o_pbc", [32, TCORE], F32, kind="ExternalOutput").ap()
    o_wmoe = nc.dram_tensor("o_wmoe", [TCORE, 8], F32, kind="ExternalOutput").ap()
    w1f = nc.dram_tensor("w1f", [8 * D, DI], F32, kind="ExternalInput").ap()
    w3f = nc.dram_tensor("w3f", [8 * D, DI], F32, kind="ExternalInput").ap()
    w2f = nc.dram_tensor("w2f", [8 * DI, D], F32, kind="ExternalInput").ap()
    opwf = nc.dram_tensor("opwf", [DI, D], F32, kind="ExternalInput").ap()
    o_w1b = nc.dram_tensor("o_w1b", [8 * D, DI], BF16, kind="ExternalOutput").ap()
    o_w3b = nc.dram_tensor("o_w3b", [8 * D, DI], BF16, kind="ExternalOutput").ap()
    o_w2b = nc.dram_tensor("o_w2b", [8 * DI, D], BF16, kind="ExternalOutput").ap()
    o_opwb = nc.dram_tensor("o_opwb", [DI, D], BF16, kind="ExternalOutput").ap()

    NT = TA // 128
    CH5 = chunks(TA, 512)
    CT4 = chunks(TCORE, 512)

    with ExitStack() as ctx:
        e = ctx.enter_context
        ident_bf_d = nc.inline_tensor(np.eye(128, dtype=BF), name="ident_bf")
        ident_f_d = nc.inline_tensor(np.eye(128, dtype=np.float32), name="ident_f")

        xnT_bf = [e(nc.sbuf_tensor(f"xnT{j}", [128, TA], BF16)) for j in range(8)]
        ident_bf = e(nc.sbuf_tensor("identbf", [128, 128], BF16))
        ident_f = e(nc.sbuf_tensor("identf", [128, 128], F32))
        wgate_sb = [e(nc.sbuf_tensor(f"wg{j}", [128, 8], F32)) for j in range(8)]
        convw_sb = [e(nc.sbuf_tensor(f"cw{m}", [128, 4], F32)) for m in range(16)]
        convb_sb = [e(nc.sbuf_tensor(f"cb{m}", [128, 1], F32)) for m in range(16)]
        dtb_sb = [e(nc.sbuf_tensor(f"dbb{m}", [128, 1], F32)) for m in range(16)]
        xpw_bf = [e(nc.sbuf_tensor(f"xpw{m}", [128, 96], BF16)) for m in range(16)]
        dtw_bf = e(nc.sbuf_tensor("dtw_bf", [64, DI], BF16))
        ipwm = [e(nc.sbuf_tensor(f"ipwm{b}", [128, 8 * 128], BF16)) for b in range(2)]
        xb_raw = e(nc.sbuf_tensor("xb_raw", [128, TA], F32))
        conv_acc = e(nc.sbuf_tensor("conv_acc", [128, TCORE], F32))
        xbs_f = e(nc.sbuf_tensor("xbs_f", [128, TCORE], F32))
        xbs_b = e(nc.sbuf_tensor("xbs_b", [128, TCORE], BF16))
        sz_t = e(nc.sbuf_tensor("sz_t", [128, TCORE], F32))
        projT_bf = e(nc.sbuf_tensor("projT_bf", [64, TCORE], BF16))
        pbc_sb = e(nc.sbuf_tensor("pbc_sb", [32, TCORE], F32))
        dtv = e(nc.sbuf_tensor("dtv", [128, TCORE], F32))
        ab = e(nc.sbuf_tensor("ab", [128, TCORE], F32))
        enl = e(nc.sbuf_tensor("enl", [128, TCORE], F32))
        rl = e(nc.sbuf_tensor("rl", [128, TCORE], F32))
        dt_t = e(nc.sbuf_tensor("dt_t", [128, TCORE], F32))
        lg_t = e(nc.sbuf_tensor("lg_t", [128, 8], F32))
        mx8 = e(nc.sbuf_tensor("mx8", [128, 8], F32))
        sc1 = e(nc.sbuf_tensor("sc1", [128, 4], F32))
        negm1 = e(nc.sbuf_tensor("negm1", [128, 1], F32))
        el = e(nc.sbuf_tensor("el", [128, 8], F32))
        msk = e(nc.sbuf_tensor("msk", [128, 8], F32))
        wtm = e(nc.sbuf_tensor("wtm", [128, 8], F32))
        epsb = e(nc.sbuf_tensor("epsb", [128, 1], F32))
        cstage_f = [e(nc.sbuf_tensor(f"cstagef{b}", [128, DI], F32)) for b in range(2)]
        cstage = [e(nc.sbuf_tensor(f"cstage{b}", [128, DI], BF16)) for b in range(2)]
        ld_cv = [e(nc.semaphore(f"ld_cv{b}")) for b in range(2)]
        st_cv = [e(nc.semaphore(f"st_cv{b}")) for b in range(2)]
        sem_g = e(nc.semaphore("sem_g"))

        ld = e(nc.semaphore("s_ld"))
        ldg = e(nc.semaphore("s_ldg"))
        ldx = [e(nc.semaphore(f"s_ldx{b}")) for b in range(2)]
        ldw = [e(nc.semaphore(f"s_ldw{b}")) for b in range(2)]
        sem_v = e(nc.semaphore("sem_v"))
        sem_s = e(nc.semaphore("sem_s"))
        sem_t = e(nc.semaphore("sem_t"))
        st = e(nc.semaphore("st"))
        st_xbs = e(nc.semaphore("st_xbs"))
        st_sz = e(nc.semaphore("st_sz"))
        st_dt = e(nc.semaphore("st_dt"))
        st_w = e(nc.semaphore("st_w"))

        V = EW(nc.vector, sem_v)
        S = EW(nc.scalar, sem_s)
        T = EW(nc.tensor, sem_t, autosync=False)
        SY = nc.sync
        G_ = nc.gpsimd
        nld = [0]
        nlg = [0]
        nst = [0]
        nldx = [0, 0]
        nldw = [0, 0]
        nst_b = dict(xbs=0, sz=0, dt=0, w=0)

        V.memset(epsb[:], LN_EPS)
        GP = EW(nc.gpsimd, sem_g)
        cp = ConvertPool(G_, GP, cstage_f, cstage, ld_cv, st_cv,
                         [(w1f, o_w1b), (w3f, o_w3b),
                          (w2f.rearrange("(a b) c -> a (b c)", b=2),
                           o_w2b.rearrange("(a b) c -> a (b c)", b=2)),
                          (opwf.rearrange("(a b) c -> a (b c)", b=2),
                           o_opwb.rearrange("(a b) c -> a (b c)", b=2))])

        SY.dma_start(out=ident_bf[:], in_=ident_bf_d.ap()).then_inc(ld, 16)
        SY.dma_start(out=ident_f[:], in_=ident_f_d.ap()).then_inc(ld, 16)
        for j in range(8):
            SY.dma_start(out=wgate_sb[j][:], in_=wgate[128 * j:128 * (j + 1), :]).then_inc(ld, 16)
        for m in range(16):
            SY.dma_start(out=convw_sb[m][:], in_=convw[128 * m:128 * (m + 1), :]).then_inc(ld, 16)
            SY.dma_start(out=convb_sb[m][:], in_=convb[128 * m:128 * (m + 1), :]).then_inc(ld, 16)
            SY.dma_start(out=dtb_sb[m][:], in_=dtb[128 * m:128 * (m + 1), :]).then_inc(ld, 16)
        nld[0] += 16 * 58
        LD_MISC = nld[0]
        for m in range(16):
            G_.dma_start(out=xpw_bf[m][:], in_=xpw[128 * m:128 * (m + 1), :]).then_inc(ldg, 16)
        G_.dma_start(out=dtw_bf[:], in_=dtw).then_inc(ldg, 16)
        nlg[0] += 16 * 17
        LD_W2 = nlg[0]

        with ExitStack() as tctx:
            te = tctx.enter_context
            xtile = [te(nc.sbuf_tensor(f"xtile{b}", [128, D], F32)) for b in range(2)]
            xnbf_t = [te(nc.sbuf_tensor(f"xnbf{b}", [128, D], BF16)) for b in range(2)]
            sqt = te(nc.sbuf_tensor("sqt", [128, D], F32))
            muv = te(nc.sbuf_tensor("muv", [128, 1], F32))
            ssv = te(nc.sbuf_tensor("ssv", [128, 1], F32))
            rstd = te(nc.sbuf_tensor("rstd", [128, 1], F32))
            tf32 = [te(nc.sbuf_tensor(f"tf32{b}", [128, 128], F32)) for b in range(2)]
            ps_tp = [te(nc.psum_tensor(f"ps_tp{b}", [128, 128], BF16)) for b in range(2)]
            ps_tf = [te(nc.psum_tensor(f"ps_tf{b}", [128, 128], F32)) for b in range(2)]
            ps_l = te(nc.psum_tensor("ps_l", [128, 8], F32))

            t_done_after = []
            ev_after_tp = {}
            glob_tp = [0]
            for i in range(NT):
                b = i % 2
                if i >= 2:
                    SY.wait_ge(sem_t, t_done_after[i - 2])
                SY.dma_start(out=xtile[b][:], in_=xi[128 * i:128 * (i + 1), :]).then_inc(ldx[b], 16)
                nldx[b] += 16
                V.wait_sem(ldx[b], nldx[b])
                V.reduce_sum(out=muv[:], in_=xtile[b][:], axis=AX.X)
                V.tensor_scalar(out=muv[:], in0=muv[:], scalar1=1.0 / D, scalar2=None, op0=ALU.mult)
                V.tensor_scalar(out=xtile[b][:], in0=xtile[b][:], scalar1=muv[:, 0:1],
                                scalar2=None, op0=ALU.subtract)
                S.wait(V)
                S.activation(out=sqt[:], in_=xtile[b][:], func=AF.Square, accum_out=ssv[:])
                S.activation(out=rstd[:], in_=ssv[:], func=AF.Ln, scale=1.0 / D, bias=epsb[:, 0:1])
                S.activation(out=rstd[:], in_=rstd[:], func=AF.Exp, scale=-0.5)
                V.wait(S)
                V.tensor_scalar(out=xtile[b][:], in0=xtile[b][:], scalar1=rstd[:, 0:1],
                                scalar2=None, op0=ALU.mult)
                V.tensor_copy(out=xnbf_t[b][:], in_=xtile[b][:])
                T.wait(V)
                if i == 0:
                    T.wait_sem(ld, LD_MISC)
                for j in range(8):
                    k = glob_tp[0]
                    if k >= 2:
                        V_after = ev_after_tp[k - 2]
                        T.wait(V, V_after)
                    T.inc_last(T.transpose(ps_tp[j % 2][:], xnbf_t[b][:, 128 * j:128 * (j + 1)],
                                           ident_bf[:]))
                    V.wait(T)
                    V.tensor_copy(out=xnT_bf[j][:, 128 * i:128 * (i + 1)], in_=ps_tp[j % 2][:])
                    ev_after_tp[k] = V.n
                    glob_tp[0] += 1
                if i >= 1:
                    for j in range(8):
                        k = glob_tp[0]
                        T.wait(V, ev_after_tp[k - 2])
                        T.inc_last(T.transpose(ps_tf[j % 2][:], xtile[b][:, 128 * j:128 * (j + 1)],
                                               ident_f[:]))
                        V.wait(T)
                        V.tensor_copy(out=tf32[j % 2][:], in_=ps_tf[j % 2][:])
                        ev_after_tp[k] = V.n
                        glob_tp[0] += 1
                        T.wait(V)
                        mm = T.matmul(ps_l[:], tf32[j % 2][:], wgate_sb[j][:],
                                      start=(j == 0), stop=(j == 7))
                        if j == 7:
                            T.inc_last(mm)
                    V.wait(T)
                    V.tensor_copy(out=lg_t[:], in_=ps_l[:])
                    V.max(out=mx8[:], in_=lg_t[:])
                    V.tensor_tensor(out=sc1[:, 0:1], in0=mx8[:, 1:2], in1=mx8[:, 0:1],
                                    op=ALU.subtract)
                    V.tensor_scalar(out=negm1[:], in0=mx8[:, 0:1], scalar1=-1.0,
                                    scalar2=None, op0=ALU.mult)
                    S.wait(V)
                    S.activation(out=sc1[:, 1:2], in_=sc1[:, 0:1], func=AF.Exp)
                    S.activation(out=el[:], in_=lg_t[:], func=AF.Exp, bias=negm1[:, 0:1])
                    V.wait(S)
                    V.tensor_scalar(out=sc1[:, 2:3], in0=sc1[:, 1:2], scalar1=1.0,
                                    scalar2=None, op0=ALU.add)
                    V.reciprocal(out=sc1[:, 3:4], in_=sc1[:, 2:3])
                    V.tensor_scalar(out=msk[:], in0=lg_t[:], scalar1=mx8[:, 1:2],
                                    scalar2=None, op0=ALU.is_ge)
                    V.tensor_scalar(out=wtm[:], in0=el[:], scalar1=sc1[:, 3:4],
                                    scalar2=None, op0=ALU.mult)
                    V.tensor_tensor(out=wtm[:], in0=wtm[:], in1=msk[:], op=ALU.mult)
                    SY.wait_ge(sem_v, V.n)
                    SY.dma_start(out=o_wmoe[128 * (i - 1):128 * i, :], in_=wtm[:]).then_inc(st_w, 16)
                    nst_b["w"] += 16
                    V.wait_sem(st_w, nst_b["w"])
                t_done_after.append(T.n)
                cp.step(3)
            SY.wait_ge(sem_v, V.n)
            for j in range(8):
                SY.dma_start(out=o_xnbf[128 * j:128 * (j + 1), :],
                             in_=xnT_bf[j][:, HALO:TA]).then_inc(st, 16)
                nst[0] += 16

        phase_barrier([nc.vector, nc.scalar, nc.tensor, nc.sync, nc.gpsimd],
                      [V, S, T],
                      [(ld, nld[0]), (ldg, nlg[0]), (ldx[0], nldx[0]), (ldx[1], nldx[1]),
                       (st_w, nst_b["w"]), (st, nst[0])])
        with ExitStack() as tctx:
            te = tctx.enter_context
            ps_ab = [te(nc.psum_tensor(f"ps_ab{b}", [128, 512], F32)) for b in range(2)]
            ps_pj = [te(nc.psum_tensor(f"ps_pj{i}", [96, 512], F32)) for i in range(4)]

            V.wait_sem(ld, LD_MISC)
            S.wait_sem(ld, LD_MISC)
            T.wait_sem(ldg, LD_W2)

            def load_ipw_slice(mcol, buf):
                for kt in range(8):
                    G_.dma_start(out=ipwm[buf][:, 128 * kt:128 * (kt + 1)],
                                 in_=ipw[128 * kt:128 * (kt + 1),
                                         128 * mcol:128 * (mcol + 1)]).then_inc(ldw[buf], 16)
                nldw[buf] += 16 * 8

            ev_after_g = {}
            ipw_ready = {}
            load_ipw_slice(0, 0)
            ipw_ready[0] = (0, nldw[0])
            t_after_m = {}
            gidx = [0]
            for m in range(16):
                nb = (m + 1) % 2
                if m + 1 < 16:
                    if m >= 1:
                        G_.wait_ge(sem_t, t_after_m[m - 1])
                    load_ipw_slice(m + 1, nb)
                    ipw_ready[m + 1] = (nb, nldw[nb])
                for ci, (c0, cl) in enumerate(CH5):
                    g = gidx[0]
                    ps = ps_ab[g % 2]
                    if g >= 2:
                        eng, cnt = ev_after_g[g - 2]
                        T.wait(eng, cnt)
                    if ci == 0:
                        rb, rv = ipw_ready[m]
                        T.wait_sem(ldw[rb], rv)
                    for kt in range(8):
                        mm = T.matmul(ps[:, 0:cl], ipwm[m % 2][:, 128 * kt:128 * (kt + 1)],
                                      xnT_bf[kt][:, c0:c0 + cl], start=(kt == 0), stop=(kt == 7))
                    T.inc_last(mm)
                    V.wait(T)
                    V.tensor_copy(out=xb_raw[:, c0:c0 + cl], in_=ps[:, 0:cl])
                    ev_after_g[g] = (V, V.n)
                    gidx[0] += 1
                V.tensor_scalar(out=conv_acc[:], in0=xb_raw[:, 125:125 + TCORE],
                                scalar1=convw_sb[m][:, 0:1], scalar2=None, op0=ALU.mult)
                for j in range(1, 4):
                    V.scalar_tensor_tensor(out=conv_acc[:], in0=xb_raw[:, 125 + j:125 + j + TCORE],
                                           scalar=convw_sb[m][:, j:j + 1], in1=conv_acc[:],
                                           op0=ALU.mult, op1=ALU.add)
                S.wait(V)
                S.wait_sem(st_xbs, nst_b["xbs"])
                S.activation(out=xbs_f[:], in_=conv_acc[:], func=AF.Silu,
                             bias=convb_sb[m][:, 0:1])
                SY.wait_ge(sem_s, S.n)
                SY.dma_start(out=o_xbs[128 * m:128 * (m + 1), :], in_=xbs_f[:]).then_inc(st_xbs, 16)
                nst_b["xbs"] += 16
                V.wait(S)
                if m >= 1:
                    V.wait(T, t_after_m[m - 1])
                V.tensor_copy(out=xbs_b[:], in_=xbs_f[:])
                T.wait(V)
                for ci, (c0, cl) in enumerate(CT4):
                    mm = T.matmul(ps_pj[ci][:], xpw_bf[m][:], xbs_b[:, c0:c0 + cl],
                                  start=(m == 0), stop=(m == 15))
                T.inc_last(mm)
                t_after_m[m] = T.n
                cp.step(3)
            for m in range(16):
                nb = m % 2
                G_.wait_ge(sem_t, T.n)
                load_ipw_slice(16 + m, nb)
                zr = nldw[nb]
                for ci, (c0, cl) in enumerate(CT4):
                    g = gidx[0]
                    ps = ps_ab[g % 2]
                    eng, cnt = ev_after_g[g - 2]
                    T.wait(eng, cnt)
                    if ci == 0:
                        T.wait_sem(ldw[nb], zr)
                    for kt in range(8):
                        mm = T.matmul(ps[:, 0:cl], ipwm[nb][:, 128 * kt:128 * (kt + 1)],
                                      xnT_bf[kt][:, HALO + c0:HALO + c0 + cl],
                                      start=(kt == 0), stop=(kt == 7))
                    T.inc_last(mm)
                    S.wait(T)
                    if ci == 0:
                        S.wait_sem(st_sz, nst_b["sz"])
                    S.activation(out=sz_t[:, c0:c0 + cl], in_=ps[:, 0:cl], func=AF.Silu)
                    ev_after_g[g] = (S, S.n)
                    gidx[0] += 1
                SY.wait_ge(sem_s, S.n)
                SY.dma_start(out=o_sz[128 * m:128 * (m + 1), :], in_=sz_t[:]).then_inc(st_sz, 16)
                nst_b["sz"] += 16

            V.wait(T)
            for ci, (c0, cl) in enumerate(CT4):
                V.tensor_copy(out=projT_bf[:, c0:c0 + cl], in_=ps_pj[ci][0:64, :])
                V.tensor_copy(out=pbc_sb[0:32, c0:c0 + cl], in_=ps_pj[ci][64:96, :])
            SY.wait_ge(sem_v, V.n)
            SY.dma_start(out=o_pbc[:, :], in_=pbc_sb[0:32, :]).then_inc(st, 16)
            nst[0] += 16
            T.wait(V)
            for m in range(16):
                for ci, (c0, cl) in enumerate(CT4):
                    g = gidx[0]
                    ps = ps_ab[g % 2]
                    eng, cnt = ev_after_g[g - 2]
                    T.wait(eng, cnt)
                    T.inc_last(T.matmul(ps[:, 0:cl], dtw_bf[:, 128 * m:128 * (m + 1)],
                                        projT_bf[:, c0:c0 + cl], start=True, stop=True))
                    V.wait(T)
                    V.tensor_scalar(out=dtv[:, c0:c0 + cl], in0=ps[:, 0:cl],
                                    scalar1=dtb_sb[m][:, 0:1], scalar2=None, op0=ALU.add)
                    ev_after_g[g] = (V, V.n)
                    gidx[0] += 1
                V.tensor_scalar(out=rl[:], in0=dtv[:], scalar1=-1.0, scalar2=None, op0=ALU.mult)
                V.tensor_tensor(out=ab[:], in0=dtv[:], in1=rl[:], op=ALU.min)
                V.tensor_scalar(out=rl[:], in0=dtv[:], scalar1=0.0, scalar2=None, op0=ALU.max)
                S.wait(V)
                S.activation(out=enl[:], in_=ab[:], func=AF.Exp)
                S.activation(out=enl[:], in_=enl[:], func=AF.Ln, bias=1.0)
                V.wait(S)
                V.wait_sem(st_dt, nst_b["dt"])
                V.tensor_tensor(out=dt_t[:], in0=rl[:], in1=enl[:], op=ALU.add)
                SY.wait_ge(sem_v, V.n)
                SY.dma_start(out=o_dt[128 * m:128 * (m + 1), :], in_=dt_t[:]).then_inc(st_dt, 16)
                nst_b["dt"] += 16

        cp.drain()
        cp.finish()
        SY.wait_ge(st, nst[0])
        SY.wait_ge(st_xbs, nst_b["xbs"])
        SY.wait_ge(st_sz, nst_b["sz"])
        SY.wait_ge(st_dt, nst_b["dt"])
        SY.wait_ge(st_w, nst_b["w"])
        G_.wait_ge(st, nst[0])
    return nc


def build_D():
    nc = bass.Bass("TRN2", target_bir_lowering=False, debug=False)
    TB = 8192
    CH = 512
    NCH = TB // CH
    dtT = nc.dram_tensor("dtT", [512, TB], F32, kind="ExternalInput").ap()
    xbsT = nc.dram_tensor("xbsT", [512, TB], F32, kind="ExternalInput").ap()
    pbcT = nc.dram_tensor("pbcT", [32, TB], F32, kind="ExternalInput").ap()
    alog = nc.dram_tensor("alog", [512, 16], F32, kind="ExternalInput").ap()
    dpar = nc.dram_tensor("dpar", [512, 1], F32, kind="ExternalInput").ap()
    o_y = nc.dram_tensor("o_y", [512, TB], F32, kind="ExternalOutput").ap()

    with ExitStack() as ctx:
        e = ctx.enter_context
        dt_c = [[e(nc.sbuf_tensor(f"dt{p}_{d}", [128, CH], F32)) for d in range(4)] for p in range(2)]
        xb_c = [[e(nc.sbuf_tensor(f"xb{p}_{d}", [128, CH], F32)) for d in range(4)] for p in range(2)]
        pb_c = [e(nc.sbuf_tensor(f"pb{p}", [1, 32 * CH], F32)) for p in range(2)]
        a_sb = [e(nc.sbuf_tensor(f"a{d}", [128, 16], F32)) for d in range(4)]
        dp_sb = [e(nc.sbuf_tensor(f"dp{d}", [128, 1], F32)) for d in range(4)]
        state = [e(nc.sbuf_tensor(f"stt{d}", [128, 16], F32)) for d in range(4)]
        ones1 = e(nc.sbuf_tensor("ones1", [1, 128], F32))
        dtu_t = e(nc.sbuf_tensor("dtu_t", [128, CH], F32))
        da_sb = [e(nc.sbuf_tensor(f"da{b}", [128, CH], F32)) for b in range(2)]
        dbu = e(nc.sbuf_tensor("dbu", [128, CH], F32))
        sct = e(nc.sbuf_tensor("sct", [128, CH], F32))
        yterm = e(nc.sbuf_tensor("yterm", [128, CH], F32))
        y_acc = [e(nc.sbuf_tensor(f"ya{d}", [128, CH], F32)) for d in range(4)]

        ps_bc = [e(nc.psum_tensor(f"ps_bc{i}", [128, CH], F32)) for i in range(4)]

        ld = e(nc.semaphore("ld"))
        ldc = [e(nc.semaphore(f"ldc{p}")) for p in range(2)]
        sem_v = e(nc.semaphore("sem_v"))
        sem_s = e(nc.semaphore("sem_s"))
        sem_t = e(nc.semaphore("sem_t"))
        st_y = [e(nc.semaphore(f"st_y{d}")) for d in range(4)]

        V = EW(nc.vector, sem_v)
        S = EW(nc.scalar, sem_s)
        T = EW(nc.tensor, sem_t, autosync=False)
        SY = nc.sync
        nld = 0
        nldc = [0, 0]
        nst = [0]

        V.memset(ones1[:], 1.0)
        for d in range(4):
            V.memset(state[d][:], 0.0)

        for d in range(4):
            SY.dma_start(out=a_sb[d][:], in_=alog[128 * d:128 * (d + 1), :]).then_inc(ld, 16)
            SY.dma_start(out=dp_sb[d][:], in_=dpar[128 * d:128 * (d + 1), :]).then_inc(ld, 16)
        nld += 16 * 8
        S.wait_sem(ld, nld)
        for d in range(4):
            S.activation(out=a_sb[d][:], in_=a_sb[d][:], func=AF.Exp, scale=1.0)
        for d in range(4):
            S.mul(out=a_sb[d][:], in_=a_sb[d][:], mul=-1.0)

        def load_chunk(ch, p):
            c0 = ch * CH
            for d in range(4):
                SY.dma_start(out=dt_c[p][d][:], in_=dtT[128 * d:128 * (d + 1), c0:c0 + CH]).then_inc(ldc[p], 16)
                SY.dma_start(out=xb_c[p][d][:], in_=xbsT[128 * d:128 * (d + 1), c0:c0 + CH]).then_inc(ldc[p], 16)
            SY.dma_start(out=pb_c[p][0:1, :].rearrange("o (r c) -> o r c", r=32),
                         in_=pbcT[:, c0:c0 + CH].unsqueeze(0)).then_inc(ldc[p], 16)
            nldc[p] += 16 * 9

        v_after_chunk = {}
        vas = {}
        load_chunk(0, 0)
        rdy = {0: nldc[0]}
        bc_reader = {}
        for ch in range(NCH):
            p = ch % 2
            if ch + 1 < NCH:
                if ch >= 1:
                    SY.wait_ge(sem_v, v_after_chunk[ch - 1])
                load_chunk(ch + 1, (ch + 1) % 2)
                rdy[ch + 1] = nldc[(ch + 1) % 2]
            V.wait_sem(ldc[p], rdy[ch])
            S.wait_sem(ldc[p], rdy[ch])
            T.wait_sem(ldc[p], rdy[ch])
            if ch == 0:
                T.wait(V)
            for d in range(4):
                if ch >= 1:
                    V.wait_sem(st_y[d], 16 * ch)
                V.tensor_mul(out=dtu_t[:], in0=dt_c[p][d][:], in1=xb_c[p][d][:])
                for s in range(16):
                    k = (d * 16 + s) % 2
                    bb, cb = ps_bc[2 * k], ps_bc[2 * k + 1]
                    for bank, ps_t, row in ((2 * k, bb, s), (2 * k + 1, cb, 16 + s)):
                        if bank in bc_reader:
                            T.wait(V, bc_reader[bank])
                        T.inc_last(T.matmul(ps_t[:], ones1[0:1, :],
                                            pb_c[p][0:1, row * CH:(row + 1) * CH],
                                            start=True, stop=True))
                    db = s % 2
                    if db in vas:
                        S.wait(V, vas[db])
                    S.activation(out=da_sb[db][:], in_=dt_c[p][d][:], func=AF.Exp,
                                 scale=a_sb[d][:, s:s + 1])
                    V.wait(T)
                    V.wait(S)
                    V.tensor_mul(out=dbu[:], in0=dtu_t[:], in1=bb[:])
                    V.tensor_tensor_scan(out=sct[:], data0=da_sb[db][:], data1=dbu[:],
                                         initial=state[d][:, s:s + 1], op0=ALU.mult,
                                         op1=ALU.add)
                    vas[db] = V.n
                    V.tensor_copy(out=state[d][:, s:s + 1], in_=sct[:, CH - 1:CH])
                    if s == 0:
                        V.tensor_mul(out=y_acc[d][:], in0=sct[:], in1=cb[:])
                    else:
                        V.tensor_mul(out=yterm[:], in0=sct[:], in1=cb[:])
                        V.tensor_add(out=y_acc[d][:], in0=y_acc[d][:], in1=yterm[:])
                    bc_reader[2 * k] = V.n
                    bc_reader[2 * k + 1] = V.n
                V.scalar_tensor_tensor(out=y_acc[d][:], in0=xb_c[p][d][:],
                                       scalar=dp_sb[d][:, 0:1], in1=y_acc[d][:],
                                       op0=ALU.mult, op1=ALU.add)
                SY.wait_ge(sem_v, V.n)
                SY.dma_start(out=o_y[128 * d:128 * (d + 1), ch * CH:(ch + 1) * CH],
                             in_=y_acc[d][:]).then_inc(st_y[d], 16)
                nst[0] += 16
            v_after_chunk[ch] = V.n
        for d in range(4):
            SY.wait_ge(st_y[d], 16 * NCH)
    return nc


def run_D(A_outs, inputs):
    dtT = np.concatenate([np.asarray(A_outs[c]["o_dt"]) for c in range(NC)], axis=1)
    xbsT = np.concatenate([np.asarray(A_outs[c]["o_xbs"]) for c in range(NC)], axis=1)
    pbcT = np.concatenate([np.asarray(A_outs[c]["o_pbc"]) for c in range(NC)], axis=1)
    alog = f32c(inputs["L0_attn_A_log"])
    dpar = f32c(inputs["L0_attn_D_param"].reshape(DI, 1))
    in_maps = []
    for c in range(NC):
        b = c // 4
        r0 = 512 * (c % 4)
        cols = slice(8192 * b, 8192 * (b + 1))
        in_maps.append({
            "dtT": f32c(dtT[r0:r0 + 512, cols]),
            "xbsT": f32c(xbsT[r0:r0 + 512, cols]),
            "pbcT": f32c(pbcT[:, cols]),
            "alog": f32c(alog[r0:r0 + 512]),
            "dpar": f32c(dpar[r0:r0 + 512]),
        })
    if "D" not in _A_CACHE:
        _A_CACHE["D"] = build_D()
    res = run_bass_kernel_spmd(_A_CACHE["D"], in_maps, core_ids=list(range(NC)))
    return res.results


def build_M():
    nc = bass.Bass("TRN2", target_bir_lowering=False, debug=False)
    yT = nc.dram_tensor("yT", [DI, TCORE], F32, kind="ExternalInput").ap()
    szT = nc.dram_tensor("szT", [DI, TCORE], F32, kind="ExternalInput").ap()
    xnbfT = nc.dram_tensor("xnbfT", [D, TCORE], BF16, kind="ExternalInput").ap()
    wmoe = nc.dram_tensor("wmoe", [TCORE, 8], F32, kind="ExternalInput").ap()
    x_tm = nc.dram_tensor("x_tm", [TCORE, D], F32, kind="ExternalInput").ap()
    opw = nc.dram_tensor("opw", [DI, D], BF16, kind="ExternalInput").ap()
    w1 = nc.dram_tensor("w1", [8 * D, DI], BF16, kind="ExternalInput").ap()
    w3 = nc.dram_tensor("w3", [8 * D, DI], BF16, kind="ExternalInput").ap()
    w2 = nc.dram_tensor("w2", [8 * DI, D], BF16, kind="ExternalInput").ap()
    lng = nc.dram_tensor("lng", [1, D], F32, kind="ExternalInput").ap()
    lnb = nc.dram_tensor("lnb", [1, D], F32, kind="ExternalInput").ap()

    o_x1 = nc.dram_tensor("o_x1", [TCORE, D], F32, kind="ExternalOutput").ap()
    o_xn1 = nc.dram_tensor("o_xn1", [D, TCORE], BF16, kind="ExternalOutput").ap()
    o_xat = nc.dram_tensor("o_xat", [D, TCORE], BF16, kind="ExternalOutput").ap()

    with ExitStack() as ctx:
        e = ctx.enter_context
        ident_bf_d = nc.inline_tensor(np.eye(128, dtype=BF), name="ident_bf")

        acc = [e(nc.sbuf_tensor(f"acc{t}", [128, D], F32)) for t in range(16)]
        xnbf = [e(nc.sbuf_tensor(f"xnbf{j}", [128, TCORE], BF16)) for j in range(8)]
        wm_sb = [e(nc.sbuf_tensor(f"wm{t}", [128, 8], F32)) for t in range(16)]
        ident_bf = e(nc.sbuf_tensor("identbf", [128, 128], BF16))
        ones1 = e(nc.sbuf_tensor("ones1", [1, 128], F32))

        epsb = e(nc.sbuf_tensor("epsb", [128, 1], F32))

        ld = e(nc.semaphore("ld"))
        ldg = e(nc.semaphore("ldg"))
        ldy = [e(nc.semaphore(f"ldy{p}")) for p in range(2)]
        ldw_s = [e(nc.semaphore(f"ldws{p}")) for p in range(2)]
        ldw2 = [e(nc.semaphore(f"ldw2{p}")) for p in range(2)]
        sem_v = e(nc.semaphore("sem_v"))
        sem_s = e(nc.semaphore("sem_s"))
        sem_t = e(nc.semaphore("sem_t"))
        st = e(nc.semaphore("st"))

        V = EW(nc.vector, sem_v)
        S = EW(nc.scalar, sem_s)
        T = EW(nc.tensor, sem_t, autosync=False)
        SY = nc.sync
        G_ = nc.gpsimd
        nld = [0]
        nlg = [0]
        nldy = [0, 0]
        nldws = [0, 0]
        nldw2 = [0, 0]
        nst = [0]

        V.memset(epsb[:], LN_EPS)
        V.memset(ones1[:], 1.0)

        SY.dma_start(out=ident_bf[:], in_=ident_bf_d.ap()).then_inc(ld, 16)
        for t in range(16):
            SY.dma_start(out=wm_sb[t][:], in_=wmoe[128 * t:128 * (t + 1), :]).then_inc(ld, 16)
            SY.dma_start(out=acc[t][:], in_=x_tm[128 * t:128 * (t + 1), :]).then_inc(ld, 16)
        for j in range(8):
            SY.dma_start(out=xnbf[j][:], in_=xnbfT[128 * j:128 * (j + 1), :]).then_inc(ld, 16)
        nld[0] += 16 * 41
        LD_BASE = nld[0]

        with ExitStack() as c1:
            t1 = c1.enter_context
            yf_bf = [t1(nc.sbuf_tensor(f"yf{k}", [128, TCORE], BF16)) for k in range(8)]
            ybuf = t1(nc.sbuf_tensor("ybuf", [128, TCORE], F32))
            zbuf = t1(nc.sbuf_tensor("zbuf", [128, TCORE], F32))
            opw_bf = [t1(nc.sbuf_tensor(f"opw{k}", [128, D], BF16)) for k in range(16)]
            ps_m = [t1(nc.psum_tensor(f"ps_m{i}", [128, 512], F32)) for i in range(4)]
            for k in range(16):
                SY.dma_start(out=opw_bf[k][:], in_=opw[128 * k:128 * (k + 1), :]).then_inc(ldg, 16)
            nlg[0] += 16 * 16
            t_after_g = {}
            ev_ps = {}
            for g in range(2):
                for k8 in range(8):
                    k = 8 * g + k8
                    SY.wait_ge(sem_v, V.n)
                    if g >= 1:
                        SY.wait_ge(sem_t, t_after_g[g - 1])
                    SY.dma_start(out=ybuf[:], in_=yT[128 * k:128 * (k + 1), :]).then_inc(ldy[0], 16)
                    SY.dma_start(out=zbuf[:], in_=szT[128 * k:128 * (k + 1), :]).then_inc(ldy[0], 16)
                    nldy[0] += 32
                    V.wait_sem(ldy[0], nldy[0])
                    V.tensor_mul(out=yf_bf[k8][:], in0=ybuf[:], in1=zbuf[:])
                if g == 0:
                    T.wait_sem(ldg, nlg[0])
                T.wait(V)
                for ts in range(16):
                    for half in range(2):
                        gi = 2 * ts + half
                        pi = gi % 4
                        if gi >= 4 or g >= 1:
                            eng, cnt = ev_ps[pi]
                            T.wait(eng, cnt)
                        for k8 in range(8):
                            mm = T.matmul(ps_m[pi][:], yf_bf[k8][:, 128 * ts:128 * (ts + 1)],
                                          opw_bf[8 * g + k8][:, 512 * half:512 * (half + 1)],
                                          start=(k8 == 0), stop=(k8 == 7))
                        T.inc_last(mm)
                        V.wait(T)
                        V.wait_sem(ld, LD_BASE)
                        V.tensor_add(out=acc[ts][:, 512 * half:512 * (half + 1)],
                                     in0=acc[ts][:, 512 * half:512 * (half + 1)],
                                     in1=ps_m[pi][:])
                        ev_ps[pi] = (V, V.n)
                t_after_g[g] = T.n

        phase_barrier([nc.vector, nc.scalar, nc.tensor, nc.sync, nc.gpsimd],
                      [V, S, T],
                      [(ld, nld[0]), (ldg, nlg[0]), (ldy[0], nldy[0])])
        with ExitStack() as c3:
            t3 = c3.enter_context
            hc = [t3(nc.sbuf_tensor(f"hc{p}", [128, 16 * 512], BF16)) for p in range(2)]
            w13m = [t3(nc.sbuf_tensor(f"w13m{p}", [128, 2 * 8 * 128], BF16)) for p in range(2)]
            w2_bf = [t3(nc.sbuf_tensor(f"w2bf{k}", [128, D], BF16)) for k in range(16)]
            hstg = [t3(nc.sbuf_tensor(f"hstg{p}", [128, 2048], BF16)) for p in range(2)]
            t1s = t3(nc.sbuf_tensor("t1s", [128, 512], F32))
            ps_h = [t3(nc.psum_tensor(f"ps_h{i}", [128, 512], F32)) for i in range(4)]
            ps_y = [t3(nc.psum_tensor(f"ps_y{i}", [128, 512], F32)) for i in range(4)]
            d_h = [nc.dram_tensor(f"d_h{b}", [DI, TCORE], BF16) for b in range(2)]

            def load_w13(eidx, hm, buf):
                SY.dma_start(out=w13m[buf][:, 0:1024].rearrange("p (k c) -> p k c", k=8),
                             in_=w1[D * eidx:D * (eidx + 1),
                                    128 * hm:128 * (hm + 1)].rearrange(
                                 "(k p) c -> p k c", p=128)).then_inc(ldw_s[buf], 16)
                SY.dma_start(out=w13m[buf][:, 1024:2048].rearrange("p (k c) -> p k c", k=8),
                             in_=w3[D * eidx:D * (eidx + 1),
                                    128 * hm:128 * (hm + 1)].rearrange(
                                 "(k p) c -> p k c", p=128)).then_inc(ldw_s[buf], 16)
                nldws[buf] += 32

            def load_w2(eidx):
                for k in range(16):
                    SY.dma_start(out=w2_bf[k][:],
                                 in_=w2[DI * eidx + 128 * k:DI * eidx + 128 * (k + 1),
                                        :]).then_inc(ldw2[0], 16)
                nldw2[0] += 16 * 16

            ev_h = {}
            ev_y = {}
            st_hd = [t3(nc.semaphore(f"st_hd{p}")) for p in range(2)]
            ldh_y = [t3(nc.semaphore(f"ldh_y{p}")) for p in range(2)]
            nst_hd = [0, 0]
            nldh_y = [0, 0]
            t_after_w13 = {}
            t_after_y = {}
            st_hd_after = {}
            t_after_h = {}

            def emit_h(ei):
                eb = ei % 2
                for hm in range(16):
                    si = 16 * ei + hm
                    sb = si % 2
                    if si >= 2:
                        SY.wait_ge(sem_t, t_after_w13[si - 2])
                    load_w13(ei, hm, sb)
                    wrdy = nldws[sb]
                    T.wait_sem(ldw_s[sb], wrdy)
                    for ci in range(4):
                        pp = ci % 2
                        pa, pb = ps_h[2 * pp], ps_h[2 * pp + 1]
                        if ("h", pp) in ev_h:
                            eng, cnt = ev_h[("h", pp)]
                            T.wait(eng, cnt)
                        for kt in range(8):
                            T.matmul(pa[:], w13m[sb][:, 128 * kt:128 * (kt + 1)],
                                     xnbf[kt][:, 512 * ci:512 * (ci + 1)],
                                     start=(kt == 0), stop=(kt == 7))
                        for kt in range(8):
                            mm = T.matmul(pb[:], w13m[sb][:, 1024 + 128 * kt:1024 + 128 * (kt + 1)],
                                          xnbf[kt][:, 512 * ci:512 * (ci + 1)],
                                          start=(kt == 0), stop=(kt == 7))
                        T.inc_last(mm)
                        S.wait(T)
                        S.wait(V)
                        S.activation(out=t1s[:], in_=pa[:], func=AF.Silu)
                        V.wait(S)
                        if ci == 0:
                            V.wait_sem(st_hd[sb], nst_hd[sb])
                        V.tensor_mul(out=hstg[sb][:, 512 * ci:512 * (ci + 1)],
                                     in0=t1s[:], in1=pb[:])
                        ev_h[("h", pp)] = (V, V.n)
                    t_after_w13[si] = T.n
                    SY.wait_ge(sem_v, V.n)
                    SY.dma_start(out=d_h[eb].ap()[128 * hm:128 * (hm + 1), :],
                                 in_=hstg[sb][:]).then_inc(st_hd[sb], 16)
                    nst_hd[sb] += 16
                t_after_h[ei] = T.n
                st_hd_after[ei] = (nst_hd[0], nst_hd[1])

            def emit_y(ei):
                eb = ei % 2
                T.wait_sem(ldw2[0], w2rdy[ei])
                for ci in range(4):
                    cb = ci % 2
                    if ci >= 2:
                        G_.wait_ge(sem_t, t_after_ci[cb])
                    elif ei >= 1:
                        G_.wait_ge(sem_t, t_after_ci[cb])
                    if ci == 0:
                        a0, a1 = st_hd_after[ei]
                        G_.wait_ge(st_hd[0], a0)
                        G_.wait_ge(st_hd[1], a1)
                    G_.dma_start(out=hc[cb][:].rearrange("p (k c) -> p k c", k=16),
                                 in_=d_h[eb].ap()[:, 512 * ci:512 * (ci + 1)].rearrange(
                                     "(k p) c -> p k c", p=128)).then_inc(ldh_y[cb], 16)
                    nldh_y[cb] += 16
                    T.wait_sem(ldh_y[cb], nldh_y[cb])
                    for tsl in range(4):
                        ts = 4 * ci + tsl
                        for nh in range(2):
                            pj = (2 * tsl + nh) % 4
                            if ("y", pj) in ev_y:
                                eng, cnt = ev_y[("y", pj)]
                                T.wait(eng, cnt)
                            for k in range(16):
                                mm = T.matmul(ps_y[pj][:], hc[cb][:, 512 * k + 128 * tsl:
                                                                  512 * k + 128 * (tsl + 1)],
                                              w2_bf[k][:, 512 * nh:512 * (nh + 1)],
                                              start=(k == 0), stop=(k == 15))
                            T.inc_last(mm)
                            V.wait(T)
                            V.scalar_tensor_tensor(out=acc[ts][:, 512 * nh:512 * (nh + 1)],
                                                   in0=ps_y[pj][:],
                                                   scalar=wm_sb[ts][:, ei:ei + 1],
                                                   in1=acc[ts][:, 512 * nh:512 * (nh + 1)],
                                                   op0=ALU.mult, op1=ALU.add)
                            ev_y[("y", pj)] = (V, V.n)
                    t_after_ci[cb] = T.n
                t_after_y[ei] = T.n

            w2rdy = {}
            t_after_ci = {0: 0, 1: 0}
            for ei in range(9):
                if ei < 8:
                    emit_h(ei)
                if ei >= 1:
                    if ei >= 2:
                        SY.wait_ge(sem_t, t_after_y[ei - 2])
                    load_w2(ei - 1)
                    w2rdy[ei - 1] = nldw2[0]
                    emit_y(ei - 1)

        phase_barrier([nc.vector, nc.scalar, nc.tensor, nc.sync, nc.gpsimd],
                      [V, S, T],
                      [(ldw_s[0], nldws[0]), (ldw_s[1], nldws[1]), (ldw2[0], nldw2[0])])
        with ExitStack() as c4:
            t4 = c4.enter_context
            g_bc = t4(nc.sbuf_tensor("g_bc", [128, D], F32))
            b_bc = t4(nc.sbuf_tensor("b_bc", [128, D], F32))
            gb_row = t4(nc.sbuf_tensor("gb_row", [1, 2 * D], F32))
            SY.dma_start(out=gb_row[0:1, 0:D], in_=lng).then_inc(ld, 16)
            SY.dma_start(out=gb_row[0:1, D:2 * D], in_=lnb).then_inc(ld, 16)
            nld[0] += 32
            xn1T = [t4(nc.sbuf_tensor(f"xn1T{j}", [128, TCORE], BF16)) for j in range(8)]
            xatT = [t4(nc.sbuf_tensor(f"xatT{j}", [128, TCORE], BF16)) for j in range(8)]
            xn1t = t4(nc.sbuf_tensor("xn1t", [128, D], F32))
            xat = t4(nc.sbuf_tensor("xat", [128, D], F32))
            xbf = [t4(nc.sbuf_tensor(f"xbfb{b}", [128, D], BF16)) for b in range(2)]
            sqt = t4(nc.sbuf_tensor("sqt", [128, D], F32))
            muv = t4(nc.sbuf_tensor("muv", [128, 1], F32))
            ssv = t4(nc.sbuf_tensor("ssv", [128, 1], F32))
            rstd = t4(nc.sbuf_tensor("rstd", [128, 1], F32))
            ps_tp = [t4(nc.psum_tensor(f"ps_tp{b}", [128, 128], BF16)) for b in range(2)]
            ps_g = [t4(nc.psum_tensor(f"ps_g{b}", [128, 512], F32)) for b in range(2)]

            T.wait_sem(ld, nld[0])
            T.wait(V)
            for half in range(2):
                T.inc_last(T.matmul(ps_g[half][:], ones1[0:1, :],
                                    gb_row[0:1, 512 * half:512 * (half + 1)],
                                    start=True, stop=True))
            V.wait(T)
            V.tensor_copy(out=g_bc[:, 0:512], in_=ps_g[0][:])
            V.tensor_copy(out=g_bc[:, 512:1024], in_=ps_g[1][:])
            for half in range(2):
                T.wait(V)
                T.inc_last(T.matmul(ps_g[half][:], ones1[0:1, :],
                                    gb_row[0:1, D + 512 * half:D + 512 * (half + 1)],
                                    start=True, stop=True))
            V.wait(T)
            V.tensor_copy(out=b_bc[:, 0:512], in_=ps_g[0][:])
            V.tensor_copy(out=b_bc[:, 512:1024], in_=ps_g[1][:])

            def ln_tmajor(src_ap, dst, scale_gb):
                V.reduce_sum(out=muv[:], in_=src_ap, axis=AX.X)
                V.tensor_scalar(out=muv[:], in0=muv[:], scalar1=1.0 / D, scalar2=None,
                                op0=ALU.mult)
                V.tensor_scalar(out=dst[:], in0=src_ap, scalar1=muv[:, 0:1], scalar2=None,
                                op0=ALU.subtract)
                S.wait(V)
                S.activation(out=sqt[:], in_=dst[:], func=AF.Square, accum_out=ssv[:])
                S.activation(out=rstd[:], in_=ssv[:], func=AF.Ln, scale=1.0 / D,
                             bias=epsb[:, 0:1])
                S.activation(out=rstd[:], in_=rstd[:], func=AF.Exp, scale=-0.5)
                V.wait(S)
                V.tensor_scalar(out=dst[:], in0=dst[:], scalar1=rstd[:, 0:1], scalar2=None,
                                op0=ALU.mult)
                if scale_gb:
                    V.tensor_mul(out=dst[:], in0=dst[:], in1=g_bc[:])
                    V.tensor_add(out=dst[:], in0=dst[:], in1=b_bc[:])

            ev_tp = {}
            ktp = [0]

            def transposes(src_bf, dstT, ti):
                for j in range(8):
                    k = ktp[0]
                    bk = k % 2
                    if k >= 2:
                        eng, cnt = ev_tp[k - 2]
                        T.wait(eng, cnt)
                    T.inc_last(T.transpose(ps_tp[bk][:], src_bf[:, 128 * j:128 * (j + 1)],
                                           ident_bf[:]))
                    V.wait(T)
                    V.tensor_copy(out=dstT[j][:, 128 * ti:128 * (ti + 1)], in_=ps_tp[bk][:])
                    ev_tp[k] = (V, V.n)
                    ktp[0] += 1

            for ts in range(16):
                ln_tmajor(acc[ts][:], xn1t, False)
                V.tensor_copy(out=xbf[0][:], in_=xn1t[:])
                T.wait(V)
                transposes(xbf[0], xn1T, ts)
                ln_tmajor(xn1t[:], xat, True)
                V.tensor_copy(out=xbf[1][:], in_=xat[:])
                T.wait(V)
                transposes(xbf[1], xatT, ts)
                SY.wait_ge(sem_v, V.n)
                SY.dma_start(out=o_x1[128 * ts:128 * (ts + 1), :], in_=acc[ts][:]).then_inc(st, 16)
                nst[0] += 16
            SY.wait_ge(sem_v, V.n)
            for j in range(8):
                SY.dma_start(out=o_xn1[128 * j:128 * (j + 1), :], in_=xn1T[j][:]).then_inc(st, 16)
                SY.dma_start(out=o_xat[128 * j:128 * (j + 1), :], in_=xatT[j][:]).then_inc(st, 16)
                nst[0] += 32
            SY.wait_ge(st, nst[0])
            G_.wait_ge(st, nst[0])
    return nc


def run_M(A_outs, D_outs, inputs):
    yT = np.concatenate(
        [np.concatenate([np.asarray(D_outs[4 * b + j]["o_y"]) for j in range(4)], axis=0)
         for b in range(2)], axis=1)
    in_maps = []
    for c in range(NC):
        cols = slice(TCORE * c, TCORE * (c + 1))
        x_tm = inputs["L0_attn_x"].reshape(16384, D)[cols]
        in_maps.append({
            "yT": f32c(yT[:, cols]),
            "szT": f32c(np.asarray(A_outs[c]["o_sz"])),
            "xnbfT": np.ascontiguousarray(np.asarray(A_outs[c]["o_xnbf"]).astype(BF)),
            "wmoe": f32c(np.asarray(A_outs[c]["o_wmoe"])),
            "x_tm": f32c(x_tm),
            "opw": np.ascontiguousarray(np.asarray(A_outs[c]["o_opwb"]).astype(BF)),
            "w1": np.ascontiguousarray(np.asarray(A_outs[c]["o_w1b"]).astype(BF)),
            "w3": np.ascontiguousarray(np.asarray(A_outs[c]["o_w3b"]).astype(BF)),
            "w2": np.ascontiguousarray(np.asarray(A_outs[c]["o_w2b"]).astype(BF)),
            "lng": f32c(inputs["L1_attn_ln_g"].reshape(1, D)),
            "lnb": f32c(inputs["L1_attn_ln_b"].reshape(1, D)),
        })
    if "M" not in _A_CACHE:
        _A_CACHE["M"] = build_M()
    res = run_bass_kernel_spmd(_A_CACHE["M"], in_maps, core_ids=list(range(NC)))
    return res.results


def build_F():
    nc = bass.Bass("TRN2", target_bir_lowering=False, debug=False)
    TB = 8192
    NKT = TB // 128
    NQC = TB // 512
    xatT = nc.dram_tensor("xatT", [D, TB], BF16, kind="ExternalInput").ap()
    wqT = nc.dram_tensor("wqT", [D, 256], F32, kind="ExternalInput").ap()
    wkT = nc.dram_tensor("wkT", [D, 256], F32, kind="ExternalInput").ap()
    wvT = nc.dram_tensor("wvT", [D, 256], F32, kind="ExternalInput").ap()
    qg = nc.dram_tensor("qg", [64, 1], F32, kind="ExternalInput").ap()
    kg = nc.dram_tensor("kg", [64, 1], F32, kind="ExternalInput").ap()
    o_oh = nc.dram_tensor("o_oh", [256, TB], BF16, kind="ExternalOutput").ap()
    H4 = 4096
    cv_shapes = [[H4, H4], [H4, H4], [D, H4], [D, H4], [H4, D], [D, D]]
    cv_in = [nc.dram_tensor(f"cv_in{i}", shp, F32, kind="ExternalInput").ap()
             for i, shp in enumerate(cv_shapes)]
    cv_out = [nc.dram_tensor(f"cv_out{i}", shp, BF16, kind="ExternalOutput").ap()
              for i, shp in enumerate(cv_shapes)]

    RMS_EPS = 1e-8
    SCALE = 0.125

    with ExitStack() as ctx:
        e = ctx.enter_context
        xat = [e(nc.sbuf_tensor(f"xat{j}", [128, TB], BF16)) for j in range(8)]
        qn_bf = e(nc.sbuf_tensor("qn_bf", [128, TB], BF16))
        kn_bf = e(nc.sbuf_tensor("kn_bf", [128, TB], BF16))
        vo_all = e(nc.sbuf_tensor("vo_all", [128, 65 * NKT], BF16))
        vo = [vo_all[:, 65 * t:65 * (t + 1)] for t in range(NKT)]
        w_sb = [e(nc.sbuf_tensor(f"w_sb{i}", [128, 256], BF16)) for i in range(24)]
        gam = e(nc.sbuf_tensor("gam", [64, 2], F32))
        ones1 = e(nc.sbuf_tensor("ones1", [1, 128], F32))
        onesc = e(nc.sbuf_tensor("onesc", [64, 1], F32))
        sq_t = e(nc.sbuf_tensor("sq_t", [64, 512], F32))
        tmp_t = e(nc.sbuf_tensor("tmp_t", [64, 512], F32))
        row_a = e(nc.sbuf_tensor("row_a", [1, 512], F32))
        row_b = e(nc.sbuf_tensor("row_b", [1, 512], F32))
        cacc = e(nc.sbuf_tensor("cacc", [1, 4], F32))
        negC_bc = e(nc.sbuf_tensor("negC_bc", [128, 1], F32))
        exp_bf = [e(nc.sbuf_tensor(f"exp{b}", [128, 1024], BF16)) for b in range(3)]
        rec_bc = e(nc.sbuf_tensor("rec_bc", [64, 512], F32))
        o_sb = e(nc.sbuf_tensor("o_sb", [64, 512], BF16))

        cstage_f = [e(nc.sbuf_tensor(f"cstagef{b}", [128, 512], F32)) for b in range(2)]
        cstage_b = [e(nc.sbuf_tensor(f"cstageb{b}", [128, 512], BF16)) for b in range(2)]
        ld_cv = [e(nc.semaphore(f"ld_cv{b}")) for b in range(2)]
        st_cv = [e(nc.semaphore(f"st_cv{b}")) for b in range(2)]
        sem_g = e(nc.semaphore("sem_g"))
        ld = e(nc.semaphore("ld"))
        ldg = e(nc.semaphore("ldg"))
        sem_v = e(nc.semaphore("sem_v"))
        sem_s = e(nc.semaphore("sem_s"))
        sem_t = e(nc.semaphore("sem_t"))
        st_o = e(nc.semaphore("st_o"))

        V = EW(nc.vector, sem_v)
        S = EW(nc.scalar, sem_s)
        T = EW(nc.tensor, sem_t, autosync=False)
        SY = nc.sync
        G_ = nc.gpsimd
        nld = [0]
        nlg = [0]
        nst = [0]

        V.memset(ones1[:], 1.0)
        V.memset(onesc[:], 1.0)
        GP = EW(nc.gpsimd, sem_g)
        cv_pairs = []
        for i in range(6):
            si, so = cv_in[i], cv_out[i]
            f = si.shape[1] // 512
            if f > 1:
                si = si.rearrange("a (b c) -> (a b) c", b=f)
                so = so.rearrange("a (b c) -> (a b) c", b=f)
            cv_pairs.append((si, so))
        cp = ConvertPool(G_, GP, cstage_f, cstage_b, ld_cv, st_cv, cv_pairs)
        for j in range(8):
            SY.dma_start(out=xat[j][:], in_=xatT[128 * j:128 * (j + 1), :]).then_inc(ld, 16)
        SY.dma_start(out=gam[0:64, 0:1], in_=qg).then_inc(ld, 16)
        SY.dma_start(out=gam[0:64, 1:2], in_=kg).then_inc(ld, 16)
        nld[0] += 16 * 10
        for j in range(8):
            G_.dma_start(out=w_sb[j][:], in_=wqT[128 * j:128 * (j + 1), :]).then_inc(ldg, 16)
            G_.dma_start(out=w_sb[8 + j][:], in_=wkT[128 * j:128 * (j + 1), :]).then_inc(ldg, 16)
            G_.dma_start(out=w_sb[16 + j][:], in_=wvT[128 * j:128 * (j + 1), :]).then_inc(ldg, 16)
        nlg[0] += 16 * 24
        V.wait_sem(ld, nld[0])
        V.tensor_scalar(out=gam[:, 0:1], in0=gam[:, 0:1], scalar1=SCALE, scalar2=None,
                        op0=ALU.mult)

        with ExitStack() as cq:
            te = cq.enter_context

            for h in range(4):
              with ExitStack() as cqk:
                tq = cqk.enter_context
                ps_q = tq(nc.psum_tensor(f"ps_q{h}", [64, 512], F32))
                ps_n = tq(nc.psum_tensor(f"ps_n{h}", [1, 512], F32))
                ps_r = tq(nc.psum_tensor(f"ps_r{h}", [128, 512], F32))
                ps_v = [tq(nc.psum_tensor(f"ps_v{h}_{b}", [128, 64], F32)) for b in range(2)]
                for which, wbase, gcol, dst, ccol in ((0, 0, 0, qn_bf, 0), (1, 8, 1, kn_bf, 1)):
                  for cidx in range(NQC):
                      c0 = 512 * cidx
                      T.wait_sem(ldg, nlg[0])
                      T.wait(V)
                      for kt in range(8):
                          mm = T.matmul(ps_q[:], w_sb[wbase + kt][:, 64 * h:64 * (h + 1)],
                                        xat[kt][:, c0:c0 + 512],
                                        start=(kt == 0), stop=(kt == 7))
                      T.inc_last(mm)
                      S.wait(T)
                      S.wait(V)
                      S.activation(out=sq_t[:], in_=ps_q[:], func=AF.Square)
                      T.wait(S)
                      T.inc_last(T.matmul(ps_n[:], onesc[:], sq_t[:], start=True, stop=True))
                      S.wait(T)
                      S.wait(V)
                      S.activation(out=row_a[:], in_=ps_n[:], func=AF.Sqrt)
                      V.wait(S)
                      V.tensor_scalar(out=row_a[:], in0=row_a[:], scalar1=SCALE,
                                      scalar2=RMS_EPS, op0=ALU.mult, op1=ALU.max)
                      V.reciprocal(out=row_b[:], in_=row_a[:])
                      T.wait(V)
                      T.inc_last(T.matmul(ps_r[0:64, :], ones1[0:1, 0:64], row_b[:],
                                          start=True, stop=True))
                      V.tensor_scalar(out=tmp_t[:], in0=ps_q[:], scalar1=gam[:, gcol:gcol + 1],
                                      scalar2=None, op0=ALU.mult)
                      V.wait(T)
                      V.tensor_mul(out=dst[0:64, c0:c0 + 512], in0=tmp_t[:], in1=ps_r[0:64, :])
                      S.wait(V)
                      S.activation(out=sq_t[:], in_=tmp_t[:], func=AF.Square)
                      T.wait(S)
                      T.inc_last(T.matmul(ps_n[:], onesc[:], sq_t[:], start=True, stop=True))
                      V.wait(T)
                      V.tensor_mul(out=row_a[:], in0=ps_n[:], in1=row_b[:])
                      V.tensor_mul(out=row_a[:], in0=row_a[:], in1=row_b[:])
                      V.reduce_max(out=row_a[0:1, 0:1], in_=row_a[:], axis=AX.X)
                      if cidx == 0:
                          V.tensor_copy(out=cacc[0:1, ccol:ccol + 1], in_=row_a[0:1, 0:1])
                      else:
                          V.tensor_max(out=cacc[0:1, ccol:ccol + 1],
                                       in0=cacc[0:1, ccol:ccol + 1], in1=row_a[0:1, 0:1])
              SY.wait_ge(sem_v, V.n)
              SY.dma_start(out=qn_bf[64:128, :], in_=qn_bf[0:64, :]).then_inc(ld, 16)
              SY.dma_start(out=kn_bf[64:128, :], in_=kn_bf[0:64, :]).then_inc(ld, 16)
              nld[0] += 32
              S.wait(V)
              S.activation(out=cacc[0:1, 0:1], in_=cacc[0:1, 0:1], func=AF.Sqrt)
              S.activation(out=cacc[0:1, 1:2], in_=cacc[0:1, 1:2], func=AF.Sqrt)
              V.wait(S)
              V.tensor_mul(out=cacc[0:1, 2:3], in0=cacc[0:1, 0:1], in1=cacc[0:1, 1:2])
              V.tensor_scalar(out=cacc[0:1, 3:4], in0=cacc[0:1, 2:3], scalar1=-1.0,
                              scalar2=None, op0=ALU.mult)
              T.wait(V)
              T.inc_last(T.matmul(ps_r[:, 0:1], ones1[0:1, :], cacc[0:1, 3:4],
                                  start=True, stop=True))
              V.wait(T)
              V.tensor_copy(out=negC_bc[0:128, 0:1], in_=ps_r[0:128, 0:1])
              for t in range(NKT):
                  b = t % 2
                  if t >= 2:
                      T.wait(V)
                  for kt in range(8):
                      mm = T.matmul(ps_v[b][:], xat[kt][:, 128 * t:128 * (t + 1)],
                                    w_sb[16 + kt][:, 64 * h:64 * (h + 1)],
                                    start=(kt == 0), stop=(kt == 7))
                  T.inc_last(mm)
                  V.wait(T)
                  V.tensor_copy(out=vo[t][:, 0:64], in_=ps_v[b][:])
                  V.memset(vo[t][:, 64:65], 1.0)
                  cp.step(2)
              with ExitStack() as ca:
                    ta = ca.enter_context
                    ps_s = [ta(nc.psum_tensor(f"ps_s{h}_{b}", [128, 1024], F32)) for b in range(3)]
                    ps_o = ta(nc.psum_tensor(f"ps_o{h}", [65, 512], F32))
                    ps_r = ta(nc.psum_tensor(f"ps_r2{h}", [64, 512], F32))
                    T.wait(V)
                    T.wait(S)
                    T.wait_sem(ld, nld[0])
                    s_after_exp = {}
                    t_after_av = {}
                    NP = NKT // 2
                    for qc in range(NQC):
                        c0 = 512 * qc
                        T.wait(V)
                        for p in range(NP + 1):
                            if p < NP:
                                pb = p % 3
                                if pb in s_after_exp:
                                    T.wait(S, s_after_exp[pb])
                                kc = 2 * p
                                T.matmul(ps_s[pb][:, 0:512],
                                         kn_bf[0:64, 128 * kc:128 * (kc + 1)],
                                         qn_bf[0:64, c0:c0 + 512],
                                         start=True, stop=True, tile_position=(0, 0))
                                mm = T.matmul(ps_s[pb][:, 512:1024],
                                              kn_bf[64:128, 128 * (kc + 1):128 * (kc + 2)],
                                              qn_bf[64:128, c0:c0 + 512],
                                              start=True, stop=True, tile_position=(64, 0))
                                T.inc_last(mm)
                                if pb in t_after_av:
                                    S.wait(T, t_after_av[pb])
                                S.wait(T)
                                S.activation(out=exp_bf[pb][:], in_=ps_s[pb][:], func=AF.Exp,
                                             bias=negC_bc[:, 0:1])
                                s_after_exp[pb] = S.n
                            if p > 0:
                                pb1 = (p - 1) % 3
                                T.wait(S, s_after_exp[pb1])
                                for u in range(2):
                                    kc = 2 * (p - 1) + u
                                    mm = T.matmul(ps_o[:], vo[kc],
                                                  exp_bf[pb1][:, 512 * u:512 * (u + 1)],
                                                  start=(kc == 0), stop=(kc == NKT - 1))
                                T.inc_last(mm)
                                t_after_av[pb1] = T.n
                        V.wait(T)
                        V.wait_sem(st_o, nst[0])
                        V.tensor_scalar(out=row_a[:], in0=ps_o[64:65, :], scalar1=1e-30,
                                        scalar2=None, op0=ALU.add)
                        V.reciprocal(out=row_b[:], in_=row_a[:])
                        T.wait(V)
                        T.inc_last(T.matmul(ps_r[0:64, :], ones1[0:1, 0:64], row_b[:],
                                            start=True, stop=True))
                        V.wait(T)
                        V.tensor_copy(out=rec_bc[:], in_=ps_r[0:64, :])
                        V.tensor_mul(out=o_sb[:], in0=ps_o[0:64, :], in1=rec_bc[:])
                        SY.wait_ge(sem_v, V.n)
                        SY.dma_start(out=o_oh[64 * h:64 * (h + 1), c0:c0 + 512],
                                     in_=o_sb[:]).then_inc(st_o, 16)
                        nst[0] += 16
                        cp.step(4)

        cp.drain()
        cp.finish()
        SY.wait_ge(st_o, nst[0])
        G_.wait_ge(st_o, nst[0])
    return nc


def run_F(M_outs, inputs):
    in_maps = []
    for c in range(NC):
        b = c // 4
        j = c % 4
        xatT = np.concatenate([np.asarray(M_outs[4 * b + i]["o_xat"]) for i in range(4)], axis=1)
        hs = slice(256 * j, 256 * (j + 1))
        in_maps.append({
            "xatT": np.ascontiguousarray(xatT.astype(BF)),
            "wqT": f32c(inputs["L1_attn_Wq"].T[:, hs]),
            "wkT": f32c(inputs["L1_attn_Wk"].T[:, hs]),
            "wvT": f32c(inputs["L1_attn_Wv"].T[:, hs]),
            "qg": f32c(inputs["L1_attn_q_gamma"].reshape(64, 1)),
            "kg": f32c(inputs["L1_attn_k_gamma"].reshape(64, 1)),
            "cv_in0": f32c(inputs["L1_ffn_h1_w"].T),
            "cv_in1": f32c(inputs["L1_ffn_h2_w"].T),
            "cv_in2": f32c(inputs["L1_ffn_w1_w"].T),
            "cv_in3": f32c(inputs["L1_ffn_w2_w"].T),
            "cv_in4": f32c(inputs["L1_ffn_w3_w"].T),
            "cv_in5": f32c(inputs["L1_attn_Wo"].T),
        })
    if "F" not in _A_CACHE:
        _A_CACHE["F"] = build_F()
    res = run_bass_kernel_spmd(_A_CACHE["F"], in_maps, core_ids=list(range(NC)))
    return res.results


def build_G():
    nc = bass.Bass("TRN2", target_bir_lowering=False, debug=False)
    H4 = 4096
    xn1T = nc.dram_tensor("xn1T", [D, TCORE], BF16, kind="ExternalInput").ap()
    ohT = nc.dram_tensor("ohT", [D, TCORE], BF16, kind="ExternalInput").ap()
    x1 = nc.dram_tensor("x1", [TCORE, D], F32, kind="ExternalInput").ap()
    woT = nc.dram_tensor("woT", [D, D], BF16, kind="ExternalInput").ap()
    bo = nc.dram_tensor("bo", [1, D], F32, kind="ExternalInput").ap()
    w1T = nc.dram_tensor("w1T", [D, H4], BF16, kind="ExternalInput").ap()
    w2T = nc.dram_tensor("w2T", [D, H4], BF16, kind="ExternalInput").ap()
    h1T = nc.dram_tensor("h1T", [H4, H4], BF16, kind="ExternalInput").ap()
    h2T = nc.dram_tensor("h2T", [H4, H4], BF16, kind="ExternalInput").ap()
    w3T = nc.dram_tensor("w3T", [H4, D], BF16, kind="ExternalInput").ap()
    s1 = nc.dram_tensor("s1", [H4, 1], F32, kind="ExternalInput").ap()
    s2 = nc.dram_tensor("s2", [H4, 1], F32, kind="ExternalInput").ap()
    sh1 = nc.dram_tensor("sh1", [H4, 1], F32, kind="ExternalInput").ap()
    sh2 = nc.dram_tensor("sh2", [H4, 1], F32, kind="ExternalInput").ap()
    s3r = nc.dram_tensor("s3r", [1, D], F32, kind="ExternalInput").ap()
    o_x2 = nc.dram_tensor("o_x2", [TCORE, D], F32, kind="ExternalOutput").ap()

    with ExitStack() as ctx:
        e = ctx.enter_context
        acc = [e(nc.sbuf_tensor(f"acc{t}", [128, D], F32)) for t in range(16)]
        xn1 = [e(nc.sbuf_tensor(f"xn1_{j}", [128, TCORE], BF16)) for j in range(8)]
        bo_bc = e(nc.sbuf_tensor("bo_bc", [128, D], F32))
        s3_bc = e(nc.sbuf_tensor("s3_bc", [128, D], F32))
        bs_row = e(nc.sbuf_tensor("bs_row", [1, 2 * D], F32))
        ones1 = e(nc.sbuf_tensor("ones1", [1, 128], F32))
        sv_all = e(nc.sbuf_tensor("sv_all", [128, 128], F32))
        sv_off = {"s1": 0, "s2": 32, "sh1": 64, "sh2": 96}
        t1s = e(nc.sbuf_tensor("t1s", [128, 512], F32))
        t2s = e(nc.sbuf_tensor("t2s", [128, 512], F32))

        ld = e(nc.semaphore("ld"))
        ldg = e(nc.semaphore("ldg"))
        ldh = [e(nc.semaphore(f"ldh{p}")) for p in range(2)]
        ldw3 = [e(nc.semaphore(f"ldw3{p}")) for p in range(2)]
        sem_v = e(nc.semaphore("sem_v"))
        sem_s = e(nc.semaphore("sem_s"))
        sem_t = e(nc.semaphore("sem_t"))
        st = e(nc.semaphore("st"))

        V = EW(nc.vector, sem_v)
        S = EW(nc.scalar, sem_s)
        T = EW(nc.tensor, sem_t, autosync=False)
        SY = nc.sync
        G_ = nc.gpsimd
        nld = [0]
        nlg = [0]
        nldh = [0, 0]
        nldw3 = [0, 0]
        nst = [0]

        V.memset(ones1[:], 1.0)
        SY.dma_start(out=bs_row[0:1, 0:D], in_=bo).then_inc(ld, 16)
        SY.dma_start(out=bs_row[0:1, D:2 * D], in_=s3r).then_inc(ld, 16)
        for t in range(16):
            SY.dma_start(out=acc[t][:], in_=x1[128 * t:128 * (t + 1), :]).then_inc(ld, 16)
        for j in range(8):
            SY.dma_start(out=xn1[j][:], in_=xn1T[128 * j:128 * (j + 1), :]).then_inc(ld, 16)
        for nm, t in (("s1", s1), ("s2", s2), ("sh1", sh1), ("sh2", sh2)):
            o = sv_off[nm]
            for m in range(32):
                SY.dma_start(out=sv_all[:, o + m:o + m + 1],
                             in_=t[128 * m:128 * (m + 1), :]).then_inc(ld, 16)
        nld[0] += 16 * (2 + 16 + 8 + 128)
        LD_BASE = nld[0]

        with ExitStack() as c0:
            t0 = c0.enter_context
            ps_b = [t0(nc.psum_tensor(f"ps_bc{b}", [128, 512], F32)) for b in range(2)]
            T.wait_sem(ld, LD_BASE)
            T.wait(V)
            for half in range(2):
                T.inc_last(T.matmul(ps_b[half][:], ones1[0:1, :],
                                    bs_row[0:1, 512 * half:512 * (half + 1)],
                                    start=True, stop=True))
            V.wait(T)
            V.tensor_copy(out=bo_bc[:, 0:512], in_=ps_b[0][:])
            V.tensor_copy(out=bo_bc[:, 512:1024], in_=ps_b[1][:])
            for half in range(2):
                T.wait(V)
                T.inc_last(T.matmul(ps_b[half][:], ones1[0:1, :],
                                    bs_row[0:1, D + 512 * half:D + 512 * (half + 1)],
                                    start=True, stop=True))
            V.wait(T)
            V.tensor_copy(out=s3_bc[:, 0:512], in_=ps_b[0][:])
            V.tensor_copy(out=s3_bc[:, 512:1024], in_=ps_b[1][:])

        phase_barrier([nc.vector, nc.scalar, nc.tensor, nc.sync, nc.gpsimd],
                      [V, S, T], [(ld, nld[0])])
        with ExitStack() as c1:
            t1 = c1.enter_context
            oh_sb = [t1(nc.sbuf_tensor(f"oh{j}", [128, TCORE], BF16)) for j in range(8)]
            wo_bf = [t1(nc.sbuf_tensor(f"wo{j}", [128, D], BF16)) for j in range(8)]
            ps_m = [t1(nc.psum_tensor(f"ps_m{i}", [128, 512], F32)) for i in range(4)]
            for j in range(8):
                SY.dma_start(out=oh_sb[j][:], in_=ohT[128 * j:128 * (j + 1), :]).then_inc(ld, 16)
                SY.dma_start(out=wo_bf[j][:], in_=woT[128 * j:128 * (j + 1), :]).then_inc(ldg, 16)
            nld[0] += 16 * 8
            nlg[0] += 16 * 8
            T.wait_sem(ld, nld[0])
            T.wait_sem(ldg, nlg[0])
            ev_ps = {}
            for ts in range(16):
                for nh in range(2):
                    gi = 2 * ts + nh
                    pi = gi % 4
                    if gi >= 4:
                        eng, cnt = ev_ps[pi]
                        T.wait(eng, cnt)
                    for kt in range(8):
                        mm = T.matmul(ps_m[pi][:], oh_sb[kt][:, 128 * ts:128 * (ts + 1)],
                                      wo_bf[kt][:, 512 * nh:512 * (nh + 1)],
                                      start=(kt == 0), stop=(kt == 7))
                    T.inc_last(mm)
                    V.wait(T)
                    V.tensor_add(out=acc[ts][:, 512 * nh:512 * (nh + 1)],
                                 in0=acc[ts][:, 512 * nh:512 * (nh + 1)], in1=ps_m[pi][:])
                    ev_ps[pi] = (V, V.n)
                V.tensor_add(out=acc[ts][:], in0=acc[ts][:], in1=bo_bc[:])

        phase_barrier([nc.vector, nc.scalar, nc.tensor, nc.sync, nc.gpsimd],
                      [V, S, T], [(ld, nld[0]), (ldg, nlg[0])])
        with ExitStack() as c2:
            t2 = c2.enter_context
            h_bf = [t2(nc.sbuf_tensor(f"hbf{k}", [128, 512], BF16)) for k in range(32)]
            hh_bf = [t2(nc.sbuf_tensor(f"hh{k}", [128, 512], BF16)) for k in range(32)]
            wsl = [t2(nc.sbuf_tensor("wsl0", [128, 2 * 8 * 128], BF16))]
            hsl = [t2(nc.sbuf_tensor(f"hsl{p}", [128, 32 * 128], BF16)) for p in range(2)]
            w3b = [t2(nc.sbuf_tensor(f"w3b{p}", [128, D], BF16)) for p in range(2)]
            ps_p = [t2(nc.psum_tensor(f"ps_p{i}", [128, 512], F32)) for i in range(4)]
            ps_q5 = [t2(nc.psum_tensor(f"ps_q5{i}", [128, 512], F32)) for i in range(4)]

            def load_w12(hm, buf):
                SY.dma_start(out=wsl[buf][:, 0:1024].rearrange("p (k c) -> p k c", k=8),
                             in_=w1T[:, 128 * hm:128 * (hm + 1)].rearrange(
                                 "(k p) c -> p k c", p=128)).then_inc(ldh[buf], 16)
                SY.dma_start(out=wsl[buf][:, 1024:2048].rearrange("p (k c) -> p k c", k=8),
                             in_=w2T[:, 128 * hm:128 * (hm + 1)].rearrange(
                                 "(k p) c -> p k c", p=128)).then_inc(ldh[buf], 16)
                nldh[buf] += 32

            def load_hsl(mat, om, buf):
                for a in range(4):
                    SY.dma_start(
                        out=hsl[buf][:, 1024 * a:1024 * (a + 1)].rearrange(
                            "p (k c) -> p k c", k=8),
                        in_=mat[1024 * a:1024 * (a + 1),
                                128 * om:128 * (om + 1)].rearrange("(k p) c -> p k c", p=128)
                    ).then_inc(ldh[buf], 16)
                nldh[buf] += 16 * 4

            sidx = [0]
            t_after_s = {}
            ev_pp = {}
            for tq in range(4):
                tb = 512 * tq
                V.wait(T)
                for hm in range(32):
                    si = sidx[0]
                    sb = 0
                    if si >= 1:
                        SY.wait_ge(sem_t, t_after_s[si - 1])
                    load_w12(hm, sb)
                    wrdy = nldh[sb]
                    sidx[0] += 1
                    T.wait_sem(ldh[sb], wrdy)
                    pp = hm % 2
                    pa, pb = ps_p[2 * pp], ps_p[2 * pp + 1]
                    if ("g2", pp) in ev_pp:
                        eng, cnt = ev_pp[("g2", pp)]
                        T.wait(eng, cnt)
                    for kt in range(8):
                        T.matmul(pa[:], wsl[sb][:, 128 * kt:128 * (kt + 1)],
                                 xn1[kt][:, tb:tb + 512], start=(kt == 0), stop=(kt == 7))
                    for kt in range(8):
                        mm = T.matmul(pb[:], wsl[sb][:, 1024 + 128 * kt:1024 + 128 * (kt + 1)],
                                      xn1[kt][:, tb:tb + 512], start=(kt == 0), stop=(kt == 7))
                    T.inc_last(mm)
                    t_after_s[si] = T.n
                    S.wait(T)
                    S.wait(V)
                    S.activation(out=t1s[:], in_=pa[:], func=AF.Silu,
                                 scale=sv_all[:, sv_off["s1"] + hm:sv_off["s1"] + hm + 1])
                    V.wait(S)
                    V.tensor_scalar(out=t2s[:], in0=pb[:],
                                    scalar1=sv_all[:, sv_off["s2"] + hm:sv_off["s2"] + hm + 1],
                                    scalar2=None, op0=ALU.mult)
                    V.tensor_mul(out=h_bf[hm][:], in0=t1s[:], in1=t2s[:])
                    ev_pp[("g2", pp)] = (V, V.n)
                for layer, mat, svec, src_t, dst_t in (
                        (0, h1T, "sh1", h_bf, hh_bf), (1, h2T, "sh2", hh_bf, h_bf)):
                    hrdy = {}
                    t_after_om = {}
                    SY.wait_ge(sem_t, T.n)
                    load_hsl(mat, 0, 0)
                    hrdy[0] = (0, nldh[0])
                    for om in range(32):
                        sb = om % 2
                        if om + 1 < 32:
                            nb = (om + 1) % 2
                            if om >= 1:
                                SY.wait_ge(sem_t, t_after_om[om - 1])
                            load_hsl(mat, om + 1, nb)
                            hrdy[om + 1] = (nb, nldh[nb])
                        rb, rv = hrdy[om]
                        T.wait_sem(ldh[rb], rv)
                        pp = om % 2
                        pa = ps_p[2 * pp]
                        if ("g3", pp) in ev_pp:
                            eng, cnt = ev_pp[("g3", pp)]
                            T.wait(eng, cnt)
                        if om == 0:
                            T.wait(S)
                            T.wait(V)
                        for kt in range(32):
                            mm = T.matmul(pa[:], hsl[sb][:, 128 * kt:128 * (kt + 1)],
                                          src_t[kt][:], start=(kt == 0), stop=(kt == 31))
                        T.inc_last(mm)
                        t_after_om[om] = T.n
                        S.wait(T)
                        S.activation(out=dst_t[om][:], in_=pa[:], func=AF.Gelu,
                                     scale=sv_all[:, sv_off[svec] + om:sv_off[svec] + om + 1])
                        ev_pp[("g3", pp)] = (S, S.n)
                psall = ps_p + ps_q5
                T.wait(S)
                T.wait(V)
                ldw3_r = {}
                t_after_w3 = {}
                nb0 = (2 * tq) % 2
                SY.wait_ge(sem_t, T.n)
                SY.dma_start(out=w3b[0][:], in_=w3T[0:128, :]).then_inc(ldw3[0], 16)
                nldw3[0] += 16
                ldw3_r[0] = nldw3[0]
                for kt in range(32):
                    kb = kt % 2
                    if kt + 1 < 32:
                        nb = (kt + 1) % 2
                        if kt >= 1:
                            SY.wait_ge(sem_t, t_after_w3[kt - 1])
                        SY.dma_start(out=w3b[nb][:],
                                     in_=w3T[128 * (kt + 1):128 * (kt + 2), :]).then_inc(ldw3[nb], 16)
                        nldw3[nb] += 16
                        ldw3_r[kt + 1] = nldw3[nb]
                    T.wait_sem(ldw3[kb], ldw3_r[kt])
                    for gi in range(8):
                        tsl, nh = divmod(gi, 2)
                        mm = T.matmul(psall[gi][:], h_bf[kt][:, 128 * tsl:128 * (tsl + 1)],
                                      w3b[kb][:, 512 * nh:512 * (nh + 1)],
                                      start=(kt == 0), stop=(kt == 31))
                    T.inc_last(mm)
                    t_after_w3[kt] = T.n
                V.wait(T)
                for gi in range(8):
                    tsl, nh = divmod(gi, 2)
                    ts = 4 * tq + tsl
                    V.tensor_mul(out=t1s[:], in0=psall[gi][:],
                                 in1=s3_bc[:, 512 * nh:512 * (nh + 1)])
                    V.tensor_add(out=acc[ts][:, 512 * nh:512 * (nh + 1)],
                                 in0=acc[ts][:, 512 * nh:512 * (nh + 1)], in1=t1s[:])
                for key in (("g2", 0), ("g2", 1), ("g3", 0), ("g3", 1)):
                    ev_pp[key] = (V, V.n)
                for tsl in range(4):
                    ts = 4 * tq + tsl
                    SY.wait_ge(sem_v, V.n)
                    SY.dma_start(out=o_x2[128 * ts:128 * (ts + 1), :],
                                 in_=acc[ts][:]).then_inc(st, 16)
                    nst[0] += 16
            SY.wait_ge(st, nst[0])
            G_.wait_ge(st, nst[0])
    return nc


def run_G(M_outs, F_outs, inputs):
    in_maps = []
    for c in range(NC):
        b = c // 4
        ohT_b = np.concatenate([np.asarray(F_outs[4 * b + i]["o_oh"]) for i in range(4)], axis=0)
        cols = slice(TCORE * (c % 4), TCORE * (c % 4 + 1))
        in_maps.append({
            "xn1T": np.ascontiguousarray(np.asarray(M_outs[c]["o_xn1"]).astype(BF)),
            "ohT": np.ascontiguousarray(ohT_b[:, cols].astype(BF)),
            "x1": f32c(np.asarray(M_outs[c]["o_x1"])),
            "woT": np.ascontiguousarray(np.asarray(F_outs[c]["cv_out5"]).astype(BF)),
            "bo": f32c(inputs["L1_attn_bo"].reshape(1, D)),
            "w1T": np.ascontiguousarray(np.asarray(F_outs[c]["cv_out2"]).astype(BF)),
            "w2T": np.ascontiguousarray(np.asarray(F_outs[c]["cv_out3"]).astype(BF)),
            "h1T": np.ascontiguousarray(np.asarray(F_outs[c]["cv_out0"]).astype(BF)),
            "h2T": np.ascontiguousarray(np.asarray(F_outs[c]["cv_out1"]).astype(BF)),
            "w3T": np.ascontiguousarray(np.asarray(F_outs[c]["cv_out4"]).astype(BF)),
            "s1": f32c(inputs["L1_ffn_w1_s"].reshape(-1, 1)),
            "s2": f32c(inputs["L1_ffn_w2_s"].reshape(-1, 1)),
            "sh1": f32c(inputs["L1_ffn_h1_s"].reshape(-1, 1)),
            "sh2": f32c(inputs["L1_ffn_h2_s"].reshape(-1, 1)),
            "s3r": f32c(inputs["L1_ffn_w3_s"].reshape(1, D)),
        })
    if "G" not in _A_CACHE:
        _A_CACHE["G"] = build_G()
    res = run_bass_kernel_spmd(_A_CACHE["G"], in_maps, core_ids=list(range(NC)))
    return res.results


    return nc


_A_CACHE = {}


def run_A(inputs):
    x = inputs["L0_attn_x"].reshape(16384, D)
    in_maps = []
    for c in range(NC):
        t0 = TCORE * c
        if c % 4 == 0:
            xh = np.concatenate([np.zeros((HALO, D), np.float32),
                                 x[t0:t0 + TCORE]], axis=0)
        else:
            xh = x[t0 - HALO:t0 + TCORE]
        in_maps.append({
            "xi": f32c(xh),
            "ipw": f32c(inputs["L0_attn_in_proj_w"]),
            "convw": f32c(inputs["L0_attn_conv_w"]),
            "convb": f32c(inputs["L0_attn_conv_b"].reshape(DI, 1)),
            "xpw": f32c(inputs["L0_attn_x_proj_w"]),
            "dtw": f32c(inputs["L0_attn_dt_proj_w"]),
            "dtb": f32c(inputs["L0_attn_dt_proj_b"].reshape(DI, 1)),
            "wgate": f32c(inputs["L0_ffn_w_gate"]),
            "w1f": f32c(inputs["L0_ffn_w1"].reshape(8 * D, DI)),
            "w3f": f32c(inputs["L0_ffn_w3"].reshape(8 * D, DI)),
            "w2f": f32c(inputs["L0_ffn_w2"].reshape(8 * DI, D)),
            "opwf": f32c(inputs["L0_attn_out_proj_w"]),
        })
    if "A" not in _A_CACHE:
        _A_CACHE["A"] = build_A()
    res = run_bass_kernel_spmd(_A_CACHE["A"], in_maps, core_ids=list(range(NC)))
    return res.results


def kernel(**inputs):
    A = run_A(inputs)
    Dr = run_D(A, inputs)
    M = run_M(A, Dr, inputs)
    F = run_F(M, inputs)
    G = run_G(M, F, inputs)
    x2 = np.concatenate([np.asarray(G[c]["o_x2"]) for c in range(NC)], axis=0)
    return np.ascontiguousarray(x2.reshape(2, 8192, D), dtype=np.float32)


# version 06
# speedup vs baseline: 1.1691x; 1.0424x over previous
import numpy as np
import ml_dtypes
from contextlib import ExitStack
import concourse.bass as bass
import concourse.mybir as mybir
from concourse.bass_utils import run_bass_kernel_spmd

F32 = mybir.dt.float32
BF16 = mybir.dt.bfloat16
AF = mybir.ActivationFunctionType
ALU = mybir.AluOpType
AX = mybir.AxisListType
BF = ml_dtypes.bfloat16

NC = 8
D = 1024
DI = 2048
DS = 16
TCORE = 2048
HALO = 128
TA = TCORE + HALO
LN_EPS = 1e-5


def f32c(a):
    return np.ascontiguousarray(a, dtype=np.float32)


def chunks(total, step, base=0):
    out = []
    o = base
    while o < base + total:
        n = min(step, base + total - o)
        out.append((o, n))
        o += n
    return out



class EW:

    def __init__(self, eng, sem, autosync=True):
        self.e = eng
        self.sem = sem
        self.n = 0
        self.autosync = autosync

    def wait(self, other, val=None):
        self.e.wait_ge(other.sem, other.n if val is None else val)

    def wait_sem(self, sem, val):
        self.e.wait_ge(sem, val)

    def raw(self):
        return self.e

    def __getattr__(self, name):
        fn = getattr(self.e, name)
        if not self.autosync or name in ("wait_ge", "nop", "sem_inc"):
            return fn

        def em(*a, **k):
            if self.n > 0:
                self.e.wait_ge(self.sem, self.n)
            inst = fn(*a, **k)
            inst.then_inc(self.sem, 1)
            self.n += 1
            return inst
        return em

    def inc_last(self, inst):
        inst.then_inc(self.sem, 1)
        self.n += 1
        return inst


class ConvertPool:

    def __init__(self, SY, GP, stage_f, stage_b, ld_sems, st_sems, pairs):
        self.SY, self.GP = SY, GP
        self.sf, self.sb = stage_f, stage_b
        self.ld, self.st = ld_sems, st_sems
        self.nld = [0, 0]
        self.nst = [0, 0]
        self.g_after = {}
        C = stage_f[0].shape[1]
        self.work = []
        for s, d in pairs:
            R, Cs = s.shape
            assert Cs == C and R % 128 == 0, (s.shape, C)
            for r in range(R // 128):
                self.work.append((s[128 * r:128 * (r + 1), :], d[128 * r:128 * (r + 1), :]))
        self.i = 0

    def step(self, n=1):
        for _ in range(n):
            if self.i >= len(self.work):
                return
            sap, dap = self.work[self.i]
            b = self.i % 2
            Q, GP = self.SY, self.GP
            if b in self.g_after:
                Q.wait_ge(GP.sem, self.g_after[b])
            Q.dma_start(out=self.sf[b][:], in_=sap).then_inc(self.ld[b], 16)
            self.nld[b] += 16
            GP.wait_sem(self.ld[b], self.nld[b])
            if self.nst[b]:
                GP.wait_sem(self.st[b], self.nst[b])
            GP.tensor_copy(out=self.sb[b][:], in_=self.sf[b][:])
            self.g_after[b] = GP.n
            Q.wait_ge(GP.sem, GP.n)
            Q.dma_start(out=dap, in_=self.sb[b][:]).then_inc(self.st[b], 16)
            self.nst[b] += 16
            self.i += 1

    def drain(self):
        self.step(len(self.work) - self.i)

    def finish(self):
        self.SY.wait_ge(self.st[0], self.nst[0])
        self.SY.wait_ge(self.st[1], self.nst[1])


def phase_barrier(engines, wrappers, dma_sems):
    for E in engines:
        for w in wrappers:
            E.wait_ge(w.sem, w.n)
        for sem, tot in dma_sems:
            E.wait_ge(sem, tot)


def build_A():
    nc = bass.Bass("TRN2", target_bir_lowering=False, debug=False)
    xi = nc.dram_tensor("xi", [TA, D], F32, kind="ExternalInput").ap()
    ipw = nc.dram_tensor("ipw", [D, 2 * DI], F32, kind="ExternalInput").ap()
    convw = nc.dram_tensor("convw", [DI, 4], F32, kind="ExternalInput").ap()
    convb = nc.dram_tensor("convb", [DI, 1], F32, kind="ExternalInput").ap()
    xpw = nc.dram_tensor("xpw", [DI, 96], F32, kind="ExternalInput").ap()
    dtw = nc.dram_tensor("dtw", [64, DI], F32, kind="ExternalInput").ap()
    dtb = nc.dram_tensor("dtb", [DI, 1], F32, kind="ExternalInput").ap()
    wgate = nc.dram_tensor("wgate", [D, 8], F32, kind="ExternalInput").ap()

    o_xnbf = nc.dram_tensor("o_xnbf", [D, TCORE], BF16, kind="ExternalOutput").ap()
    o_xbs = nc.dram_tensor("o_xbs", [DI, TCORE], F32, kind="ExternalOutput").ap()
    o_sz = nc.dram_tensor("o_sz", [DI, TCORE], F32, kind="ExternalOutput").ap()
    o_dt = nc.dram_tensor("o_dt", [DI, TCORE], F32, kind="ExternalOutput").ap()
    o_pbc = nc.dram_tensor("o_pbc", [32, TCORE], F32, kind="ExternalOutput").ap()
    o_wmoe = nc.dram_tensor("o_wmoe", [TCORE, 8], F32, kind="ExternalOutput").ap()

    NT = TA // 128
    CH5 = chunks(TA, 512)
    CT4 = chunks(TCORE, 512)

    with ExitStack() as ctx:
        e = ctx.enter_context
        ident_bf_d = nc.inline_tensor(np.eye(128, dtype=BF), name="ident_bf")
        ident_f_d = nc.inline_tensor(np.eye(128, dtype=np.float32), name="ident_f")

        xnT_bf = [e(nc.sbuf_tensor(f"xnT{j}", [128, TA], BF16)) for j in range(8)]
        ident_bf = e(nc.sbuf_tensor("identbf", [128, 128], BF16))
        ident_f = e(nc.sbuf_tensor("identf", [128, 128], F32))
        wgate_sb = [e(nc.sbuf_tensor(f"wg{j}", [128, 8], F32)) for j in range(8)]
        convw_sb = [e(nc.sbuf_tensor(f"cw{m}", [128, 4], F32)) for m in range(16)]
        convb_sb = [e(nc.sbuf_tensor(f"cb{m}", [128, 1], F32)) for m in range(16)]
        dtb_sb = [e(nc.sbuf_tensor(f"dbb{m}", [128, 1], F32)) for m in range(16)]
        xpw_bf = [e(nc.sbuf_tensor(f"xpw{m}", [128, 96], BF16)) for m in range(16)]
        dtw_bf = e(nc.sbuf_tensor("dtw_bf", [64, DI], BF16))
        ipwm = [e(nc.sbuf_tensor(f"ipwm{b}", [128, 8 * 128], BF16)) for b in range(2)]
        xb_raw = e(nc.sbuf_tensor("xb_raw", [128, TA], F32))
        conv_acc = e(nc.sbuf_tensor("conv_acc", [128, TCORE], F32))
        xbs_f = e(nc.sbuf_tensor("xbs_f", [128, TCORE], F32))
        xbs_b = e(nc.sbuf_tensor("xbs_b", [128, TCORE], BF16))
        sz_t = e(nc.sbuf_tensor("sz_t", [128, TCORE], F32))
        projT_bf = e(nc.sbuf_tensor("projT_bf", [64, TCORE], BF16))
        pbc_sb = e(nc.sbuf_tensor("pbc_sb", [32, TCORE], F32))
        dtv = e(nc.sbuf_tensor("dtv", [128, TCORE], F32))
        ab = e(nc.sbuf_tensor("ab", [128, TCORE], F32))
        enl = e(nc.sbuf_tensor("enl", [128, TCORE], F32))
        rl = e(nc.sbuf_tensor("rl", [128, TCORE], F32))
        dt_t = e(nc.sbuf_tensor("dt_t", [128, TCORE], F32))
        lg_t = e(nc.sbuf_tensor("lg_t", [128, 8], F32))
        mx8 = e(nc.sbuf_tensor("mx8", [128, 8], F32))
        sc1 = e(nc.sbuf_tensor("sc1", [128, 4], F32))
        negm1 = e(nc.sbuf_tensor("negm1", [128, 1], F32))
        el = e(nc.sbuf_tensor("el", [128, 8], F32))
        msk = e(nc.sbuf_tensor("msk", [128, 8], F32))
        wtm = e(nc.sbuf_tensor("wtm", [128, 8], F32))
        epsb = e(nc.sbuf_tensor("epsb", [128, 1], F32))

        ld = e(nc.semaphore("s_ld"))
        ldg = e(nc.semaphore("s_ldg"))
        ldx = [e(nc.semaphore(f"s_ldx{b}")) for b in range(2)]
        ldw = [e(nc.semaphore(f"s_ldw{b}")) for b in range(2)]
        sem_v = e(nc.semaphore("sem_v"))
        sem_s = e(nc.semaphore("sem_s"))
        sem_t = e(nc.semaphore("sem_t"))
        st = e(nc.semaphore("st"))
        st_xbs = e(nc.semaphore("st_xbs"))
        st_sz = e(nc.semaphore("st_sz"))
        st_dt = e(nc.semaphore("st_dt"))
        st_w = e(nc.semaphore("st_w"))

        V = EW(nc.vector, sem_v)
        S = EW(nc.scalar, sem_s)
        T = EW(nc.tensor, sem_t, autosync=False)
        SY = nc.sync
        G_ = nc.gpsimd
        nld = [0]
        nlg = [0]
        nst = [0]
        nldx = [0, 0]
        nldw = [0, 0]
        nst_b = dict(xbs=0, sz=0, dt=0, w=0)

        V.memset(epsb[:], LN_EPS)

        class _NoCp:
            def step(self, n=1):
                pass

            def drain(self):
                pass

            def finish(self):
                pass
        cp = _NoCp()

        SY.dma_start(out=ident_bf[:], in_=ident_bf_d.ap()).then_inc(ld, 16)
        SY.dma_start(out=ident_f[:], in_=ident_f_d.ap()).then_inc(ld, 16)
        for j in range(8):
            SY.dma_start(out=wgate_sb[j][:], in_=wgate[128 * j:128 * (j + 1), :]).then_inc(ld, 16)
        for m in range(16):
            SY.dma_start(out=convw_sb[m][:], in_=convw[128 * m:128 * (m + 1), :]).then_inc(ld, 16)
            SY.dma_start(out=convb_sb[m][:], in_=convb[128 * m:128 * (m + 1), :]).then_inc(ld, 16)
            SY.dma_start(out=dtb_sb[m][:], in_=dtb[128 * m:128 * (m + 1), :]).then_inc(ld, 16)
        nld[0] += 16 * 58
        LD_MISC = nld[0]
        for m in range(16):
            G_.dma_start(out=xpw_bf[m][:], in_=xpw[128 * m:128 * (m + 1), :]).then_inc(ldg, 16)
        G_.dma_start(out=dtw_bf[:], in_=dtw).then_inc(ldg, 16)
        nlg[0] += 16 * 17
        LD_W2 = nlg[0]

        with ExitStack() as tctx:
            te = tctx.enter_context
            xtile = [te(nc.sbuf_tensor(f"xtile{b}", [128, D], F32)) for b in range(2)]
            xnbf_t = [te(nc.sbuf_tensor(f"xnbf{b}", [128, D], BF16)) for b in range(2)]
            sqt = te(nc.sbuf_tensor("sqt", [128, D], F32))
            muv = te(nc.sbuf_tensor("muv", [128, 1], F32))
            ssv = te(nc.sbuf_tensor("ssv", [128, 1], F32))
            rstd = te(nc.sbuf_tensor("rstd", [128, 1], F32))
            tf32 = [te(nc.sbuf_tensor(f"tf32{b}", [128, 128], F32)) for b in range(2)]
            ps_tp = [te(nc.psum_tensor(f"ps_tp{b}", [128, 128], BF16)) for b in range(2)]
            ps_tf = [te(nc.psum_tensor(f"ps_tf{b}", [128, 128], F32)) for b in range(2)]
            ps_l = te(nc.psum_tensor("ps_l", [128, 8], F32))

            t_done_after = []
            ev_after_tp = {}
            glob_tp = [0]
            for i in range(NT):
                b = i % 2
                if i >= 2:
                    SY.wait_ge(sem_t, t_done_after[i - 2])
                SY.dma_start(out=xtile[b][:], in_=xi[128 * i:128 * (i + 1), :]).then_inc(ldx[b], 16)
                nldx[b] += 16
                V.wait_sem(ldx[b], nldx[b])
                V.reduce_sum(out=muv[:], in_=xtile[b][:], axis=AX.X)
                V.tensor_scalar(out=muv[:], in0=muv[:], scalar1=1.0 / D, scalar2=None, op0=ALU.mult)
                V.tensor_scalar(out=xtile[b][:], in0=xtile[b][:], scalar1=muv[:, 0:1],
                                scalar2=None, op0=ALU.subtract)
                S.wait(V)
                S.activation(out=sqt[:], in_=xtile[b][:], func=AF.Square, accum_out=ssv[:])
                S.activation(out=rstd[:], in_=ssv[:], func=AF.Ln, scale=1.0 / D, bias=epsb[:, 0:1])
                S.activation(out=rstd[:], in_=rstd[:], func=AF.Exp, scale=-0.5)
                V.wait(S)
                V.tensor_scalar(out=xtile[b][:], in0=xtile[b][:], scalar1=rstd[:, 0:1],
                                scalar2=None, op0=ALU.mult)
                V.tensor_copy(out=xnbf_t[b][:], in_=xtile[b][:])
                T.wait(V)
                if i == 0:
                    T.wait_sem(ld, LD_MISC)
                for j in range(8):
                    k = glob_tp[0]
                    if k >= 2:
                        V_after = ev_after_tp[k - 2]
                        T.wait(V, V_after)
                    T.inc_last(T.transpose(ps_tp[j % 2][:], xnbf_t[b][:, 128 * j:128 * (j + 1)],
                                           ident_bf[:]))
                    V.wait(T)
                    V.tensor_copy(out=xnT_bf[j][:, 128 * i:128 * (i + 1)], in_=ps_tp[j % 2][:])
                    ev_after_tp[k] = V.n
                    glob_tp[0] += 1
                if i >= 1:
                    for j in range(8):
                        k = glob_tp[0]
                        T.wait(V, ev_after_tp[k - 2])
                        T.inc_last(T.transpose(ps_tf[j % 2][:], xtile[b][:, 128 * j:128 * (j + 1)],
                                               ident_f[:]))
                        V.wait(T)
                        V.tensor_copy(out=tf32[j % 2][:], in_=ps_tf[j % 2][:])
                        ev_after_tp[k] = V.n
                        glob_tp[0] += 1
                        T.wait(V)
                        mm = T.matmul(ps_l[:], tf32[j % 2][:], wgate_sb[j][:],
                                      start=(j == 0), stop=(j == 7))
                        if j == 7:
                            T.inc_last(mm)
                    V.wait(T)
                    V.tensor_copy(out=lg_t[:], in_=ps_l[:])
                    V.max(out=mx8[:], in_=lg_t[:])
                    V.tensor_tensor(out=sc1[:, 0:1], in0=mx8[:, 1:2], in1=mx8[:, 0:1],
                                    op=ALU.subtract)
                    V.tensor_scalar(out=negm1[:], in0=mx8[:, 0:1], scalar1=-1.0,
                                    scalar2=None, op0=ALU.mult)
                    S.wait(V)
                    S.activation(out=sc1[:, 1:2], in_=sc1[:, 0:1], func=AF.Exp)
                    S.activation(out=el[:], in_=lg_t[:], func=AF.Exp, bias=negm1[:, 0:1])
                    V.wait(S)
                    V.tensor_scalar(out=sc1[:, 2:3], in0=sc1[:, 1:2], scalar1=1.0,
                                    scalar2=None, op0=ALU.add)
                    V.reciprocal(out=sc1[:, 3:4], in_=sc1[:, 2:3])
                    V.tensor_scalar(out=msk[:], in0=lg_t[:], scalar1=mx8[:, 1:2],
                                    scalar2=None, op0=ALU.is_ge)
                    V.tensor_scalar(out=wtm[:], in0=el[:], scalar1=sc1[:, 3:4],
                                    scalar2=None, op0=ALU.mult)
                    V.tensor_tensor(out=wtm[:], in0=wtm[:], in1=msk[:], op=ALU.mult)
                    SY.wait_ge(sem_v, V.n)
                    SY.dma_start(out=o_wmoe[128 * (i - 1):128 * i, :], in_=wtm[:]).then_inc(st_w, 16)
                    nst_b["w"] += 16
                    V.wait_sem(st_w, nst_b["w"])
                t_done_after.append(T.n)
                cp.step(3)
            SY.wait_ge(sem_v, V.n)
            for j in range(8):
                SY.dma_start(out=o_xnbf[128 * j:128 * (j + 1), :],
                             in_=xnT_bf[j][:, HALO:TA]).then_inc(st, 16)
                nst[0] += 16

        phase_barrier([nc.vector, nc.scalar, nc.tensor, nc.sync, nc.gpsimd],
                      [V, S, T],
                      [(ld, nld[0]), (ldg, nlg[0]), (ldx[0], nldx[0]), (ldx[1], nldx[1]),
                       (st_w, nst_b["w"]), (st, nst[0])])
        with ExitStack() as tctx:
            te = tctx.enter_context
            ps_ab = [te(nc.psum_tensor(f"ps_ab{b}", [128, 512], F32)) for b in range(2)]
            ps_pj = [te(nc.psum_tensor(f"ps_pj{i}", [96, 512], F32)) for i in range(4)]

            V.wait_sem(ld, LD_MISC)
            S.wait_sem(ld, LD_MISC)
            T.wait_sem(ldg, LD_W2)

            def load_ipw_slice(mcol, buf):
                for kt in range(8):
                    G_.dma_start(out=ipwm[buf][:, 128 * kt:128 * (kt + 1)],
                                 in_=ipw[128 * kt:128 * (kt + 1),
                                         128 * mcol:128 * (mcol + 1)]).then_inc(ldw[buf], 16)
                nldw[buf] += 16 * 8

            ev_after_g = {}
            ipw_ready = {}
            load_ipw_slice(0, 0)
            ipw_ready[0] = (0, nldw[0])
            t_after_m = {}
            gidx = [0]
            for m in range(16):
                nb = (m + 1) % 2
                if m + 1 < 16:
                    if m >= 1:
                        G_.wait_ge(sem_t, t_after_m[m - 1])
                    load_ipw_slice(m + 1, nb)
                    ipw_ready[m + 1] = (nb, nldw[nb])
                for ci, (c0, cl) in enumerate(CH5):
                    g = gidx[0]
                    ps = ps_ab[g % 2]
                    if g >= 2:
                        eng, cnt = ev_after_g[g - 2]
                        T.wait(eng, cnt)
                    if ci == 0:
                        rb, rv = ipw_ready[m]
                        T.wait_sem(ldw[rb], rv)
                    for kt in range(8):
                        mm = T.matmul(ps[:, 0:cl], ipwm[m % 2][:, 128 * kt:128 * (kt + 1)],
                                      xnT_bf[kt][:, c0:c0 + cl], start=(kt == 0), stop=(kt == 7))
                    T.inc_last(mm)
                    V.wait(T)
                    V.tensor_copy(out=xb_raw[:, c0:c0 + cl], in_=ps[:, 0:cl])
                    ev_after_g[g] = (V, V.n)
                    gidx[0] += 1
                V.tensor_scalar(out=conv_acc[:], in0=xb_raw[:, 125:125 + TCORE],
                                scalar1=convw_sb[m][:, 0:1], scalar2=None, op0=ALU.mult)
                for j in range(1, 4):
                    V.scalar_tensor_tensor(out=conv_acc[:], in0=xb_raw[:, 125 + j:125 + j + TCORE],
                                           scalar=convw_sb[m][:, j:j + 1], in1=conv_acc[:],
                                           op0=ALU.mult, op1=ALU.add)
                S.wait(V)
                S.wait_sem(st_xbs, nst_b["xbs"])
                S.activation(out=xbs_f[:], in_=conv_acc[:], func=AF.Silu,
                             bias=convb_sb[m][:, 0:1])
                SY.wait_ge(sem_s, S.n)
                SY.dma_start(out=o_xbs[128 * m:128 * (m + 1), :], in_=xbs_f[:]).then_inc(st_xbs, 16)
                nst_b["xbs"] += 16
                V.wait(S)
                if m >= 1:
                    V.wait(T, t_after_m[m - 1])
                V.tensor_copy(out=xbs_b[:], in_=xbs_f[:])
                T.wait(V)
                for ci, (c0, cl) in enumerate(CT4):
                    mm = T.matmul(ps_pj[ci][:], xpw_bf[m][:], xbs_b[:, c0:c0 + cl],
                                  start=(m == 0), stop=(m == 15))
                T.inc_last(mm)
                t_after_m[m] = T.n
                cp.step(3)
            for m in range(16):
                nb = m % 2
                G_.wait_ge(sem_t, T.n)
                load_ipw_slice(16 + m, nb)
                zr = nldw[nb]
                for ci, (c0, cl) in enumerate(CT4):
                    g = gidx[0]
                    ps = ps_ab[g % 2]
                    eng, cnt = ev_after_g[g - 2]
                    T.wait(eng, cnt)
                    if ci == 0:
                        T.wait_sem(ldw[nb], zr)
                    for kt in range(8):
                        mm = T.matmul(ps[:, 0:cl], ipwm[nb][:, 128 * kt:128 * (kt + 1)],
                                      xnT_bf[kt][:, HALO + c0:HALO + c0 + cl],
                                      start=(kt == 0), stop=(kt == 7))
                    T.inc_last(mm)
                    S.wait(T)
                    if ci == 0:
                        S.wait_sem(st_sz, nst_b["sz"])
                    S.activation(out=sz_t[:, c0:c0 + cl], in_=ps[:, 0:cl], func=AF.Silu)
                    ev_after_g[g] = (S, S.n)
                    gidx[0] += 1
                SY.wait_ge(sem_s, S.n)
                SY.dma_start(out=o_sz[128 * m:128 * (m + 1), :], in_=sz_t[:]).then_inc(st_sz, 16)
                nst_b["sz"] += 16

            V.wait(T)
            for ci, (c0, cl) in enumerate(CT4):
                V.tensor_copy(out=projT_bf[:, c0:c0 + cl], in_=ps_pj[ci][0:64, :])
                V.tensor_copy(out=pbc_sb[0:32, c0:c0 + cl], in_=ps_pj[ci][64:96, :])
            SY.wait_ge(sem_v, V.n)
            SY.dma_start(out=o_pbc[:, :], in_=pbc_sb[0:32, :]).then_inc(st, 16)
            nst[0] += 16
            T.wait(V)
            for m in range(16):
                for ci, (c0, cl) in enumerate(CT4):
                    g = gidx[0]
                    ps = ps_ab[g % 2]
                    eng, cnt = ev_after_g[g - 2]
                    T.wait(eng, cnt)
                    T.inc_last(T.matmul(ps[:, 0:cl], dtw_bf[:, 128 * m:128 * (m + 1)],
                                        projT_bf[:, c0:c0 + cl], start=True, stop=True))
                    V.wait(T)
                    V.tensor_scalar(out=dtv[:, c0:c0 + cl], in0=ps[:, 0:cl],
                                    scalar1=dtb_sb[m][:, 0:1], scalar2=None, op0=ALU.add)
                    ev_after_g[g] = (V, V.n)
                    gidx[0] += 1
                V.tensor_scalar(out=rl[:], in0=dtv[:], scalar1=-1.0, scalar2=None, op0=ALU.mult)
                V.tensor_tensor(out=ab[:], in0=dtv[:], in1=rl[:], op=ALU.min)
                V.tensor_scalar(out=rl[:], in0=dtv[:], scalar1=0.0, scalar2=None, op0=ALU.max)
                S.wait(V)
                S.activation(out=enl[:], in_=ab[:], func=AF.Exp)
                S.activation(out=enl[:], in_=enl[:], func=AF.Ln, bias=1.0)
                V.wait(S)
                V.wait_sem(st_dt, nst_b["dt"])
                V.tensor_tensor(out=dt_t[:], in0=rl[:], in1=enl[:], op=ALU.add)
                SY.wait_ge(sem_v, V.n)
                SY.dma_start(out=o_dt[128 * m:128 * (m + 1), :], in_=dt_t[:]).then_inc(st_dt, 16)
                nst_b["dt"] += 16

        cp.drain()
        cp.finish()
        SY.wait_ge(st, nst[0])
        SY.wait_ge(st_xbs, nst_b["xbs"])
        SY.wait_ge(st_sz, nst_b["sz"])
        SY.wait_ge(st_dt, nst_b["dt"])
        SY.wait_ge(st_w, nst_b["w"])
        G_.wait_ge(st, nst[0])
    return nc


def build_D():
    nc = bass.Bass("TRN2", target_bir_lowering=False, debug=False)
    TB = 8192
    CH = 512
    NCH = TB // CH
    dtT = nc.dram_tensor("dtT", [512, TB], F32, kind="ExternalInput").ap()
    xbsT = nc.dram_tensor("xbsT", [512, TB], F32, kind="ExternalInput").ap()
    pbcT = nc.dram_tensor("pbcT", [32, TB], F32, kind="ExternalInput").ap()
    alog = nc.dram_tensor("alog", [512, 16], F32, kind="ExternalInput").ap()
    dpar = nc.dram_tensor("dpar", [512, 1], F32, kind="ExternalInput").ap()
    o_y = nc.dram_tensor("o_y", [512, TB], F32, kind="ExternalOutput").ap()
    w1f = nc.dram_tensor("w1f", [8 * D, DI], F32, kind="ExternalInput").ap()
    w3f = nc.dram_tensor("w3f", [8 * D, DI], F32, kind="ExternalInput").ap()
    w2f = nc.dram_tensor("w2f", [8 * DI, D], F32, kind="ExternalInput").ap()
    opwf = nc.dram_tensor("opwf", [DI, D], F32, kind="ExternalInput").ap()
    o_w1b = nc.dram_tensor("o_w1b", [8 * D, DI], BF16, kind="ExternalOutput").ap()
    o_w3b = nc.dram_tensor("o_w3b", [8 * D, DI], BF16, kind="ExternalOutput").ap()
    o_w2b = nc.dram_tensor("o_w2b", [8 * DI, D], BF16, kind="ExternalOutput").ap()
    o_opwb = nc.dram_tensor("o_opwb", [DI, D], BF16, kind="ExternalOutput").ap()

    with ExitStack() as ctx:
        e = ctx.enter_context
        dt_c = [[e(nc.sbuf_tensor(f"dt{p}_{d}", [128, CH], F32)) for d in range(4)] for p in range(2)]
        xb_c = [[e(nc.sbuf_tensor(f"xb{p}_{d}", [128, CH], F32)) for d in range(4)] for p in range(2)]
        pb_c = [e(nc.sbuf_tensor(f"pb{p}", [1, 32 * CH], F32)) for p in range(2)]
        a_sb = [e(nc.sbuf_tensor(f"a{d}", [128, 16], F32)) for d in range(4)]
        dp_sb = [e(nc.sbuf_tensor(f"dp{d}", [128, 1], F32)) for d in range(4)]
        state = [e(nc.sbuf_tensor(f"stt{d}", [128, 16], F32)) for d in range(4)]
        ones1 = e(nc.sbuf_tensor("ones1", [1, 128], F32))
        dtu_t = e(nc.sbuf_tensor("dtu_t", [128, CH], F32))
        da_sb = [e(nc.sbuf_tensor(f"da{b}", [128, CH], F32)) for b in range(2)]
        dbu = e(nc.sbuf_tensor("dbu", [128, CH], F32))
        sct = e(nc.sbuf_tensor("sct", [128, CH], F32))
        yterm = e(nc.sbuf_tensor("yterm", [128, CH], F32))
        y_acc = [e(nc.sbuf_tensor(f"ya{d}", [128, CH], F32)) for d in range(4)]

        ps_bc = [e(nc.psum_tensor(f"ps_bc{i}", [128, CH], F32)) for i in range(4)]

        cstage_f = [e(nc.sbuf_tensor(f"cstagef{b}", [128, DI], F32)) for b in range(2)]
        cstage = [e(nc.sbuf_tensor(f"cstage{b}", [128, DI], BF16)) for b in range(2)]
        ld_cv = [e(nc.semaphore(f"ld_cv{b}")) for b in range(2)]
        st_cv = [e(nc.semaphore(f"st_cv{b}")) for b in range(2)]
        sem_g = e(nc.semaphore("sem_g"))
        ld = e(nc.semaphore("ld"))
        ldc = [e(nc.semaphore(f"ldc{p}")) for p in range(2)]
        sem_v = e(nc.semaphore("sem_v"))
        sem_s = e(nc.semaphore("sem_s"))
        sem_t = e(nc.semaphore("sem_t"))
        st_y = [e(nc.semaphore(f"st_y{d}")) for d in range(4)]

        V = EW(nc.vector, sem_v)
        S = EW(nc.scalar, sem_s)
        T = EW(nc.tensor, sem_t, autosync=False)
        SY = nc.sync
        nld = 0
        nldc = [0, 0]
        nst = [0]

        V.memset(ones1[:], 1.0)
        for d in range(4):
            V.memset(state[d][:], 0.0)
        GP = EW(nc.gpsimd, sem_g)
        cp = ConvertPool(nc.gpsimd, GP, cstage_f, cstage, ld_cv, st_cv,
                         [(w1f, o_w1b), (w3f, o_w3b),
                          (w2f.rearrange("(a b) c -> a (b c)", b=2),
                           o_w2b.rearrange("(a b) c -> a (b c)", b=2)),
                          (opwf.rearrange("(a b) c -> a (b c)", b=2),
                           o_opwb.rearrange("(a b) c -> a (b c)", b=2))])

        for d in range(4):
            SY.dma_start(out=a_sb[d][:], in_=alog[128 * d:128 * (d + 1), :]).then_inc(ld, 16)
            SY.dma_start(out=dp_sb[d][:], in_=dpar[128 * d:128 * (d + 1), :]).then_inc(ld, 16)
        nld += 16 * 8
        S.wait_sem(ld, nld)
        for d in range(4):
            S.activation(out=a_sb[d][:], in_=a_sb[d][:], func=AF.Exp, scale=1.0)
        for d in range(4):
            S.mul(out=a_sb[d][:], in_=a_sb[d][:], mul=-1.0)

        def load_chunk(ch, p):
            c0 = ch * CH
            for d in range(4):
                SY.dma_start(out=dt_c[p][d][:], in_=dtT[128 * d:128 * (d + 1), c0:c0 + CH]).then_inc(ldc[p], 16)
                SY.dma_start(out=xb_c[p][d][:], in_=xbsT[128 * d:128 * (d + 1), c0:c0 + CH]).then_inc(ldc[p], 16)
            SY.dma_start(out=pb_c[p][0:1, :].rearrange("o (r c) -> o r c", r=32),
                         in_=pbcT[:, c0:c0 + CH].unsqueeze(0)).then_inc(ldc[p], 16)
            nldc[p] += 16 * 9

        v_after_chunk = {}
        vas = {}
        load_chunk(0, 0)
        rdy = {0: nldc[0]}
        bc_reader = {}
        for ch in range(NCH):
            p = ch % 2
            if ch + 1 < NCH:
                if ch >= 1:
                    SY.wait_ge(sem_v, v_after_chunk[ch - 1])
                load_chunk(ch + 1, (ch + 1) % 2)
                rdy[ch + 1] = nldc[(ch + 1) % 2]
            V.wait_sem(ldc[p], rdy[ch])
            S.wait_sem(ldc[p], rdy[ch])
            T.wait_sem(ldc[p], rdy[ch])
            if ch == 0:
                T.wait(V)
            for d in range(4):
                if ch >= 1:
                    V.wait_sem(st_y[d], 16 * ch)
                V.tensor_mul(out=dtu_t[:], in0=dt_c[p][d][:], in1=xb_c[p][d][:])
                for s in range(16):
                    k = (d * 16 + s) % 2
                    bb, cb = ps_bc[2 * k], ps_bc[2 * k + 1]
                    for bank, ps_t, row in ((2 * k, bb, s), (2 * k + 1, cb, 16 + s)):
                        if bank in bc_reader:
                            T.wait(V, bc_reader[bank])
                        T.inc_last(T.matmul(ps_t[:], ones1[0:1, :],
                                            pb_c[p][0:1, row * CH:(row + 1) * CH],
                                            start=True, stop=True))
                    db = s % 2
                    if db in vas:
                        S.wait(V, vas[db])
                    S.activation(out=da_sb[db][:], in_=dt_c[p][d][:], func=AF.Exp,
                                 scale=a_sb[d][:, s:s + 1])
                    V.wait(T)
                    V.wait(S)
                    V.tensor_mul(out=dbu[:], in0=dtu_t[:], in1=bb[:])
                    V.tensor_tensor_scan(out=sct[:], data0=da_sb[db][:], data1=dbu[:],
                                         initial=state[d][:, s:s + 1], op0=ALU.mult,
                                         op1=ALU.add)
                    vas[db] = V.n
                    V.tensor_copy(out=state[d][:, s:s + 1], in_=sct[:, CH - 1:CH])
                    if s == 0:
                        V.tensor_mul(out=y_acc[d][:], in0=sct[:], in1=cb[:])
                    else:
                        V.tensor_mul(out=yterm[:], in0=sct[:], in1=cb[:])
                        V.tensor_add(out=y_acc[d][:], in0=y_acc[d][:], in1=yterm[:])
                    bc_reader[2 * k] = V.n
                    bc_reader[2 * k + 1] = V.n
                V.scalar_tensor_tensor(out=y_acc[d][:], in0=xb_c[p][d][:],
                                       scalar=dp_sb[d][:, 0:1], in1=y_acc[d][:],
                                       op0=ALU.mult, op1=ALU.add)
                SY.wait_ge(sem_v, V.n)
                SY.dma_start(out=o_y[128 * d:128 * (d + 1), ch * CH:(ch + 1) * CH],
                             in_=y_acc[d][:]).then_inc(st_y[d], 16)
                nst[0] += 16
                cp.step(4)
            v_after_chunk[ch] = V.n
        cp.drain()
        cp.finish()
        for d in range(4):
            SY.wait_ge(st_y[d], 16 * NCH)
    return nc


def run_D(A_outs, inputs):
    dtT = np.concatenate([np.asarray(A_outs[c]["o_dt"]) for c in range(NC)], axis=1)
    xbsT = np.concatenate([np.asarray(A_outs[c]["o_xbs"]) for c in range(NC)], axis=1)
    pbcT = np.concatenate([np.asarray(A_outs[c]["o_pbc"]) for c in range(NC)], axis=1)
    alog = f32c(inputs["L0_attn_A_log"])
    dpar = f32c(inputs["L0_attn_D_param"].reshape(DI, 1))
    in_maps = []
    for c in range(NC):
        b = c // 4
        r0 = 512 * (c % 4)
        cols = slice(8192 * b, 8192 * (b + 1))
        in_maps.append({
            "dtT": f32c(dtT[r0:r0 + 512, cols]),
            "xbsT": f32c(xbsT[r0:r0 + 512, cols]),
            "pbcT": f32c(pbcT[:, cols]),
            "alog": f32c(alog[r0:r0 + 512]),
            "dpar": f32c(dpar[r0:r0 + 512]),
            "w1f": f32c(inputs["L0_ffn_w1"].reshape(8 * D, DI)),
            "w3f": f32c(inputs["L0_ffn_w3"].reshape(8 * D, DI)),
            "w2f": f32c(inputs["L0_ffn_w2"].reshape(8 * DI, D)),
            "opwf": f32c(inputs["L0_attn_out_proj_w"]),
        })
    if "D" not in _A_CACHE:
        _A_CACHE["D"] = build_D()
    res = run_bass_kernel_spmd(_A_CACHE["D"], in_maps, core_ids=list(range(NC)))
    return res.results


def build_M():
    nc = bass.Bass("TRN2", target_bir_lowering=False, debug=False)
    yT = nc.dram_tensor("yT", [DI, TCORE], F32, kind="ExternalInput").ap()
    szT = nc.dram_tensor("szT", [DI, TCORE], F32, kind="ExternalInput").ap()
    xnbfT = nc.dram_tensor("xnbfT", [D, TCORE], BF16, kind="ExternalInput").ap()
    wmoe = nc.dram_tensor("wmoe", [TCORE, 8], F32, kind="ExternalInput").ap()
    x_tm = nc.dram_tensor("x_tm", [TCORE, D], F32, kind="ExternalInput").ap()
    opw = nc.dram_tensor("opw", [DI, D], BF16, kind="ExternalInput").ap()
    w1 = nc.dram_tensor("w1", [8 * D, DI], BF16, kind="ExternalInput").ap()
    w3 = nc.dram_tensor("w3", [8 * D, DI], BF16, kind="ExternalInput").ap()
    w2 = nc.dram_tensor("w2", [8 * DI, D], BF16, kind="ExternalInput").ap()
    lng = nc.dram_tensor("lng", [1, D], F32, kind="ExternalInput").ap()
    lnb = nc.dram_tensor("lnb", [1, D], F32, kind="ExternalInput").ap()

    o_x1 = nc.dram_tensor("o_x1", [TCORE, D], F32, kind="ExternalOutput").ap()
    o_xn1 = nc.dram_tensor("o_xn1", [D, TCORE], BF16, kind="ExternalOutput").ap()
    o_xat = nc.dram_tensor("o_xat", [D, TCORE], BF16, kind="ExternalOutput").ap()

    with ExitStack() as ctx:
        e = ctx.enter_context
        ident_bf_d = nc.inline_tensor(np.eye(128, dtype=BF), name="ident_bf")

        acc = [e(nc.sbuf_tensor(f"acc{t}", [128, D], F32)) for t in range(16)]
        xnbf = [e(nc.sbuf_tensor(f"xnbf{j}", [128, TCORE], BF16)) for j in range(8)]
        wm_sb = [e(nc.sbuf_tensor(f"wm{t}", [128, 8], F32)) for t in range(16)]
        ident_bf = e(nc.sbuf_tensor("identbf", [128, 128], BF16))
        ones1 = e(nc.sbuf_tensor("ones1", [1, 128], F32))

        epsb = e(nc.sbuf_tensor("epsb", [128, 1], F32))

        ld = e(nc.semaphore("ld"))
        ldg = e(nc.semaphore("ldg"))
        ldy = [e(nc.semaphore(f"ldy{p}")) for p in range(2)]
        ldw_s = [e(nc.semaphore(f"ldws{p}")) for p in range(2)]
        ldw2 = [e(nc.semaphore(f"ldw2{p}")) for p in range(2)]
        sem_v = e(nc.semaphore("sem_v"))
        sem_s = e(nc.semaphore("sem_s"))
        sem_t = e(nc.semaphore("sem_t"))
        st = e(nc.semaphore("st"))

        V = EW(nc.vector, sem_v)
        S = EW(nc.scalar, sem_s)
        T = EW(nc.tensor, sem_t, autosync=False)
        SY = nc.sync
        G_ = nc.gpsimd
        nld = [0]
        nlg = [0]
        nldy = [0, 0]
        nldws = [0, 0]
        nldw2 = [0, 0]
        nst = [0]

        V.memset(epsb[:], LN_EPS)
        V.memset(ones1[:], 1.0)

        SY.dma_start(out=ident_bf[:], in_=ident_bf_d.ap()).then_inc(ld, 16)
        for t in range(16):
            SY.dma_start(out=wm_sb[t][:], in_=wmoe[128 * t:128 * (t + 1), :]).then_inc(ld, 16)
            SY.dma_start(out=acc[t][:], in_=x_tm[128 * t:128 * (t + 1), :]).then_inc(ld, 16)
        for j in range(8):
            SY.dma_start(out=xnbf[j][:], in_=xnbfT[128 * j:128 * (j + 1), :]).then_inc(ld, 16)
        nld[0] += 16 * 41
        LD_BASE = nld[0]

        with ExitStack() as c1:
            t1 = c1.enter_context
            yf_bf = [t1(nc.sbuf_tensor(f"yf{k}", [128, TCORE], BF16)) for k in range(8)]
            ybuf = t1(nc.sbuf_tensor("ybuf", [128, TCORE], F32))
            zbuf = t1(nc.sbuf_tensor("zbuf", [128, TCORE], F32))
            opw_bf = [t1(nc.sbuf_tensor(f"opw{k}", [128, D], BF16)) for k in range(16)]
            ps_m = [t1(nc.psum_tensor(f"ps_m{i}", [128, 512], F32)) for i in range(4)]
            for k in range(16):
                SY.dma_start(out=opw_bf[k][:], in_=opw[128 * k:128 * (k + 1), :]).then_inc(ldg, 16)
            nlg[0] += 16 * 16
            t_after_g = {}
            ev_ps = {}
            for g in range(2):
                for k8 in range(8):
                    k = 8 * g + k8
                    SY.wait_ge(sem_v, V.n)
                    if g >= 1:
                        SY.wait_ge(sem_t, t_after_g[g - 1])
                    SY.dma_start(out=ybuf[:], in_=yT[128 * k:128 * (k + 1), :]).then_inc(ldy[0], 16)
                    SY.dma_start(out=zbuf[:], in_=szT[128 * k:128 * (k + 1), :]).then_inc(ldy[0], 16)
                    nldy[0] += 32
                    V.wait_sem(ldy[0], nldy[0])
                    V.tensor_mul(out=yf_bf[k8][:], in0=ybuf[:], in1=zbuf[:])
                if g == 0:
                    T.wait_sem(ldg, nlg[0])
                T.wait(V)
                for ts in range(16):
                    for half in range(2):
                        gi = 2 * ts + half
                        pi = gi % 4
                        if gi >= 4 or g >= 1:
                            eng, cnt = ev_ps[pi]
                            T.wait(eng, cnt)
                        for k8 in range(8):
                            mm = T.matmul(ps_m[pi][:], yf_bf[k8][:, 128 * ts:128 * (ts + 1)],
                                          opw_bf[8 * g + k8][:, 512 * half:512 * (half + 1)],
                                          start=(k8 == 0), stop=(k8 == 7))
                        T.inc_last(mm)
                        V.wait(T)
                        V.wait_sem(ld, LD_BASE)
                        V.tensor_add(out=acc[ts][:, 512 * half:512 * (half + 1)],
                                     in0=acc[ts][:, 512 * half:512 * (half + 1)],
                                     in1=ps_m[pi][:])
                        ev_ps[pi] = (V, V.n)
                t_after_g[g] = T.n

        phase_barrier([nc.vector, nc.scalar, nc.tensor, nc.sync, nc.gpsimd],
                      [V, S, T],
                      [(ld, nld[0]), (ldg, nlg[0]), (ldy[0], nldy[0])])
        with ExitStack() as c3:
            t3 = c3.enter_context
            hc = [t3(nc.sbuf_tensor(f"hc{p}", [128, 16 * 512], BF16)) for p in range(2)]
            w13m = [t3(nc.sbuf_tensor(f"w13m{p}", [128, 2 * 8 * 128], BF16)) for p in range(2)]
            w2_bf = [t3(nc.sbuf_tensor(f"w2bf{k}", [128, D], BF16)) for k in range(16)]
            hstg = [t3(nc.sbuf_tensor(f"hstg{p}", [128, 2048], BF16)) for p in range(2)]
            t1s = t3(nc.sbuf_tensor("t1s", [128, 512], F32))
            ps_h = [t3(nc.psum_tensor(f"ps_h{i}", [128, 512], F32)) for i in range(4)]
            ps_y = [t3(nc.psum_tensor(f"ps_y{i}", [128, 512], F32)) for i in range(4)]
            d_h = [nc.dram_tensor(f"d_h{b}", [DI, TCORE], BF16) for b in range(2)]

            def load_w13(eidx, hm, buf):
                SY.dma_start(out=w13m[buf][:, 0:1024].rearrange("p (k c) -> p k c", k=8),
                             in_=w1[D * eidx:D * (eidx + 1),
                                    128 * hm:128 * (hm + 1)].rearrange(
                                 "(k p) c -> p k c", p=128)).then_inc(ldw_s[buf], 16)
                SY.dma_start(out=w13m[buf][:, 1024:2048].rearrange("p (k c) -> p k c", k=8),
                             in_=w3[D * eidx:D * (eidx + 1),
                                    128 * hm:128 * (hm + 1)].rearrange(
                                 "(k p) c -> p k c", p=128)).then_inc(ldw_s[buf], 16)
                nldws[buf] += 32

            def load_w2(eidx):
                for k in range(16):
                    SY.dma_start(out=w2_bf[k][:],
                                 in_=w2[DI * eidx + 128 * k:DI * eidx + 128 * (k + 1),
                                        :]).then_inc(ldw2[0], 16)
                nldw2[0] += 16 * 16

            ev_h = {}
            ev_y = {}
            st_hd = [t3(nc.semaphore(f"st_hd{p}")) for p in range(2)]
            ldh_y = [t3(nc.semaphore(f"ldh_y{p}")) for p in range(2)]
            nst_hd = [0, 0]
            nldh_y = [0, 0]
            t_after_w13 = {}
            t_after_y = {}
            st_hd_after = {}
            t_after_h = {}

            def emit_h(ei):
                eb = ei % 2
                for hm in range(16):
                    si = 16 * ei + hm
                    sb = si % 2
                    if si >= 2:
                        SY.wait_ge(sem_t, t_after_w13[si - 2])
                    load_w13(ei, hm, sb)
                    wrdy = nldws[sb]
                    T.wait_sem(ldw_s[sb], wrdy)
                    for ci in range(4):
                        pp = ci % 2
                        pa, pb = ps_h[2 * pp], ps_h[2 * pp + 1]
                        if ("h", pp) in ev_h:
                            eng, cnt = ev_h[("h", pp)]
                            T.wait(eng, cnt)
                        for kt in range(8):
                            T.matmul(pa[:], w13m[sb][:, 128 * kt:128 * (kt + 1)],
                                     xnbf[kt][:, 512 * ci:512 * (ci + 1)],
                                     start=(kt == 0), stop=(kt == 7))
                        for kt in range(8):
                            mm = T.matmul(pb[:], w13m[sb][:, 1024 + 128 * kt:1024 + 128 * (kt + 1)],
                                          xnbf[kt][:, 512 * ci:512 * (ci + 1)],
                                          start=(kt == 0), stop=(kt == 7))
                        T.inc_last(mm)
                        S.wait(T)
                        S.wait(V)
                        S.activation(out=t1s[:], in_=pa[:], func=AF.Silu)
                        V.wait(S)
                        if ci == 0:
                            V.wait_sem(st_hd[sb], nst_hd[sb])
                        V.tensor_mul(out=hstg[sb][:, 512 * ci:512 * (ci + 1)],
                                     in0=t1s[:], in1=pb[:])
                        ev_h[("h", pp)] = (V, V.n)
                    t_after_w13[si] = T.n
                    SY.wait_ge(sem_v, V.n)
                    SY.dma_start(out=d_h[eb].ap()[128 * hm:128 * (hm + 1), :],
                                 in_=hstg[sb][:]).then_inc(st_hd[sb], 16)
                    nst_hd[sb] += 16
                t_after_h[ei] = T.n
                st_hd_after[ei] = (nst_hd[0], nst_hd[1])

            def emit_y(ei):
                eb = ei % 2
                T.wait_sem(ldw2[0], w2rdy[ei])
                for ci in range(4):
                    cb = ci % 2
                    if ci >= 2:
                        G_.wait_ge(sem_t, t_after_ci[cb])
                    elif ei >= 1:
                        G_.wait_ge(sem_t, t_after_ci[cb])
                    if ci == 0:
                        a0, a1 = st_hd_after[ei]
                        G_.wait_ge(st_hd[0], a0)
                        G_.wait_ge(st_hd[1], a1)
                    G_.dma_start(out=hc[cb][:].rearrange("p (k c) -> p k c", k=16),
                                 in_=d_h[eb].ap()[:, 512 * ci:512 * (ci + 1)].rearrange(
                                     "(k p) c -> p k c", p=128)).then_inc(ldh_y[cb], 16)
                    nldh_y[cb] += 16
                    T.wait_sem(ldh_y[cb], nldh_y[cb])
                    for tsl in range(4):
                        ts = 4 * ci + tsl
                        for nh in range(2):
                            pj = (2 * tsl + nh) % 4
                            if ("y", pj) in ev_y:
                                eng, cnt = ev_y[("y", pj)]
                                T.wait(eng, cnt)
                            for k in range(16):
                                mm = T.matmul(ps_y[pj][:], hc[cb][:, 512 * k + 128 * tsl:
                                                                  512 * k + 128 * (tsl + 1)],
                                              w2_bf[k][:, 512 * nh:512 * (nh + 1)],
                                              start=(k == 0), stop=(k == 15))
                            T.inc_last(mm)
                            V.wait(T)
                            V.scalar_tensor_tensor(out=acc[ts][:, 512 * nh:512 * (nh + 1)],
                                                   in0=ps_y[pj][:],
                                                   scalar=wm_sb[ts][:, ei:ei + 1],
                                                   in1=acc[ts][:, 512 * nh:512 * (nh + 1)],
                                                   op0=ALU.mult, op1=ALU.add)
                            ev_y[("y", pj)] = (V, V.n)
                    t_after_ci[cb] = T.n
                t_after_y[ei] = T.n

            w2rdy = {}
            t_after_ci = {0: 0, 1: 0}
            for ei in range(9):
                if ei < 8:
                    emit_h(ei)
                if ei >= 1:
                    if ei >= 2:
                        SY.wait_ge(sem_t, t_after_y[ei - 2])
                    load_w2(ei - 1)
                    w2rdy[ei - 1] = nldw2[0]
                    emit_y(ei - 1)

        phase_barrier([nc.vector, nc.scalar, nc.tensor, nc.sync, nc.gpsimd],
                      [V, S, T],
                      [(ldw_s[0], nldws[0]), (ldw_s[1], nldws[1]), (ldw2[0], nldw2[0])])
        with ExitStack() as c4:
            t4 = c4.enter_context
            g_bc = t4(nc.sbuf_tensor("g_bc", [128, D], F32))
            b_bc = t4(nc.sbuf_tensor("b_bc", [128, D], F32))
            gb_row = t4(nc.sbuf_tensor("gb_row", [1, 2 * D], F32))
            SY.dma_start(out=gb_row[0:1, 0:D], in_=lng).then_inc(ld, 16)
            SY.dma_start(out=gb_row[0:1, D:2 * D], in_=lnb).then_inc(ld, 16)
            nld[0] += 32
            xn1T = [t4(nc.sbuf_tensor(f"xn1T{j}", [128, TCORE], BF16)) for j in range(8)]
            xatT = [t4(nc.sbuf_tensor(f"xatT{j}", [128, TCORE], BF16)) for j in range(8)]
            xn1t = t4(nc.sbuf_tensor("xn1t", [128, D], F32))
            xat = t4(nc.sbuf_tensor("xat", [128, D], F32))
            xbf = [t4(nc.sbuf_tensor(f"xbfb{b}", [128, D], BF16)) for b in range(2)]
            sqt = t4(nc.sbuf_tensor("sqt", [128, D], F32))
            muv = t4(nc.sbuf_tensor("muv", [128, 1], F32))
            ssv = t4(nc.sbuf_tensor("ssv", [128, 1], F32))
            rstd = t4(nc.sbuf_tensor("rstd", [128, 1], F32))
            ps_tp = [t4(nc.psum_tensor(f"ps_tp{b}", [128, 128], BF16)) for b in range(2)]
            ps_g = [t4(nc.psum_tensor(f"ps_g{b}", [128, 512], F32)) for b in range(2)]

            T.wait_sem(ld, nld[0])
            T.wait(V)
            for half in range(2):
                T.inc_last(T.matmul(ps_g[half][:], ones1[0:1, :],
                                    gb_row[0:1, 512 * half:512 * (half + 1)],
                                    start=True, stop=True))
            V.wait(T)
            V.tensor_copy(out=g_bc[:, 0:512], in_=ps_g[0][:])
            V.tensor_copy(out=g_bc[:, 512:1024], in_=ps_g[1][:])
            for half in range(2):
                T.wait(V)
                T.inc_last(T.matmul(ps_g[half][:], ones1[0:1, :],
                                    gb_row[0:1, D + 512 * half:D + 512 * (half + 1)],
                                    start=True, stop=True))
            V.wait(T)
            V.tensor_copy(out=b_bc[:, 0:512], in_=ps_g[0][:])
            V.tensor_copy(out=b_bc[:, 512:1024], in_=ps_g[1][:])

            def ln_tmajor(src_ap, dst, scale_gb):
                V.reduce_sum(out=muv[:], in_=src_ap, axis=AX.X)
                V.tensor_scalar(out=muv[:], in0=muv[:], scalar1=1.0 / D, scalar2=None,
                                op0=ALU.mult)
                V.tensor_scalar(out=dst[:], in0=src_ap, scalar1=muv[:, 0:1], scalar2=None,
                                op0=ALU.subtract)
                S.wait(V)
                S.activation(out=sqt[:], in_=dst[:], func=AF.Square, accum_out=ssv[:])
                S.activation(out=rstd[:], in_=ssv[:], func=AF.Ln, scale=1.0 / D,
                             bias=epsb[:, 0:1])
                S.activation(out=rstd[:], in_=rstd[:], func=AF.Exp, scale=-0.5)
                V.wait(S)
                V.tensor_scalar(out=dst[:], in0=dst[:], scalar1=rstd[:, 0:1], scalar2=None,
                                op0=ALU.mult)
                if scale_gb:
                    V.tensor_mul(out=dst[:], in0=dst[:], in1=g_bc[:])
                    V.tensor_add(out=dst[:], in0=dst[:], in1=b_bc[:])

            ev_tp = {}
            ktp = [0]

            def transposes(src_bf, dstT, ti):
                for j in range(8):
                    k = ktp[0]
                    bk = k % 2
                    if k >= 2:
                        eng, cnt = ev_tp[k - 2]
                        T.wait(eng, cnt)
                    T.inc_last(T.transpose(ps_tp[bk][:], src_bf[:, 128 * j:128 * (j + 1)],
                                           ident_bf[:]))
                    V.wait(T)
                    V.tensor_copy(out=dstT[j][:, 128 * ti:128 * (ti + 1)], in_=ps_tp[bk][:])
                    ev_tp[k] = (V, V.n)
                    ktp[0] += 1

            for ts in range(16):
                ln_tmajor(acc[ts][:], xn1t, False)
                V.tensor_copy(out=xbf[0][:], in_=xn1t[:])
                T.wait(V)
                transposes(xbf[0], xn1T, ts)
                ln_tmajor(xn1t[:], xat, True)
                V.tensor_copy(out=xbf[1][:], in_=xat[:])
                T.wait(V)
                transposes(xbf[1], xatT, ts)
                SY.wait_ge(sem_v, V.n)
                SY.dma_start(out=o_x1[128 * ts:128 * (ts + 1), :], in_=acc[ts][:]).then_inc(st, 16)
                nst[0] += 16
            SY.wait_ge(sem_v, V.n)
            for j in range(8):
                SY.dma_start(out=o_xn1[128 * j:128 * (j + 1), :], in_=xn1T[j][:]).then_inc(st, 16)
                SY.dma_start(out=o_xat[128 * j:128 * (j + 1), :], in_=xatT[j][:]).then_inc(st, 16)
                nst[0] += 32
            SY.wait_ge(st, nst[0])
            G_.wait_ge(st, nst[0])
    return nc


def run_M(A_outs, D_outs, inputs):
    yT = np.concatenate(
        [np.concatenate([np.asarray(D_outs[4 * b + j]["o_y"]) for j in range(4)], axis=0)
         for b in range(2)], axis=1)
    in_maps = []
    for c in range(NC):
        cols = slice(TCORE * c, TCORE * (c + 1))
        x_tm = inputs["L0_attn_x"].reshape(16384, D)[cols]
        in_maps.append({
            "yT": f32c(yT[:, cols]),
            "szT": f32c(np.asarray(A_outs[c]["o_sz"])),
            "xnbfT": np.ascontiguousarray(np.asarray(A_outs[c]["o_xnbf"]).astype(BF)),
            "wmoe": f32c(np.asarray(A_outs[c]["o_wmoe"])),
            "x_tm": f32c(x_tm),
            "opw": np.ascontiguousarray(np.asarray(D_outs[c]["o_opwb"]).astype(BF)),
            "w1": np.ascontiguousarray(np.asarray(D_outs[c]["o_w1b"]).astype(BF)),
            "w3": np.ascontiguousarray(np.asarray(D_outs[c]["o_w3b"]).astype(BF)),
            "w2": np.ascontiguousarray(np.asarray(D_outs[c]["o_w2b"]).astype(BF)),
            "lng": f32c(inputs["L1_attn_ln_g"].reshape(1, D)),
            "lnb": f32c(inputs["L1_attn_ln_b"].reshape(1, D)),
        })
    if "M" not in _A_CACHE:
        _A_CACHE["M"] = build_M()
    res = run_bass_kernel_spmd(_A_CACHE["M"], in_maps, core_ids=list(range(NC)))
    return res.results


def build_F():
    nc = bass.Bass("TRN2", target_bir_lowering=False, debug=False)
    TB = 8192
    NKT = TB // 128
    NQC = TB // 512
    xatT = nc.dram_tensor("xatT", [D, TB], BF16, kind="ExternalInput").ap()
    wqT = nc.dram_tensor("wqT", [D, 256], F32, kind="ExternalInput").ap()
    wkT = nc.dram_tensor("wkT", [D, 256], F32, kind="ExternalInput").ap()
    wvT = nc.dram_tensor("wvT", [D, 256], F32, kind="ExternalInput").ap()
    qg = nc.dram_tensor("qg", [64, 1], F32, kind="ExternalInput").ap()
    kg = nc.dram_tensor("kg", [64, 1], F32, kind="ExternalInput").ap()
    o_oh = nc.dram_tensor("o_oh", [256, TB], BF16, kind="ExternalOutput").ap()
    H4 = 4096
    cv_shapes = [[H4, H4], [H4, H4], [D, H4], [D, H4], [H4, D], [D, D]]
    cv_in = [nc.dram_tensor(f"cv_in{i}", shp, F32, kind="ExternalInput").ap()
             for i, shp in enumerate(cv_shapes)]
    cv_out = [nc.dram_tensor(f"cv_out{i}", shp, BF16, kind="ExternalOutput").ap()
              for i, shp in enumerate(cv_shapes)]

    RMS_EPS = 1e-8
    SCALE = 0.125

    with ExitStack() as ctx:
        e = ctx.enter_context
        xat = [e(nc.sbuf_tensor(f"xat{j}", [128, TB], BF16)) for j in range(8)]
        qn_bf = e(nc.sbuf_tensor("qn_bf", [128, TB], BF16))
        kn_bf = e(nc.sbuf_tensor("kn_bf", [128, TB], BF16))
        vo_all = e(nc.sbuf_tensor("vo_all", [128, 65 * NKT], BF16))
        vo = [vo_all[:, 65 * t:65 * (t + 1)] for t in range(NKT)]
        w_sb = [e(nc.sbuf_tensor(f"w_sb{i}", [128, 256], BF16)) for i in range(24)]
        gam = e(nc.sbuf_tensor("gam", [64, 2], F32))
        ones1 = e(nc.sbuf_tensor("ones1", [1, 128], F32))
        onesc = e(nc.sbuf_tensor("onesc", [64, 1], F32))
        sq_t = e(nc.sbuf_tensor("sq_t", [64, 512], F32))
        tmp_t = e(nc.sbuf_tensor("tmp_t", [64, 512], F32))
        row_a = e(nc.sbuf_tensor("row_a", [1, 512], F32))
        row_b = e(nc.sbuf_tensor("row_b", [1, 512], F32))
        cacc = e(nc.sbuf_tensor("cacc", [1, 4], F32))
        negC_bc = e(nc.sbuf_tensor("negC_bc", [128, 1], F32))
        exp_bf = [e(nc.sbuf_tensor(f"exp{b}", [128, 1024], BF16)) for b in range(3)]
        rec_bc = e(nc.sbuf_tensor("rec_bc", [64, 512], F32))
        o_sb = e(nc.sbuf_tensor("o_sb", [64, 512], BF16))

        cstage_f = [e(nc.sbuf_tensor(f"cstagef{b}", [128, 512], F32)) for b in range(2)]
        cstage_b = [e(nc.sbuf_tensor(f"cstageb{b}", [128, 512], BF16)) for b in range(2)]
        ld_cv = [e(nc.semaphore(f"ld_cv{b}")) for b in range(2)]
        st_cv = [e(nc.semaphore(f"st_cv{b}")) for b in range(2)]
        sem_g = e(nc.semaphore("sem_g"))
        ld = e(nc.semaphore("ld"))
        ldg = e(nc.semaphore("ldg"))
        sem_v = e(nc.semaphore("sem_v"))
        sem_s = e(nc.semaphore("sem_s"))
        sem_t = e(nc.semaphore("sem_t"))
        st_o = e(nc.semaphore("st_o"))

        V = EW(nc.vector, sem_v)
        S = EW(nc.scalar, sem_s)
        T = EW(nc.tensor, sem_t, autosync=False)
        SY = nc.sync
        G_ = nc.gpsimd
        nld = [0]
        nlg = [0]
        nst = [0]

        V.memset(ones1[:], 1.0)
        V.memset(onesc[:], 1.0)
        GP = EW(nc.gpsimd, sem_g)
        cv_pairs = []
        for i in range(6):
            si, so = cv_in[i], cv_out[i]
            f = si.shape[1] // 512
            if f > 1:
                si = si.rearrange("a (b c) -> (a b) c", b=f)
                so = so.rearrange("a (b c) -> (a b) c", b=f)
            cv_pairs.append((si, so))
        cp = ConvertPool(G_, GP, cstage_f, cstage_b, ld_cv, st_cv, cv_pairs)
        for j in range(8):
            SY.dma_start(out=xat[j][:], in_=xatT[128 * j:128 * (j + 1), :]).then_inc(ld, 16)
        SY.dma_start(out=gam[0:64, 0:1], in_=qg).then_inc(ld, 16)
        SY.dma_start(out=gam[0:64, 1:2], in_=kg).then_inc(ld, 16)
        nld[0] += 16 * 10
        for j in range(8):
            G_.dma_start(out=w_sb[j][:], in_=wqT[128 * j:128 * (j + 1), :]).then_inc(ldg, 16)
            G_.dma_start(out=w_sb[8 + j][:], in_=wkT[128 * j:128 * (j + 1), :]).then_inc(ldg, 16)
            G_.dma_start(out=w_sb[16 + j][:], in_=wvT[128 * j:128 * (j + 1), :]).then_inc(ldg, 16)
        nlg[0] += 16 * 24
        V.wait_sem(ld, nld[0])
        V.tensor_scalar(out=gam[:, 0:1], in0=gam[:, 0:1], scalar1=SCALE, scalar2=None,
                        op0=ALU.mult)

        with ExitStack() as cq:
            te = cq.enter_context

            for h in range(4):
              with ExitStack() as cqk:
                tq = cqk.enter_context
                ps_q = tq(nc.psum_tensor(f"ps_q{h}", [64, 512], F32))
                ps_n = tq(nc.psum_tensor(f"ps_n{h}", [1, 512], F32))
                ps_r = tq(nc.psum_tensor(f"ps_r{h}", [128, 512], F32))
                ps_v = [tq(nc.psum_tensor(f"ps_v{h}_{b}", [128, 64], F32)) for b in range(2)]
                for which, wbase, gcol, dst, ccol in ((0, 0, 0, qn_bf, 0), (1, 8, 1, kn_bf, 1)):
                  for cidx in range(NQC):
                      c0 = 512 * cidx
                      T.wait_sem(ldg, nlg[0])
                      T.wait(V)
                      for kt in range(8):
                          mm = T.matmul(ps_q[:], w_sb[wbase + kt][:, 64 * h:64 * (h + 1)],
                                        xat[kt][:, c0:c0 + 512],
                                        start=(kt == 0), stop=(kt == 7))
                      T.inc_last(mm)
                      S.wait(T)
                      S.wait(V)
                      S.activation(out=sq_t[:], in_=ps_q[:], func=AF.Square)
                      T.wait(S)
                      T.inc_last(T.matmul(ps_n[:], onesc[:], sq_t[:], start=True, stop=True))
                      S.wait(T)
                      S.wait(V)
                      S.activation(out=row_a[:], in_=ps_n[:], func=AF.Sqrt)
                      V.wait(S)
                      V.tensor_scalar(out=row_a[:], in0=row_a[:], scalar1=SCALE,
                                      scalar2=RMS_EPS, op0=ALU.mult, op1=ALU.max)
                      V.reciprocal(out=row_b[:], in_=row_a[:])
                      T.wait(V)
                      T.inc_last(T.matmul(ps_r[0:64, :], ones1[0:1, 0:64], row_b[:],
                                          start=True, stop=True))
                      V.tensor_scalar(out=tmp_t[:], in0=ps_q[:], scalar1=gam[:, gcol:gcol + 1],
                                      scalar2=None, op0=ALU.mult)
                      V.wait(T)
                      V.tensor_mul(out=dst[0:64, c0:c0 + 512], in0=tmp_t[:], in1=ps_r[0:64, :])
                      S.wait(V)
                      S.activation(out=sq_t[:], in_=tmp_t[:], func=AF.Square)
                      T.wait(S)
                      T.inc_last(T.matmul(ps_n[:], onesc[:], sq_t[:], start=True, stop=True))
                      V.wait(T)
                      V.tensor_mul(out=row_a[:], in0=ps_n[:], in1=row_b[:])
                      V.tensor_mul(out=row_a[:], in0=row_a[:], in1=row_b[:])
                      V.reduce_max(out=row_a[0:1, 0:1], in_=row_a[:], axis=AX.X)
                      if cidx == 0:
                          V.tensor_copy(out=cacc[0:1, ccol:ccol + 1], in_=row_a[0:1, 0:1])
                      else:
                          V.tensor_max(out=cacc[0:1, ccol:ccol + 1],
                                       in0=cacc[0:1, ccol:ccol + 1], in1=row_a[0:1, 0:1])
              SY.wait_ge(sem_v, V.n)
              SY.dma_start(out=qn_bf[64:128, :], in_=qn_bf[0:64, :]).then_inc(ld, 16)
              SY.dma_start(out=kn_bf[64:128, :], in_=kn_bf[0:64, :]).then_inc(ld, 16)
              nld[0] += 32
              S.wait(V)
              S.activation(out=cacc[0:1, 0:1], in_=cacc[0:1, 0:1], func=AF.Sqrt)
              S.activation(out=cacc[0:1, 1:2], in_=cacc[0:1, 1:2], func=AF.Sqrt)
              V.wait(S)
              V.tensor_mul(out=cacc[0:1, 2:3], in0=cacc[0:1, 0:1], in1=cacc[0:1, 1:2])
              V.tensor_scalar(out=cacc[0:1, 3:4], in0=cacc[0:1, 2:3], scalar1=-1.0,
                              scalar2=None, op0=ALU.mult)
              T.wait(V)
              T.inc_last(T.matmul(ps_r[:, 0:1], ones1[0:1, :], cacc[0:1, 3:4],
                                  start=True, stop=True))
              V.wait(T)
              V.tensor_copy(out=negC_bc[0:128, 0:1], in_=ps_r[0:128, 0:1])
              for t in range(NKT):
                  b = t % 2
                  if t >= 2:
                      T.wait(V)
                  for kt in range(8):
                      mm = T.matmul(ps_v[b][:], xat[kt][:, 128 * t:128 * (t + 1)],
                                    w_sb[16 + kt][:, 64 * h:64 * (h + 1)],
                                    start=(kt == 0), stop=(kt == 7))
                  T.inc_last(mm)
                  V.wait(T)
                  V.tensor_copy(out=vo[t][:, 0:64], in_=ps_v[b][:])
                  V.memset(vo[t][:, 64:65], 1.0)
                  cp.step(2)
              with ExitStack() as ca:
                    ta = ca.enter_context
                    ps_s = [ta(nc.psum_tensor(f"ps_s{h}_{b}", [128, 1024], F32)) for b in range(3)]
                    ps_o = ta(nc.psum_tensor(f"ps_o{h}", [65, 512], F32))
                    ps_r = ta(nc.psum_tensor(f"ps_r2{h}", [64, 512], F32))
                    T.wait(V)
                    T.wait(S)
                    T.wait_sem(ld, nld[0])
                    s_after_exp = {}
                    t_after_av = {}
                    NP = NKT // 2
                    for qc in range(NQC):
                        c0 = 512 * qc
                        T.wait(V)
                        for p in range(NP + 1):
                            if p < NP:
                                pb = p % 3
                                if pb in s_after_exp:
                                    T.wait(S, s_after_exp[pb])
                                kc = 2 * p
                                T.matmul(ps_s[pb][:, 0:512],
                                         kn_bf[0:64, 128 * kc:128 * (kc + 1)],
                                         qn_bf[0:64, c0:c0 + 512],
                                         start=True, stop=True, tile_position=(0, 0))
                                mm = T.matmul(ps_s[pb][:, 512:1024],
                                              kn_bf[64:128, 128 * (kc + 1):128 * (kc + 2)],
                                              qn_bf[64:128, c0:c0 + 512],
                                              start=True, stop=True, tile_position=(64, 0))
                                T.inc_last(mm)
                                if pb in t_after_av:
                                    S.wait(T, t_after_av[pb])
                                S.wait(T)
                                S.activation(out=exp_bf[pb][:], in_=ps_s[pb][:], func=AF.Exp,
                                             bias=negC_bc[:, 0:1])
                                s_after_exp[pb] = S.n
                            if p > 0:
                                pb1 = (p - 1) % 3
                                T.wait(S, s_after_exp[pb1])
                                for u in range(2):
                                    kc = 2 * (p - 1) + u
                                    mm = T.matmul(ps_o[:], vo[kc],
                                                  exp_bf[pb1][:, 512 * u:512 * (u + 1)],
                                                  start=(kc == 0), stop=(kc == NKT - 1))
                                T.inc_last(mm)
                                t_after_av[pb1] = T.n
                        V.wait(T)
                        V.wait_sem(st_o, nst[0])
                        V.tensor_scalar(out=row_a[:], in0=ps_o[64:65, :], scalar1=1e-30,
                                        scalar2=None, op0=ALU.add)
                        V.reciprocal(out=row_b[:], in_=row_a[:])
                        T.wait(V)
                        T.inc_last(T.matmul(ps_r[0:64, :], ones1[0:1, 0:64], row_b[:],
                                            start=True, stop=True))
                        V.wait(T)
                        V.tensor_copy(out=rec_bc[:], in_=ps_r[0:64, :])
                        V.tensor_mul(out=o_sb[:], in0=ps_o[0:64, :], in1=rec_bc[:])
                        SY.wait_ge(sem_v, V.n)
                        SY.dma_start(out=o_oh[64 * h:64 * (h + 1), c0:c0 + 512],
                                     in_=o_sb[:]).then_inc(st_o, 16)
                        nst[0] += 16
                        cp.step(4)

        cp.drain()
        cp.finish()
        SY.wait_ge(st_o, nst[0])
        G_.wait_ge(st_o, nst[0])
    return nc


def run_F(M_outs, inputs):
    in_maps = []
    for c in range(NC):
        b = c // 4
        j = c % 4
        xatT = np.concatenate([np.asarray(M_outs[4 * b + i]["o_xat"]) for i in range(4)], axis=1)
        hs = slice(256 * j, 256 * (j + 1))
        in_maps.append({
            "xatT": np.ascontiguousarray(xatT.astype(BF)),
            "wqT": f32c(inputs["L1_attn_Wq"].T[:, hs]),
            "wkT": f32c(inputs["L1_attn_Wk"].T[:, hs]),
            "wvT": f32c(inputs["L1_attn_Wv"].T[:, hs]),
            "qg": f32c(inputs["L1_attn_q_gamma"].reshape(64, 1)),
            "kg": f32c(inputs["L1_attn_k_gamma"].reshape(64, 1)),
            "cv_in0": f32c(inputs["L1_ffn_h1_w"].T),
            "cv_in1": f32c(inputs["L1_ffn_h2_w"].T),
            "cv_in2": f32c(inputs["L1_ffn_w1_w"].T),
            "cv_in3": f32c(inputs["L1_ffn_w2_w"].T),
            "cv_in4": f32c(inputs["L1_ffn_w3_w"].T),
            "cv_in5": f32c(inputs["L1_attn_Wo"].T),
        })
    if "F" not in _A_CACHE:
        _A_CACHE["F"] = build_F()
    res = run_bass_kernel_spmd(_A_CACHE["F"], in_maps, core_ids=list(range(NC)))
    return res.results


def build_G():
    nc = bass.Bass("TRN2", target_bir_lowering=False, debug=False)
    H4 = 4096
    xn1T = nc.dram_tensor("xn1T", [D, TCORE], BF16, kind="ExternalInput").ap()
    ohT = nc.dram_tensor("ohT", [D, TCORE], BF16, kind="ExternalInput").ap()
    x1 = nc.dram_tensor("x1", [TCORE, D], F32, kind="ExternalInput").ap()
    woT = nc.dram_tensor("woT", [D, D], BF16, kind="ExternalInput").ap()
    bo = nc.dram_tensor("bo", [1, D], F32, kind="ExternalInput").ap()
    w1T = nc.dram_tensor("w1T", [D, H4], BF16, kind="ExternalInput").ap()
    w2T = nc.dram_tensor("w2T", [D, H4], BF16, kind="ExternalInput").ap()
    h1T = nc.dram_tensor("h1T", [H4, H4], BF16, kind="ExternalInput").ap()
    h2T = nc.dram_tensor("h2T", [H4, H4], BF16, kind="ExternalInput").ap()
    w3T = nc.dram_tensor("w3T", [H4, D], BF16, kind="ExternalInput").ap()
    s1 = nc.dram_tensor("s1", [H4, 1], F32, kind="ExternalInput").ap()
    s2 = nc.dram_tensor("s2", [H4, 1], F32, kind="ExternalInput").ap()
    sh1 = nc.dram_tensor("sh1", [H4, 1], F32, kind="ExternalInput").ap()
    sh2 = nc.dram_tensor("sh2", [H4, 1], F32, kind="ExternalInput").ap()
    s3r = nc.dram_tensor("s3r", [1, D], F32, kind="ExternalInput").ap()
    o_x2 = nc.dram_tensor("o_x2", [TCORE, D], F32, kind="ExternalOutput").ap()

    with ExitStack() as ctx:
        e = ctx.enter_context
        acc = [e(nc.sbuf_tensor(f"acc{t}", [128, D], F32)) for t in range(16)]
        xn1 = [e(nc.sbuf_tensor(f"xn1_{j}", [128, TCORE], BF16)) for j in range(8)]
        bo_bc = e(nc.sbuf_tensor("bo_bc", [128, D], F32))
        s3_bc = e(nc.sbuf_tensor("s3_bc", [128, D], F32))
        bs_row = e(nc.sbuf_tensor("bs_row", [1, 2 * D], F32))
        ones1 = e(nc.sbuf_tensor("ones1", [1, 128], F32))
        sv_all = e(nc.sbuf_tensor("sv_all", [128, 128], F32))
        sv_off = {"s1": 0, "s2": 32, "sh1": 64, "sh2": 96}
        t1s = e(nc.sbuf_tensor("t1s", [128, 512], F32))
        t2s = e(nc.sbuf_tensor("t2s", [128, 512], F32))

        ld = e(nc.semaphore("ld"))
        ldg = e(nc.semaphore("ldg"))
        ldh = [e(nc.semaphore(f"ldh{p}")) for p in range(2)]
        ldw3 = [e(nc.semaphore(f"ldw3{p}")) for p in range(2)]
        sem_v = e(nc.semaphore("sem_v"))
        sem_s = e(nc.semaphore("sem_s"))
        sem_t = e(nc.semaphore("sem_t"))
        st = e(nc.semaphore("st"))

        V = EW(nc.vector, sem_v)
        S = EW(nc.scalar, sem_s)
        T = EW(nc.tensor, sem_t, autosync=False)
        SY = nc.sync
        G_ = nc.gpsimd
        nld = [0]
        nlg = [0]
        nldh = [0, 0]
        nldw3 = [0, 0]
        nst = [0]

        V.memset(ones1[:], 1.0)
        SY.dma_start(out=bs_row[0:1, 0:D], in_=bo).then_inc(ld, 16)
        SY.dma_start(out=bs_row[0:1, D:2 * D], in_=s3r).then_inc(ld, 16)
        for t in range(16):
            SY.dma_start(out=acc[t][:], in_=x1[128 * t:128 * (t + 1), :]).then_inc(ld, 16)
        for j in range(8):
            SY.dma_start(out=xn1[j][:], in_=xn1T[128 * j:128 * (j + 1), :]).then_inc(ld, 16)
        for nm, t in (("s1", s1), ("s2", s2), ("sh1", sh1), ("sh2", sh2)):
            o = sv_off[nm]
            for m in range(32):
                SY.dma_start(out=sv_all[:, o + m:o + m + 1],
                             in_=t[128 * m:128 * (m + 1), :]).then_inc(ld, 16)
        nld[0] += 16 * (2 + 16 + 8 + 128)
        LD_BASE = nld[0]

        with ExitStack() as c0:
            t0 = c0.enter_context
            ps_b = [t0(nc.psum_tensor(f"ps_bc{b}", [128, 512], F32)) for b in range(2)]
            T.wait_sem(ld, LD_BASE)
            T.wait(V)
            for half in range(2):
                T.inc_last(T.matmul(ps_b[half][:], ones1[0:1, :],
                                    bs_row[0:1, 512 * half:512 * (half + 1)],
                                    start=True, stop=True))
            V.wait(T)
            V.tensor_copy(out=bo_bc[:, 0:512], in_=ps_b[0][:])
            V.tensor_copy(out=bo_bc[:, 512:1024], in_=ps_b[1][:])
            for half in range(2):
                T.wait(V)
                T.inc_last(T.matmul(ps_b[half][:], ones1[0:1, :],
                                    bs_row[0:1, D + 512 * half:D + 512 * (half + 1)],
                                    start=True, stop=True))
            V.wait(T)
            V.tensor_copy(out=s3_bc[:, 0:512], in_=ps_b[0][:])
            V.tensor_copy(out=s3_bc[:, 512:1024], in_=ps_b[1][:])

        phase_barrier([nc.vector, nc.scalar, nc.tensor, nc.sync, nc.gpsimd],
                      [V, S, T], [(ld, nld[0])])
        with ExitStack() as c1:
            t1 = c1.enter_context
            oh_sb = [t1(nc.sbuf_tensor(f"oh{j}", [128, TCORE], BF16)) for j in range(8)]
            wo_bf = [t1(nc.sbuf_tensor(f"wo{j}", [128, D], BF16)) for j in range(8)]
            ps_m = [t1(nc.psum_tensor(f"ps_m{i}", [128, 512], F32)) for i in range(4)]
            for j in range(8):
                SY.dma_start(out=oh_sb[j][:], in_=ohT[128 * j:128 * (j + 1), :]).then_inc(ld, 16)
                SY.dma_start(out=wo_bf[j][:], in_=woT[128 * j:128 * (j + 1), :]).then_inc(ldg, 16)
            nld[0] += 16 * 8
            nlg[0] += 16 * 8
            T.wait_sem(ld, nld[0])
            T.wait_sem(ldg, nlg[0])
            ev_ps = {}
            for ts in range(16):
                for nh in range(2):
                    gi = 2 * ts + nh
                    pi = gi % 4
                    if gi >= 4:
                        eng, cnt = ev_ps[pi]
                        T.wait(eng, cnt)
                    for kt in range(8):
                        mm = T.matmul(ps_m[pi][:], oh_sb[kt][:, 128 * ts:128 * (ts + 1)],
                                      wo_bf[kt][:, 512 * nh:512 * (nh + 1)],
                                      start=(kt == 0), stop=(kt == 7))
                    T.inc_last(mm)
                    V.wait(T)
                    V.tensor_add(out=acc[ts][:, 512 * nh:512 * (nh + 1)],
                                 in0=acc[ts][:, 512 * nh:512 * (nh + 1)], in1=ps_m[pi][:])
                    ev_ps[pi] = (V, V.n)
                V.tensor_add(out=acc[ts][:], in0=acc[ts][:], in1=bo_bc[:])

        phase_barrier([nc.vector, nc.scalar, nc.tensor, nc.sync, nc.gpsimd],
                      [V, S, T], [(ld, nld[0]), (ldg, nlg[0])])
        with ExitStack() as c2:
            t2 = c2.enter_context
            h_bf = [t2(nc.sbuf_tensor(f"hbf{k}", [128, 512], BF16)) for k in range(32)]
            hh_bf = [t2(nc.sbuf_tensor(f"hh{k}", [128, 512], BF16)) for k in range(32)]
            wsl = [t2(nc.sbuf_tensor("wsl0", [128, 2 * 8 * 128], BF16))]
            hsl = [t2(nc.sbuf_tensor(f"hsl{p}", [128, 32 * 128], BF16)) for p in range(2)]
            w3b = [t2(nc.sbuf_tensor(f"w3b{p}", [128, D], BF16)) for p in range(2)]
            ps_p = [t2(nc.psum_tensor(f"ps_p{i}", [128, 512], F32)) for i in range(4)]
            ps_q5 = [t2(nc.psum_tensor(f"ps_q5{i}", [128, 512], F32)) for i in range(4)]

            def load_w12(hm, buf):
                SY.dma_start(out=wsl[buf][:, 0:1024].rearrange("p (k c) -> p k c", k=8),
                             in_=w1T[:, 128 * hm:128 * (hm + 1)].rearrange(
                                 "(k p) c -> p k c", p=128)).then_inc(ldh[buf], 16)
                SY.dma_start(out=wsl[buf][:, 1024:2048].rearrange("p (k c) -> p k c", k=8),
                             in_=w2T[:, 128 * hm:128 * (hm + 1)].rearrange(
                                 "(k p) c -> p k c", p=128)).then_inc(ldh[buf], 16)
                nldh[buf] += 32

            def load_hsl(mat, om, buf):
                for a in range(4):
                    SY.dma_start(
                        out=hsl[buf][:, 1024 * a:1024 * (a + 1)].rearrange(
                            "p (k c) -> p k c", k=8),
                        in_=mat[1024 * a:1024 * (a + 1),
                                128 * om:128 * (om + 1)].rearrange("(k p) c -> p k c", p=128)
                    ).then_inc(ldh[buf], 16)
                nldh[buf] += 16 * 4

            sidx = [0]
            t_after_s = {}
            ev_pp = {}
            for tq in range(4):
                tb = 512 * tq
                V.wait(T)
                for hm in range(32):
                    si = sidx[0]
                    sb = 0
                    if si >= 1:
                        SY.wait_ge(sem_t, t_after_s[si - 1])
                    load_w12(hm, sb)
                    wrdy = nldh[sb]
                    sidx[0] += 1
                    T.wait_sem(ldh[sb], wrdy)
                    pp = hm % 2
                    pa, pb = ps_p[2 * pp], ps_p[2 * pp + 1]
                    if ("g2", pp) in ev_pp:
                        eng, cnt = ev_pp[("g2", pp)]
                        T.wait(eng, cnt)
                    for kt in range(8):
                        T.matmul(pa[:], wsl[sb][:, 128 * kt:128 * (kt + 1)],
                                 xn1[kt][:, tb:tb + 512], start=(kt == 0), stop=(kt == 7))
                    for kt in range(8):
                        mm = T.matmul(pb[:], wsl[sb][:, 1024 + 128 * kt:1024 + 128 * (kt + 1)],
                                      xn1[kt][:, tb:tb + 512], start=(kt == 0), stop=(kt == 7))
                    T.inc_last(mm)
                    t_after_s[si] = T.n
                    S.wait(T)
                    S.wait(V)
                    S.activation(out=t1s[:], in_=pa[:], func=AF.Silu,
                                 scale=sv_all[:, sv_off["s1"] + hm:sv_off["s1"] + hm + 1])
                    V.wait(S)
                    V.tensor_scalar(out=t2s[:], in0=pb[:],
                                    scalar1=sv_all[:, sv_off["s2"] + hm:sv_off["s2"] + hm + 1],
                                    scalar2=None, op0=ALU.mult)
                    V.tensor_mul(out=h_bf[hm][:], in0=t1s[:], in1=t2s[:])
                    ev_pp[("g2", pp)] = (V, V.n)
                for layer, mat, svec, src_t, dst_t in (
                        (0, h1T, "sh1", h_bf, hh_bf), (1, h2T, "sh2", hh_bf, h_bf)):
                    hrdy = {}
                    t_after_om = {}
                    SY.wait_ge(sem_t, T.n)
                    load_hsl(mat, 0, 0)
                    hrdy[0] = (0, nldh[0])
                    for om in range(32):
                        sb = om % 2
                        if om + 1 < 32:
                            nb = (om + 1) % 2
                            if om >= 1:
                                SY.wait_ge(sem_t, t_after_om[om - 1])
                            load_hsl(mat, om + 1, nb)
                            hrdy[om + 1] = (nb, nldh[nb])
                        rb, rv = hrdy[om]
                        T.wait_sem(ldh[rb], rv)
                        pp = om % 2
                        pa = ps_p[2 * pp]
                        if ("g3", pp) in ev_pp:
                            eng, cnt = ev_pp[("g3", pp)]
                            T.wait(eng, cnt)
                        if om == 0:
                            T.wait(S)
                            T.wait(V)
                        for kt in range(32):
                            mm = T.matmul(pa[:], hsl[sb][:, 128 * kt:128 * (kt + 1)],
                                          src_t[kt][:], start=(kt == 0), stop=(kt == 31))
                        T.inc_last(mm)
                        t_after_om[om] = T.n
                        S.wait(T)
                        S.activation(out=dst_t[om][:], in_=pa[:], func=AF.Gelu,
                                     scale=sv_all[:, sv_off[svec] + om:sv_off[svec] + om + 1])
                        ev_pp[("g3", pp)] = (S, S.n)
                psall = ps_p + ps_q5
                T.wait(S)
                T.wait(V)
                ldw3_r = {}
                t_after_w3 = {}
                nb0 = (2 * tq) % 2
                SY.wait_ge(sem_t, T.n)
                SY.dma_start(out=w3b[0][:], in_=w3T[0:128, :]).then_inc(ldw3[0], 16)
                nldw3[0] += 16
                ldw3_r[0] = nldw3[0]
                for kt in range(32):
                    kb = kt % 2
                    if kt + 1 < 32:
                        nb = (kt + 1) % 2
                        if kt >= 1:
                            SY.wait_ge(sem_t, t_after_w3[kt - 1])
                        SY.dma_start(out=w3b[nb][:],
                                     in_=w3T[128 * (kt + 1):128 * (kt + 2), :]).then_inc(ldw3[nb], 16)
                        nldw3[nb] += 16
                        ldw3_r[kt + 1] = nldw3[nb]
                    T.wait_sem(ldw3[kb], ldw3_r[kt])
                    for gi in range(8):
                        tsl, nh = divmod(gi, 2)
                        mm = T.matmul(psall[gi][:], h_bf[kt][:, 128 * tsl:128 * (tsl + 1)],
                                      w3b[kb][:, 512 * nh:512 * (nh + 1)],
                                      start=(kt == 0), stop=(kt == 31))
                    T.inc_last(mm)
                    t_after_w3[kt] = T.n
                V.wait(T)
                for gi in range(8):
                    tsl, nh = divmod(gi, 2)
                    ts = 4 * tq + tsl
                    V.tensor_mul(out=t1s[:], in0=psall[gi][:],
                                 in1=s3_bc[:, 512 * nh:512 * (nh + 1)])
                    V.tensor_add(out=acc[ts][:, 512 * nh:512 * (nh + 1)],
                                 in0=acc[ts][:, 512 * nh:512 * (nh + 1)], in1=t1s[:])
                for key in (("g2", 0), ("g2", 1), ("g3", 0), ("g3", 1)):
                    ev_pp[key] = (V, V.n)
                for tsl in range(4):
                    ts = 4 * tq + tsl
                    SY.wait_ge(sem_v, V.n)
                    SY.dma_start(out=o_x2[128 * ts:128 * (ts + 1), :],
                                 in_=acc[ts][:]).then_inc(st, 16)
                    nst[0] += 16
            SY.wait_ge(st, nst[0])
            G_.wait_ge(st, nst[0])
    return nc


def run_G(M_outs, F_outs, inputs):
    in_maps = []
    for c in range(NC):
        b = c // 4
        ohT_b = np.concatenate([np.asarray(F_outs[4 * b + i]["o_oh"]) for i in range(4)], axis=0)
        cols = slice(TCORE * (c % 4), TCORE * (c % 4 + 1))
        in_maps.append({
            "xn1T": np.ascontiguousarray(np.asarray(M_outs[c]["o_xn1"]).astype(BF)),
            "ohT": np.ascontiguousarray(ohT_b[:, cols].astype(BF)),
            "x1": f32c(np.asarray(M_outs[c]["o_x1"])),
            "woT": np.ascontiguousarray(np.asarray(F_outs[c]["cv_out5"]).astype(BF)),
            "bo": f32c(inputs["L1_attn_bo"].reshape(1, D)),
            "w1T": np.ascontiguousarray(np.asarray(F_outs[c]["cv_out2"]).astype(BF)),
            "w2T": np.ascontiguousarray(np.asarray(F_outs[c]["cv_out3"]).astype(BF)),
            "h1T": np.ascontiguousarray(np.asarray(F_outs[c]["cv_out0"]).astype(BF)),
            "h2T": np.ascontiguousarray(np.asarray(F_outs[c]["cv_out1"]).astype(BF)),
            "w3T": np.ascontiguousarray(np.asarray(F_outs[c]["cv_out4"]).astype(BF)),
            "s1": f32c(inputs["L1_ffn_w1_s"].reshape(-1, 1)),
            "s2": f32c(inputs["L1_ffn_w2_s"].reshape(-1, 1)),
            "sh1": f32c(inputs["L1_ffn_h1_s"].reshape(-1, 1)),
            "sh2": f32c(inputs["L1_ffn_h2_s"].reshape(-1, 1)),
            "s3r": f32c(inputs["L1_ffn_w3_s"].reshape(1, D)),
        })
    if "G" not in _A_CACHE:
        _A_CACHE["G"] = build_G()
    res = run_bass_kernel_spmd(_A_CACHE["G"], in_maps, core_ids=list(range(NC)))
    return res.results


    return nc


_A_CACHE = {}


def run_A(inputs):
    x = inputs["L0_attn_x"].reshape(16384, D)
    in_maps = []
    for c in range(NC):
        t0 = TCORE * c
        if c % 4 == 0:
            xh = np.concatenate([np.zeros((HALO, D), np.float32),
                                 x[t0:t0 + TCORE]], axis=0)
        else:
            xh = x[t0 - HALO:t0 + TCORE]
        in_maps.append({
            "xi": f32c(xh),
            "ipw": f32c(inputs["L0_attn_in_proj_w"]),
            "convw": f32c(inputs["L0_attn_conv_w"]),
            "convb": f32c(inputs["L0_attn_conv_b"].reshape(DI, 1)),
            "xpw": f32c(inputs["L0_attn_x_proj_w"]),
            "dtw": f32c(inputs["L0_attn_dt_proj_w"]),
            "dtb": f32c(inputs["L0_attn_dt_proj_b"].reshape(DI, 1)),
            "wgate": f32c(inputs["L0_ffn_w_gate"]),
        })
    if "A" not in _A_CACHE:
        _A_CACHE["A"] = build_A()
    res = run_bass_kernel_spmd(_A_CACHE["A"], in_maps, core_ids=list(range(NC)))
    return res.results


def kernel(**inputs):
    A = run_A(inputs)
    Dr = run_D(A, inputs)
    M = run_M(A, Dr, inputs)
    F = run_F(M, inputs)
    G = run_G(M, F, inputs)
    x2 = np.concatenate([np.asarray(G[c]["o_x2"]) for c in range(NC)], axis=0)
    return np.ascontiguousarray(x2.reshape(2, 8192, D), dtype=np.float32)


# version 07
# speedup vs baseline: 1.1934x; 1.0208x over previous
import numpy as np
import ml_dtypes
from contextlib import ExitStack
import concourse.bass as bass
import concourse.mybir as mybir
from concourse.bass_utils import run_bass_kernel_spmd

F32 = mybir.dt.float32
BF16 = mybir.dt.bfloat16
AF = mybir.ActivationFunctionType
ALU = mybir.AluOpType
AX = mybir.AxisListType
BF = ml_dtypes.bfloat16

NC = 8
D = 1024
DI = 2048
DS = 16
TCORE = 2048
HALO = 128
TA = TCORE + HALO
LN_EPS = 1e-5


def f32c(a):
    return np.ascontiguousarray(a, dtype=np.float32)


def chunks(total, step, base=0):
    out = []
    o = base
    while o < base + total:
        n = min(step, base + total - o)
        out.append((o, n))
        o += n
    return out



class EW:

    def __init__(self, eng, sem, autosync=True):
        self.e = eng
        self.sem = sem
        self.n = 0
        self.autosync = autosync

    def wait(self, other, val=None):
        self.e.wait_ge(other.sem, other.n if val is None else val)

    def wait_sem(self, sem, val):
        self.e.wait_ge(sem, val)

    def raw(self):
        return self.e

    def __getattr__(self, name):
        fn = getattr(self.e, name)
        if not self.autosync or name in ("wait_ge", "nop", "sem_inc"):
            return fn

        def em(*a, **k):
            if self.n > 0:
                self.e.wait_ge(self.sem, self.n)
            inst = fn(*a, **k)
            inst.then_inc(self.sem, 1)
            self.n += 1
            return inst
        return em

    def inc_last(self, inst):
        inst.then_inc(self.sem, 1)
        self.n += 1
        return inst


class ConvertPool:

    def __init__(self, SY, GP, stage_f, stage_b, ld_sems, st_sems, pairs):
        self.SY, self.GP = SY, GP
        self.sf, self.sb = stage_f, stage_b
        self.ld, self.st = ld_sems, st_sems
        self.nld = [0, 0]
        self.nst = [0, 0]
        self.g_after = {}
        C = stage_f[0].shape[1]
        self.work = []
        for s, d in pairs:
            R, Cs = s.shape
            assert Cs == C and R % 128 == 0, (s.shape, C)
            for r in range(R // 128):
                self.work.append((s[128 * r:128 * (r + 1), :], d[128 * r:128 * (r + 1), :]))
        self.i = 0

    def step(self, n=1):
        for _ in range(n):
            if self.i >= len(self.work):
                return
            sap, dap = self.work[self.i]
            b = self.i % 2
            Q, GP = self.SY, self.GP
            if b in self.g_after:
                Q.wait_ge(GP.sem, self.g_after[b])
            Q.dma_start(out=self.sf[b][:], in_=sap).then_inc(self.ld[b], 16)
            self.nld[b] += 16
            GP.wait_sem(self.ld[b], self.nld[b])
            if self.nst[b]:
                GP.wait_sem(self.st[b], self.nst[b])
            GP.tensor_copy(out=self.sb[b][:], in_=self.sf[b][:])
            self.g_after[b] = GP.n
            Q.wait_ge(GP.sem, GP.n)
            Q.dma_start(out=dap, in_=self.sb[b][:]).then_inc(self.st[b], 16)
            self.nst[b] += 16
            self.i += 1

    def drain(self):
        self.step(len(self.work) - self.i)

    def finish(self):
        self.SY.wait_ge(self.st[0], self.nst[0])
        self.SY.wait_ge(self.st[1], self.nst[1])


def phase_barrier(engines, wrappers, dma_sems):
    for E in engines:
        for w in wrappers:
            E.wait_ge(w.sem, w.n)
        for sem, tot in dma_sems:
            E.wait_ge(sem, tot)


def build_A():
    nc = bass.Bass("TRN2", target_bir_lowering=False, debug=False)
    xi = nc.dram_tensor("xi", [TA, D], F32, kind="ExternalInput").ap()
    ipw = nc.dram_tensor("ipw", [D, 2 * DI], F32, kind="ExternalInput").ap()
    convw = nc.dram_tensor("convw", [DI, 4], F32, kind="ExternalInput").ap()
    convb = nc.dram_tensor("convb", [DI, 1], F32, kind="ExternalInput").ap()
    xpw = nc.dram_tensor("xpw", [DI, 96], F32, kind="ExternalInput").ap()
    dtw = nc.dram_tensor("dtw", [64, DI], F32, kind="ExternalInput").ap()
    dtb = nc.dram_tensor("dtb", [DI, 1], F32, kind="ExternalInput").ap()
    wgate = nc.dram_tensor("wgate", [D, 8], F32, kind="ExternalInput").ap()

    o_xnbf = nc.dram_tensor("o_xnbf", [D, TCORE], BF16, kind="ExternalOutput").ap()
    o_xbs = nc.dram_tensor("o_xbs", [DI, TCORE], F32, kind="ExternalOutput").ap()
    o_sz = nc.dram_tensor("o_sz", [DI, TCORE], F32, kind="ExternalOutput").ap()
    o_dt = nc.dram_tensor("o_dt", [DI, TCORE], F32, kind="ExternalOutput").ap()
    o_pbc = nc.dram_tensor("o_pbc", [32, TCORE], F32, kind="ExternalOutput").ap()
    o_wmoe = nc.dram_tensor("o_wmoe", [TCORE, 8], F32, kind="ExternalOutput").ap()

    NT = TA // 128
    CH5 = chunks(TA, 512)
    CT4 = chunks(TCORE, 512)

    with ExitStack() as ctx:
        e = ctx.enter_context
        ident_bf_d = nc.inline_tensor(np.eye(128, dtype=BF), name="ident_bf")
        ident_f_d = nc.inline_tensor(np.eye(128, dtype=np.float32), name="ident_f")

        xnT_bf = [e(nc.sbuf_tensor(f"xnT{j}", [128, TA], BF16)) for j in range(8)]
        ident_bf = e(nc.sbuf_tensor("identbf", [128, 128], BF16))
        ident_f = e(nc.sbuf_tensor("identf", [128, 128], F32))
        wgate_sb = [e(nc.sbuf_tensor(f"wg{j}", [128, 8], F32)) for j in range(8)]
        convw_sb = [e(nc.sbuf_tensor(f"cw{m}", [128, 4], F32)) for m in range(16)]
        convb_sb = [e(nc.sbuf_tensor(f"cb{m}", [128, 1], F32)) for m in range(16)]
        dtb_sb = [e(nc.sbuf_tensor(f"dbb{m}", [128, 1], F32)) for m in range(16)]
        xpw_bf = [e(nc.sbuf_tensor(f"xpw{m}", [128, 96], BF16)) for m in range(16)]
        dtw_bf = e(nc.sbuf_tensor("dtw_bf", [64, DI], BF16))
        ipwm = [e(nc.sbuf_tensor(f"ipwm{b}", [128, 8 * 128], BF16)) for b in range(2)]
        xb_raw = e(nc.sbuf_tensor("xb_raw", [128, TA], F32))
        conv_acc = e(nc.sbuf_tensor("conv_acc", [128, TCORE], F32))
        xbs_f = e(nc.sbuf_tensor("xbs_f", [128, TCORE], F32))
        xbs_b = e(nc.sbuf_tensor("xbs_b", [128, TCORE], BF16))
        sz_t = e(nc.sbuf_tensor("sz_t", [128, TCORE], F32))
        projT_bf = e(nc.sbuf_tensor("projT_bf", [64, TCORE], BF16))
        pbc_sb = e(nc.sbuf_tensor("pbc_sb", [32, TCORE], F32))
        dtv = e(nc.sbuf_tensor("dtv", [128, TCORE], F32))
        ab = e(nc.sbuf_tensor("ab", [128, TCORE], F32))
        enl = e(nc.sbuf_tensor("enl", [128, TCORE], F32))
        rl = e(nc.sbuf_tensor("rl", [128, TCORE], F32))
        dt_t = e(nc.sbuf_tensor("dt_t", [128, TCORE], F32))
        lg_t = e(nc.sbuf_tensor("lg_t", [128, 8], F32))
        mx8 = e(nc.sbuf_tensor("mx8", [128, 8], F32))
        sc1 = e(nc.sbuf_tensor("sc1", [128, 4], F32))
        negm1 = e(nc.sbuf_tensor("negm1", [128, 1], F32))
        el = e(nc.sbuf_tensor("el", [128, 8], F32))
        msk = e(nc.sbuf_tensor("msk", [128, 8], F32))
        wtm = e(nc.sbuf_tensor("wtm", [128, 8], F32))
        epsb = e(nc.sbuf_tensor("epsb", [128, 1], F32))

        ld = e(nc.semaphore("s_ld"))
        ldg = e(nc.semaphore("s_ldg"))
        ldx = [e(nc.semaphore(f"s_ldx{b}")) for b in range(2)]
        ldw = [e(nc.semaphore(f"s_ldw{b}")) for b in range(2)]
        sem_v = e(nc.semaphore("sem_v"))
        sem_s = e(nc.semaphore("sem_s"))
        sem_t = e(nc.semaphore("sem_t"))
        st = e(nc.semaphore("st"))
        st_xbs = e(nc.semaphore("st_xbs"))
        st_sz = e(nc.semaphore("st_sz"))
        st_dt = e(nc.semaphore("st_dt"))
        st_w = e(nc.semaphore("st_w"))

        V = EW(nc.vector, sem_v)
        S = EW(nc.scalar, sem_s)
        T = EW(nc.tensor, sem_t, autosync=False)
        SY = nc.sync
        G_ = nc.gpsimd
        nld = [0]
        nlg = [0]
        nst = [0]
        nldx = [0, 0]
        nldw = [0, 0]
        nst_b = dict(xbs=0, sz=0, dt=0, w=0)

        V.memset(epsb[:], LN_EPS)

        class _NoCp:
            def step(self, n=1):
                pass

            def drain(self):
                pass

            def finish(self):
                pass
        cp = _NoCp()

        SY.dma_start(out=ident_bf[:], in_=ident_bf_d.ap()).then_inc(ld, 16)
        SY.dma_start(out=ident_f[:], in_=ident_f_d.ap()).then_inc(ld, 16)
        for j in range(8):
            SY.dma_start(out=wgate_sb[j][:], in_=wgate[128 * j:128 * (j + 1), :]).then_inc(ld, 16)
        for m in range(16):
            SY.dma_start(out=convw_sb[m][:], in_=convw[128 * m:128 * (m + 1), :]).then_inc(ld, 16)
            SY.dma_start(out=convb_sb[m][:], in_=convb[128 * m:128 * (m + 1), :]).then_inc(ld, 16)
            SY.dma_start(out=dtb_sb[m][:], in_=dtb[128 * m:128 * (m + 1), :]).then_inc(ld, 16)
        nld[0] += 16 * 58
        LD_MISC = nld[0]
        for m in range(16):
            G_.dma_start(out=xpw_bf[m][:], in_=xpw[128 * m:128 * (m + 1), :]).then_inc(ldg, 16)
        G_.dma_start(out=dtw_bf[:], in_=dtw).then_inc(ldg, 16)
        nlg[0] += 16 * 17
        LD_W2 = nlg[0]

        with ExitStack() as tctx:
            te = tctx.enter_context
            xtile = [te(nc.sbuf_tensor(f"xtile{b}", [128, D], F32)) for b in range(2)]
            xnbf_t = [te(nc.sbuf_tensor(f"xnbf{b}", [128, D], BF16)) for b in range(2)]
            sqt = te(nc.sbuf_tensor("sqt", [128, D], F32))
            muv = te(nc.sbuf_tensor("muv", [128, 1], F32))
            ssv = te(nc.sbuf_tensor("ssv", [128, 1], F32))
            rstd = te(nc.sbuf_tensor("rstd", [128, 1], F32))
            tf32 = [te(nc.sbuf_tensor(f"tf32{b}", [128, 128], F32)) for b in range(2)]
            ps_tp = [te(nc.psum_tensor(f"ps_tp{b}", [128, 128], BF16)) for b in range(2)]
            ps_tf = [te(nc.psum_tensor(f"ps_tf{b}", [128, 128], F32)) for b in range(2)]
            ps_l = te(nc.psum_tensor("ps_l", [128, 8], F32))

            t_done_after = []
            ev_after_tp = {}
            glob_tp = [0]
            for i in range(NT):
                b = i % 2
                if i >= 2:
                    SY.wait_ge(sem_t, t_done_after[i - 2])
                SY.dma_start(out=xtile[b][:], in_=xi[128 * i:128 * (i + 1), :]).then_inc(ldx[b], 16)
                nldx[b] += 16
                V.wait_sem(ldx[b], nldx[b])
                V.reduce_sum(out=muv[:], in_=xtile[b][:], axis=AX.X)
                V.tensor_scalar(out=muv[:], in0=muv[:], scalar1=1.0 / D, scalar2=None, op0=ALU.mult)
                V.tensor_scalar(out=xtile[b][:], in0=xtile[b][:], scalar1=muv[:, 0:1],
                                scalar2=None, op0=ALU.subtract)
                S.wait(V)
                S.activation(out=sqt[:], in_=xtile[b][:], func=AF.Square, accum_out=ssv[:])
                S.activation(out=rstd[:], in_=ssv[:], func=AF.Ln, scale=1.0 / D, bias=epsb[:, 0:1])
                S.activation(out=rstd[:], in_=rstd[:], func=AF.Exp, scale=-0.5)
                V.wait(S)
                V.tensor_scalar(out=xtile[b][:], in0=xtile[b][:], scalar1=rstd[:, 0:1],
                                scalar2=None, op0=ALU.mult)
                V.tensor_copy(out=xnbf_t[b][:], in_=xtile[b][:])
                T.wait(V)
                if i == 0:
                    T.wait_sem(ld, LD_MISC)
                for j in range(8):
                    k = glob_tp[0]
                    if k >= 2:
                        V_after = ev_after_tp[k - 2]
                        T.wait(V, V_after)
                    T.inc_last(T.transpose(ps_tp[j % 2][:], xnbf_t[b][:, 128 * j:128 * (j + 1)],
                                           ident_bf[:]))
                    V.wait(T)
                    V.tensor_copy(out=xnT_bf[j][:, 128 * i:128 * (i + 1)], in_=ps_tp[j % 2][:])
                    ev_after_tp[k] = V.n
                    glob_tp[0] += 1
                if i >= 1:
                    for j in range(8):
                        k = glob_tp[0]
                        T.wait(V, ev_after_tp[k - 2])
                        T.inc_last(T.transpose(ps_tf[j % 2][:], xtile[b][:, 128 * j:128 * (j + 1)],
                                               ident_f[:]))
                        V.wait(T)
                        V.tensor_copy(out=tf32[j % 2][:], in_=ps_tf[j % 2][:])
                        ev_after_tp[k] = V.n
                        glob_tp[0] += 1
                        T.wait(V)
                        mm = T.matmul(ps_l[:], tf32[j % 2][:], wgate_sb[j][:],
                                      start=(j == 0), stop=(j == 7))
                        if j == 7:
                            T.inc_last(mm)
                    V.wait(T)
                    V.tensor_copy(out=lg_t[:], in_=ps_l[:])
                    V.max(out=mx8[:], in_=lg_t[:])
                    V.tensor_tensor(out=sc1[:, 0:1], in0=mx8[:, 1:2], in1=mx8[:, 0:1],
                                    op=ALU.subtract)
                    V.tensor_scalar(out=negm1[:], in0=mx8[:, 0:1], scalar1=-1.0,
                                    scalar2=None, op0=ALU.mult)
                    S.wait(V)
                    S.activation(out=sc1[:, 1:2], in_=sc1[:, 0:1], func=AF.Exp)
                    S.activation(out=el[:], in_=lg_t[:], func=AF.Exp, bias=negm1[:, 0:1])
                    V.wait(S)
                    V.tensor_scalar(out=sc1[:, 2:3], in0=sc1[:, 1:2], scalar1=1.0,
                                    scalar2=None, op0=ALU.add)
                    V.reciprocal(out=sc1[:, 3:4], in_=sc1[:, 2:3])
                    V.tensor_scalar(out=msk[:], in0=lg_t[:], scalar1=mx8[:, 1:2],
                                    scalar2=None, op0=ALU.is_ge)
                    V.tensor_scalar(out=wtm[:], in0=el[:], scalar1=sc1[:, 3:4],
                                    scalar2=None, op0=ALU.mult)
                    V.tensor_tensor(out=wtm[:], in0=wtm[:], in1=msk[:], op=ALU.mult)
                    SY.wait_ge(sem_v, V.n)
                    SY.dma_start(out=o_wmoe[128 * (i - 1):128 * i, :], in_=wtm[:]).then_inc(st_w, 16)
                    nst_b["w"] += 16
                    V.wait_sem(st_w, nst_b["w"])
                t_done_after.append(T.n)
                cp.step(3)
            SY.wait_ge(sem_v, V.n)
            for j in range(8):
                SY.dma_start(out=o_xnbf[128 * j:128 * (j + 1), :],
                             in_=xnT_bf[j][:, HALO:TA]).then_inc(st, 16)
                nst[0] += 16

        phase_barrier([nc.vector, nc.scalar, nc.tensor, nc.sync, nc.gpsimd],
                      [V, S, T],
                      [(ld, nld[0]), (ldg, nlg[0]), (ldx[0], nldx[0]), (ldx[1], nldx[1]),
                       (st_w, nst_b["w"]), (st, nst[0])])
        with ExitStack() as tctx:
            te = tctx.enter_context
            ps_ab = [te(nc.psum_tensor(f"ps_ab{b}", [128, 512], F32)) for b in range(2)]
            ps_pj = [te(nc.psum_tensor(f"ps_pj{i}", [96, 512], F32)) for i in range(4)]

            V.wait_sem(ld, LD_MISC)
            S.wait_sem(ld, LD_MISC)
            T.wait_sem(ldg, LD_W2)

            def load_ipw_slice(mcol, buf):
                for kt in range(8):
                    G_.dma_start(out=ipwm[buf][:, 128 * kt:128 * (kt + 1)],
                                 in_=ipw[128 * kt:128 * (kt + 1),
                                         128 * mcol:128 * (mcol + 1)]).then_inc(ldw[buf], 16)
                nldw[buf] += 16 * 8

            ev_after_g = {}
            ipw_ready = {}
            load_ipw_slice(0, 0)
            ipw_ready[0] = (0, nldw[0])
            t_after_m = {}
            gidx = [0]
            for m in range(16):
                nb = (m + 1) % 2
                if m + 1 < 16:
                    if m >= 1:
                        G_.wait_ge(sem_t, t_after_m[m - 1])
                    load_ipw_slice(m + 1, nb)
                    ipw_ready[m + 1] = (nb, nldw[nb])
                for ci, (c0, cl) in enumerate(CH5):
                    g = gidx[0]
                    ps = ps_ab[g % 2]
                    if g >= 2:
                        eng, cnt = ev_after_g[g - 2]
                        T.wait(eng, cnt)
                    if ci == 0:
                        rb, rv = ipw_ready[m]
                        T.wait_sem(ldw[rb], rv)
                    for kt in range(8):
                        mm = T.matmul(ps[:, 0:cl], ipwm[m % 2][:, 128 * kt:128 * (kt + 1)],
                                      xnT_bf[kt][:, c0:c0 + cl], start=(kt == 0), stop=(kt == 7))
                    T.inc_last(mm)
                    V.wait(T)
                    V.tensor_copy(out=xb_raw[:, c0:c0 + cl], in_=ps[:, 0:cl])
                    ev_after_g[g] = (V, V.n)
                    gidx[0] += 1
                V.tensor_scalar(out=conv_acc[:], in0=xb_raw[:, 125:125 + TCORE],
                                scalar1=convw_sb[m][:, 0:1], scalar2=None, op0=ALU.mult)
                for j in range(1, 4):
                    V.scalar_tensor_tensor(out=conv_acc[:], in0=xb_raw[:, 125 + j:125 + j + TCORE],
                                           scalar=convw_sb[m][:, j:j + 1], in1=conv_acc[:],
                                           op0=ALU.mult, op1=ALU.add)
                S.wait(V)
                S.wait_sem(st_xbs, nst_b["xbs"])
                S.activation(out=xbs_f[:], in_=conv_acc[:], func=AF.Silu,
                             bias=convb_sb[m][:, 0:1])
                SY.wait_ge(sem_s, S.n)
                SY.dma_start(out=o_xbs[128 * m:128 * (m + 1), :], in_=xbs_f[:]).then_inc(st_xbs, 16)
                nst_b["xbs"] += 16
                V.wait(S)
                if m >= 1:
                    V.wait(T, t_after_m[m - 1])
                V.tensor_copy(out=xbs_b[:], in_=xbs_f[:])
                T.wait(V)
                for ci, (c0, cl) in enumerate(CT4):
                    mm = T.matmul(ps_pj[ci][:], xpw_bf[m][:], xbs_b[:, c0:c0 + cl],
                                  start=(m == 0), stop=(m == 15))
                T.inc_last(mm)
                t_after_m[m] = T.n
                cp.step(3)
            for m in range(16):
                nb = m % 2
                G_.wait_ge(sem_t, T.n)
                load_ipw_slice(16 + m, nb)
                zr = nldw[nb]
                for ci, (c0, cl) in enumerate(CT4):
                    g = gidx[0]
                    ps = ps_ab[g % 2]
                    eng, cnt = ev_after_g[g - 2]
                    T.wait(eng, cnt)
                    if ci == 0:
                        T.wait_sem(ldw[nb], zr)
                    for kt in range(8):
                        mm = T.matmul(ps[:, 0:cl], ipwm[nb][:, 128 * kt:128 * (kt + 1)],
                                      xnT_bf[kt][:, HALO + c0:HALO + c0 + cl],
                                      start=(kt == 0), stop=(kt == 7))
                    T.inc_last(mm)
                    S.wait(T)
                    if ci == 0:
                        S.wait_sem(st_sz, nst_b["sz"])
                    S.activation(out=sz_t[:, c0:c0 + cl], in_=ps[:, 0:cl], func=AF.Silu)
                    ev_after_g[g] = (S, S.n)
                    gidx[0] += 1
                SY.wait_ge(sem_s, S.n)
                SY.dma_start(out=o_sz[128 * m:128 * (m + 1), :], in_=sz_t[:]).then_inc(st_sz, 16)
                nst_b["sz"] += 16

            V.wait(T)
            for ci, (c0, cl) in enumerate(CT4):
                V.tensor_copy(out=projT_bf[:, c0:c0 + cl], in_=ps_pj[ci][0:64, :])
                V.tensor_copy(out=pbc_sb[0:32, c0:c0 + cl], in_=ps_pj[ci][64:96, :])
            SY.wait_ge(sem_v, V.n)
            SY.dma_start(out=o_pbc[:, :], in_=pbc_sb[0:32, :]).then_inc(st, 16)
            nst[0] += 16
            T.wait(V)
            for m in range(16):
                for ci, (c0, cl) in enumerate(CT4):
                    g = gidx[0]
                    ps = ps_ab[g % 2]
                    eng, cnt = ev_after_g[g - 2]
                    T.wait(eng, cnt)
                    T.inc_last(T.matmul(ps[:, 0:cl], dtw_bf[:, 128 * m:128 * (m + 1)],
                                        projT_bf[:, c0:c0 + cl], start=True, stop=True))
                    V.wait(T)
                    V.tensor_scalar(out=dtv[:, c0:c0 + cl], in0=ps[:, 0:cl],
                                    scalar1=dtb_sb[m][:, 0:1], scalar2=None, op0=ALU.add)
                    ev_after_g[g] = (V, V.n)
                    gidx[0] += 1
                V.tensor_scalar(out=rl[:], in0=dtv[:], scalar1=-1.0, scalar2=None, op0=ALU.mult)
                V.tensor_tensor(out=ab[:], in0=dtv[:], in1=rl[:], op=ALU.min)
                V.tensor_scalar(out=rl[:], in0=dtv[:], scalar1=0.0, scalar2=None, op0=ALU.max)
                S.wait(V)
                S.activation(out=enl[:], in_=ab[:], func=AF.Exp)
                S.activation(out=enl[:], in_=enl[:], func=AF.Ln, bias=1.0)
                V.wait(S)
                V.wait_sem(st_dt, nst_b["dt"])
                V.tensor_tensor(out=dt_t[:], in0=rl[:], in1=enl[:], op=ALU.add)
                SY.wait_ge(sem_v, V.n)
                SY.dma_start(out=o_dt[128 * m:128 * (m + 1), :], in_=dt_t[:]).then_inc(st_dt, 16)
                nst_b["dt"] += 16

        cp.drain()
        cp.finish()
        SY.wait_ge(st, nst[0])
        SY.wait_ge(st_xbs, nst_b["xbs"])
        SY.wait_ge(st_sz, nst_b["sz"])
        SY.wait_ge(st_dt, nst_b["dt"])
        SY.wait_ge(st_w, nst_b["w"])
        G_.wait_ge(st, nst[0])
    return nc


def build_D():
    nc = bass.Bass("TRN2", target_bir_lowering=False, debug=False)
    TB = 8192
    CH = 512
    NCH = TB // CH
    dtT = nc.dram_tensor("dtT", [512, TB], F32, kind="ExternalInput").ap()
    xbsT = nc.dram_tensor("xbsT", [512, TB], F32, kind="ExternalInput").ap()
    pbcT = nc.dram_tensor("pbcT", [32, TB], F32, kind="ExternalInput").ap()
    alog = nc.dram_tensor("alog", [512, 16], F32, kind="ExternalInput").ap()
    dpar = nc.dram_tensor("dpar", [512, 1], F32, kind="ExternalInput").ap()
    o_y = nc.dram_tensor("o_y", [512, TB], F32, kind="ExternalOutput").ap()
    w1f = nc.dram_tensor("w1f", [8 * D, DI], F32, kind="ExternalInput").ap()
    w3f = nc.dram_tensor("w3f", [8 * D, DI], F32, kind="ExternalInput").ap()
    w2f = nc.dram_tensor("w2f", [8 * DI, D], F32, kind="ExternalInput").ap()
    opwf = nc.dram_tensor("opwf", [DI, D], F32, kind="ExternalInput").ap()
    o_w1b = nc.dram_tensor("o_w1b", [8 * D, DI], BF16, kind="ExternalOutput").ap()
    o_w3b = nc.dram_tensor("o_w3b", [8 * D, DI], BF16, kind="ExternalOutput").ap()
    o_w2b = nc.dram_tensor("o_w2b", [8 * DI, D], BF16, kind="ExternalOutput").ap()
    o_opwb = nc.dram_tensor("o_opwb", [DI, D], BF16, kind="ExternalOutput").ap()

    with ExitStack() as ctx:
        e = ctx.enter_context
        dt_c = [[e(nc.sbuf_tensor(f"dt{p}_{d}", [128, CH], F32)) for d in range(4)] for p in range(2)]
        xb_c = [[e(nc.sbuf_tensor(f"xb{p}_{d}", [128, CH], F32)) for d in range(4)] for p in range(2)]
        pb_c = [e(nc.sbuf_tensor(f"pb{p}", [1, 32 * CH], F32)) for p in range(2)]
        a_sb = [e(nc.sbuf_tensor(f"a{d}", [128, 16], F32)) for d in range(4)]
        dp_sb = [e(nc.sbuf_tensor(f"dp{d}", [128, 1], F32)) for d in range(4)]
        state = [e(nc.sbuf_tensor(f"stt{d}", [128, 16], F32)) for d in range(4)]
        ones1 = e(nc.sbuf_tensor("ones1", [1, 128], F32))
        dtu_t = e(nc.sbuf_tensor("dtu_t", [128, CH], F32))
        da_sb = [e(nc.sbuf_tensor(f"da{b}", [128, CH], F32)) for b in range(2)]
        dbu = e(nc.sbuf_tensor("dbu", [128, CH], F32))
        sct = e(nc.sbuf_tensor("sct", [128, CH], F32))
        yterm = e(nc.sbuf_tensor("yterm", [128, CH], F32))
        y_acc = [e(nc.sbuf_tensor(f"ya{d}", [128, CH], F32)) for d in range(4)]

        ps_bc = [e(nc.psum_tensor(f"ps_bc{i}", [128, CH], F32)) for i in range(4)]

        cstage_f = [e(nc.sbuf_tensor(f"cstagef{b}", [128, DI], F32)) for b in range(2)]
        cstage = [e(nc.sbuf_tensor(f"cstage{b}", [128, DI], BF16)) for b in range(2)]
        ld_cv = [e(nc.semaphore(f"ld_cv{b}")) for b in range(2)]
        st_cv = [e(nc.semaphore(f"st_cv{b}")) for b in range(2)]
        sem_g = e(nc.semaphore("sem_g"))
        ld = e(nc.semaphore("ld"))
        ldc = [e(nc.semaphore(f"ldc{p}")) for p in range(2)]
        sem_v = e(nc.semaphore("sem_v"))
        sem_s = e(nc.semaphore("sem_s"))
        sem_t = e(nc.semaphore("sem_t"))
        st_y = [e(nc.semaphore(f"st_y{d}")) for d in range(4)]

        V = EW(nc.vector, sem_v)
        S = EW(nc.scalar, sem_s)
        T = EW(nc.tensor, sem_t, autosync=False)
        SY = nc.sync
        nld = 0
        nldc = [0, 0]
        nst = [0]

        V.memset(ones1[:], 1.0)
        for d in range(4):
            V.memset(state[d][:], 0.0)
        GP = EW(nc.gpsimd, sem_g)
        cp = ConvertPool(nc.gpsimd, GP, cstage_f, cstage, ld_cv, st_cv,
                         [(w1f, o_w1b), (w3f, o_w3b),
                          (w2f.rearrange("(a b) c -> a (b c)", b=2),
                           o_w2b.rearrange("(a b) c -> a (b c)", b=2)),
                          (opwf.rearrange("(a b) c -> a (b c)", b=2),
                           o_opwb.rearrange("(a b) c -> a (b c)", b=2))])

        for d in range(4):
            SY.dma_start(out=a_sb[d][:], in_=alog[128 * d:128 * (d + 1), :]).then_inc(ld, 16)
            SY.dma_start(out=dp_sb[d][:], in_=dpar[128 * d:128 * (d + 1), :]).then_inc(ld, 16)
        nld += 16 * 8
        S.wait_sem(ld, nld)
        for d in range(4):
            S.activation(out=a_sb[d][:], in_=a_sb[d][:], func=AF.Exp, scale=1.0)
        for d in range(4):
            S.mul(out=a_sb[d][:], in_=a_sb[d][:], mul=-1.0)

        def load_chunk(ch, p):
            c0 = ch * CH
            for d in range(4):
                SY.dma_start(out=dt_c[p][d][:], in_=dtT[128 * d:128 * (d + 1), c0:c0 + CH]).then_inc(ldc[p], 16)
                SY.dma_start(out=xb_c[p][d][:], in_=xbsT[128 * d:128 * (d + 1), c0:c0 + CH]).then_inc(ldc[p], 16)
            SY.dma_start(out=pb_c[p][0:1, :].rearrange("o (r c) -> o r c", r=32),
                         in_=pbcT[:, c0:c0 + CH].unsqueeze(0)).then_inc(ldc[p], 16)
            nldc[p] += 16 * 9

        v_after_chunk = {}
        vas = {}
        load_chunk(0, 0)
        rdy = {0: nldc[0]}
        bc_reader = {}
        for ch in range(NCH):
            p = ch % 2
            if ch + 1 < NCH:
                if ch >= 1:
                    SY.wait_ge(sem_v, v_after_chunk[ch - 1])
                load_chunk(ch + 1, (ch + 1) % 2)
                rdy[ch + 1] = nldc[(ch + 1) % 2]
            V.wait_sem(ldc[p], rdy[ch])
            S.wait_sem(ldc[p], rdy[ch])
            T.wait_sem(ldc[p], rdy[ch])
            if ch == 0:
                T.wait(V)
            for d in range(4):
                if ch >= 1:
                    V.wait_sem(st_y[d], 16 * ch)
                V.tensor_mul(out=dtu_t[:], in0=dt_c[p][d][:], in1=xb_c[p][d][:])
                for s in range(16):
                    k = (d * 16 + s) % 2
                    bb, cb = ps_bc[2 * k], ps_bc[2 * k + 1]
                    for bank, ps_t, row in ((2 * k, bb, s), (2 * k + 1, cb, 16 + s)):
                        if bank in bc_reader:
                            T.wait(V, bc_reader[bank])
                        T.inc_last(T.matmul(ps_t[:], ones1[0:1, :],
                                            pb_c[p][0:1, row * CH:(row + 1) * CH],
                                            start=True, stop=True))
                    db = s % 2
                    if db in vas:
                        S.wait(V, vas[db])
                    S.activation(out=da_sb[db][:], in_=dt_c[p][d][:], func=AF.Exp,
                                 scale=a_sb[d][:, s:s + 1])
                    V.wait(T)
                    V.wait(S)
                    V.tensor_mul(out=dbu[:], in0=dtu_t[:], in1=bb[:])
                    V.tensor_tensor_scan(out=sct[:], data0=da_sb[db][:], data1=dbu[:],
                                         initial=state[d][:, s:s + 1], op0=ALU.mult,
                                         op1=ALU.add)
                    vas[db] = V.n
                    V.tensor_copy(out=state[d][:, s:s + 1], in_=sct[:, CH - 1:CH])
                    if s == 0:
                        V.tensor_mul(out=y_acc[d][:], in0=sct[:], in1=cb[:])
                    else:
                        V.tensor_mul(out=yterm[:], in0=sct[:], in1=cb[:])
                        V.tensor_add(out=y_acc[d][:], in0=y_acc[d][:], in1=yterm[:])
                    bc_reader[2 * k] = V.n
                    bc_reader[2 * k + 1] = V.n
                V.scalar_tensor_tensor(out=y_acc[d][:], in0=xb_c[p][d][:],
                                       scalar=dp_sb[d][:, 0:1], in1=y_acc[d][:],
                                       op0=ALU.mult, op1=ALU.add)
                SY.wait_ge(sem_v, V.n)
                SY.dma_start(out=o_y[128 * d:128 * (d + 1), ch * CH:(ch + 1) * CH],
                             in_=y_acc[d][:]).then_inc(st_y[d], 16)
                nst[0] += 16
                cp.step(4)
            v_after_chunk[ch] = V.n
        cp.drain()
        cp.finish()
        for d in range(4):
            SY.wait_ge(st_y[d], 16 * NCH)
    return nc


def run_D(A_outs, inputs):
    dtT = np.concatenate([np.asarray(A_outs[c]["o_dt"]) for c in range(NC)], axis=1)
    xbsT = np.concatenate([np.asarray(A_outs[c]["o_xbs"]) for c in range(NC)], axis=1)
    pbcT = np.concatenate([np.asarray(A_outs[c]["o_pbc"]) for c in range(NC)], axis=1)
    alog = f32c(inputs["L0_attn_A_log"])
    dpar = f32c(inputs["L0_attn_D_param"].reshape(DI, 1))
    in_maps = []
    for c in range(NC):
        b = c // 4
        r0 = 512 * (c % 4)
        cols = slice(8192 * b, 8192 * (b + 1))
        in_maps.append({
            "dtT": f32c(dtT[r0:r0 + 512, cols]),
            "xbsT": f32c(xbsT[r0:r0 + 512, cols]),
            "pbcT": f32c(pbcT[:, cols]),
            "alog": f32c(alog[r0:r0 + 512]),
            "dpar": f32c(dpar[r0:r0 + 512]),
            "w1f": f32c(inputs["L0_ffn_w1"].reshape(8 * D, DI)),
            "w3f": f32c(inputs["L0_ffn_w3"].reshape(8 * D, DI)),
            "w2f": f32c(inputs["L0_ffn_w2"].reshape(8 * DI, D)),
            "opwf": f32c(inputs["L0_attn_out_proj_w"]),
        })
    if "D" not in _A_CACHE:
        _A_CACHE["D"] = build_D()
    res = run_bass_kernel_spmd(_A_CACHE["D"], in_maps, core_ids=list(range(NC)))
    return res.results


def build_M():
    nc = bass.Bass("TRN2", target_bir_lowering=False, debug=False)
    yT = nc.dram_tensor("yT", [DI, TCORE], F32, kind="ExternalInput").ap()
    szT = nc.dram_tensor("szT", [DI, TCORE], F32, kind="ExternalInput").ap()
    xnbfT = nc.dram_tensor("xnbfT", [D, TCORE], BF16, kind="ExternalInput").ap()
    wmoe = nc.dram_tensor("wmoe", [TCORE, 8], F32, kind="ExternalInput").ap()
    x_tm = nc.dram_tensor("x_tm", [TCORE, D], F32, kind="ExternalInput").ap()
    opw = nc.dram_tensor("opw", [DI, D], BF16, kind="ExternalInput").ap()
    w1 = nc.dram_tensor("w1", [8 * D, DI], BF16, kind="ExternalInput").ap()
    w3 = nc.dram_tensor("w3", [8 * D, DI], BF16, kind="ExternalInput").ap()
    w2 = nc.dram_tensor("w2", [8 * DI, D], BF16, kind="ExternalInput").ap()
    lng = nc.dram_tensor("lng", [1, D], F32, kind="ExternalInput").ap()
    lnb = nc.dram_tensor("lnb", [1, D], F32, kind="ExternalInput").ap()

    o_x1 = nc.dram_tensor("o_x1", [TCORE, D], F32, kind="ExternalOutput").ap()
    o_xn1 = nc.dram_tensor("o_xn1", [D, TCORE], BF16, kind="ExternalOutput").ap()
    o_xat = nc.dram_tensor("o_xat", [D, TCORE], BF16, kind="ExternalOutput").ap()

    with ExitStack() as ctx:
        e = ctx.enter_context
        ident_bf_d = nc.inline_tensor(np.eye(128, dtype=BF), name="ident_bf")

        acc = [e(nc.sbuf_tensor(f"acc{t}", [128, D], F32)) for t in range(16)]
        xnbf = [e(nc.sbuf_tensor(f"xnbf{j}", [128, TCORE], BF16)) for j in range(8)]
        wm_sb = [e(nc.sbuf_tensor(f"wm{t}", [128, 8], F32)) for t in range(16)]
        ident_bf = e(nc.sbuf_tensor("identbf", [128, 128], BF16))
        ones1 = e(nc.sbuf_tensor("ones1", [1, 128], F32))

        epsb = e(nc.sbuf_tensor("epsb", [128, 1], F32))

        ld = e(nc.semaphore("ld"))
        ldg = e(nc.semaphore("ldg"))
        ldy = [e(nc.semaphore(f"ldy{p}")) for p in range(2)]
        ldw_s = [e(nc.semaphore(f"ldws{p}")) for p in range(2)]
        ldw2 = [e(nc.semaphore(f"ldw2{p}")) for p in range(2)]
        sem_v = e(nc.semaphore("sem_v"))
        sem_s = e(nc.semaphore("sem_s"))
        sem_t = e(nc.semaphore("sem_t"))
        st = e(nc.semaphore("st"))

        V = EW(nc.vector, sem_v)
        S = EW(nc.scalar, sem_s)
        T = EW(nc.tensor, sem_t, autosync=False)
        SY = nc.sync
        G_ = nc.gpsimd
        nld = [0]
        nlg = [0]
        nldy = [0, 0]
        nldws = [0, 0]
        nldw2 = [0, 0]
        nst = [0]

        V.memset(epsb[:], LN_EPS)
        V.memset(ones1[:], 1.0)

        SY.dma_start(out=ident_bf[:], in_=ident_bf_d.ap()).then_inc(ld, 16)
        for t in range(16):
            SY.dma_start(out=wm_sb[t][:], in_=wmoe[128 * t:128 * (t + 1), :]).then_inc(ld, 16)
            SY.dma_start(out=acc[t][:], in_=x_tm[128 * t:128 * (t + 1), :]).then_inc(ld, 16)
        for j in range(8):
            SY.dma_start(out=xnbf[j][:], in_=xnbfT[128 * j:128 * (j + 1), :]).then_inc(ld, 16)
        nld[0] += 16 * 41
        LD_BASE = nld[0]

        with ExitStack() as c1:
            t1 = c1.enter_context
            yf_bf = [t1(nc.sbuf_tensor(f"yf{k}", [128, TCORE], BF16)) for k in range(8)]
            ybuf = t1(nc.sbuf_tensor("ybuf", [128, TCORE], F32))
            zbuf = t1(nc.sbuf_tensor("zbuf", [128, TCORE], F32))
            opw_bf = [t1(nc.sbuf_tensor(f"opw{k}", [128, D], BF16)) for k in range(16)]
            ps_m = [t1(nc.psum_tensor(f"ps_m{i}", [128, 512], F32)) for i in range(4)]
            for k in range(16):
                SY.dma_start(out=opw_bf[k][:], in_=opw[128 * k:128 * (k + 1), :]).then_inc(ldg, 16)
            nlg[0] += 16 * 16
            t_after_g = {}
            ev_ps = {}
            for g in range(2):
                for k8 in range(8):
                    k = 8 * g + k8
                    SY.wait_ge(sem_v, V.n)
                    if g >= 1:
                        SY.wait_ge(sem_t, t_after_g[g - 1])
                    SY.dma_start(out=ybuf[:], in_=yT[128 * k:128 * (k + 1), :]).then_inc(ldy[0], 16)
                    SY.dma_start(out=zbuf[:], in_=szT[128 * k:128 * (k + 1), :]).then_inc(ldy[0], 16)
                    nldy[0] += 32
                    V.wait_sem(ldy[0], nldy[0])
                    V.tensor_mul(out=yf_bf[k8][:], in0=ybuf[:], in1=zbuf[:])
                if g == 0:
                    T.wait_sem(ldg, nlg[0])
                T.wait(V)
                for ts in range(16):
                    for half in range(2):
                        gi = 2 * ts + half
                        pi = gi % 4
                        if gi >= 4 or g >= 1:
                            eng, cnt = ev_ps[pi]
                            T.wait(eng, cnt)
                        for k8 in range(8):
                            mm = T.matmul(ps_m[pi][:], yf_bf[k8][:, 128 * ts:128 * (ts + 1)],
                                          opw_bf[8 * g + k8][:, 512 * half:512 * (half + 1)],
                                          start=(k8 == 0), stop=(k8 == 7))
                        T.inc_last(mm)
                        V.wait(T)
                        V.wait_sem(ld, LD_BASE)
                        V.tensor_add(out=acc[ts][:, 512 * half:512 * (half + 1)],
                                     in0=acc[ts][:, 512 * half:512 * (half + 1)],
                                     in1=ps_m[pi][:])
                        ev_ps[pi] = (V, V.n)
                t_after_g[g] = T.n

        phase_barrier([nc.vector, nc.scalar, nc.tensor, nc.sync, nc.gpsimd],
                      [V, S, T],
                      [(ld, nld[0]), (ldg, nlg[0]), (ldy[0], nldy[0])])
        with ExitStack() as c3:
            t3 = c3.enter_context
            hc = [t3(nc.sbuf_tensor(f"hc{p}", [128, 16 * 512], BF16)) for p in range(2)]
            w13m = [t3(nc.sbuf_tensor(f"w13m{p}", [128, 2 * 8 * 128], BF16)) for p in range(2)]
            w2_bf = [t3(nc.sbuf_tensor(f"w2bf{k}", [128, D], BF16)) for k in range(16)]
            hstg = [t3(nc.sbuf_tensor(f"hstg{p}", [128, 2048], BF16)) for p in range(2)]
            t1s = t3(nc.sbuf_tensor("t1s", [128, 512], F32))
            ps_h = [t3(nc.psum_tensor(f"ps_h{i}", [128, 512], F32)) for i in range(4)]
            ps_y = [t3(nc.psum_tensor(f"ps_y{i}", [128, 512], F32)) for i in range(4)]
            d_h = [nc.dram_tensor(f"d_h{b}", [DI, TCORE], BF16) for b in range(2)]

            def load_w13(eidx, hm, buf):
                SY.dma_start(out=w13m[buf][:, 0:1024].rearrange("p (k c) -> p k c", k=8),
                             in_=w1[D * eidx:D * (eidx + 1),
                                    128 * hm:128 * (hm + 1)].rearrange(
                                 "(k p) c -> p k c", p=128)).then_inc(ldw_s[buf], 16)
                SY.dma_start(out=w13m[buf][:, 1024:2048].rearrange("p (k c) -> p k c", k=8),
                             in_=w3[D * eidx:D * (eidx + 1),
                                    128 * hm:128 * (hm + 1)].rearrange(
                                 "(k p) c -> p k c", p=128)).then_inc(ldw_s[buf], 16)
                nldws[buf] += 32

            def load_w2(eidx):
                for k in range(16):
                    SY.dma_start(out=w2_bf[k][:],
                                 in_=w2[DI * eidx + 128 * k:DI * eidx + 128 * (k + 1),
                                        :]).then_inc(ldw2[0], 16)
                nldw2[0] += 16 * 16

            ev_h = {}
            ev_y = {}
            st_hd = [t3(nc.semaphore(f"st_hd{p}")) for p in range(2)]
            ldh_y = [t3(nc.semaphore(f"ldh_y{p}")) for p in range(2)]
            nst_hd = [0, 0]
            nldh_y = [0, 0]
            t_after_w13 = {}
            t_after_y = {}
            st_hd_after = {}
            t_after_h = {}

            def emit_h(ei):
                eb = ei % 2
                for hm in range(16):
                    si = 16 * ei + hm
                    sb = si % 2
                    if si >= 2:
                        SY.wait_ge(sem_t, t_after_w13[si - 2])
                    load_w13(ei, hm, sb)
                    wrdy = nldws[sb]
                    T.wait_sem(ldw_s[sb], wrdy)
                    for ci in range(4):
                        pp = ci % 2
                        pa, pb = ps_h[2 * pp], ps_h[2 * pp + 1]
                        if ("h", pp) in ev_h:
                            eng, cnt = ev_h[("h", pp)]
                            T.wait(eng, cnt)
                        for kt in range(8):
                            T.matmul(pa[:], w13m[sb][:, 128 * kt:128 * (kt + 1)],
                                     xnbf[kt][:, 512 * ci:512 * (ci + 1)],
                                     start=(kt == 0), stop=(kt == 7))
                        for kt in range(8):
                            mm = T.matmul(pb[:], w13m[sb][:, 1024 + 128 * kt:1024 + 128 * (kt + 1)],
                                          xnbf[kt][:, 512 * ci:512 * (ci + 1)],
                                          start=(kt == 0), stop=(kt == 7))
                        T.inc_last(mm)
                        S.wait(T)
                        S.wait(V)
                        S.activation(out=t1s[:], in_=pa[:], func=AF.Silu)
                        V.wait(S)
                        if ci == 0:
                            V.wait_sem(st_hd[sb], nst_hd[sb])
                        V.tensor_mul(out=hstg[sb][:, 512 * ci:512 * (ci + 1)],
                                     in0=t1s[:], in1=pb[:])
                        ev_h[("h", pp)] = (V, V.n)
                    t_after_w13[si] = T.n
                    SY.wait_ge(sem_v, V.n)
                    SY.dma_start(out=d_h[eb].ap()[128 * hm:128 * (hm + 1), :],
                                 in_=hstg[sb][:]).then_inc(st_hd[sb], 16)
                    nst_hd[sb] += 16
                t_after_h[ei] = T.n
                st_hd_after[ei] = (nst_hd[0], nst_hd[1])

            def emit_y(ei):
                eb = ei % 2
                T.wait_sem(ldw2[0], w2rdy[ei])
                for ci in range(4):
                    cb = ci % 2
                    if ci >= 2:
                        G_.wait_ge(sem_t, t_after_ci[cb])
                    elif ei >= 1:
                        G_.wait_ge(sem_t, t_after_ci[cb])
                    if ci == 0:
                        a0, a1 = st_hd_after[ei]
                        G_.wait_ge(st_hd[0], a0)
                        G_.wait_ge(st_hd[1], a1)
                    G_.dma_start(out=hc[cb][:].rearrange("p (k c) -> p k c", k=16),
                                 in_=d_h[eb].ap()[:, 512 * ci:512 * (ci + 1)].rearrange(
                                     "(k p) c -> p k c", p=128)).then_inc(ldh_y[cb], 16)
                    nldh_y[cb] += 16
                    T.wait_sem(ldh_y[cb], nldh_y[cb])
                    for tsl in range(4):
                        ts = 4 * ci + tsl
                        for nh in range(2):
                            pj = (2 * tsl + nh) % 4
                            if ("y", pj) in ev_y:
                                eng, cnt = ev_y[("y", pj)]
                                T.wait(eng, cnt)
                            for k in range(16):
                                mm = T.matmul(ps_y[pj][:], hc[cb][:, 512 * k + 128 * tsl:
                                                                  512 * k + 128 * (tsl + 1)],
                                              w2_bf[k][:, 512 * nh:512 * (nh + 1)],
                                              start=(k == 0), stop=(k == 15))
                            T.inc_last(mm)
                            V.wait(T)
                            V.scalar_tensor_tensor(out=acc[ts][:, 512 * nh:512 * (nh + 1)],
                                                   in0=ps_y[pj][:],
                                                   scalar=wm_sb[ts][:, ei:ei + 1],
                                                   in1=acc[ts][:, 512 * nh:512 * (nh + 1)],
                                                   op0=ALU.mult, op1=ALU.add)
                            ev_y[("y", pj)] = (V, V.n)
                    t_after_ci[cb] = T.n
                t_after_y[ei] = T.n

            w2rdy = {}
            t_after_ci = {0: 0, 1: 0}
            for ei in range(9):
                if ei < 8:
                    emit_h(ei)
                if ei >= 1:
                    if ei >= 2:
                        SY.wait_ge(sem_t, t_after_y[ei - 2])
                    load_w2(ei - 1)
                    w2rdy[ei - 1] = nldw2[0]
                    emit_y(ei - 1)

        phase_barrier([nc.vector, nc.scalar, nc.tensor, nc.sync, nc.gpsimd],
                      [V, S, T],
                      [(ldw_s[0], nldws[0]), (ldw_s[1], nldws[1]), (ldw2[0], nldw2[0])])
        with ExitStack() as c4:
            t4 = c4.enter_context
            g_bc = t4(nc.sbuf_tensor("g_bc", [128, D], F32))
            b_bc = t4(nc.sbuf_tensor("b_bc", [128, D], F32))
            gb_row = t4(nc.sbuf_tensor("gb_row", [1, 2 * D], F32))
            SY.dma_start(out=gb_row[0:1, 0:D], in_=lng).then_inc(ld, 16)
            SY.dma_start(out=gb_row[0:1, D:2 * D], in_=lnb).then_inc(ld, 16)
            nld[0] += 32
            xn1T = [t4(nc.sbuf_tensor(f"xn1T{j}", [128, TCORE], BF16)) for j in range(8)]
            xatT = [t4(nc.sbuf_tensor(f"xatT{j}", [128, TCORE], BF16)) for j in range(8)]
            xn1t = t4(nc.sbuf_tensor("xn1t", [128, D], F32))
            xat = t4(nc.sbuf_tensor("xat", [128, D], F32))
            xbf = [t4(nc.sbuf_tensor(f"xbfb{b}", [128, D], BF16)) for b in range(2)]
            sqt = t4(nc.sbuf_tensor("sqt", [128, D], F32))
            muv = t4(nc.sbuf_tensor("muv", [128, 1], F32))
            ssv = t4(nc.sbuf_tensor("ssv", [128, 1], F32))
            rstd = t4(nc.sbuf_tensor("rstd", [128, 1], F32))
            ps_tp = [t4(nc.psum_tensor(f"ps_tp{b}", [128, 128], BF16)) for b in range(2)]
            ps_g = [t4(nc.psum_tensor(f"ps_g{b}", [128, 512], F32)) for b in range(2)]

            T.wait_sem(ld, nld[0])
            T.wait(V)
            for half in range(2):
                T.inc_last(T.matmul(ps_g[half][:], ones1[0:1, :],
                                    gb_row[0:1, 512 * half:512 * (half + 1)],
                                    start=True, stop=True))
            V.wait(T)
            V.tensor_copy(out=g_bc[:, 0:512], in_=ps_g[0][:])
            V.tensor_copy(out=g_bc[:, 512:1024], in_=ps_g[1][:])
            for half in range(2):
                T.wait(V)
                T.inc_last(T.matmul(ps_g[half][:], ones1[0:1, :],
                                    gb_row[0:1, D + 512 * half:D + 512 * (half + 1)],
                                    start=True, stop=True))
            V.wait(T)
            V.tensor_copy(out=b_bc[:, 0:512], in_=ps_g[0][:])
            V.tensor_copy(out=b_bc[:, 512:1024], in_=ps_g[1][:])

            def ln_tmajor(src_ap, dst, scale_gb):
                V.reduce_sum(out=muv[:], in_=src_ap, axis=AX.X)
                V.tensor_scalar(out=muv[:], in0=muv[:], scalar1=1.0 / D, scalar2=None,
                                op0=ALU.mult)
                V.tensor_scalar(out=dst[:], in0=src_ap, scalar1=muv[:, 0:1], scalar2=None,
                                op0=ALU.subtract)
                S.wait(V)
                S.activation(out=sqt[:], in_=dst[:], func=AF.Square, accum_out=ssv[:])
                S.activation(out=rstd[:], in_=ssv[:], func=AF.Ln, scale=1.0 / D,
                             bias=epsb[:, 0:1])
                S.activation(out=rstd[:], in_=rstd[:], func=AF.Exp, scale=-0.5)
                V.wait(S)
                V.tensor_scalar(out=dst[:], in0=dst[:], scalar1=rstd[:, 0:1], scalar2=None,
                                op0=ALU.mult)
                if scale_gb:
                    V.tensor_mul(out=dst[:], in0=dst[:], in1=g_bc[:])
                    V.tensor_add(out=dst[:], in0=dst[:], in1=b_bc[:])

            ev_tp = {}
            ktp = [0]

            def transposes(src_bf, dstT, ti):
                for j in range(8):
                    k = ktp[0]
                    bk = k % 2
                    if k >= 2:
                        eng, cnt = ev_tp[k - 2]
                        T.wait(eng, cnt)
                    T.inc_last(T.transpose(ps_tp[bk][:], src_bf[:, 128 * j:128 * (j + 1)],
                                           ident_bf[:]))
                    V.wait(T)
                    V.tensor_copy(out=dstT[j][:, 128 * ti:128 * (ti + 1)], in_=ps_tp[bk][:])
                    ev_tp[k] = (V, V.n)
                    ktp[0] += 1

            for ts in range(16):
                ln_tmajor(acc[ts][:], xn1t, False)
                V.tensor_copy(out=xbf[0][:], in_=xn1t[:])
                T.wait(V)
                transposes(xbf[0], xn1T, ts)
                ln_tmajor(xn1t[:], xat, True)
                V.tensor_copy(out=xbf[1][:], in_=xat[:])
                T.wait(V)
                transposes(xbf[1], xatT, ts)
                SY.wait_ge(sem_v, V.n)
                SY.dma_start(out=o_x1[128 * ts:128 * (ts + 1), :], in_=acc[ts][:]).then_inc(st, 16)
                nst[0] += 16
            SY.wait_ge(sem_v, V.n)
            for j in range(8):
                SY.dma_start(out=o_xn1[128 * j:128 * (j + 1), :], in_=xn1T[j][:]).then_inc(st, 16)
                SY.dma_start(out=o_xat[128 * j:128 * (j + 1), :], in_=xatT[j][:]).then_inc(st, 16)
                nst[0] += 32
            SY.wait_ge(st, nst[0])
            G_.wait_ge(st, nst[0])
    return nc


def run_M(A_outs, D_outs, inputs):
    yT = np.concatenate(
        [np.concatenate([np.asarray(D_outs[4 * b + j]["o_y"]) for j in range(4)], axis=0)
         for b in range(2)], axis=1)
    in_maps = []
    for c in range(NC):
        cols = slice(TCORE * c, TCORE * (c + 1))
        x_tm = inputs["L0_attn_x"].reshape(16384, D)[cols]
        in_maps.append({
            "yT": f32c(yT[:, cols]),
            "szT": f32c(np.asarray(A_outs[c]["o_sz"])),
            "xnbfT": np.ascontiguousarray(np.asarray(A_outs[c]["o_xnbf"]).astype(BF)),
            "wmoe": f32c(np.asarray(A_outs[c]["o_wmoe"])),
            "x_tm": f32c(x_tm),
            "opw": np.ascontiguousarray(np.asarray(D_outs[c]["o_opwb"]).astype(BF)),
            "w1": np.ascontiguousarray(np.asarray(D_outs[c]["o_w1b"]).astype(BF)),
            "w3": np.ascontiguousarray(np.asarray(D_outs[c]["o_w3b"]).astype(BF)),
            "w2": np.ascontiguousarray(np.asarray(D_outs[c]["o_w2b"]).astype(BF)),
            "lng": f32c(inputs["L1_attn_ln_g"].reshape(1, D)),
            "lnb": f32c(inputs["L1_attn_ln_b"].reshape(1, D)),
        })
    if "M" not in _A_CACHE:
        _A_CACHE["M"] = build_M()
    res = run_bass_kernel_spmd(_A_CACHE["M"], in_maps, core_ids=list(range(NC)))
    return res.results


def build_F():
    nc = bass.Bass("TRN2", target_bir_lowering=False, debug=False)
    TB = 8192
    NKT = TB // 128
    NQC = TB // 512
    xatT = nc.dram_tensor("xatT", [D, TB], BF16, kind="ExternalInput").ap()
    wqT = nc.dram_tensor("wqT", [D, 256], F32, kind="ExternalInput").ap()
    wkT = nc.dram_tensor("wkT", [D, 256], F32, kind="ExternalInput").ap()
    wvT = nc.dram_tensor("wvT", [D, 256], F32, kind="ExternalInput").ap()
    qg = nc.dram_tensor("qg", [64, 1], F32, kind="ExternalInput").ap()
    kg = nc.dram_tensor("kg", [64, 1], F32, kind="ExternalInput").ap()
    o_oh = nc.dram_tensor("o_oh", [256, TB], BF16, kind="ExternalOutput").ap()
    H4 = 4096
    cv_shapes = [[H4, H4], [H4, H4], [D, H4], [D, H4], [H4, D], [D, D]]
    cv_in = [nc.dram_tensor(f"cv_in{i}", shp, F32, kind="ExternalInput").ap()
             for i, shp in enumerate(cv_shapes)]
    cv_out = [nc.dram_tensor(f"cv_out{i}", shp, BF16, kind="ExternalOutput").ap()
              for i, shp in enumerate(cv_shapes)]

    RMS_EPS = 1e-8
    SCALE = 0.125

    with ExitStack() as ctx:
        e = ctx.enter_context
        xat = [e(nc.sbuf_tensor(f"xat{j}", [128, TB], BF16)) for j in range(8)]
        qn_bf = e(nc.sbuf_tensor("qn_bf", [128, TB], BF16))
        kn_bf = e(nc.sbuf_tensor("kn_bf", [128, TB], BF16))
        vo_all = e(nc.sbuf_tensor("vo_all", [128, 65 * NKT], BF16))
        vo = [vo_all[:, 65 * t:65 * (t + 1)] for t in range(NKT)]
        w_sb = [e(nc.sbuf_tensor(f"w_sb{i}", [128, 256], BF16)) for i in range(24)]
        gam = e(nc.sbuf_tensor("gam", [64, 2], F32))
        ones1 = e(nc.sbuf_tensor("ones1", [1, 128], F32))
        onesc = e(nc.sbuf_tensor("onesc", [64, 1], F32))
        sq_t = e(nc.sbuf_tensor("sq_t", [64, 512], F32))
        tmp_t = e(nc.sbuf_tensor("tmp_t", [64, 512], F32))
        row_a = e(nc.sbuf_tensor("row_a", [1, 512], F32))
        row_b = e(nc.sbuf_tensor("row_b", [1, 512], F32))
        cacc = e(nc.sbuf_tensor("cacc", [1, 4], F32))
        negC_bc = e(nc.sbuf_tensor("negC_bc", [128, 1], F32))
        exp_bf = [e(nc.sbuf_tensor(f"exp{b}", [128, 1024], BF16)) for b in range(3)]
        rec_bc = e(nc.sbuf_tensor("rec_bc", [64, 512], F32))
        o_sb = e(nc.sbuf_tensor("o_sb", [64, 512], BF16))
        o_f32 = e(nc.sbuf_tensor("o_f32", [65, 512], F32))
        row_c = e(nc.sbuf_tensor("row_c", [1, 512], F32))

        cstage_f = [e(nc.sbuf_tensor(f"cstagef{b}", [128, 512], F32)) for b in range(2)]
        cstage_b = [e(nc.sbuf_tensor(f"cstageb{b}", [128, 512], BF16)) for b in range(2)]
        ld_cv = [e(nc.semaphore(f"ld_cv{b}")) for b in range(2)]
        st_cv = [e(nc.semaphore(f"st_cv{b}")) for b in range(2)]
        sem_g = e(nc.semaphore("sem_g"))
        ld = e(nc.semaphore("ld"))
        ldg = e(nc.semaphore("ldg"))
        sem_v = e(nc.semaphore("sem_v"))
        sem_s = e(nc.semaphore("sem_s"))
        sem_t = e(nc.semaphore("sem_t"))
        st_o = e(nc.semaphore("st_o"))

        V = EW(nc.vector, sem_v)
        S = EW(nc.scalar, sem_s)
        T = EW(nc.tensor, sem_t, autosync=False)
        SY = nc.sync
        G_ = nc.gpsimd
        nld = [0]
        nlg = [0]
        nst = [0]

        V.memset(ones1[:], 1.0)
        V.memset(onesc[:], 1.0)
        GP = EW(nc.gpsimd, sem_g)
        cv_pairs = []
        for i in range(6):
            si, so = cv_in[i], cv_out[i]
            f = si.shape[1] // 512
            if f > 1:
                si = si.rearrange("a (b c) -> (a b) c", b=f)
                so = so.rearrange("a (b c) -> (a b) c", b=f)
            cv_pairs.append((si, so))
        cp = ConvertPool(G_, GP, cstage_f, cstage_b, ld_cv, st_cv, cv_pairs)
        for j in range(8):
            SY.dma_start(out=xat[j][:], in_=xatT[128 * j:128 * (j + 1), :]).then_inc(ld, 16)
        SY.dma_start(out=gam[0:64, 0:1], in_=qg).then_inc(ld, 16)
        SY.dma_start(out=gam[0:64, 1:2], in_=kg).then_inc(ld, 16)
        nld[0] += 16 * 10
        for j in range(8):
            G_.dma_start(out=w_sb[j][:], in_=wqT[128 * j:128 * (j + 1), :]).then_inc(ldg, 16)
            G_.dma_start(out=w_sb[8 + j][:], in_=wkT[128 * j:128 * (j + 1), :]).then_inc(ldg, 16)
            G_.dma_start(out=w_sb[16 + j][:], in_=wvT[128 * j:128 * (j + 1), :]).then_inc(ldg, 16)
        nlg[0] += 16 * 24
        V.wait_sem(ld, nld[0])
        V.tensor_scalar(out=gam[:, 0:1], in0=gam[:, 0:1], scalar1=SCALE, scalar2=None,
                        op0=ALU.mult)

        with ExitStack() as cq:
            te = cq.enter_context

            for h in range(4):
              with ExitStack() as cqk:
                tq = cqk.enter_context
                ps_q = tq(nc.psum_tensor(f"ps_q{h}", [64, 512], F32))
                ps_n = tq(nc.psum_tensor(f"ps_n{h}", [1, 512], F32))
                ps_r = tq(nc.psum_tensor(f"ps_r{h}", [128, 512], F32))
                ps_v = [tq(nc.psum_tensor(f"ps_v{h}_{b}", [128, 64], F32)) for b in range(2)]
                for which, wbase, gcol, dst, ccol in ((0, 0, 0, qn_bf, 0), (1, 8, 1, kn_bf, 1)):
                  for cidx in range(NQC):
                      c0 = 512 * cidx
                      T.wait_sem(ldg, nlg[0])
                      T.wait(V)
                      for kt in range(8):
                          mm = T.matmul(ps_q[:], w_sb[wbase + kt][:, 64 * h:64 * (h + 1)],
                                        xat[kt][:, c0:c0 + 512],
                                        start=(kt == 0), stop=(kt == 7))
                      T.inc_last(mm)
                      S.wait(T)
                      S.wait(V)
                      S.activation(out=sq_t[:], in_=ps_q[:], func=AF.Square)
                      T.wait(S)
                      T.inc_last(T.matmul(ps_n[:], onesc[:], sq_t[:], start=True, stop=True))
                      V.wait(T)
                      V.tensor_scalar(out=row_a[:], in0=ps_n[:], scalar1=64e-16,
                                      scalar2=None, op0=ALU.max)
                      S.wait(V)
                      S.activation(out=row_b[:], in_=row_a[:], func=AF.Ln, scale=1.0 / 64.0)
                      S.activation(out=row_b[:], in_=row_b[:], func=AF.Exp, scale=-0.5)
                      T.wait(S)
                      T.inc_last(T.matmul(ps_r[0:64, :], ones1[0:1, 0:64], row_b[:],
                                          start=True, stop=True))
                      V.tensor_scalar(out=tmp_t[:], in0=ps_q[:], scalar1=gam[:, gcol:gcol + 1],
                                      scalar2=None, op0=ALU.mult)
                      V.wait(T)
                      V.tensor_mul(out=dst[0:64, c0:c0 + 512], in0=tmp_t[:], in1=ps_r[0:64, :])
                      S.wait(V)
                      S.activation(out=sq_t[:], in_=tmp_t[:], func=AF.Square)
                      T.wait(S)
                      T.inc_last(T.matmul(ps_n[:], onesc[:], sq_t[:], start=True, stop=True))
                      V.wait(T)
                      V.tensor_mul(out=row_a[:], in0=ps_n[:], in1=row_b[:])
                      V.tensor_mul(out=row_a[:], in0=row_a[:], in1=row_b[:])
                      V.reduce_max(out=row_a[0:1, 0:1], in_=row_a[:], axis=AX.X)
                      if cidx == 0:
                          V.tensor_copy(out=cacc[0:1, ccol:ccol + 1], in_=row_a[0:1, 0:1])
                      else:
                          V.tensor_max(out=cacc[0:1, ccol:ccol + 1],
                                       in0=cacc[0:1, ccol:ccol + 1], in1=row_a[0:1, 0:1])
              SY.wait_ge(sem_v, V.n)
              SY.dma_start(out=qn_bf[64:128, :], in_=qn_bf[0:64, :]).then_inc(ld, 16)
              SY.dma_start(out=kn_bf[64:128, :], in_=kn_bf[0:64, :]).then_inc(ld, 16)
              nld[0] += 32
              S.wait(V)
              S.activation(out=cacc[0:1, 0:1], in_=cacc[0:1, 0:1], func=AF.Sqrt)
              S.activation(out=cacc[0:1, 1:2], in_=cacc[0:1, 1:2], func=AF.Sqrt)
              V.wait(S)
              V.tensor_mul(out=cacc[0:1, 2:3], in0=cacc[0:1, 0:1], in1=cacc[0:1, 1:2])
              V.tensor_scalar(out=cacc[0:1, 3:4], in0=cacc[0:1, 2:3], scalar1=-1.0,
                              scalar2=None, op0=ALU.mult)
              T.wait(V)
              T.inc_last(T.matmul(ps_r[:, 0:1], ones1[0:1, :], cacc[0:1, 3:4],
                                  start=True, stop=True))
              V.wait(T)
              V.tensor_copy(out=negC_bc[0:128, 0:1], in_=ps_r[0:128, 0:1])
              for t in range(NKT):
                  b = t % 2
                  if t >= 2:
                      T.wait(V)
                  for kt in range(8):
                      mm = T.matmul(ps_v[b][:], xat[kt][:, 128 * t:128 * (t + 1)],
                                    w_sb[16 + kt][:, 64 * h:64 * (h + 1)],
                                    start=(kt == 0), stop=(kt == 7))
                  T.inc_last(mm)
                  V.wait(T)
                  V.tensor_copy(out=vo[t][:, 0:64], in_=ps_v[b][:])
                  V.memset(vo[t][:, 64:65], 1.0)
                  cp.step(2)
              with ExitStack() as ca:
                    ta = ca.enter_context
                    ps_s = [ta(nc.psum_tensor(f"ps_s{h}_{b}", [128, 1024], F32)) for b in range(3)]
                    ps_o = ta(nc.psum_tensor(f"ps_o{h}", [65, 512], F32))
                    ps_r = ta(nc.psum_tensor(f"ps_r2{h}", [64, 512], F32))
                    T.wait(V)
                    T.wait(S)
                    T.wait_sem(ld, nld[0])
                    s_after_exp = {}
                    t_after_av = {}
                    NP = NKT // 2
                    v_evac = {}
                    for qc in range(NQC):
                        c0 = 512 * qc
                        if qc >= 1:
                            T.wait(V, v_evac[qc - 1])
                        for p in range(NP + 1):
                            if p < NP:
                                pb = p % 3
                                if pb in s_after_exp:
                                    T.wait(S, s_after_exp[pb])
                                kc = 2 * p
                                T.matmul(ps_s[pb][:, 0:512],
                                         kn_bf[0:64, 128 * kc:128 * (kc + 1)],
                                         qn_bf[0:64, c0:c0 + 512],
                                         start=True, stop=True, tile_position=(0, 0))
                                mm = T.matmul(ps_s[pb][:, 512:1024],
                                              kn_bf[64:128, 128 * (kc + 1):128 * (kc + 2)],
                                              qn_bf[64:128, c0:c0 + 512],
                                              start=True, stop=True, tile_position=(64, 0))
                                T.inc_last(mm)
                                if pb in t_after_av:
                                    S.wait(T, t_after_av[pb])
                                S.wait(T)
                                S.activation(out=exp_bf[pb][:], in_=ps_s[pb][:], func=AF.Exp,
                                             bias=negC_bc[:, 0:1])
                                s_after_exp[pb] = S.n
                            if p > 0:
                                pb1 = (p - 1) % 3
                                T.wait(S, s_after_exp[pb1])
                                for u in range(2):
                                    kc = 2 * (p - 1) + u
                                    mm = T.matmul(ps_o[:], vo[kc],
                                                  exp_bf[pb1][:, 512 * u:512 * (u + 1)],
                                                  start=(kc == 0), stop=(kc == NKT - 1))
                                T.inc_last(mm)
                                t_after_av[pb1] = T.n
                        V.wait(T)
                        V.tensor_copy(out=o_f32[:], in_=ps_o[:])
                        v_evac[qc] = V.n
                        V.tensor_scalar(out=row_c[:], in0=o_f32[64:65, :], scalar1=1e-30,
                                        scalar2=None, op0=ALU.add)
                        S.wait(V)
                        S.activation(out=row_c[:], in_=row_c[:], func=AF.Ln)
                        S.activation(out=row_c[:], in_=row_c[:], func=AF.Exp, scale=-1.0)
                        if qc >= 1:
                            T.wait(V, v_recbc)
                        T.wait(S)
                        T.inc_last(T.matmul(ps_r[0:64, :], ones1[0:1, 0:64], row_c[:],
                                            start=True, stop=True))
                        V.wait(T)
                        V.tensor_copy(out=rec_bc[:], in_=ps_r[0:64, :])
                        v_recbc = V.n
                        V.wait_sem(st_o, nst[0])
                        V.tensor_mul(out=o_sb[:], in0=o_f32[0:64, :], in1=rec_bc[:])
                        SY.wait_ge(sem_v, V.n)
                        SY.dma_start(out=o_oh[64 * h:64 * (h + 1), c0:c0 + 512],
                                     in_=o_sb[:]).then_inc(st_o, 16)
                        nst[0] += 16
                        cp.step(4)

        cp.drain()
        cp.finish()
        SY.wait_ge(st_o, nst[0])
        G_.wait_ge(st_o, nst[0])
    return nc


def run_F(M_outs, inputs):
    in_maps = []
    for c in range(NC):
        b = c // 4
        j = c % 4
        xatT = np.concatenate([np.asarray(M_outs[4 * b + i]["o_xat"]) for i in range(4)], axis=1)
        hs = slice(256 * j, 256 * (j + 1))
        in_maps.append({
            "xatT": np.ascontiguousarray(xatT.astype(BF)),
            "wqT": f32c(inputs["L1_attn_Wq"].T[:, hs]),
            "wkT": f32c(inputs["L1_attn_Wk"].T[:, hs]),
            "wvT": f32c(inputs["L1_attn_Wv"].T[:, hs]),
            "qg": f32c(inputs["L1_attn_q_gamma"].reshape(64, 1)),
            "kg": f32c(inputs["L1_attn_k_gamma"].reshape(64, 1)),
            "cv_in0": f32c(inputs["L1_ffn_h1_w"].T),
            "cv_in1": f32c(inputs["L1_ffn_h2_w"].T),
            "cv_in2": f32c(inputs["L1_ffn_w1_w"].T),
            "cv_in3": f32c(inputs["L1_ffn_w2_w"].T),
            "cv_in4": f32c(inputs["L1_ffn_w3_w"].T),
            "cv_in5": f32c(inputs["L1_attn_Wo"].T),
        })
    if "F" not in _A_CACHE:
        _A_CACHE["F"] = build_F()
    res = run_bass_kernel_spmd(_A_CACHE["F"], in_maps, core_ids=list(range(NC)))
    return res.results


def build_G():
    nc = bass.Bass("TRN2", target_bir_lowering=False, debug=False)
    H4 = 4096
    xn1T = nc.dram_tensor("xn1T", [D, TCORE], BF16, kind="ExternalInput").ap()
    ohT = nc.dram_tensor("ohT", [D, TCORE], BF16, kind="ExternalInput").ap()
    x1 = nc.dram_tensor("x1", [TCORE, D], F32, kind="ExternalInput").ap()
    woT = nc.dram_tensor("woT", [D, D], BF16, kind="ExternalInput").ap()
    bo = nc.dram_tensor("bo", [1, D], F32, kind="ExternalInput").ap()
    w1T = nc.dram_tensor("w1T", [D, H4], BF16, kind="ExternalInput").ap()
    w2T = nc.dram_tensor("w2T", [D, H4], BF16, kind="ExternalInput").ap()
    h1T = nc.dram_tensor("h1T", [H4, H4], BF16, kind="ExternalInput").ap()
    h2T = nc.dram_tensor("h2T", [H4, H4], BF16, kind="ExternalInput").ap()
    w3T = nc.dram_tensor("w3T", [H4, D], BF16, kind="ExternalInput").ap()
    s1 = nc.dram_tensor("s1", [H4, 1], F32, kind="ExternalInput").ap()
    s2 = nc.dram_tensor("s2", [H4, 1], F32, kind="ExternalInput").ap()
    sh1 = nc.dram_tensor("sh1", [H4, 1], F32, kind="ExternalInput").ap()
    sh2 = nc.dram_tensor("sh2", [H4, 1], F32, kind="ExternalInput").ap()
    s3r = nc.dram_tensor("s3r", [1, D], F32, kind="ExternalInput").ap()
    o_x2 = nc.dram_tensor("o_x2", [TCORE, D], F32, kind="ExternalOutput").ap()

    with ExitStack() as ctx:
        e = ctx.enter_context
        acc = [e(nc.sbuf_tensor(f"acc{t}", [128, D], F32)) for t in range(16)]
        xn1 = [e(nc.sbuf_tensor(f"xn1_{j}", [128, TCORE], BF16)) for j in range(8)]
        bo_bc = e(nc.sbuf_tensor("bo_bc", [128, D], F32))
        s3_bc = e(nc.sbuf_tensor("s3_bc", [128, D], F32))
        bs_row = e(nc.sbuf_tensor("bs_row", [1, 2 * D], F32))
        ones1 = e(nc.sbuf_tensor("ones1", [1, 128], F32))
        sv_all = e(nc.sbuf_tensor("sv_all", [128, 128], F32))
        sv_off = {"s1": 0, "s2": 32, "sh1": 64, "sh2": 96}
        t1s = e(nc.sbuf_tensor("t1s", [128, 512], F32))
        t2s = e(nc.sbuf_tensor("t2s", [128, 512], F32))

        ld = e(nc.semaphore("ld"))
        ldg = e(nc.semaphore("ldg"))
        ldh = [e(nc.semaphore(f"ldh{p}")) for p in range(2)]
        ldw3 = [e(nc.semaphore(f"ldw3{p}")) for p in range(2)]
        sem_v = e(nc.semaphore("sem_v"))
        sem_s = e(nc.semaphore("sem_s"))
        sem_t = e(nc.semaphore("sem_t"))
        st = e(nc.semaphore("st"))

        V = EW(nc.vector, sem_v)
        S = EW(nc.scalar, sem_s)
        T = EW(nc.tensor, sem_t, autosync=False)
        SY = nc.sync
        G_ = nc.gpsimd
        nld = [0]
        nlg = [0]
        nldh = [0, 0]
        nldw3 = [0, 0]
        nst = [0]

        V.memset(ones1[:], 1.0)
        SY.dma_start(out=bs_row[0:1, 0:D], in_=bo).then_inc(ld, 16)
        SY.dma_start(out=bs_row[0:1, D:2 * D], in_=s3r).then_inc(ld, 16)
        for t in range(16):
            SY.dma_start(out=acc[t][:], in_=x1[128 * t:128 * (t + 1), :]).then_inc(ld, 16)
        for j in range(8):
            SY.dma_start(out=xn1[j][:], in_=xn1T[128 * j:128 * (j + 1), :]).then_inc(ld, 16)
        for nm, t in (("s1", s1), ("s2", s2), ("sh1", sh1), ("sh2", sh2)):
            o = sv_off[nm]
            for m in range(32):
                SY.dma_start(out=sv_all[:, o + m:o + m + 1],
                             in_=t[128 * m:128 * (m + 1), :]).then_inc(ld, 16)
        nld[0] += 16 * (2 + 16 + 8 + 128)
        LD_BASE = nld[0]

        with ExitStack() as c0:
            t0 = c0.enter_context
            ps_b = [t0(nc.psum_tensor(f"ps_bc{b}", [128, 512], F32)) for b in range(2)]
            T.wait_sem(ld, LD_BASE)
            T.wait(V)
            for half in range(2):
                T.inc_last(T.matmul(ps_b[half][:], ones1[0:1, :],
                                    bs_row[0:1, 512 * half:512 * (half + 1)],
                                    start=True, stop=True))
            V.wait(T)
            V.tensor_copy(out=bo_bc[:, 0:512], in_=ps_b[0][:])
            V.tensor_copy(out=bo_bc[:, 512:1024], in_=ps_b[1][:])
            for half in range(2):
                T.wait(V)
                T.inc_last(T.matmul(ps_b[half][:], ones1[0:1, :],
                                    bs_row[0:1, D + 512 * half:D + 512 * (half + 1)],
                                    start=True, stop=True))
            V.wait(T)
            V.tensor_copy(out=s3_bc[:, 0:512], in_=ps_b[0][:])
            V.tensor_copy(out=s3_bc[:, 512:1024], in_=ps_b[1][:])

        phase_barrier([nc.vector, nc.scalar, nc.tensor, nc.sync, nc.gpsimd],
                      [V, S, T], [(ld, nld[0])])
        with ExitStack() as c1:
            t1 = c1.enter_context
            oh_sb = [t1(nc.sbuf_tensor(f"oh{j}", [128, TCORE], BF16)) for j in range(8)]
            wo_bf = [t1(nc.sbuf_tensor(f"wo{j}", [128, D], BF16)) for j in range(8)]
            ps_m = [t1(nc.psum_tensor(f"ps_m{i}", [128, 512], F32)) for i in range(4)]
            for j in range(8):
                SY.dma_start(out=oh_sb[j][:], in_=ohT[128 * j:128 * (j + 1), :]).then_inc(ld, 16)
                SY.dma_start(out=wo_bf[j][:], in_=woT[128 * j:128 * (j + 1), :]).then_inc(ldg, 16)
            nld[0] += 16 * 8
            nlg[0] += 16 * 8
            T.wait_sem(ld, nld[0])
            T.wait_sem(ldg, nlg[0])
            ev_ps = {}
            for ts in range(16):
                for nh in range(2):
                    gi = 2 * ts + nh
                    pi = gi % 4
                    if gi >= 4:
                        eng, cnt = ev_ps[pi]
                        T.wait(eng, cnt)
                    for kt in range(8):
                        mm = T.matmul(ps_m[pi][:], oh_sb[kt][:, 128 * ts:128 * (ts + 1)],
                                      wo_bf[kt][:, 512 * nh:512 * (nh + 1)],
                                      start=(kt == 0), stop=(kt == 7))
                    T.inc_last(mm)
                    V.wait(T)
                    V.tensor_add(out=acc[ts][:, 512 * nh:512 * (nh + 1)],
                                 in0=acc[ts][:, 512 * nh:512 * (nh + 1)], in1=ps_m[pi][:])
                    ev_ps[pi] = (V, V.n)
                V.tensor_add(out=acc[ts][:], in0=acc[ts][:], in1=bo_bc[:])

        phase_barrier([nc.vector, nc.scalar, nc.tensor, nc.sync, nc.gpsimd],
                      [V, S, T], [(ld, nld[0]), (ldg, nlg[0])])
        with ExitStack() as c2:
            t2 = c2.enter_context
            h_bf = [t2(nc.sbuf_tensor(f"hbf{k}", [128, 512], BF16)) for k in range(32)]
            hh_bf = [t2(nc.sbuf_tensor(f"hh{k}", [128, 512], BF16)) for k in range(32)]
            wsl = [t2(nc.sbuf_tensor("wsl0", [128, 2 * 8 * 128], BF16))]
            hsl = [t2(nc.sbuf_tensor(f"hsl{p}", [128, 32 * 128], BF16)) for p in range(2)]
            w3b = [t2(nc.sbuf_tensor(f"w3b{p}", [128, D], BF16)) for p in range(2)]
            ps_p = [t2(nc.psum_tensor(f"ps_p{i}", [128, 512], F32)) for i in range(4)]
            ps_q5 = [t2(nc.psum_tensor(f"ps_q5{i}", [128, 512], F32)) for i in range(4)]

            def load_w12(hm, buf):
                SY.dma_start(out=wsl[buf][:, 0:1024].rearrange("p (k c) -> p k c", k=8),
                             in_=w1T[:, 128 * hm:128 * (hm + 1)].rearrange(
                                 "(k p) c -> p k c", p=128)).then_inc(ldh[buf], 16)
                SY.dma_start(out=wsl[buf][:, 1024:2048].rearrange("p (k c) -> p k c", k=8),
                             in_=w2T[:, 128 * hm:128 * (hm + 1)].rearrange(
                                 "(k p) c -> p k c", p=128)).then_inc(ldh[buf], 16)
                nldh[buf] += 32

            def load_hsl(mat, om, buf):
                for a in range(4):
                    SY.dma_start(
                        out=hsl[buf][:, 1024 * a:1024 * (a + 1)].rearrange(
                            "p (k c) -> p k c", k=8),
                        in_=mat[1024 * a:1024 * (a + 1),
                                128 * om:128 * (om + 1)].rearrange("(k p) c -> p k c", p=128)
                    ).then_inc(ldh[buf], 16)
                nldh[buf] += 16 * 4

            sidx = [0]
            t_after_s = {}
            ev_pp = {}
            for tq in range(4):
                tb = 512 * tq
                V.wait(T)
                for hm in range(32):
                    si = sidx[0]
                    sb = 0
                    if si >= 1:
                        SY.wait_ge(sem_t, t_after_s[si - 1])
                    load_w12(hm, sb)
                    wrdy = nldh[sb]
                    sidx[0] += 1
                    T.wait_sem(ldh[sb], wrdy)
                    pp = hm % 2
                    pa, pb = ps_p[2 * pp], ps_p[2 * pp + 1]
                    if ("g2", pp) in ev_pp:
                        eng, cnt = ev_pp[("g2", pp)]
                        T.wait(eng, cnt)
                    for kt in range(8):
                        T.matmul(pa[:], wsl[sb][:, 128 * kt:128 * (kt + 1)],
                                 xn1[kt][:, tb:tb + 512], start=(kt == 0), stop=(kt == 7))
                    for kt in range(8):
                        mm = T.matmul(pb[:], wsl[sb][:, 1024 + 128 * kt:1024 + 128 * (kt + 1)],
                                      xn1[kt][:, tb:tb + 512], start=(kt == 0), stop=(kt == 7))
                    T.inc_last(mm)
                    t_after_s[si] = T.n
                    S.wait(T)
                    S.wait(V)
                    S.activation(out=t1s[:], in_=pa[:], func=AF.Silu,
                                 scale=sv_all[:, sv_off["s1"] + hm:sv_off["s1"] + hm + 1])
                    V.wait(S)
                    V.tensor_scalar(out=t2s[:], in0=pb[:],
                                    scalar1=sv_all[:, sv_off["s2"] + hm:sv_off["s2"] + hm + 1],
                                    scalar2=None, op0=ALU.mult)
                    V.tensor_mul(out=h_bf[hm][:], in0=t1s[:], in1=t2s[:])
                    ev_pp[("g2", pp)] = (V, V.n)
                for layer, mat, svec, src_t, dst_t in (
                        (0, h1T, "sh1", h_bf, hh_bf), (1, h2T, "sh2", hh_bf, h_bf)):
                    hrdy = {}
                    t_after_om = {}
                    SY.wait_ge(sem_t, T.n)
                    load_hsl(mat, 0, 0)
                    hrdy[0] = (0, nldh[0])
                    for om in range(32):
                        sb = om % 2
                        if om + 1 < 32:
                            nb = (om + 1) % 2
                            if om >= 1:
                                SY.wait_ge(sem_t, t_after_om[om - 1])
                            load_hsl(mat, om + 1, nb)
                            hrdy[om + 1] = (nb, nldh[nb])
                        rb, rv = hrdy[om]
                        T.wait_sem(ldh[rb], rv)
                        pp = om % 2
                        pa = ps_p[2 * pp]
                        if ("g3", pp) in ev_pp:
                            eng, cnt = ev_pp[("g3", pp)]
                            T.wait(eng, cnt)
                        if om == 0:
                            T.wait(S)
                            T.wait(V)
                        for kt in range(32):
                            mm = T.matmul(pa[:], hsl[sb][:, 128 * kt:128 * (kt + 1)],
                                          src_t[kt][:], start=(kt == 0), stop=(kt == 31))
                        T.inc_last(mm)
                        t_after_om[om] = T.n
                        S.wait(T)
                        S.activation(out=dst_t[om][:], in_=pa[:], func=AF.Gelu,
                                     scale=sv_all[:, sv_off[svec] + om:sv_off[svec] + om + 1])
                        ev_pp[("g3", pp)] = (S, S.n)
                psall = ps_p + ps_q5
                T.wait(S)
                T.wait(V)
                ldw3_r = {}
                t_after_w3 = {}
                nb0 = (2 * tq) % 2
                SY.wait_ge(sem_t, T.n)
                SY.dma_start(out=w3b[0][:], in_=w3T[0:128, :]).then_inc(ldw3[0], 16)
                nldw3[0] += 16
                ldw3_r[0] = nldw3[0]
                for kt in range(32):
                    kb = kt % 2
                    if kt + 1 < 32:
                        nb = (kt + 1) % 2
                        if kt >= 1:
                            SY.wait_ge(sem_t, t_after_w3[kt - 1])
                        SY.dma_start(out=w3b[nb][:],
                                     in_=w3T[128 * (kt + 1):128 * (kt + 2), :]).then_inc(ldw3[nb], 16)
                        nldw3[nb] += 16
                        ldw3_r[kt + 1] = nldw3[nb]
                    T.wait_sem(ldw3[kb], ldw3_r[kt])
                    for gi in range(8):
                        tsl, nh = divmod(gi, 2)
                        mm = T.matmul(psall[gi][:], h_bf[kt][:, 128 * tsl:128 * (tsl + 1)],
                                      w3b[kb][:, 512 * nh:512 * (nh + 1)],
                                      start=(kt == 0), stop=(kt == 31))
                    T.inc_last(mm)
                    t_after_w3[kt] = T.n
                V.wait(T)
                for gi in range(8):
                    tsl, nh = divmod(gi, 2)
                    ts = 4 * tq + tsl
                    V.tensor_mul(out=t1s[:], in0=psall[gi][:],
                                 in1=s3_bc[:, 512 * nh:512 * (nh + 1)])
                    V.tensor_add(out=acc[ts][:, 512 * nh:512 * (nh + 1)],
                                 in0=acc[ts][:, 512 * nh:512 * (nh + 1)], in1=t1s[:])
                for key in (("g2", 0), ("g2", 1), ("g3", 0), ("g3", 1)):
                    ev_pp[key] = (V, V.n)
                for tsl in range(4):
                    ts = 4 * tq + tsl
                    SY.wait_ge(sem_v, V.n)
                    SY.dma_start(out=o_x2[128 * ts:128 * (ts + 1), :],
                                 in_=acc[ts][:]).then_inc(st, 16)
                    nst[0] += 16
            SY.wait_ge(st, nst[0])
            G_.wait_ge(st, nst[0])
    return nc


def run_G(M_outs, F_outs, inputs):
    in_maps = []
    for c in range(NC):
        b = c // 4
        ohT_b = np.concatenate([np.asarray(F_outs[4 * b + i]["o_oh"]) for i in range(4)], axis=0)
        cols = slice(TCORE * (c % 4), TCORE * (c % 4 + 1))
        in_maps.append({
            "xn1T": np.ascontiguousarray(np.asarray(M_outs[c]["o_xn1"]).astype(BF)),
            "ohT": np.ascontiguousarray(ohT_b[:, cols].astype(BF)),
            "x1": f32c(np.asarray(M_outs[c]["o_x1"])),
            "woT": np.ascontiguousarray(np.asarray(F_outs[c]["cv_out5"]).astype(BF)),
            "bo": f32c(inputs["L1_attn_bo"].reshape(1, D)),
            "w1T": np.ascontiguousarray(np.asarray(F_outs[c]["cv_out2"]).astype(BF)),
            "w2T": np.ascontiguousarray(np.asarray(F_outs[c]["cv_out3"]).astype(BF)),
            "h1T": np.ascontiguousarray(np.asarray(F_outs[c]["cv_out0"]).astype(BF)),
            "h2T": np.ascontiguousarray(np.asarray(F_outs[c]["cv_out1"]).astype(BF)),
            "w3T": np.ascontiguousarray(np.asarray(F_outs[c]["cv_out4"]).astype(BF)),
            "s1": f32c(inputs["L1_ffn_w1_s"].reshape(-1, 1)),
            "s2": f32c(inputs["L1_ffn_w2_s"].reshape(-1, 1)),
            "sh1": f32c(inputs["L1_ffn_h1_s"].reshape(-1, 1)),
            "sh2": f32c(inputs["L1_ffn_h2_s"].reshape(-1, 1)),
            "s3r": f32c(inputs["L1_ffn_w3_s"].reshape(1, D)),
        })
    if "G" not in _A_CACHE:
        _A_CACHE["G"] = build_G()
    res = run_bass_kernel_spmd(_A_CACHE["G"], in_maps, core_ids=list(range(NC)))
    return res.results


    return nc


_A_CACHE = {}


def run_A(inputs):
    x = inputs["L0_attn_x"].reshape(16384, D)
    in_maps = []
    for c in range(NC):
        t0 = TCORE * c
        if c % 4 == 0:
            xh = np.concatenate([np.zeros((HALO, D), np.float32),
                                 x[t0:t0 + TCORE]], axis=0)
        else:
            xh = x[t0 - HALO:t0 + TCORE]
        in_maps.append({
            "xi": f32c(xh),
            "ipw": f32c(inputs["L0_attn_in_proj_w"]),
            "convw": f32c(inputs["L0_attn_conv_w"]),
            "convb": f32c(inputs["L0_attn_conv_b"].reshape(DI, 1)),
            "xpw": f32c(inputs["L0_attn_x_proj_w"]),
            "dtw": f32c(inputs["L0_attn_dt_proj_w"]),
            "dtb": f32c(inputs["L0_attn_dt_proj_b"].reshape(DI, 1)),
            "wgate": f32c(inputs["L0_ffn_w_gate"]),
        })
    if "A" not in _A_CACHE:
        _A_CACHE["A"] = build_A()
    res = run_bass_kernel_spmd(_A_CACHE["A"], in_maps, core_ids=list(range(NC)))
    return res.results


def kernel(**inputs):
    A = run_A(inputs)
    Dr = run_D(A, inputs)
    M = run_M(A, Dr, inputs)
    F = run_F(M, inputs)
    G = run_G(M, F, inputs)
    x2 = np.concatenate([np.asarray(G[c]["o_x2"]) for c in range(NC)], axis=0)
    return np.ascontiguousarray(x2.reshape(2, 8192, D), dtype=np.float32)


# version 08 (final)
# speedup vs baseline: 1.1969x; 1.0029x over previous
import numpy as np
import ml_dtypes
from contextlib import ExitStack
import concourse.bass as bass
import concourse.mybir as mybir
from concourse.bass_utils import run_bass_kernel_spmd

F32 = mybir.dt.float32
BF16 = mybir.dt.bfloat16
AF = mybir.ActivationFunctionType
ALU = mybir.AluOpType
AX = mybir.AxisListType
BF = ml_dtypes.bfloat16

NC = 8
D = 1024
DI = 2048
DS = 16
TCORE = 2048
HALO = 128
TA = TCORE + HALO
LN_EPS = 1e-5


def f32c(a):
    return np.ascontiguousarray(a, dtype=np.float32)


def chunks(total, step, base=0):
    out = []
    o = base
    while o < base + total:
        n = min(step, base + total - o)
        out.append((o, n))
        o += n
    return out



class EW:

    def __init__(self, eng, sem, autosync=True):
        self.e = eng
        self.sem = sem
        self.n = 0
        self.autosync = autosync

    def wait(self, other, val=None):
        self.e.wait_ge(other.sem, other.n if val is None else val)

    def wait_sem(self, sem, val):
        self.e.wait_ge(sem, val)

    def raw(self):
        return self.e

    def __getattr__(self, name):
        fn = getattr(self.e, name)
        if not self.autosync or name in ("wait_ge", "nop", "sem_inc"):
            return fn

        def em(*a, **k):
            if self.n > 0:
                self.e.wait_ge(self.sem, self.n)
            inst = fn(*a, **k)
            inst.then_inc(self.sem, 1)
            self.n += 1
            return inst
        return em

    def inc_last(self, inst):
        inst.then_inc(self.sem, 1)
        self.n += 1
        return inst


class ConvertPool:

    def __init__(self, SY, GP, stage_f, stage_b, ld_sems, st_sems, pairs):
        self.SY, self.GP = SY, GP
        self.sf, self.sb = stage_f, stage_b
        self.ld, self.st = ld_sems, st_sems
        self.nld = [0, 0]
        self.nst = [0, 0]
        self.g_after = {}
        C = stage_f[0].shape[1]
        self.work = []
        for s, d in pairs:
            R, Cs = s.shape
            assert Cs == C and R % 128 == 0, (s.shape, C)
            for r in range(R // 128):
                self.work.append((s[128 * r:128 * (r + 1), :], d[128 * r:128 * (r + 1), :]))
        self.i = 0

    def step(self, n=1):
        for _ in range(n):
            if self.i >= len(self.work):
                return
            sap, dap = self.work[self.i]
            b = self.i % 2
            Q, GP = self.SY, self.GP
            if b in self.g_after:
                Q.wait_ge(GP.sem, self.g_after[b])
            Q.dma_start(out=self.sf[b][:], in_=sap).then_inc(self.ld[b], 16)
            self.nld[b] += 16
            GP.wait_sem(self.ld[b], self.nld[b])
            if self.nst[b]:
                GP.wait_sem(self.st[b], self.nst[b])
            GP.tensor_copy(out=self.sb[b][:], in_=self.sf[b][:])
            self.g_after[b] = GP.n
            Q.wait_ge(GP.sem, GP.n)
            Q.dma_start(out=dap, in_=self.sb[b][:]).then_inc(self.st[b], 16)
            self.nst[b] += 16
            self.i += 1

    def drain(self):
        self.step(len(self.work) - self.i)

    def finish(self):
        self.SY.wait_ge(self.st[0], self.nst[0])
        self.SY.wait_ge(self.st[1], self.nst[1])


def phase_barrier(engines, wrappers, dma_sems):
    for E in engines:
        for w in wrappers:
            E.wait_ge(w.sem, w.n)
        for sem, tot in dma_sems:
            E.wait_ge(sem, tot)


def build_A():
    nc = bass.Bass("TRN2", target_bir_lowering=False, debug=False)
    xi = nc.dram_tensor("xi", [TA, D], F32, kind="ExternalInput").ap()
    ipw = nc.dram_tensor("ipw", [D, 2 * DI], F32, kind="ExternalInput").ap()
    convw = nc.dram_tensor("convw", [DI, 4], F32, kind="ExternalInput").ap()
    convb = nc.dram_tensor("convb", [DI, 1], F32, kind="ExternalInput").ap()
    xpw = nc.dram_tensor("xpw", [DI, 96], F32, kind="ExternalInput").ap()
    dtw = nc.dram_tensor("dtw", [64, DI], F32, kind="ExternalInput").ap()
    dtb = nc.dram_tensor("dtb", [DI, 1], F32, kind="ExternalInput").ap()
    wgate = nc.dram_tensor("wgate", [D, 8], F32, kind="ExternalInput").ap()

    o_xnbf = nc.dram_tensor("o_xnbf", [D, TCORE], BF16, kind="ExternalOutput").ap()
    o_xbs = nc.dram_tensor("o_xbs", [DI, TCORE], F32, kind="ExternalOutput").ap()
    o_sz = nc.dram_tensor("o_sz", [DI, TCORE], F32, kind="ExternalOutput").ap()
    o_dt = nc.dram_tensor("o_dt", [DI, TCORE], F32, kind="ExternalOutput").ap()
    o_pbc = nc.dram_tensor("o_pbc", [32, TCORE], F32, kind="ExternalOutput").ap()
    o_wmoe = nc.dram_tensor("o_wmoe", [TCORE, 8], F32, kind="ExternalOutput").ap()

    NT = TA // 128
    CH5 = chunks(TA, 512)
    CT4 = chunks(TCORE, 512)

    with ExitStack() as ctx:
        e = ctx.enter_context
        ident_bf_d = nc.inline_tensor(np.eye(128, dtype=BF), name="ident_bf")
        ident_f_d = nc.inline_tensor(np.eye(128, dtype=np.float32), name="ident_f")

        xnT_bf = [e(nc.sbuf_tensor(f"xnT{j}", [128, TA], BF16)) for j in range(8)]
        ident_bf = e(nc.sbuf_tensor("identbf", [128, 128], BF16))
        ident_f = e(nc.sbuf_tensor("identf", [128, 128], F32))
        wgate_sb = [e(nc.sbuf_tensor(f"wg{j}", [128, 8], F32)) for j in range(8)]
        convw_sb = [e(nc.sbuf_tensor(f"cw{m}", [128, 4], F32)) for m in range(16)]
        convb_sb = [e(nc.sbuf_tensor(f"cb{m}", [128, 1], F32)) for m in range(16)]
        dtb_sb = [e(nc.sbuf_tensor(f"dbb{m}", [128, 1], F32)) for m in range(16)]
        xpw_bf = [e(nc.sbuf_tensor(f"xpw{m}", [128, 96], BF16)) for m in range(16)]
        dtw_bf = e(nc.sbuf_tensor("dtw_bf", [64, DI], BF16))
        ipwm = [e(nc.sbuf_tensor(f"ipwm{b}", [128, 8 * 128], BF16)) for b in range(2)]
        xb_raw = e(nc.sbuf_tensor("xb_raw", [128, TA], F32))
        conv_acc = e(nc.sbuf_tensor("conv_acc", [128, TCORE], F32))
        xbs_f = e(nc.sbuf_tensor("xbs_f", [128, TCORE], F32))
        xbs_b = e(nc.sbuf_tensor("xbs_b", [128, TCORE], BF16))
        sz_t = e(nc.sbuf_tensor("sz_t", [128, TCORE], F32))
        projT_bf = e(nc.sbuf_tensor("projT_bf", [64, TCORE], BF16))
        pbc_sb = e(nc.sbuf_tensor("pbc_sb", [32, TCORE], F32))
        dtv = e(nc.sbuf_tensor("dtv", [128, TCORE], F32))
        ab = e(nc.sbuf_tensor("ab", [128, TCORE], F32))
        enl = e(nc.sbuf_tensor("enl", [128, TCORE], F32))
        rl = e(nc.sbuf_tensor("rl", [128, TCORE], F32))
        dt_t = e(nc.sbuf_tensor("dt_t", [128, TCORE], F32))
        lg_t = e(nc.sbuf_tensor("lg_t", [128, 8], F32))
        mx8 = e(nc.sbuf_tensor("mx8", [128, 8], F32))
        sc1 = e(nc.sbuf_tensor("sc1", [128, 4], F32))
        negm1 = e(nc.sbuf_tensor("negm1", [128, 1], F32))
        el = e(nc.sbuf_tensor("el", [128, 8], F32))
        msk = e(nc.sbuf_tensor("msk", [128, 8], F32))
        wtm = e(nc.sbuf_tensor("wtm", [128, 8], F32))
        epsb = e(nc.sbuf_tensor("epsb", [128, 1], F32))

        ld = e(nc.semaphore("s_ld"))
        ldg = e(nc.semaphore("s_ldg"))
        ldx = [e(nc.semaphore(f"s_ldx{b}")) for b in range(2)]
        ldw = [e(nc.semaphore(f"s_ldw{b}")) for b in range(2)]
        sem_v = e(nc.semaphore("sem_v"))
        sem_s = e(nc.semaphore("sem_s"))
        sem_t = e(nc.semaphore("sem_t"))
        st = e(nc.semaphore("st"))
        st_xbs = e(nc.semaphore("st_xbs"))
        st_sz = e(nc.semaphore("st_sz"))
        st_dt = e(nc.semaphore("st_dt"))
        st_w = e(nc.semaphore("st_w"))

        V = EW(nc.vector, sem_v)
        S = EW(nc.scalar, sem_s)
        T = EW(nc.tensor, sem_t, autosync=False)
        SY = nc.sync
        G_ = nc.gpsimd
        nld = [0]
        nlg = [0]
        nst = [0]
        nldx = [0, 0]
        nldw = [0, 0]
        nst_b = dict(xbs=0, sz=0, dt=0, w=0)

        V.memset(epsb[:], LN_EPS)

        class _NoCp:
            def step(self, n=1):
                pass

            def drain(self):
                pass

            def finish(self):
                pass
        cp = _NoCp()

        SY.dma_start(out=ident_bf[:], in_=ident_bf_d.ap()).then_inc(ld, 16)
        SY.dma_start(out=ident_f[:], in_=ident_f_d.ap()).then_inc(ld, 16)
        for j in range(8):
            SY.dma_start(out=wgate_sb[j][:], in_=wgate[128 * j:128 * (j + 1), :]).then_inc(ld, 16)
        for m in range(16):
            SY.dma_start(out=convw_sb[m][:], in_=convw[128 * m:128 * (m + 1), :]).then_inc(ld, 16)
            SY.dma_start(out=convb_sb[m][:], in_=convb[128 * m:128 * (m + 1), :]).then_inc(ld, 16)
            SY.dma_start(out=dtb_sb[m][:], in_=dtb[128 * m:128 * (m + 1), :]).then_inc(ld, 16)
        nld[0] += 16 * 58
        LD_MISC = nld[0]
        for m in range(16):
            G_.dma_start(out=xpw_bf[m][:], in_=xpw[128 * m:128 * (m + 1), :]).then_inc(ldg, 16)
        G_.dma_start(out=dtw_bf[:], in_=dtw).then_inc(ldg, 16)
        nlg[0] += 16 * 17
        LD_W2 = nlg[0]

        with ExitStack() as tctx:
            te = tctx.enter_context
            xtile = [te(nc.sbuf_tensor(f"xtile{b}", [128, D], F32)) for b in range(2)]
            xnbf_t = [te(nc.sbuf_tensor(f"xnbf{b}", [128, D], BF16)) for b in range(2)]
            sqt = te(nc.sbuf_tensor("sqt", [128, D], F32))
            muv = te(nc.sbuf_tensor("muv", [128, 1], F32))
            ssv = te(nc.sbuf_tensor("ssv", [128, 1], F32))
            rstd = te(nc.sbuf_tensor("rstd", [128, 1], F32))
            tf32 = [te(nc.sbuf_tensor(f"tf32{b}", [128, 128], F32)) for b in range(2)]
            ps_tp = [te(nc.psum_tensor(f"ps_tp{b}", [128, 128], BF16)) for b in range(2)]
            ps_tf = [te(nc.psum_tensor(f"ps_tf{b}", [128, 128], F32)) for b in range(2)]
            ps_l = te(nc.psum_tensor("ps_l", [128, 8], F32))

            t_done_after = []
            ev_after_tp = {}
            glob_tp = [0]
            for i in range(NT):
                b = i % 2
                if i >= 2:
                    SY.wait_ge(sem_t, t_done_after[i - 2])
                SY.dma_start(out=xtile[b][:], in_=xi[128 * i:128 * (i + 1), :]).then_inc(ldx[b], 16)
                nldx[b] += 16
                V.wait_sem(ldx[b], nldx[b])
                V.reduce_sum(out=muv[:], in_=xtile[b][:], axis=AX.X)
                V.tensor_scalar(out=muv[:], in0=muv[:], scalar1=1.0 / D, scalar2=None, op0=ALU.mult)
                V.tensor_scalar(out=xtile[b][:], in0=xtile[b][:], scalar1=muv[:, 0:1],
                                scalar2=None, op0=ALU.subtract)
                S.wait(V)
                S.activation(out=sqt[:], in_=xtile[b][:], func=AF.Square, accum_out=ssv[:])
                S.activation(out=rstd[:], in_=ssv[:], func=AF.Ln, scale=1.0 / D, bias=epsb[:, 0:1])
                S.activation(out=rstd[:], in_=rstd[:], func=AF.Exp, scale=-0.5)
                V.wait(S)
                V.tensor_scalar(out=xtile[b][:], in0=xtile[b][:], scalar1=rstd[:, 0:1],
                                scalar2=None, op0=ALU.mult)
                V.tensor_copy(out=xnbf_t[b][:], in_=xtile[b][:])
                T.wait(V)
                if i == 0:
                    T.wait_sem(ld, LD_MISC)
                for j in range(8):
                    k = glob_tp[0]
                    if k >= 2:
                        V_after = ev_after_tp[k - 2]
                        T.wait(V, V_after)
                    T.inc_last(T.transpose(ps_tp[j % 2][:], xnbf_t[b][:, 128 * j:128 * (j + 1)],
                                           ident_bf[:]))
                    V.wait(T)
                    V.tensor_copy(out=xnT_bf[j][:, 128 * i:128 * (i + 1)], in_=ps_tp[j % 2][:])
                    ev_after_tp[k] = V.n
                    glob_tp[0] += 1
                if i >= 1:
                    for j in range(8):
                        k = glob_tp[0]
                        T.wait(V, ev_after_tp[k - 2])
                        T.inc_last(T.transpose(ps_tf[j % 2][:], xtile[b][:, 128 * j:128 * (j + 1)],
                                               ident_f[:]))
                        V.wait(T)
                        V.tensor_copy(out=tf32[j % 2][:], in_=ps_tf[j % 2][:])
                        ev_after_tp[k] = V.n
                        glob_tp[0] += 1
                        T.wait(V)
                        mm = T.matmul(ps_l[:], tf32[j % 2][:], wgate_sb[j][:],
                                      start=(j == 0), stop=(j == 7))
                        if j == 7:
                            T.inc_last(mm)
                    V.wait(T)
                    V.tensor_copy(out=lg_t[:], in_=ps_l[:])
                    V.max(out=mx8[:], in_=lg_t[:])
                    V.tensor_tensor(out=sc1[:, 0:1], in0=mx8[:, 1:2], in1=mx8[:, 0:1],
                                    op=ALU.subtract)
                    V.tensor_scalar(out=negm1[:], in0=mx8[:, 0:1], scalar1=-1.0,
                                    scalar2=None, op0=ALU.mult)
                    S.wait(V)
                    S.activation(out=sc1[:, 1:2], in_=sc1[:, 0:1], func=AF.Exp)
                    S.activation(out=el[:], in_=lg_t[:], func=AF.Exp, bias=negm1[:, 0:1])
                    V.wait(S)
                    V.tensor_scalar(out=sc1[:, 2:3], in0=sc1[:, 1:2], scalar1=1.0,
                                    scalar2=None, op0=ALU.add)
                    V.reciprocal(out=sc1[:, 3:4], in_=sc1[:, 2:3])
                    V.tensor_scalar(out=msk[:], in0=lg_t[:], scalar1=mx8[:, 1:2],
                                    scalar2=None, op0=ALU.is_ge)
                    V.tensor_scalar(out=wtm[:], in0=el[:], scalar1=sc1[:, 3:4],
                                    scalar2=None, op0=ALU.mult)
                    V.tensor_tensor(out=wtm[:], in0=wtm[:], in1=msk[:], op=ALU.mult)
                    SY.wait_ge(sem_v, V.n)
                    SY.dma_start(out=o_wmoe[128 * (i - 1):128 * i, :], in_=wtm[:]).then_inc(st_w, 16)
                    nst_b["w"] += 16
                    V.wait_sem(st_w, nst_b["w"])
                t_done_after.append(T.n)
                cp.step(3)
            SY.wait_ge(sem_v, V.n)
            for j in range(8):
                SY.dma_start(out=o_xnbf[128 * j:128 * (j + 1), :],
                             in_=xnT_bf[j][:, HALO:TA]).then_inc(st, 16)
                nst[0] += 16

        phase_barrier([nc.vector, nc.scalar, nc.tensor, nc.sync, nc.gpsimd],
                      [V, S, T],
                      [(ld, nld[0]), (ldg, nlg[0]), (ldx[0], nldx[0]), (ldx[1], nldx[1]),
                       (st_w, nst_b["w"]), (st, nst[0])])
        with ExitStack() as tctx:
            te = tctx.enter_context
            ps_ab = [te(nc.psum_tensor(f"ps_ab{b}", [128, 512], F32)) for b in range(2)]
            ps_pj = [te(nc.psum_tensor(f"ps_pj{i}", [96, 512], F32)) for i in range(4)]

            V.wait_sem(ld, LD_MISC)
            S.wait_sem(ld, LD_MISC)
            T.wait_sem(ldg, LD_W2)

            def load_ipw_slice(mcol, buf):
                for kt in range(8):
                    G_.dma_start(out=ipwm[buf][:, 128 * kt:128 * (kt + 1)],
                                 in_=ipw[128 * kt:128 * (kt + 1),
                                         128 * mcol:128 * (mcol + 1)]).then_inc(ldw[buf], 16)
                nldw[buf] += 16 * 8

            ev_after_g = {}
            ipw_ready = {}
            load_ipw_slice(0, 0)
            ipw_ready[0] = (0, nldw[0])
            t_after_m = {}
            gidx = [0]
            for m in range(16):
                nb = (m + 1) % 2
                if m + 1 < 16:
                    if m >= 1:
                        G_.wait_ge(sem_t, t_after_m[m - 1])
                    load_ipw_slice(m + 1, nb)
                    ipw_ready[m + 1] = (nb, nldw[nb])
                for ci, (c0, cl) in enumerate(CH5):
                    g = gidx[0]
                    ps = ps_ab[g % 2]
                    if g >= 2:
                        eng, cnt = ev_after_g[g - 2]
                        T.wait(eng, cnt)
                    if ci == 0:
                        rb, rv = ipw_ready[m]
                        T.wait_sem(ldw[rb], rv)
                    for kt in range(8):
                        mm = T.matmul(ps[:, 0:cl], ipwm[m % 2][:, 128 * kt:128 * (kt + 1)],
                                      xnT_bf[kt][:, c0:c0 + cl], start=(kt == 0), stop=(kt == 7))
                    T.inc_last(mm)
                    V.wait(T)
                    V.tensor_copy(out=xb_raw[:, c0:c0 + cl], in_=ps[:, 0:cl])
                    ev_after_g[g] = (V, V.n)
                    gidx[0] += 1
                V.tensor_scalar(out=conv_acc[:], in0=xb_raw[:, 125:125 + TCORE],
                                scalar1=convw_sb[m][:, 0:1], scalar2=None, op0=ALU.mult)
                for j in range(1, 4):
                    V.scalar_tensor_tensor(out=conv_acc[:], in0=xb_raw[:, 125 + j:125 + j + TCORE],
                                           scalar=convw_sb[m][:, j:j + 1], in1=conv_acc[:],
                                           op0=ALU.mult, op1=ALU.add)
                S.wait(V)
                S.wait_sem(st_xbs, nst_b["xbs"])
                S.activation(out=xbs_f[:], in_=conv_acc[:], func=AF.Silu,
                             bias=convb_sb[m][:, 0:1])
                SY.wait_ge(sem_s, S.n)
                SY.dma_start(out=o_xbs[128 * m:128 * (m + 1), :], in_=xbs_f[:]).then_inc(st_xbs, 16)
                nst_b["xbs"] += 16
                V.wait(S)
                if m >= 1:
                    V.wait(T, t_after_m[m - 1])
                V.tensor_copy(out=xbs_b[:], in_=xbs_f[:])
                T.wait(V)
                for ci, (c0, cl) in enumerate(CT4):
                    mm = T.matmul(ps_pj[ci][:], xpw_bf[m][:], xbs_b[:, c0:c0 + cl],
                                  start=(m == 0), stop=(m == 15))
                T.inc_last(mm)
                t_after_m[m] = T.n
                cp.step(3)
            for m in range(16):
                nb = m % 2
                G_.wait_ge(sem_t, T.n)
                load_ipw_slice(16 + m, nb)
                zr = nldw[nb]
                for ci, (c0, cl) in enumerate(CT4):
                    g = gidx[0]
                    ps = ps_ab[g % 2]
                    eng, cnt = ev_after_g[g - 2]
                    T.wait(eng, cnt)
                    if ci == 0:
                        T.wait_sem(ldw[nb], zr)
                    for kt in range(8):
                        mm = T.matmul(ps[:, 0:cl], ipwm[nb][:, 128 * kt:128 * (kt + 1)],
                                      xnT_bf[kt][:, HALO + c0:HALO + c0 + cl],
                                      start=(kt == 0), stop=(kt == 7))
                    T.inc_last(mm)
                    S.wait(T)
                    if ci == 0:
                        S.wait_sem(st_sz, nst_b["sz"])
                    S.activation(out=sz_t[:, c0:c0 + cl], in_=ps[:, 0:cl], func=AF.Silu)
                    ev_after_g[g] = (S, S.n)
                    gidx[0] += 1
                SY.wait_ge(sem_s, S.n)
                SY.dma_start(out=o_sz[128 * m:128 * (m + 1), :], in_=sz_t[:]).then_inc(st_sz, 16)
                nst_b["sz"] += 16

            V.wait(T)
            for ci, (c0, cl) in enumerate(CT4):
                V.tensor_copy(out=projT_bf[:, c0:c0 + cl], in_=ps_pj[ci][0:64, :])
                V.tensor_copy(out=pbc_sb[0:32, c0:c0 + cl], in_=ps_pj[ci][64:96, :])
            SY.wait_ge(sem_v, V.n)
            SY.dma_start(out=o_pbc[:, :], in_=pbc_sb[0:32, :]).then_inc(st, 16)
            nst[0] += 16
            T.wait(V)
            for m in range(16):
                for ci, (c0, cl) in enumerate(CT4):
                    g = gidx[0]
                    ps = ps_ab[g % 2]
                    eng, cnt = ev_after_g[g - 2]
                    T.wait(eng, cnt)
                    T.inc_last(T.matmul(ps[:, 0:cl], dtw_bf[:, 128 * m:128 * (m + 1)],
                                        projT_bf[:, c0:c0 + cl], start=True, stop=True))
                    V.wait(T)
                    V.tensor_scalar(out=dtv[:, c0:c0 + cl], in0=ps[:, 0:cl],
                                    scalar1=dtb_sb[m][:, 0:1], scalar2=None, op0=ALU.add)
                    ev_after_g[g] = (V, V.n)
                    gidx[0] += 1
                V.tensor_scalar(out=rl[:], in0=dtv[:], scalar1=-1.0, scalar2=None, op0=ALU.mult)
                V.tensor_tensor(out=ab[:], in0=dtv[:], in1=rl[:], op=ALU.min)
                V.tensor_scalar(out=rl[:], in0=dtv[:], scalar1=0.0, scalar2=None, op0=ALU.max)
                S.wait(V)
                S.activation(out=enl[:], in_=ab[:], func=AF.Exp)
                S.activation(out=enl[:], in_=enl[:], func=AF.Ln, bias=1.0)
                V.wait(S)
                V.wait_sem(st_dt, nst_b["dt"])
                V.tensor_tensor(out=dt_t[:], in0=rl[:], in1=enl[:], op=ALU.add)
                SY.wait_ge(sem_v, V.n)
                SY.dma_start(out=o_dt[128 * m:128 * (m + 1), :], in_=dt_t[:]).then_inc(st_dt, 16)
                nst_b["dt"] += 16

        cp.drain()
        cp.finish()
        SY.wait_ge(st, nst[0])
        SY.wait_ge(st_xbs, nst_b["xbs"])
        SY.wait_ge(st_sz, nst_b["sz"])
        SY.wait_ge(st_dt, nst_b["dt"])
        SY.wait_ge(st_w, nst_b["w"])
        G_.wait_ge(st, nst[0])
    return nc


def build_D():
    nc = bass.Bass("TRN2", target_bir_lowering=False, debug=False)
    TB = 8192
    CH = 512
    NCH = TB // CH
    dtT = nc.dram_tensor("dtT", [512, TB], F32, kind="ExternalInput").ap()
    xbsT = nc.dram_tensor("xbsT", [512, TB], F32, kind="ExternalInput").ap()
    pbcT = nc.dram_tensor("pbcT", [32, TB], F32, kind="ExternalInput").ap()
    alog = nc.dram_tensor("alog", [512, 16], F32, kind="ExternalInput").ap()
    dpar = nc.dram_tensor("dpar", [512, 1], F32, kind="ExternalInput").ap()
    o_y = nc.dram_tensor("o_y", [512, TB], F32, kind="ExternalOutput").ap()
    w1f = nc.dram_tensor("w1f", [8 * D, DI], F32, kind="ExternalInput").ap()
    w3f = nc.dram_tensor("w3f", [8 * D, DI], F32, kind="ExternalInput").ap()
    w2f = nc.dram_tensor("w2f", [8 * DI, D], F32, kind="ExternalInput").ap()
    opwf = nc.dram_tensor("opwf", [DI, D], F32, kind="ExternalInput").ap()
    o_w1b = nc.dram_tensor("o_w1b", [8 * D, DI], BF16, kind="ExternalOutput").ap()
    o_w3b = nc.dram_tensor("o_w3b", [8 * D, DI], BF16, kind="ExternalOutput").ap()
    o_w2b = nc.dram_tensor("o_w2b", [8 * DI, D], BF16, kind="ExternalOutput").ap()
    o_opwb = nc.dram_tensor("o_opwb", [DI, D], BF16, kind="ExternalOutput").ap()

    with ExitStack() as ctx:
        e = ctx.enter_context
        dt_c = [[e(nc.sbuf_tensor(f"dt{p}_{d}", [128, CH], F32)) for d in range(4)] for p in range(2)]
        xb_c = [[e(nc.sbuf_tensor(f"xb{p}_{d}", [128, CH], F32)) for d in range(4)] for p in range(2)]
        pb_c = [e(nc.sbuf_tensor(f"pb{p}", [1, 32 * CH], F32)) for p in range(2)]
        a_sb = [e(nc.sbuf_tensor(f"a{d}", [128, 16], F32)) for d in range(4)]
        dp_sb = [e(nc.sbuf_tensor(f"dp{d}", [128, 1], F32)) for d in range(4)]
        state = [e(nc.sbuf_tensor(f"stt{d}", [128, 16], F32)) for d in range(4)]
        ones1 = e(nc.sbuf_tensor("ones1", [1, 128], F32))
        dtu_t = e(nc.sbuf_tensor("dtu_t", [128, CH], F32))
        da_sb = [e(nc.sbuf_tensor(f"da{b}", [128, CH], F32)) for b in range(2)]
        dbu = e(nc.sbuf_tensor("dbu", [128, CH], F32))
        sct = e(nc.sbuf_tensor("sct", [128, CH], F32))
        yterm = e(nc.sbuf_tensor("yterm", [128, CH], F32))
        y_acc = [e(nc.sbuf_tensor(f"ya{d}", [128, CH], F32)) for d in range(4)]

        ps_bc = [e(nc.psum_tensor(f"ps_bc{i}", [128, CH], F32)) for i in range(4)]

        cstage_f = [e(nc.sbuf_tensor(f"cstagef{b}", [128, DI], F32)) for b in range(2)]
        cstage = [e(nc.sbuf_tensor(f"cstage{b}", [128, DI], BF16)) for b in range(2)]
        ld_cv = [e(nc.semaphore(f"ld_cv{b}")) for b in range(2)]
        st_cv = [e(nc.semaphore(f"st_cv{b}")) for b in range(2)]
        sem_g = e(nc.semaphore("sem_g"))
        ld = e(nc.semaphore("ld"))
        ldc = [e(nc.semaphore(f"ldc{p}")) for p in range(2)]
        sem_v = e(nc.semaphore("sem_v"))
        sem_s = e(nc.semaphore("sem_s"))
        sem_t = e(nc.semaphore("sem_t"))
        st_y = [e(nc.semaphore(f"st_y{d}")) for d in range(4)]

        V = EW(nc.vector, sem_v)
        S = EW(nc.scalar, sem_s)
        T = EW(nc.tensor, sem_t, autosync=False)
        SY = nc.sync
        nld = 0
        nldc = [0, 0]
        nst = [0]

        V.memset(ones1[:], 1.0)
        for d in range(4):
            V.memset(state[d][:], 0.0)
        GP = EW(nc.gpsimd, sem_g)
        cp = ConvertPool(nc.gpsimd, GP, cstage_f, cstage, ld_cv, st_cv,
                         [(w1f, o_w1b), (w3f, o_w3b),
                          (w2f.rearrange("(a b) c -> a (b c)", b=2),
                           o_w2b.rearrange("(a b) c -> a (b c)", b=2)),
                          (opwf.rearrange("(a b) c -> a (b c)", b=2),
                           o_opwb.rearrange("(a b) c -> a (b c)", b=2))])

        for d in range(4):
            SY.dma_start(out=a_sb[d][:], in_=alog[128 * d:128 * (d + 1), :]).then_inc(ld, 16)
            SY.dma_start(out=dp_sb[d][:], in_=dpar[128 * d:128 * (d + 1), :]).then_inc(ld, 16)
        nld += 16 * 8
        S.wait_sem(ld, nld)
        for d in range(4):
            S.activation(out=a_sb[d][:], in_=a_sb[d][:], func=AF.Exp, scale=1.0)
        for d in range(4):
            S.mul(out=a_sb[d][:], in_=a_sb[d][:], mul=-1.0)

        def load_chunk(ch, p):
            c0 = ch * CH
            for d in range(4):
                SY.dma_start(out=dt_c[p][d][:], in_=dtT[128 * d:128 * (d + 1), c0:c0 + CH]).then_inc(ldc[p], 16)
                SY.dma_start(out=xb_c[p][d][:], in_=xbsT[128 * d:128 * (d + 1), c0:c0 + CH]).then_inc(ldc[p], 16)
            SY.dma_start(out=pb_c[p][0:1, :].rearrange("o (r c) -> o r c", r=32),
                         in_=pbcT[:, c0:c0 + CH].unsqueeze(0)).then_inc(ldc[p], 16)
            nldc[p] += 16 * 9

        v_after_chunk = {}
        vas = {}
        load_chunk(0, 0)
        rdy = {0: nldc[0]}
        bc_reader = {}
        for ch in range(NCH):
            p = ch % 2
            if ch + 1 < NCH:
                if ch >= 1:
                    SY.wait_ge(sem_v, v_after_chunk[ch - 1])
                load_chunk(ch + 1, (ch + 1) % 2)
                rdy[ch + 1] = nldc[(ch + 1) % 2]
            V.wait_sem(ldc[p], rdy[ch])
            S.wait_sem(ldc[p], rdy[ch])
            T.wait_sem(ldc[p], rdy[ch])
            if ch == 0:
                T.wait(V)
            for d in range(4):
                if ch >= 1:
                    V.wait_sem(st_y[d], 16 * ch)
                V.tensor_mul(out=dtu_t[:], in0=dt_c[p][d][:], in1=xb_c[p][d][:])
                for s in range(16):
                    k = (d * 16 + s) % 2
                    bb, cb = ps_bc[2 * k], ps_bc[2 * k + 1]
                    for bank, ps_t, row in ((2 * k, bb, s), (2 * k + 1, cb, 16 + s)):
                        if bank in bc_reader:
                            T.wait(V, bc_reader[bank])
                        T.inc_last(T.matmul(ps_t[:], ones1[0:1, :],
                                            pb_c[p][0:1, row * CH:(row + 1) * CH],
                                            start=True, stop=True))
                    db = s % 2
                    if db in vas:
                        S.wait(V, vas[db])
                    S.activation(out=da_sb[db][:], in_=dt_c[p][d][:], func=AF.Exp,
                                 scale=a_sb[d][:, s:s + 1])
                    V.wait(T)
                    V.wait(S)
                    V.tensor_mul(out=dbu[:], in0=dtu_t[:], in1=bb[:])
                    V.tensor_tensor_scan(out=sct[:], data0=da_sb[db][:], data1=dbu[:],
                                         initial=state[d][:, s:s + 1], op0=ALU.mult,
                                         op1=ALU.add)
                    vas[db] = V.n
                    V.tensor_copy(out=state[d][:, s:s + 1], in_=sct[:, CH - 1:CH])
                    if s == 0:
                        V.tensor_mul(out=y_acc[d][:], in0=sct[:], in1=cb[:])
                    else:
                        V.tensor_mul(out=yterm[:], in0=sct[:], in1=cb[:])
                        V.tensor_add(out=y_acc[d][:], in0=y_acc[d][:], in1=yterm[:])
                    bc_reader[2 * k] = V.n
                    bc_reader[2 * k + 1] = V.n
                V.scalar_tensor_tensor(out=y_acc[d][:], in0=xb_c[p][d][:],
                                       scalar=dp_sb[d][:, 0:1], in1=y_acc[d][:],
                                       op0=ALU.mult, op1=ALU.add)
                SY.wait_ge(sem_v, V.n)
                SY.dma_start(out=o_y[128 * d:128 * (d + 1), ch * CH:(ch + 1) * CH],
                             in_=y_acc[d][:]).then_inc(st_y[d], 16)
                nst[0] += 16
                cp.step(4)
            v_after_chunk[ch] = V.n
        cp.drain()
        cp.finish()
        for d in range(4):
            SY.wait_ge(st_y[d], 16 * NCH)
    return nc


def run_D(A_outs, inputs):
    dtT = np.concatenate([np.asarray(A_outs[c]["o_dt"]) for c in range(NC)], axis=1)
    xbsT = np.concatenate([np.asarray(A_outs[c]["o_xbs"]) for c in range(NC)], axis=1)
    pbcT = np.concatenate([np.asarray(A_outs[c]["o_pbc"]) for c in range(NC)], axis=1)
    alog = f32c(inputs["L0_attn_A_log"])
    dpar = f32c(inputs["L0_attn_D_param"].reshape(DI, 1))
    in_maps = []
    for c in range(NC):
        b = c // 4
        r0 = 512 * (c % 4)
        cols = slice(8192 * b, 8192 * (b + 1))
        in_maps.append({
            "dtT": f32c(dtT[r0:r0 + 512, cols]),
            "xbsT": f32c(xbsT[r0:r0 + 512, cols]),
            "pbcT": f32c(pbcT[:, cols]),
            "alog": f32c(alog[r0:r0 + 512]),
            "dpar": f32c(dpar[r0:r0 + 512]),
            "w1f": f32c(inputs["L0_ffn_w1"].reshape(8 * D, DI)),
            "w3f": f32c(inputs["L0_ffn_w3"].reshape(8 * D, DI)),
            "w2f": f32c(inputs["L0_ffn_w2"].reshape(8 * DI, D)),
            "opwf": f32c(inputs["L0_attn_out_proj_w"]),
        })
    if "D" not in _A_CACHE:
        _A_CACHE["D"] = build_D()
    res = run_bass_kernel_spmd(_A_CACHE["D"], in_maps, core_ids=list(range(NC)))
    return res.results


def build_M():
    nc = bass.Bass("TRN2", target_bir_lowering=False, debug=False)
    yT = nc.dram_tensor("yT", [DI, TCORE], F32, kind="ExternalInput").ap()
    szT = nc.dram_tensor("szT", [DI, TCORE], F32, kind="ExternalInput").ap()
    xnbfT = nc.dram_tensor("xnbfT", [D, TCORE], BF16, kind="ExternalInput").ap()
    wmoe = nc.dram_tensor("wmoe", [TCORE, 8], F32, kind="ExternalInput").ap()
    x_tm = nc.dram_tensor("x_tm", [TCORE, D], F32, kind="ExternalInput").ap()
    opw = nc.dram_tensor("opw", [DI, D], BF16, kind="ExternalInput").ap()
    w1 = nc.dram_tensor("w1", [8 * D, DI], BF16, kind="ExternalInput").ap()
    w3 = nc.dram_tensor("w3", [8 * D, DI], BF16, kind="ExternalInput").ap()
    w2 = nc.dram_tensor("w2", [8 * DI, D], BF16, kind="ExternalInput").ap()
    lng = nc.dram_tensor("lng", [1, D], F32, kind="ExternalInput").ap()
    lnb = nc.dram_tensor("lnb", [1, D], F32, kind="ExternalInput").ap()

    o_x1 = nc.dram_tensor("o_x1", [TCORE, D], F32, kind="ExternalOutput").ap()
    o_xn1 = nc.dram_tensor("o_xn1", [D, TCORE], BF16, kind="ExternalOutput").ap()
    o_xat = nc.dram_tensor("o_xat", [D, TCORE], BF16, kind="ExternalOutput").ap()

    with ExitStack() as ctx:
        e = ctx.enter_context
        ident_bf_d = nc.inline_tensor(np.eye(128, dtype=BF), name="ident_bf")

        acc = [e(nc.sbuf_tensor(f"acc{t}", [128, D], F32)) for t in range(16)]
        xnbf = [e(nc.sbuf_tensor(f"xnbf{j}", [128, TCORE], BF16)) for j in range(8)]
        wm_sb = [e(nc.sbuf_tensor(f"wm{t}", [128, 8], F32)) for t in range(16)]
        ident_bf = e(nc.sbuf_tensor("identbf", [128, 128], BF16))
        ones1 = e(nc.sbuf_tensor("ones1", [1, 128], F32))

        epsb = e(nc.sbuf_tensor("epsb", [128, 1], F32))

        ld = e(nc.semaphore("ld"))
        ldg = e(nc.semaphore("ldg"))
        ldy = [e(nc.semaphore(f"ldy{p}")) for p in range(2)]
        ldw_s = [e(nc.semaphore(f"ldws{p}")) for p in range(2)]
        ldw2 = [e(nc.semaphore(f"ldw2{p}")) for p in range(2)]
        sem_v = e(nc.semaphore("sem_v"))
        sem_s = e(nc.semaphore("sem_s"))
        sem_t = e(nc.semaphore("sem_t"))
        st = e(nc.semaphore("st"))

        V = EW(nc.vector, sem_v)
        S = EW(nc.scalar, sem_s)
        T = EW(nc.tensor, sem_t, autosync=False)
        SY = nc.sync
        G_ = nc.gpsimd
        nld = [0]
        nlg = [0]
        nldy = [0, 0]
        nldws = [0, 0]
        nldw2 = [0, 0]
        nst = [0]

        V.memset(epsb[:], LN_EPS)
        V.memset(ones1[:], 1.0)

        SY.dma_start(out=ident_bf[:], in_=ident_bf_d.ap()).then_inc(ld, 16)
        for t in range(16):
            SY.dma_start(out=wm_sb[t][:], in_=wmoe[128 * t:128 * (t + 1), :]).then_inc(ld, 16)
            SY.dma_start(out=acc[t][:], in_=x_tm[128 * t:128 * (t + 1), :]).then_inc(ld, 16)
        for j in range(8):
            SY.dma_start(out=xnbf[j][:], in_=xnbfT[128 * j:128 * (j + 1), :]).then_inc(ld, 16)
        nld[0] += 16 * 41
        LD_BASE = nld[0]

        with ExitStack() as c1:
            t1 = c1.enter_context
            yf_bf = [t1(nc.sbuf_tensor(f"yf{k}", [128, TCORE], BF16)) for k in range(8)]
            ybuf = t1(nc.sbuf_tensor("ybuf", [128, TCORE], F32))
            zbuf = t1(nc.sbuf_tensor("zbuf", [128, TCORE], F32))
            opw_bf = [t1(nc.sbuf_tensor(f"opw{k}", [128, D], BF16)) for k in range(16)]
            ps_m = [t1(nc.psum_tensor(f"ps_m{i}", [128, 512], F32)) for i in range(4)]
            for k in range(16):
                SY.dma_start(out=opw_bf[k][:], in_=opw[128 * k:128 * (k + 1), :]).then_inc(ldg, 16)
            nlg[0] += 16 * 16
            t_after_g = {}
            ev_ps = {}
            for g in range(2):
                for k8 in range(8):
                    k = 8 * g + k8
                    SY.wait_ge(sem_v, V.n)
                    if g >= 1:
                        SY.wait_ge(sem_t, t_after_g[g - 1])
                    SY.dma_start(out=ybuf[:], in_=yT[128 * k:128 * (k + 1), :]).then_inc(ldy[0], 16)
                    SY.dma_start(out=zbuf[:], in_=szT[128 * k:128 * (k + 1), :]).then_inc(ldy[0], 16)
                    nldy[0] += 32
                    V.wait_sem(ldy[0], nldy[0])
                    V.tensor_mul(out=yf_bf[k8][:], in0=ybuf[:], in1=zbuf[:])
                if g == 0:
                    T.wait_sem(ldg, nlg[0])
                T.wait(V)
                for ts in range(16):
                    for half in range(2):
                        gi = 2 * ts + half
                        pi = gi % 4
                        if gi >= 4 or g >= 1:
                            eng, cnt = ev_ps[pi]
                            T.wait(eng, cnt)
                        for k8 in range(8):
                            mm = T.matmul(ps_m[pi][:], yf_bf[k8][:, 128 * ts:128 * (ts + 1)],
                                          opw_bf[8 * g + k8][:, 512 * half:512 * (half + 1)],
                                          start=(k8 == 0), stop=(k8 == 7))
                        T.inc_last(mm)
                        V.wait(T)
                        V.wait_sem(ld, LD_BASE)
                        V.tensor_add(out=acc[ts][:, 512 * half:512 * (half + 1)],
                                     in0=acc[ts][:, 512 * half:512 * (half + 1)],
                                     in1=ps_m[pi][:])
                        ev_ps[pi] = (V, V.n)
                t_after_g[g] = T.n

        phase_barrier([nc.vector, nc.scalar, nc.tensor, nc.sync, nc.gpsimd],
                      [V, S, T],
                      [(ld, nld[0]), (ldg, nlg[0]), (ldy[0], nldy[0])])
        with ExitStack() as c3:
            t3 = c3.enter_context
            hc = [t3(nc.sbuf_tensor(f"hc{p}", [128, 16 * 512], BF16)) for p in range(2)]
            w13m = [t3(nc.sbuf_tensor(f"w13m{p}", [128, 2 * 8 * 128], BF16)) for p in range(2)]
            w2_bf = [t3(nc.sbuf_tensor(f"w2bf{k}", [128, D], BF16)) for k in range(16)]
            hstg = [t3(nc.sbuf_tensor(f"hstg{p}", [128, 2048], BF16)) for p in range(2)]
            t1s = [t3(nc.sbuf_tensor(f"t1s{b}", [128, 512], F32)) for b in range(2)]
            ps_h = [t3(nc.psum_tensor(f"ps_h{i}", [128, 512], F32)) for i in range(4)]
            ps_y = [t3(nc.psum_tensor(f"ps_y{i}", [128, 512], F32)) for i in range(4)]
            d_h = [nc.dram_tensor(f"d_h{b}", [DI, TCORE], BF16) for b in range(2)]

            def load_w13(eidx, hm, buf):
                SY.dma_start(out=w13m[buf][:, 0:1024].rearrange("p (k c) -> p k c", k=8),
                             in_=w1[D * eidx:D * (eidx + 1),
                                    128 * hm:128 * (hm + 1)].rearrange(
                                 "(k p) c -> p k c", p=128)).then_inc(ldw_s[buf], 16)
                SY.dma_start(out=w13m[buf][:, 1024:2048].rearrange("p (k c) -> p k c", k=8),
                             in_=w3[D * eidx:D * (eidx + 1),
                                    128 * hm:128 * (hm + 1)].rearrange(
                                 "(k p) c -> p k c", p=128)).then_inc(ldw_s[buf], 16)
                nldws[buf] += 32

            def load_w2(eidx):
                for k in range(16):
                    SY.dma_start(out=w2_bf[k][:],
                                 in_=w2[DI * eidx + 128 * k:DI * eidx + 128 * (k + 1),
                                        :]).then_inc(ldw2[0], 16)
                nldw2[0] += 16 * 16

            ev_h = {}
            ev_y = {}
            st_hd = [t3(nc.semaphore(f"st_hd{p}")) for p in range(2)]
            ldh_y = [t3(nc.semaphore(f"ldh_y{p}")) for p in range(2)]
            nst_hd = [0, 0]
            nldh_y = [0, 0]
            t_after_w13 = {}
            t_after_y = {}
            st_hd_after = {}
            t_after_h = {}

            def emit_h(ei):
                eb = ei % 2
                for hm in range(16):
                    si = 16 * ei + hm
                    sb = si % 2
                    if si >= 2:
                        SY.wait_ge(sem_t, t_after_w13[si - 2])
                    load_w13(ei, hm, sb)
                    wrdy = nldws[sb]
                    T.wait_sem(ldw_s[sb], wrdy)
                    for ci in range(4):
                        pp = ci % 2
                        pa, pb = ps_h[2 * pp], ps_h[2 * pp + 1]
                        if ("h", pp) in ev_h:
                            eng, cnt = ev_h[("h", pp)]
                            T.wait(eng, cnt)
                        for kt in range(8):
                            T.matmul(pa[:], w13m[sb][:, 128 * kt:128 * (kt + 1)],
                                     xnbf[kt][:, 512 * ci:512 * (ci + 1)],
                                     start=(kt == 0), stop=(kt == 7))
                        for kt in range(8):
                            mm = T.matmul(pb[:], w13m[sb][:, 1024 + 128 * kt:1024 + 128 * (kt + 1)],
                                          xnbf[kt][:, 512 * ci:512 * (ci + 1)],
                                          start=(kt == 0), stop=(kt == 7))
                        T.inc_last(mm)
                        tb_ = (16 * hm + ci) % 2
                        S.wait(T)
                        if ("t1s", tb_) in ev_h:
                            S.wait(V, ev_h[("t1s", tb_)])
                        S.activation(out=t1s[tb_][:], in_=pa[:], func=AF.Silu)
                        V.wait(S)
                        if ci == 0:
                            V.wait_sem(st_hd[sb], nst_hd[sb])
                        V.tensor_mul(out=hstg[sb][:, 512 * ci:512 * (ci + 1)],
                                     in0=t1s[tb_][:], in1=pb[:])
                        ev_h[("h", pp)] = (V, V.n)
                        ev_h[("t1s", tb_)] = V.n
                    t_after_w13[si] = T.n
                    SY.wait_ge(sem_v, V.n)
                    SY.dma_start(out=d_h[eb].ap()[128 * hm:128 * (hm + 1), :],
                                 in_=hstg[sb][:]).then_inc(st_hd[sb], 16)
                    nst_hd[sb] += 16
                t_after_h[ei] = T.n
                st_hd_after[ei] = (nst_hd[0], nst_hd[1])

            def emit_y(ei):
                eb = ei % 2
                T.wait_sem(ldw2[0], w2rdy[ei])
                for ci in range(4):
                    cb = ci % 2
                    if ci >= 2:
                        G_.wait_ge(sem_t, t_after_ci[cb])
                    elif ei >= 1:
                        G_.wait_ge(sem_t, t_after_ci[cb])
                    if ci == 0:
                        a0, a1 = st_hd_after[ei]
                        G_.wait_ge(st_hd[0], a0)
                        G_.wait_ge(st_hd[1], a1)
                    G_.dma_start(out=hc[cb][:].rearrange("p (k c) -> p k c", k=16),
                                 in_=d_h[eb].ap()[:, 512 * ci:512 * (ci + 1)].rearrange(
                                     "(k p) c -> p k c", p=128)).then_inc(ldh_y[cb], 16)
                    nldh_y[cb] += 16
                    T.wait_sem(ldh_y[cb], nldh_y[cb])
                    for tsl in range(4):
                        ts = 4 * ci + tsl
                        for nh in range(2):
                            pj = (2 * tsl + nh) % 4
                            if ("y", pj) in ev_y:
                                eng, cnt = ev_y[("y", pj)]
                                T.wait(eng, cnt)
                            for k in range(16):
                                mm = T.matmul(ps_y[pj][:], hc[cb][:, 512 * k + 128 * tsl:
                                                                  512 * k + 128 * (tsl + 1)],
                                              w2_bf[k][:, 512 * nh:512 * (nh + 1)],
                                              start=(k == 0), stop=(k == 15))
                            T.inc_last(mm)
                            V.wait(T)
                            V.scalar_tensor_tensor(out=acc[ts][:, 512 * nh:512 * (nh + 1)],
                                                   in0=ps_y[pj][:],
                                                   scalar=wm_sb[ts][:, ei:ei + 1],
                                                   in1=acc[ts][:, 512 * nh:512 * (nh + 1)],
                                                   op0=ALU.mult, op1=ALU.add)
                            ev_y[("y", pj)] = (V, V.n)
                    t_after_ci[cb] = T.n
                t_after_y[ei] = T.n

            w2rdy = {}
            t_after_ci = {0: 0, 1: 0}
            for ei in range(9):
                if ei < 8:
                    emit_h(ei)
                if ei >= 1:
                    if ei >= 2:
                        SY.wait_ge(sem_t, t_after_y[ei - 2])
                    load_w2(ei - 1)
                    w2rdy[ei - 1] = nldw2[0]
                    emit_y(ei - 1)

        phase_barrier([nc.vector, nc.scalar, nc.tensor, nc.sync, nc.gpsimd],
                      [V, S, T],
                      [(ldw_s[0], nldws[0]), (ldw_s[1], nldws[1]), (ldw2[0], nldw2[0])])
        with ExitStack() as c4:
            t4 = c4.enter_context
            g_bc = t4(nc.sbuf_tensor("g_bc", [128, D], F32))
            b_bc = t4(nc.sbuf_tensor("b_bc", [128, D], F32))
            gb_row = t4(nc.sbuf_tensor("gb_row", [1, 2 * D], F32))
            SY.dma_start(out=gb_row[0:1, 0:D], in_=lng).then_inc(ld, 16)
            SY.dma_start(out=gb_row[0:1, D:2 * D], in_=lnb).then_inc(ld, 16)
            nld[0] += 32
            xn1T = [t4(nc.sbuf_tensor(f"xn1T{j}", [128, TCORE], BF16)) for j in range(8)]
            xatT = [t4(nc.sbuf_tensor(f"xatT{j}", [128, TCORE], BF16)) for j in range(8)]
            xn1t = t4(nc.sbuf_tensor("xn1t", [128, D], F32))
            xat = t4(nc.sbuf_tensor("xat", [128, D], F32))
            xbf = [t4(nc.sbuf_tensor(f"xbfb{b}", [128, D], BF16)) for b in range(2)]
            sqt = t4(nc.sbuf_tensor("sqt", [128, D], F32))
            muv = t4(nc.sbuf_tensor("muv", [128, 1], F32))
            ssv = t4(nc.sbuf_tensor("ssv", [128, 1], F32))
            rstd = t4(nc.sbuf_tensor("rstd", [128, 1], F32))
            ps_tp = [t4(nc.psum_tensor(f"ps_tp{b}", [128, 128], BF16)) for b in range(2)]
            ps_g = [t4(nc.psum_tensor(f"ps_g{b}", [128, 512], F32)) for b in range(2)]

            T.wait_sem(ld, nld[0])
            T.wait(V)
            for half in range(2):
                T.inc_last(T.matmul(ps_g[half][:], ones1[0:1, :],
                                    gb_row[0:1, 512 * half:512 * (half + 1)],
                                    start=True, stop=True))
            V.wait(T)
            V.tensor_copy(out=g_bc[:, 0:512], in_=ps_g[0][:])
            V.tensor_copy(out=g_bc[:, 512:1024], in_=ps_g[1][:])
            for half in range(2):
                T.wait(V)
                T.inc_last(T.matmul(ps_g[half][:], ones1[0:1, :],
                                    gb_row[0:1, D + 512 * half:D + 512 * (half + 1)],
                                    start=True, stop=True))
            V.wait(T)
            V.tensor_copy(out=b_bc[:, 0:512], in_=ps_g[0][:])
            V.tensor_copy(out=b_bc[:, 512:1024], in_=ps_g[1][:])

            def ln_tmajor(src_ap, dst, scale_gb):
                V.reduce_sum(out=muv[:], in_=src_ap, axis=AX.X)
                V.tensor_scalar(out=muv[:], in0=muv[:], scalar1=1.0 / D, scalar2=None,
                                op0=ALU.mult)
                V.tensor_scalar(out=dst[:], in0=src_ap, scalar1=muv[:, 0:1], scalar2=None,
                                op0=ALU.subtract)
                S.wait(V)
                S.activation(out=sqt[:], in_=dst[:], func=AF.Square, accum_out=ssv[:])
                S.activation(out=rstd[:], in_=ssv[:], func=AF.Ln, scale=1.0 / D,
                             bias=epsb[:, 0:1])
                S.activation(out=rstd[:], in_=rstd[:], func=AF.Exp, scale=-0.5)
                V.wait(S)
                V.tensor_scalar(out=dst[:], in0=dst[:], scalar1=rstd[:, 0:1], scalar2=None,
                                op0=ALU.mult)
                if scale_gb:
                    V.tensor_mul(out=dst[:], in0=dst[:], in1=g_bc[:])
                    V.tensor_add(out=dst[:], in0=dst[:], in1=b_bc[:])

            ev_tp = {}
            ktp = [0]

            def transposes(src_bf, dstT, ti):
                for j in range(8):
                    k = ktp[0]
                    bk = k % 2
                    if k >= 2:
                        eng, cnt = ev_tp[k - 2]
                        T.wait(eng, cnt)
                    T.inc_last(T.transpose(ps_tp[bk][:], src_bf[:, 128 * j:128 * (j + 1)],
                                           ident_bf[:]))
                    V.wait(T)
                    V.tensor_copy(out=dstT[j][:, 128 * ti:128 * (ti + 1)], in_=ps_tp[bk][:])
                    ev_tp[k] = (V, V.n)
                    ktp[0] += 1

            for ts in range(16):
                ln_tmajor(acc[ts][:], xn1t, False)
                V.tensor_copy(out=xbf[0][:], in_=xn1t[:])
                T.wait(V)
                transposes(xbf[0], xn1T, ts)
                ln_tmajor(xn1t[:], xat, True)
                V.tensor_copy(out=xbf[1][:], in_=xat[:])
                T.wait(V)
                transposes(xbf[1], xatT, ts)
                SY.wait_ge(sem_v, V.n)
                SY.dma_start(out=o_x1[128 * ts:128 * (ts + 1), :], in_=acc[ts][:]).then_inc(st, 16)
                nst[0] += 16
            SY.wait_ge(sem_v, V.n)
            for j in range(8):
                SY.dma_start(out=o_xn1[128 * j:128 * (j + 1), :], in_=xn1T[j][:]).then_inc(st, 16)
                SY.dma_start(out=o_xat[128 * j:128 * (j + 1), :], in_=xatT[j][:]).then_inc(st, 16)
                nst[0] += 32
            SY.wait_ge(st, nst[0])
            G_.wait_ge(st, nst[0])
    return nc


def run_M(A_outs, D_outs, inputs):
    yT = np.concatenate(
        [np.concatenate([np.asarray(D_outs[4 * b + j]["o_y"]) for j in range(4)], axis=0)
         for b in range(2)], axis=1)
    in_maps = []
    for c in range(NC):
        cols = slice(TCORE * c, TCORE * (c + 1))
        x_tm = inputs["L0_attn_x"].reshape(16384, D)[cols]
        in_maps.append({
            "yT": f32c(yT[:, cols]),
            "szT": f32c(np.asarray(A_outs[c]["o_sz"])),
            "xnbfT": np.ascontiguousarray(np.asarray(A_outs[c]["o_xnbf"]).astype(BF)),
            "wmoe": f32c(np.asarray(A_outs[c]["o_wmoe"])),
            "x_tm": f32c(x_tm),
            "opw": np.ascontiguousarray(np.asarray(D_outs[c]["o_opwb"]).astype(BF)),
            "w1": np.ascontiguousarray(np.asarray(D_outs[c]["o_w1b"]).astype(BF)),
            "w3": np.ascontiguousarray(np.asarray(D_outs[c]["o_w3b"]).astype(BF)),
            "w2": np.ascontiguousarray(np.asarray(D_outs[c]["o_w2b"]).astype(BF)),
            "lng": f32c(inputs["L1_attn_ln_g"].reshape(1, D)),
            "lnb": f32c(inputs["L1_attn_ln_b"].reshape(1, D)),
        })
    if "M" not in _A_CACHE:
        _A_CACHE["M"] = build_M()
    res = run_bass_kernel_spmd(_A_CACHE["M"], in_maps, core_ids=list(range(NC)))
    return res.results


def build_F():
    nc = bass.Bass("TRN2", target_bir_lowering=False, debug=False)
    TB = 8192
    NKT = TB // 128
    NQC = TB // 512
    xatT = nc.dram_tensor("xatT", [D, TB], BF16, kind="ExternalInput").ap()
    wqT = nc.dram_tensor("wqT", [D, 256], F32, kind="ExternalInput").ap()
    wkT = nc.dram_tensor("wkT", [D, 256], F32, kind="ExternalInput").ap()
    wvT = nc.dram_tensor("wvT", [D, 256], F32, kind="ExternalInput").ap()
    qg = nc.dram_tensor("qg", [64, 1], F32, kind="ExternalInput").ap()
    kg = nc.dram_tensor("kg", [64, 1], F32, kind="ExternalInput").ap()
    o_oh = nc.dram_tensor("o_oh", [256, TB], BF16, kind="ExternalOutput").ap()
    H4 = 4096
    cv_shapes = [[H4, H4], [H4, H4], [D, H4], [D, H4], [H4, D], [D, D]]
    cv_in = [nc.dram_tensor(f"cv_in{i}", shp, F32, kind="ExternalInput").ap()
             for i, shp in enumerate(cv_shapes)]
    cv_out = [nc.dram_tensor(f"cv_out{i}", shp, BF16, kind="ExternalOutput").ap()
              for i, shp in enumerate(cv_shapes)]

    RMS_EPS = 1e-8
    SCALE = 0.125

    with ExitStack() as ctx:
        e = ctx.enter_context
        xat = [e(nc.sbuf_tensor(f"xat{j}", [128, TB], BF16)) for j in range(8)]
        qn_bf = e(nc.sbuf_tensor("qn_bf", [128, TB], BF16))
        kn_bf = e(nc.sbuf_tensor("kn_bf", [128, TB], BF16))
        vo_all = e(nc.sbuf_tensor("vo_all", [128, 65 * NKT], BF16))
        vo = [vo_all[:, 65 * t:65 * (t + 1)] for t in range(NKT)]
        w_sb = [e(nc.sbuf_tensor(f"w_sb{i}", [128, 256], BF16)) for i in range(24)]
        gam = e(nc.sbuf_tensor("gam", [64, 2], F32))
        ones1 = e(nc.sbuf_tensor("ones1", [1, 128], F32))
        onesc = e(nc.sbuf_tensor("onesc", [64, 1], F32))
        sq_t = e(nc.sbuf_tensor("sq_t", [64, 512], F32))
        tmp_t = e(nc.sbuf_tensor("tmp_t", [64, 512], F32))
        row_a = e(nc.sbuf_tensor("row_a", [1, 512], F32))
        row_b = e(nc.sbuf_tensor("row_b", [1, 512], F32))
        cacc = e(nc.sbuf_tensor("cacc", [1, 4], F32))
        negC_bc = e(nc.sbuf_tensor("negC_bc", [128, 1], F32))
        exp_bf = [e(nc.sbuf_tensor(f"exp{b}", [128, 1024], BF16)) for b in range(3)]
        rec_bc = e(nc.sbuf_tensor("rec_bc", [64, 512], F32))
        o_sb = e(nc.sbuf_tensor("o_sb", [64, 512], BF16))
        o_f32 = e(nc.sbuf_tensor("o_f32", [65, 512], F32))
        row_c = e(nc.sbuf_tensor("row_c", [1, 512], F32))

        cstage_f = [e(nc.sbuf_tensor(f"cstagef{b}", [128, 512], F32)) for b in range(2)]
        cstage_b = [e(nc.sbuf_tensor(f"cstageb{b}", [128, 512], BF16)) for b in range(2)]
        ld_cv = [e(nc.semaphore(f"ld_cv{b}")) for b in range(2)]
        st_cv = [e(nc.semaphore(f"st_cv{b}")) for b in range(2)]
        sem_g = e(nc.semaphore("sem_g"))
        ld = e(nc.semaphore("ld"))
        ldg = e(nc.semaphore("ldg"))
        sem_v = e(nc.semaphore("sem_v"))
        sem_s = e(nc.semaphore("sem_s"))
        sem_t = e(nc.semaphore("sem_t"))
        st_o = e(nc.semaphore("st_o"))

        V = EW(nc.vector, sem_v)
        S = EW(nc.scalar, sem_s)
        T = EW(nc.tensor, sem_t, autosync=False)
        SY = nc.sync
        G_ = nc.gpsimd
        nld = [0]
        nlg = [0]
        nst = [0]

        V.memset(ones1[:], 1.0)
        V.memset(onesc[:], 1.0)
        GP = EW(nc.gpsimd, sem_g)
        cv_pairs = []
        for i in range(6):
            si, so = cv_in[i], cv_out[i]
            f = si.shape[1] // 512
            if f > 1:
                si = si.rearrange("a (b c) -> (a b) c", b=f)
                so = so.rearrange("a (b c) -> (a b) c", b=f)
            cv_pairs.append((si, so))
        cp = ConvertPool(G_, GP, cstage_f, cstage_b, ld_cv, st_cv, cv_pairs)
        for j in range(8):
            SY.dma_start(out=xat[j][:], in_=xatT[128 * j:128 * (j + 1), :]).then_inc(ld, 16)
        SY.dma_start(out=gam[0:64, 0:1], in_=qg).then_inc(ld, 16)
        SY.dma_start(out=gam[0:64, 1:2], in_=kg).then_inc(ld, 16)
        nld[0] += 16 * 10
        for j in range(8):
            G_.dma_start(out=w_sb[j][:], in_=wqT[128 * j:128 * (j + 1), :]).then_inc(ldg, 16)
            G_.dma_start(out=w_sb[8 + j][:], in_=wkT[128 * j:128 * (j + 1), :]).then_inc(ldg, 16)
            G_.dma_start(out=w_sb[16 + j][:], in_=wvT[128 * j:128 * (j + 1), :]).then_inc(ldg, 16)
        nlg[0] += 16 * 24
        V.wait_sem(ld, nld[0])
        V.tensor_scalar(out=gam[:, 0:1], in0=gam[:, 0:1], scalar1=SCALE, scalar2=None,
                        op0=ALU.mult)

        with ExitStack() as cq:
            te = cq.enter_context

            for h in range(4):
              with ExitStack() as cqk:
                tq = cqk.enter_context
                ps_q = tq(nc.psum_tensor(f"ps_q{h}", [64, 512], F32))
                ps_n = tq(nc.psum_tensor(f"ps_n{h}", [1, 512], F32))
                ps_r = tq(nc.psum_tensor(f"ps_r{h}", [128, 512], F32))
                ps_v = [tq(nc.psum_tensor(f"ps_v{h}_{b}", [128, 64], F32)) for b in range(2)]
                for which, wbase, gcol, dst, ccol in ((0, 0, 0, qn_bf, 0), (1, 8, 1, kn_bf, 1)):
                  for cidx in range(NQC):
                      c0 = 512 * cidx
                      T.wait_sem(ldg, nlg[0])
                      T.wait(V)
                      for kt in range(8):
                          mm = T.matmul(ps_q[:], w_sb[wbase + kt][:, 64 * h:64 * (h + 1)],
                                        xat[kt][:, c0:c0 + 512],
                                        start=(kt == 0), stop=(kt == 7))
                      T.inc_last(mm)
                      S.wait(T)
                      S.wait(V)
                      S.activation(out=sq_t[:], in_=ps_q[:], func=AF.Square)
                      T.wait(S)
                      T.inc_last(T.matmul(ps_n[:], onesc[:], sq_t[:], start=True, stop=True))
                      V.wait(T)
                      V.tensor_scalar(out=row_a[:], in0=ps_n[:], scalar1=64e-16,
                                      scalar2=None, op0=ALU.max)
                      S.wait(V)
                      S.activation(out=row_b[:], in_=row_a[:], func=AF.Ln, scale=1.0 / 64.0)
                      S.activation(out=row_b[:], in_=row_b[:], func=AF.Exp, scale=-0.5)
                      T.wait(S)
                      T.inc_last(T.matmul(ps_r[0:64, :], ones1[0:1, 0:64], row_b[:],
                                          start=True, stop=True))
                      V.tensor_scalar(out=tmp_t[:], in0=ps_q[:], scalar1=gam[:, gcol:gcol + 1],
                                      scalar2=None, op0=ALU.mult)
                      V.wait(T)
                      V.tensor_mul(out=dst[0:64, c0:c0 + 512], in0=tmp_t[:], in1=ps_r[0:64, :])
                      S.wait(V)
                      S.activation(out=sq_t[:], in_=tmp_t[:], func=AF.Square)
                      T.wait(S)
                      T.inc_last(T.matmul(ps_n[:], onesc[:], sq_t[:], start=True, stop=True))
                      V.wait(T)
                      V.tensor_mul(out=row_a[:], in0=ps_n[:], in1=row_b[:])
                      V.tensor_mul(out=row_a[:], in0=row_a[:], in1=row_b[:])
                      V.reduce_max(out=row_a[0:1, 0:1], in_=row_a[:], axis=AX.X)
                      if cidx == 0:
                          V.tensor_copy(out=cacc[0:1, ccol:ccol + 1], in_=row_a[0:1, 0:1])
                      else:
                          V.tensor_max(out=cacc[0:1, ccol:ccol + 1],
                                       in0=cacc[0:1, ccol:ccol + 1], in1=row_a[0:1, 0:1])
              SY.wait_ge(sem_v, V.n)
              SY.dma_start(out=qn_bf[64:128, :], in_=qn_bf[0:64, :]).then_inc(ld, 16)
              SY.dma_start(out=kn_bf[64:128, :], in_=kn_bf[0:64, :]).then_inc(ld, 16)
              nld[0] += 32
              S.wait(V)
              S.activation(out=cacc[0:1, 0:1], in_=cacc[0:1, 0:1], func=AF.Sqrt)
              S.activation(out=cacc[0:1, 1:2], in_=cacc[0:1, 1:2], func=AF.Sqrt)
              V.wait(S)
              V.tensor_mul(out=cacc[0:1, 2:3], in0=cacc[0:1, 0:1], in1=cacc[0:1, 1:2])
              V.tensor_scalar(out=cacc[0:1, 3:4], in0=cacc[0:1, 2:3], scalar1=-1.0,
                              scalar2=None, op0=ALU.mult)
              T.wait(V)
              T.inc_last(T.matmul(ps_r[:, 0:1], ones1[0:1, :], cacc[0:1, 3:4],
                                  start=True, stop=True))
              V.wait(T)
              V.tensor_copy(out=negC_bc[0:128, 0:1], in_=ps_r[0:128, 0:1])
              for t in range(NKT):
                  b = t % 2
                  if t >= 2:
                      T.wait(V)
                  for kt in range(8):
                      mm = T.matmul(ps_v[b][:], xat[kt][:, 128 * t:128 * (t + 1)],
                                    w_sb[16 + kt][:, 64 * h:64 * (h + 1)],
                                    start=(kt == 0), stop=(kt == 7))
                  T.inc_last(mm)
                  V.wait(T)
                  V.tensor_copy(out=vo[t][:, 0:64], in_=ps_v[b][:])
                  V.memset(vo[t][:, 64:65], 1.0)
                  cp.step(2)
              with ExitStack() as ca:
                    ta = ca.enter_context
                    ps_s = [ta(nc.psum_tensor(f"ps_s{h}_{b}", [128, 1024], F32)) for b in range(3)]
                    ps_o = ta(nc.psum_tensor(f"ps_o{h}", [65, 512], F32))
                    ps_r = ta(nc.psum_tensor(f"ps_r2{h}", [64, 512], F32))
                    T.wait(V)
                    T.wait(S)
                    T.wait_sem(ld, nld[0])
                    s_after_exp = {}
                    t_after_av = {}
                    NP = NKT // 2
                    v_evac = {}
                    for qc in range(NQC):
                        c0 = 512 * qc
                        if qc >= 1:
                            T.wait(V, v_evac[qc - 1])
                        for p in range(NP + 1):
                            if p < NP:
                                pb = p % 3
                                if pb in s_after_exp:
                                    T.wait(S, s_after_exp[pb])
                                kc = 2 * p
                                T.matmul(ps_s[pb][:, 0:512],
                                         kn_bf[0:64, 128 * kc:128 * (kc + 1)],
                                         qn_bf[0:64, c0:c0 + 512],
                                         start=True, stop=True, tile_position=(0, 0))
                                mm = T.matmul(ps_s[pb][:, 512:1024],
                                              kn_bf[64:128, 128 * (kc + 1):128 * (kc + 2)],
                                              qn_bf[64:128, c0:c0 + 512],
                                              start=True, stop=True, tile_position=(64, 0))
                                T.inc_last(mm)
                                if pb in t_after_av:
                                    S.wait(T, t_after_av[pb])
                                S.wait(T)
                                S.activation(out=exp_bf[pb][:], in_=ps_s[pb][:], func=AF.Exp,
                                             bias=negC_bc[:, 0:1])
                                s_after_exp[pb] = S.n
                            if p > 0:
                                pb1 = (p - 1) % 3
                                T.wait(S, s_after_exp[pb1])
                                for u in range(2):
                                    kc = 2 * (p - 1) + u
                                    mm = T.matmul(ps_o[:], vo[kc],
                                                  exp_bf[pb1][:, 512 * u:512 * (u + 1)],
                                                  start=(kc == 0), stop=(kc == NKT - 1))
                                T.inc_last(mm)
                                t_after_av[pb1] = T.n
                        V.wait(T)
                        V.tensor_copy(out=o_f32[:], in_=ps_o[:])
                        v_evac[qc] = V.n
                        V.tensor_scalar(out=row_c[:], in0=o_f32[64:65, :], scalar1=1e-30,
                                        scalar2=None, op0=ALU.add)
                        S.wait(V)
                        S.activation(out=row_c[:], in_=row_c[:], func=AF.Ln)
                        S.activation(out=row_c[:], in_=row_c[:], func=AF.Exp, scale=-1.0)
                        if qc >= 1:
                            T.wait(V, v_recbc)
                        T.wait(S)
                        T.inc_last(T.matmul(ps_r[0:64, :], ones1[0:1, 0:64], row_c[:],
                                            start=True, stop=True))
                        V.wait(T)
                        V.tensor_copy(out=rec_bc[:], in_=ps_r[0:64, :])
                        v_recbc = V.n
                        V.wait_sem(st_o, nst[0])
                        V.tensor_mul(out=o_sb[:], in0=o_f32[0:64, :], in1=rec_bc[:])
                        SY.wait_ge(sem_v, V.n)
                        SY.dma_start(out=o_oh[64 * h:64 * (h + 1), c0:c0 + 512],
                                     in_=o_sb[:]).then_inc(st_o, 16)
                        nst[0] += 16
                        cp.step(4)

        cp.drain()
        cp.finish()
        SY.wait_ge(st_o, nst[0])
        G_.wait_ge(st_o, nst[0])
    return nc


def run_F(M_outs, inputs):
    in_maps = []
    for c in range(NC):
        b = c // 4
        j = c % 4
        xatT = np.concatenate([np.asarray(M_outs[4 * b + i]["o_xat"]) for i in range(4)], axis=1)
        hs = slice(256 * j, 256 * (j + 1))
        in_maps.append({
            "xatT": np.ascontiguousarray(xatT.astype(BF)),
            "wqT": f32c(inputs["L1_attn_Wq"].T[:, hs]),
            "wkT": f32c(inputs["L1_attn_Wk"].T[:, hs]),
            "wvT": f32c(inputs["L1_attn_Wv"].T[:, hs]),
            "qg": f32c(inputs["L1_attn_q_gamma"].reshape(64, 1)),
            "kg": f32c(inputs["L1_attn_k_gamma"].reshape(64, 1)),
            "cv_in0": f32c(inputs["L1_ffn_h1_w"].T),
            "cv_in1": f32c(inputs["L1_ffn_h2_w"].T),
            "cv_in2": f32c(inputs["L1_ffn_w1_w"].T),
            "cv_in3": f32c(inputs["L1_ffn_w2_w"].T),
            "cv_in4": f32c(inputs["L1_ffn_w3_w"].T),
            "cv_in5": f32c(inputs["L1_attn_Wo"].T),
        })
    if "F" not in _A_CACHE:
        _A_CACHE["F"] = build_F()
    res = run_bass_kernel_spmd(_A_CACHE["F"], in_maps, core_ids=list(range(NC)))
    return res.results


def build_G():
    nc = bass.Bass("TRN2", target_bir_lowering=False, debug=False)
    H4 = 4096
    xn1T = nc.dram_tensor("xn1T", [D, TCORE], BF16, kind="ExternalInput").ap()
    ohT = nc.dram_tensor("ohT", [D, TCORE], BF16, kind="ExternalInput").ap()
    x1 = nc.dram_tensor("x1", [TCORE, D], F32, kind="ExternalInput").ap()
    woT = nc.dram_tensor("woT", [D, D], BF16, kind="ExternalInput").ap()
    bo = nc.dram_tensor("bo", [1, D], F32, kind="ExternalInput").ap()
    w1T = nc.dram_tensor("w1T", [D, H4], BF16, kind="ExternalInput").ap()
    w2T = nc.dram_tensor("w2T", [D, H4], BF16, kind="ExternalInput").ap()
    h1T = nc.dram_tensor("h1T", [H4, H4], BF16, kind="ExternalInput").ap()
    h2T = nc.dram_tensor("h2T", [H4, H4], BF16, kind="ExternalInput").ap()
    w3T = nc.dram_tensor("w3T", [H4, D], BF16, kind="ExternalInput").ap()
    s1 = nc.dram_tensor("s1", [H4, 1], F32, kind="ExternalInput").ap()
    s2 = nc.dram_tensor("s2", [H4, 1], F32, kind="ExternalInput").ap()
    sh1 = nc.dram_tensor("sh1", [H4, 1], F32, kind="ExternalInput").ap()
    sh2 = nc.dram_tensor("sh2", [H4, 1], F32, kind="ExternalInput").ap()
    s3r = nc.dram_tensor("s3r", [1, D], F32, kind="ExternalInput").ap()
    o_x2 = nc.dram_tensor("o_x2", [TCORE, D], F32, kind="ExternalOutput").ap()

    with ExitStack() as ctx:
        e = ctx.enter_context
        acc = [e(nc.sbuf_tensor(f"acc{t}", [128, D], F32)) for t in range(16)]
        xn1 = [e(nc.sbuf_tensor(f"xn1_{j}", [128, TCORE], BF16)) for j in range(8)]
        bo_bc = e(nc.sbuf_tensor("bo_bc", [128, D], F32))
        s3_bc = e(nc.sbuf_tensor("s3_bc", [128, D], F32))
        bs_row = e(nc.sbuf_tensor("bs_row", [1, 2 * D], F32))
        ones1 = e(nc.sbuf_tensor("ones1", [1, 128], F32))
        sv_all = e(nc.sbuf_tensor("sv_all", [128, 128], F32))
        sv_off = {"s1": 0, "s2": 32, "sh1": 64, "sh2": 96}
        t1s = e(nc.sbuf_tensor("t1s", [128, 512], F32))
        t2s = e(nc.sbuf_tensor("t2s", [128, 512], F32))

        ld = e(nc.semaphore("ld"))
        ldg = e(nc.semaphore("ldg"))
        ldh = [e(nc.semaphore(f"ldh{p}")) for p in range(2)]
        ldw3 = [e(nc.semaphore(f"ldw3{p}")) for p in range(2)]
        sem_v = e(nc.semaphore("sem_v"))
        sem_s = e(nc.semaphore("sem_s"))
        sem_t = e(nc.semaphore("sem_t"))
        st = e(nc.semaphore("st"))

        V = EW(nc.vector, sem_v)
        S = EW(nc.scalar, sem_s)
        T = EW(nc.tensor, sem_t, autosync=False)
        SY = nc.sync
        G_ = nc.gpsimd
        nld = [0]
        nlg = [0]
        nldh = [0, 0]
        nldw3 = [0, 0]
        nst = [0]

        V.memset(ones1[:], 1.0)
        SY.dma_start(out=bs_row[0:1, 0:D], in_=bo).then_inc(ld, 16)
        SY.dma_start(out=bs_row[0:1, D:2 * D], in_=s3r).then_inc(ld, 16)
        for t in range(16):
            SY.dma_start(out=acc[t][:], in_=x1[128 * t:128 * (t + 1), :]).then_inc(ld, 16)
        for j in range(8):
            SY.dma_start(out=xn1[j][:], in_=xn1T[128 * j:128 * (j + 1), :]).then_inc(ld, 16)
        for nm, t in (("s1", s1), ("s2", s2), ("sh1", sh1), ("sh2", sh2)):
            o = sv_off[nm]
            for m in range(32):
                SY.dma_start(out=sv_all[:, o + m:o + m + 1],
                             in_=t[128 * m:128 * (m + 1), :]).then_inc(ld, 16)
        nld[0] += 16 * (2 + 16 + 8 + 128)
        LD_BASE = nld[0]

        with ExitStack() as c0:
            t0 = c0.enter_context
            ps_b = [t0(nc.psum_tensor(f"ps_bc{b}", [128, 512], F32)) for b in range(2)]
            T.wait_sem(ld, LD_BASE)
            T.wait(V)
            for half in range(2):
                T.inc_last(T.matmul(ps_b[half][:], ones1[0:1, :],
                                    bs_row[0:1, 512 * half:512 * (half + 1)],
                                    start=True, stop=True))
            V.wait(T)
            V.tensor_copy(out=bo_bc[:, 0:512], in_=ps_b[0][:])
            V.tensor_copy(out=bo_bc[:, 512:1024], in_=ps_b[1][:])
            for half in range(2):
                T.wait(V)
                T.inc_last(T.matmul(ps_b[half][:], ones1[0:1, :],
                                    bs_row[0:1, D + 512 * half:D + 512 * (half + 1)],
                                    start=True, stop=True))
            V.wait(T)
            V.tensor_copy(out=s3_bc[:, 0:512], in_=ps_b[0][:])
            V.tensor_copy(out=s3_bc[:, 512:1024], in_=ps_b[1][:])

        phase_barrier([nc.vector, nc.scalar, nc.tensor, nc.sync, nc.gpsimd],
                      [V, S, T], [(ld, nld[0])])
        with ExitStack() as c1:
            t1 = c1.enter_context
            oh_sb = [t1(nc.sbuf_tensor(f"oh{j}", [128, TCORE], BF16)) for j in range(8)]
            wo_bf = [t1(nc.sbuf_tensor(f"wo{j}", [128, D], BF16)) for j in range(8)]
            ps_m = [t1(nc.psum_tensor(f"ps_m{i}", [128, 512], F32)) for i in range(4)]
            for j in range(8):
                SY.dma_start(out=oh_sb[j][:], in_=ohT[128 * j:128 * (j + 1), :]).then_inc(ld, 16)
                SY.dma_start(out=wo_bf[j][:], in_=woT[128 * j:128 * (j + 1), :]).then_inc(ldg, 16)
            nld[0] += 16 * 8
            nlg[0] += 16 * 8
            T.wait_sem(ld, nld[0])
            T.wait_sem(ldg, nlg[0])
            ev_ps = {}
            for ts in range(16):
                for nh in range(2):
                    gi = 2 * ts + nh
                    pi = gi % 4
                    if gi >= 4:
                        eng, cnt = ev_ps[pi]
                        T.wait(eng, cnt)
                    for kt in range(8):
                        mm = T.matmul(ps_m[pi][:], oh_sb[kt][:, 128 * ts:128 * (ts + 1)],
                                      wo_bf[kt][:, 512 * nh:512 * (nh + 1)],
                                      start=(kt == 0), stop=(kt == 7))
                    T.inc_last(mm)
                    V.wait(T)
                    V.tensor_add(out=acc[ts][:, 512 * nh:512 * (nh + 1)],
                                 in0=acc[ts][:, 512 * nh:512 * (nh + 1)], in1=ps_m[pi][:])
                    ev_ps[pi] = (V, V.n)
                V.tensor_add(out=acc[ts][:], in0=acc[ts][:], in1=bo_bc[:])

        phase_barrier([nc.vector, nc.scalar, nc.tensor, nc.sync, nc.gpsimd],
                      [V, S, T], [(ld, nld[0]), (ldg, nlg[0])])
        with ExitStack() as c2:
            t2 = c2.enter_context
            h_bf = [t2(nc.sbuf_tensor(f"hbf{k}", [128, 512], BF16)) for k in range(32)]
            hh_bf = [t2(nc.sbuf_tensor(f"hh{k}", [128, 512], BF16)) for k in range(32)]
            wsl = [t2(nc.sbuf_tensor("wsl0", [128, 2 * 8 * 128], BF16))]
            hsl = [t2(nc.sbuf_tensor(f"hsl{p}", [128, 32 * 128], BF16)) for p in range(2)]
            w3b = [t2(nc.sbuf_tensor(f"w3b{p}", [128, D], BF16)) for p in range(2)]
            ps_p = [t2(nc.psum_tensor(f"ps_p{i}", [128, 512], F32)) for i in range(4)]
            ps_q5 = [t2(nc.psum_tensor(f"ps_q5{i}", [128, 512], F32)) for i in range(4)]

            def load_w12(hm, buf):
                SY.dma_start(out=wsl[buf][:, 0:1024].rearrange("p (k c) -> p k c", k=8),
                             in_=w1T[:, 128 * hm:128 * (hm + 1)].rearrange(
                                 "(k p) c -> p k c", p=128)).then_inc(ldh[buf], 16)
                SY.dma_start(out=wsl[buf][:, 1024:2048].rearrange("p (k c) -> p k c", k=8),
                             in_=w2T[:, 128 * hm:128 * (hm + 1)].rearrange(
                                 "(k p) c -> p k c", p=128)).then_inc(ldh[buf], 16)
                nldh[buf] += 32

            def load_hsl(mat, om, buf):
                for a in range(4):
                    SY.dma_start(
                        out=hsl[buf][:, 1024 * a:1024 * (a + 1)].rearrange(
                            "p (k c) -> p k c", k=8),
                        in_=mat[1024 * a:1024 * (a + 1),
                                128 * om:128 * (om + 1)].rearrange("(k p) c -> p k c", p=128)
                    ).then_inc(ldh[buf], 16)
                nldh[buf] += 16 * 4

            sidx = [0]
            t_after_s = {}
            ev_pp = {}
            for tq in range(4):
                tb = 512 * tq
                V.wait(T)
                for hm in range(32):
                    si = sidx[0]
                    sb = 0
                    if si >= 1:
                        SY.wait_ge(sem_t, t_after_s[si - 1])
                    load_w12(hm, sb)
                    wrdy = nldh[sb]
                    sidx[0] += 1
                    T.wait_sem(ldh[sb], wrdy)
                    pp = hm % 2
                    pa, pb = ps_p[2 * pp], ps_p[2 * pp + 1]
                    if ("g2", pp) in ev_pp:
                        eng, cnt = ev_pp[("g2", pp)]
                        T.wait(eng, cnt)
                    for kt in range(8):
                        T.matmul(pa[:], wsl[sb][:, 128 * kt:128 * (kt + 1)],
                                 xn1[kt][:, tb:tb + 512], start=(kt == 0), stop=(kt == 7))
                    for kt in range(8):
                        mm = T.matmul(pb[:], wsl[sb][:, 1024 + 128 * kt:1024 + 128 * (kt + 1)],
                                      xn1[kt][:, tb:tb + 512], start=(kt == 0), stop=(kt == 7))
                    T.inc_last(mm)
                    t_after_s[si] = T.n
                    S.wait(T)
                    S.wait(V)
                    S.activation(out=t1s[:], in_=pa[:], func=AF.Silu,
                                 scale=sv_all[:, sv_off["s1"] + hm:sv_off["s1"] + hm + 1])
                    V.wait(S)
                    V.tensor_scalar(out=t2s[:], in0=pb[:],
                                    scalar1=sv_all[:, sv_off["s2"] + hm:sv_off["s2"] + hm + 1],
                                    scalar2=None, op0=ALU.mult)
                    V.tensor_mul(out=h_bf[hm][:], in0=t1s[:], in1=t2s[:])
                    ev_pp[("g2", pp)] = (V, V.n)
                for layer, mat, svec, src_t, dst_t in (
                        (0, h1T, "sh1", h_bf, hh_bf), (1, h2T, "sh2", hh_bf, h_bf)):
                    hrdy = {}
                    t_after_om = {}
                    SY.wait_ge(sem_t, T.n)
                    load_hsl(mat, 0, 0)
                    hrdy[0] = (0, nldh[0])
                    for om in range(32):
                        sb = om % 2
                        if om + 1 < 32:
                            nb = (om + 1) % 2
                            if om >= 1:
                                SY.wait_ge(sem_t, t_after_om[om - 1])
                            load_hsl(mat, om + 1, nb)
                            hrdy[om + 1] = (nb, nldh[nb])
                        rb, rv = hrdy[om]
                        T.wait_sem(ldh[rb], rv)
                        pp = om % 2
                        pa = ps_p[2 * pp]
                        if ("g3", pp) in ev_pp:
                            eng, cnt = ev_pp[("g3", pp)]
                            T.wait(eng, cnt)
                        if om == 0:
                            T.wait(S)
                            T.wait(V)
                        for kt in range(32):
                            mm = T.matmul(pa[:], hsl[sb][:, 128 * kt:128 * (kt + 1)],
                                          src_t[kt][:], start=(kt == 0), stop=(kt == 31))
                        T.inc_last(mm)
                        t_after_om[om] = T.n
                        S.wait(T)
                        S.activation(out=dst_t[om][:], in_=pa[:], func=AF.Gelu,
                                     scale=sv_all[:, sv_off[svec] + om:sv_off[svec] + om + 1])
                        ev_pp[("g3", pp)] = (S, S.n)
                psall = ps_p + ps_q5
                T.wait(S)
                T.wait(V)
                ldw3_r = {}
                t_after_w3 = {}
                nb0 = (2 * tq) % 2
                SY.wait_ge(sem_t, T.n)
                SY.dma_start(out=w3b[0][:], in_=w3T[0:128, :]).then_inc(ldw3[0], 16)
                nldw3[0] += 16
                ldw3_r[0] = nldw3[0]
                for kt in range(32):
                    kb = kt % 2
                    if kt + 1 < 32:
                        nb = (kt + 1) % 2
                        if kt >= 1:
                            SY.wait_ge(sem_t, t_after_w3[kt - 1])
                        SY.dma_start(out=w3b[nb][:],
                                     in_=w3T[128 * (kt + 1):128 * (kt + 2), :]).then_inc(ldw3[nb], 16)
                        nldw3[nb] += 16
                        ldw3_r[kt + 1] = nldw3[nb]
                    T.wait_sem(ldw3[kb], ldw3_r[kt])
                    for gi in range(8):
                        tsl, nh = divmod(gi, 2)
                        mm = T.matmul(psall[gi][:], h_bf[kt][:, 128 * tsl:128 * (tsl + 1)],
                                      w3b[kb][:, 512 * nh:512 * (nh + 1)],
                                      start=(kt == 0), stop=(kt == 31))
                    T.inc_last(mm)
                    t_after_w3[kt] = T.n
                V.wait(T)
                for gi in range(8):
                    tsl, nh = divmod(gi, 2)
                    ts = 4 * tq + tsl
                    V.tensor_mul(out=t1s[:], in0=psall[gi][:],
                                 in1=s3_bc[:, 512 * nh:512 * (nh + 1)])
                    V.tensor_add(out=acc[ts][:, 512 * nh:512 * (nh + 1)],
                                 in0=acc[ts][:, 512 * nh:512 * (nh + 1)], in1=t1s[:])
                for key in (("g2", 0), ("g2", 1), ("g3", 0), ("g3", 1)):
                    ev_pp[key] = (V, V.n)
                for tsl in range(4):
                    ts = 4 * tq + tsl
                    SY.wait_ge(sem_v, V.n)
                    SY.dma_start(out=o_x2[128 * ts:128 * (ts + 1), :],
                                 in_=acc[ts][:]).then_inc(st, 16)
                    nst[0] += 16
            SY.wait_ge(st, nst[0])
            G_.wait_ge(st, nst[0])
    return nc


def run_G(M_outs, F_outs, inputs):
    in_maps = []
    for c in range(NC):
        b = c // 4
        ohT_b = np.concatenate([np.asarray(F_outs[4 * b + i]["o_oh"]) for i in range(4)], axis=0)
        cols = slice(TCORE * (c % 4), TCORE * (c % 4 + 1))
        in_maps.append({
            "xn1T": np.ascontiguousarray(np.asarray(M_outs[c]["o_xn1"]).astype(BF)),
            "ohT": np.ascontiguousarray(ohT_b[:, cols].astype(BF)),
            "x1": f32c(np.asarray(M_outs[c]["o_x1"])),
            "woT": np.ascontiguousarray(np.asarray(F_outs[c]["cv_out5"]).astype(BF)),
            "bo": f32c(inputs["L1_attn_bo"].reshape(1, D)),
            "w1T": np.ascontiguousarray(np.asarray(F_outs[c]["cv_out2"]).astype(BF)),
            "w2T": np.ascontiguousarray(np.asarray(F_outs[c]["cv_out3"]).astype(BF)),
            "h1T": np.ascontiguousarray(np.asarray(F_outs[c]["cv_out0"]).astype(BF)),
            "h2T": np.ascontiguousarray(np.asarray(F_outs[c]["cv_out1"]).astype(BF)),
            "w3T": np.ascontiguousarray(np.asarray(F_outs[c]["cv_out4"]).astype(BF)),
            "s1": f32c(inputs["L1_ffn_w1_s"].reshape(-1, 1)),
            "s2": f32c(inputs["L1_ffn_w2_s"].reshape(-1, 1)),
            "sh1": f32c(inputs["L1_ffn_h1_s"].reshape(-1, 1)),
            "sh2": f32c(inputs["L1_ffn_h2_s"].reshape(-1, 1)),
            "s3r": f32c(inputs["L1_ffn_w3_s"].reshape(1, D)),
        })
    if "G" not in _A_CACHE:
        _A_CACHE["G"] = build_G()
    res = run_bass_kernel_spmd(_A_CACHE["G"], in_maps, core_ids=list(range(NC)))
    return res.results


    return nc


_A_CACHE = {}


def run_A(inputs):
    x = inputs["L0_attn_x"].reshape(16384, D)
    in_maps = []
    for c in range(NC):
        t0 = TCORE * c
        if c % 4 == 0:
            xh = np.concatenate([np.zeros((HALO, D), np.float32),
                                 x[t0:t0 + TCORE]], axis=0)
        else:
            xh = x[t0 - HALO:t0 + TCORE]
        in_maps.append({
            "xi": f32c(xh),
            "ipw": f32c(inputs["L0_attn_in_proj_w"]),
            "convw": f32c(inputs["L0_attn_conv_w"]),
            "convb": f32c(inputs["L0_attn_conv_b"].reshape(DI, 1)),
            "xpw": f32c(inputs["L0_attn_x_proj_w"]),
            "dtw": f32c(inputs["L0_attn_dt_proj_w"]),
            "dtb": f32c(inputs["L0_attn_dt_proj_b"].reshape(DI, 1)),
            "wgate": f32c(inputs["L0_ffn_w_gate"]),
        })
    if "A" not in _A_CACHE:
        _A_CACHE["A"] = build_A()
    res = run_bass_kernel_spmd(_A_CACHE["A"], in_maps, core_ids=list(range(NC)))
    return res.results


def kernel(**inputs):
    A = run_A(inputs)
    Dr = run_D(A, inputs)
    M = run_M(A, Dr, inputs)
    F = run_F(M, inputs)
    G = run_G(M, F, inputs)
    x2 = np.concatenate([np.asarray(G[c]["o_x2"]) for c in range(NC)], axis=0)
    return np.ascontiguousarray(x2.reshape(2, 8192, D), dtype=np.float32)
